# version 04
# baseline (speedup 1.0000x reference)
import os
import numpy as np
import ml_dtypes
from contextlib import ExitStack

ATH_PH = os.environ.get("ATH_PH", "full")
ATH_TC = os.environ.get("ATH_TC", "full")

import concourse.bass as bass
import concourse.mybir as mybir
from concourse.bass_utils import run_bass_kernel_spmd

F32 = mybir.dt.float32
BF16 = mybir.dt.bfloat16
I32 = mybir.dt.int32
U32 = mybir.dt.uint32
AF = mybir.ActivationFunctionType
ALU = mybir.AluOpType
AX = mybir.AxisListType

T = 4096
DM = 1024
BLK = 256
NBLK = 8
TOWN = 2048
L0H = 8
L1H = 16
DCV, DRP = 256, 128
FF0, FF1 = 3072, 4096
QG = 256
NQG = 8
EPS = 1e-5

AB_COLS = 45600
AF_COLS = 18432
STRM = 0
FS = 16384
TKA, TKB = 0, 4096
TQA, TQB = 8192, 10240

XT = 0
KTA = 32768
QTA = 36864
VA = 38912
VW0 = 129
EA = 43040
OST0 = 45088
TSC = 45600

XTO = 0
ORB = 16384

F0_WS = [16384, 28672]
F0_GS = [40960, 41984]
F0_DS = [43008, 43520]
F0_SIL = 44032
F0_ST = 44544

L1_X2T = 0
L1_CKV = 32768
L1_KRG = 40960
PSTG = 45056
PSTG = 32768

L1_KT = 0
L1_QT = 8192
L1_V = 12288
VW1 = 260
L1_E = 20608
L1_OST = 22656

F1_WS = [16384, 24576]
F1_TS = 32768
F1_HT = 35840
F1_GS = [FS + 1024]



def _perm_blocks(r):
    return [p + r if p % 2 == 0 else p - r for p in range(2 * NBLK)]


def _rope_tabs(pos):
    i = np.arange(64, dtype=np.float64)
    invf = np.power(10000.0, -i / 64.0)
    rad = pos[None, :].astype(np.float64) * invf[:, None]
    cos = np.cos(rad).astype(np.float32)
    sin = np.sin(rad).astype(np.float32)
    return (np.concatenate([cos, cos], axis=0),
            np.concatenate([-sin, sin], axis=0))


def _masks(r):
    k = np.arange(128)[:, None]
    q = np.arange(QG)[None, :]
    tri0 = (k <= q).astype(np.float32)
    tri1 = ((128 + k) <= q).astype(np.float32)
    full = np.full((128, QG), 1.0 if r == 1 else 0.0, np.float32)
    return np.concatenate([tri0, tri1, full, full], axis=1).astype(ml_dtypes.bfloat16)


def _selA():
    m = np.zeros((128, 144), np.float32)
    for tc in range(12):
        m[:, tc * 12 + tc] = 1.0
    return m.astype(ml_dtypes.bfloat16)


def _selB():
    m = np.zeros((128, 32), np.float32)
    for v in range(4):
        base = 2 * v
        m[0:64, v * 8 + base] = 1.0
        m[64:128, v * 8 + base + 1] = 1.0
    return m.astype(ml_dtypes.bfloat16)


def _rot_halves(w, width):
    g = w.shape[1] // width
    return np.ascontiguousarray(
        w.reshape(w.shape[0], g, 2, width // 2)[:, :, ::-1, :].reshape(w.shape))


def make_core_inputs(inputs, s, r):
    x = np.asarray(inputs["L0_attn_x"], dtype=np.float32)
    perm = _perm_blocks(r)
    rows = np.concatenate([np.arange(b * BLK, (b + 1) * BLK) for b in perm])
    x_perm = np.ascontiguousarray(x[s][rows])
    xT_perm = np.ascontiguousarray(x_perm.T)
    own_rows = rows.reshape(2 * NBLK, BLK)[0::2].reshape(-1)
    tka, tkb = _rope_tabs(rows.astype(np.float64))
    tqa, tqb = _rope_tabs(own_rows.astype(np.float64))
    f32 = lambda a: np.ascontiguousarray(np.asarray(a, dtype=np.float32))
    d = {
        "x_perm": x_perm, "xT_perm": xT_perm,
        "tabk_a": tka, "tabk_b": tkb, "tabq_a": tqa, "tabq_b": tqb,
        "masks": _masks(r),
        "ident": np.eye(128, dtype=ml_dtypes.bfloat16),
        "ones_b": np.ones((128, 128), dtype=ml_dtypes.bfloat16),
        "ones_f": np.ones((128, 128), dtype=np.float32),
        "selA": _selA(), "selB": _selB(),
        "peer_off": np.array([[(1 - r) * TOWN * DM]], dtype=np.uint32),
    }
    d["l0_wq"] = f32(np.asarray(inputs["L0_attn_wq"]).reshape(DM, DM))
    d["l0_wk"] = f32(np.asarray(inputs["L0_attn_wk"]).reshape(DM, DM))
    d["l0_wv"] = f32(np.asarray(inputs["L0_attn_wv"]).reshape(DM, DM))
    d["l0_wo"] = f32(np.asarray(inputs["L0_attn_wo"]).reshape(DM, DM))
    d["l0_wq_rot"] = _rot_halves(d["l0_wq"], 128)
    d["l0_wk_rot"] = _rot_halves(d["l0_wk"], 128)
    for nm_src, nm in (("L0_ffn_w_gate", "l0_wg"), ("L0_ffn_w_up", "l0_wu"),
                       ("L0_ffn_w_down", "l0_wd")):
        d[nm] = np.ascontiguousarray(np.asarray(inputs[nm_src], dtype=np.int32))
    for nm_src, nm in (("L1_attn_wq", "l1_wq"), ("L1_attn_wqc", "l1_wqc"),
                       ("L1_attn_wqr", "l1_wqr"), ("L1_attn_wkv", "l1_wkv"),
                       ("L1_attn_wkr", "l1_wkr"), ("L1_attn_wkc", "l1_wkc"),
                       ("L1_attn_wv", "l1_wv"), ("L1_attn_wo", "l1_wo"),
                       ("L1_ffn_W1", "l1_W1"), ("L1_ffn_W2", "l1_W2")):
        d[nm] = f32(inputs[nm_src])
    d["l1_wqr_rot"] = _rot_halves(d["l1_wqr"], 128)
    d["l1_wkr_rot"] = _rot_halves(d["l1_wkr"], 128)

    def cols(v):
        v = np.asarray(v, dtype=np.float32).reshape(-1)
        return np.ascontiguousarray(v.reshape(-1, 128).T)

    def rot_col(v):
        v = np.asarray(v, dtype=np.float32).reshape(128)
        return np.ascontiguousarray(np.concatenate([v[64:], v[:64]]).reshape(128, 1))

    d["b_bq"] = cols(inputs["L1_attn_bq"])
    d["b_bqc"] = cols(inputs["L1_attn_bqc"])
    d["b_bqr"] = cols(inputs["L1_attn_bqr"])
    d["b_bqr_rot"] = rot_col(inputs["L1_attn_bqr"])
    d["b_bkv"] = cols(inputs["L1_attn_bkv"])
    d["b_bkr"] = cols(inputs["L1_attn_bkr"])
    d["b_bkr_rot"] = rot_col(inputs["L1_attn_bkr"])
    d["b_bkc"] = cols(inputs["L1_attn_bkc"])
    d["b_b1"] = cols(inputs["L1_ffn_b1"])
    bv = np.asarray(inputs["L1_attn_bv"], dtype=np.float32).reshape(L1H, 64).T
    d["b_bv"] = np.ascontiguousarray(np.concatenate([bv, np.zeros((64, L1H), np.float32)]))
    d["r_bo"] = np.asarray(inputs["L1_attn_bo"], dtype=np.float32).reshape(1, DM)
    d["r_b2"] = np.asarray(inputs["L1_ffn_b2"], dtype=np.float32).reshape(1, DM)
    return d



class Sem:
    def __init__(self, ctx, nc, name):
        self.h = ctx.enter_context(nc.semaphore(name))
        self.n = 0


class B:
    pass


ENGS = ("t", "s", "v", "g", "y")


def em(eng, fn):
    B.streams[eng].append(fn)


def barrier(name):
    if not hasattr(B, "barsem"):
        B.barsem = {k: Sem(B.ctx, B.nc, "bar_" + k) for k in ENGS}
        B.barn = 0
    B.barn += 1
    n = B.barn
    for k in ENGS:
        qs = ()
        if hasattr(B, "qsem"):
            qs = tuple((B.qsem[q].h, B.qsem[q].n) for q in B.qsem
                       if q.startswith(k) and B.qsem[q].n > 0)
        def _f(e, _h=B.barsem[k].h, _k=k, _qs=qs):
            for (_qh, _qn) in _qs:
                e.wait_ge(_qh, _qn)
            if _k != "y":
                e.drain()
            e.nop().then_inc(_h, 1)
        em(k, _f)
    for k in ENGS:
        for k2 in ENGS:
            if k2 != k:
                em(k, lambda e, _h=B.barsem[k2].h, _n=n: e.wait_ge(_h, _n))


def dma(q, dst, src, waits=()):
    sem = B.qsem[q]
    eng = "y" if q.startswith("y") else "g"

    def f(e, _d=dst, _s=src, _h=sem.h, _w=tuple(waits)):
        for (sm, val) in _w:
            e.wait_ge(sm.h, val)
        e.dma_start(out=_d, in_=_s).then_inc(_h, 16)
    em(eng, f)
    sem.n += 16
    return sem.n


def t_mm(out, lhsT, rhs, start, stop, waits=(), inc=False):
    s_t = B.S["t"]

    def f(e, _o=out, _l=lhsT, _r=rhs, _a=start, _b=stop, _w=tuple(waits), _i=inc):
        for (sm, val) in _w:
            e.wait_ge(sm.h, val)
        ins = e.matmul(_o, _l, _r, start=_a, stop=_b)
        if _i:
            ins.then_inc(s_t.h, 1)
    em("t", f)
    if inc:
        s_t.n += 1
    return s_t.n


def vc(e, ins, eng="v"):
    sem = B.chain[eng]
    B.chain_n[eng] += 1
    ins.then_inc(sem.h, 1)
    e.wait_ge(sem.h, B.chain_n[eng])


def e_op(eng, fn, waits=(), inc=False):
    s = B.S[eng]

    def f(e, _f=fn, _w=tuple(waits), _i=inc):
        for (sm, val) in _w:
            e.wait_ge(sm.h, val)
        ins = _f(e)
        if _i:
            ins.then_inc(s.h, 1)
    em(eng, f)
    if inc:
        s.n += 1
    return s.n


def AP(t, off, pat):
    return bass.AP(t, off, pat)



_CACHE = {}

WB_COLS = 20480
W_SLOT = [0, 5120]
W_KIND = {"l0_wq": 0, "l0_wq_rot": 1024, "l0_wk": 2048, "l0_wk_rot": 3072,
          "l0_wv": 4096}
W_WO = 0
W_C = {"l1_wq": 0, "l1_wqr": 2048, "l1_wqr_rot": 3072, "l1_wkv": 4096,
       "l1_wkr": 6144, "l1_wkr_rot": 7168}
W_PS = [8192, 9728]
W_BO = 11264
W_B2 = 12288
W_CQ = 13312
W_QRG = 17408


def build_program():
    if "nc" in _CACHE:
        return _CACHE["nc"]
    nc = bass.Bass("TRN2", target_bir_lowering=False)
    ctx = ExitStack()
    B.nc, B.ctx = nc, ctx
    B.streams = {k: [] for k in ENGS}
    B.S = {k: Sem(ctx, nc, "p_" + k) for k in ("t", "s", "v", "g")}
    B.qsem = {k: Sem(ctx, nc, "q_" + k) for k in
              ("y", "g", "gw0", "gw1", "gu0", "gu1", "gd0", "gd1", "go",
               "ys", "yc0", "yc1", "gp0", "gp1", "yx")}
    B.ccs = Sem(ctx, nc, "q_cc")
    B.chain = {"v": Sem(ctx, nc, "ch_v"), "s": Sem(ctx, nc, "ch_s")}
    B.chain_n = {"v": 0, "s": 0}
    B._ptev = {0: 0, 1: 0}

    d = {}
    B.d = d

    def din(name, shape, dt):
        d[name] = nc.dram_tensor(name, shape, dt, kind="ExternalInput").ap()

    din("x_perm", [T, DM], F32)
    din("xT_perm", [DM, T], F32)
    din("tabk_a", [128, T], F32)
    din("tabk_b", [128, T], F32)
    din("tabq_a", [128, TOWN], F32)
    din("tabq_b", [128, TOWN], F32)
    din("masks", [128, 4 * QG], BF16)
    din("ident", [128, 128], BF16)
    din("ones_b", [128, 128], BF16)
    din("ones_f", [128, 128], F32)
    din("selA", [128, 144], BF16)
    din("selB", [128, 32], BF16)
    din("peer_off", [1, 1], U32)
    for nm in ("l0_wq", "l0_wk", "l0_wv", "l0_wo", "l0_wq_rot", "l0_wk_rot", "l1_wo"):
        din(nm, [DM, DM], F32)
    din("l0_wg", [FF0, DM], I32)
    din("l0_wu", [FF0, DM], I32)
    din("l0_wd", [DM, FF0], I32)
    din("l1_wq", [DM, DCV], F32)
    din("l1_wqc", [DCV, DM - DRP], F32)
    din("l1_wqr", [DM, DRP], F32)
    din("l1_wqr_rot", [DM, DRP], F32)
    din("l1_wkv", [DM, DCV], F32)
    din("l1_wkr", [DM, DRP], F32)
    din("l1_wkr_rot", [DM, DRP], F32)
    din("l1_wkc", [DCV, DM - DRP], F32)
    din("l1_wv", [DCV, DM], F32)
    din("l1_W1", [FF1, DM], F32)
    din("l1_W2", [DM, FF1], F32)
    for nm, n in (("b_bq", 2), ("b_bqc", 7), ("b_bqr", 1), ("b_bqr_rot", 1),
                  ("b_bkv", 2), ("b_bkr", 1), ("b_bkr_rot", 1), ("b_bkc", 7),
                  ("b_b1", 32), ("b_bv", 16)):
        din(nm, [128, n], F32)
    din("r_bo", [1, DM], F32)
    din("r_b2", [1, DM], F32)
    d["y"] = nc.dram_tensor("y", [TOWN, DM], F32, kind="ExternalOutput").ap()
    d["cc_in"] = nc.dram_tensor("cc_in", [TOWN, DM], BF16).ap()
    d["cc_out"] = nc.dram_tensor("cc_out", [2 * TOWN, DM], BF16).ap()
    d["spill_x"] = nc.dram_tensor("spill_x", [TOWN, DM], F32).ap()
    d["spill_o"] = nc.dram_tensor("spill_o", [DM, TOWN], BF16).ap()

    sb = {}
    B.sb = sb

    def sbuf(name, shape, dt):
        sb[name] = ctx.enter_context(nc.sbuf_tensor("s_" + name, shape, dt))
        return sb[name]

    B.AB = sbuf("ABar", [128, AB_COLS], BF16)
    B.AF = sbuf("AFar", [128, AF_COLS], F32)
    B.WB = sbuf("WBar", [128, WB_COLS], BF16)
    for nm, shape, dt in (
            ("ident", [128, 128], BF16), ("ones_b", [128, 128], BF16),
            ("ones_f", [128, 128], F32), ("selA", [128, 144], BF16),
            ("selB", [128, 32], BF16), ("masks", [128, 4 * QG], BF16),
            ("cbias", [128, 32], F32), ("nscr", [128, 8], F32),
            ("tiny", [128, 16], F32), ("tinyb", [128, 16], BF16),
            ("sumb", [128, 512], F32), ("lnscr", [128, 8], F32),
            ("u32s", [1, 1], U32), ("thr", [128, 4], F32),
            ("asum", [128, 4], F32)):
        sbuf(nm, shape, dt)
    for nm, n in (("b_bq", 2), ("b_bqc", 7), ("b_bqr", 1), ("b_bqr_rot", 1),
                  ("b_bkv", 2), ("b_bkr", 1), ("b_bkr_rot", 1), ("b_bkc", 7),
                  ("b_b1", 32), ("b_bv", 16)):
        sbuf(nm, [128, n], F32)

    ps = {}
    B.ps = ps
    ps["pS0"] = ctx.enter_context(nc.psum_tensor("pS0", [128, 1024], F32))
    ps["pS1"] = ctx.enter_context(nc.psum_tensor("pS1", [128, 1024], F32))
    ps["pAV"] = ctx.enter_context(nc.psum_tensor("pAV", [128, 512], F32))
    ps["pA"] = ctx.enter_context(nc.psum_tensor("pA", [128, 512], F32))
    ps["pB"] = ctx.enter_context(nc.psum_tensor("pB", [128, 512], F32))
    ps["pT"] = ctx.enter_context(nc.psum_tensor("pT", [128, 1024], BF16))

    _emit()

    with nc.Block() as block:
        @block.tensor
        def _(e):
            for fn in B.streams["t"]:
                fn(e)

        @block.scalar
        def _(e):
            for fn in B.streams["s"]:
                fn(e)

        @block.vector
        def _(e):
            for fn in B.streams["v"]:
                fn(e)

        @block.gpsimd
        def _(e):
            for fn in B.streams["g"]:
                fn(e)

        @block.sync
        def _(e):
            for fn in B.streams["y"]:
                fn(e)

    _CACHE["nc"] = nc
    return nc


def _load_l0_head_w(h, waits=()):
    slot = W_SLOT[h % 2]
    q = "gw0" if h % 2 == 0 else "gw1"
    n = 0
    for nm, ko in W_KIND.items():
        src = AP(B.d[nm].tensor, h * 128, [[1024, 128], [131072, 8], [1, 128]])
        dst = AP(B.WB, slot + ko, [[WB_COLS, 128], [128, 8], [1, 128]])
        n = dma(q, dst, src, waits=waits)
        waits = ()
    return n


def _load_w_mk(q, nm, ncols, nchunk, woff, waits=()):
    src = B.d[nm]
    dst = AP(B.WB, woff, [[WB_COLS, 128], [ncols, nchunk], [1, ncols]])
    sap = AP(src.tensor, 0, [[ncols, 128], [128 * ncols, nchunk], [1, ncols]])
    return dma(q, dst, sap, waits=waits)


def _emit():
    d, sb, AB, AFt = B.d, B.sb, B.AB, B.AF

    for nm in ("ident", "ones_b", "ones_f", "selA", "selB", "masks",
               "b_bq", "b_bqc", "b_bqr", "b_bqr_rot", "b_bkv", "b_bkr",
               "b_bkr_rot", "b_bkc", "b_b1", "b_bv"):
        dma("y", sb[nm][:], d[nm])
    dma("y", sb["u32s"][:], d["peer_off"])
    dma("y", AFt[:, TKA:TKA + T], d["tabk_a"])
    dma("y", AFt[:, TKB:TKB + T], d["tabk_b"])
    dma("y", AFt[:, TQA:TQA + TOWN], d["tabq_a"])
    dma("y", AFt[:, TQB:TQB + TOWN], d["tabq_b"])
    for mc in range(8):
        dma("g", AB[:, XT + mc * T:XT + (mc + 1) * T],
            d["xT_perm"][mc * 128:(mc + 1) * 128, :])
    n0 = _load_l0_head_w(0)
    n1 = _load_l0_head_w(1)

    nld, ngl = B.qsem["y"].n, B.qsem["g"].n
    for k in ENGS:
        em(k, lambda e, _h=B.qsem["y"].h, _n=nld: e.wait_ge(_h, _n))
        em(k, lambda e, _h=B.qsem["g"].h, _n=ngl: e.wait_ge(_h, _n))

    B._attn_prev_ev = None
    B._attn_ost_dma = None
    B._l0_wld = {0: n0, 1: n1}

    def cut(ph):
        if ATH_PH == ph:
            _dbg_out()
            return True
        return False

    if cut("none"):
        return
    nheads = L0H if ATH_PH not in ("h1", "h2") else int(ATH_PH[1])
    for h in range(nheads):
        _proj_head_l0(h)
        _bound_l0(h)
        _attn_head(0, h)
    if cut("l0a") or ATH_PH in ("h1", "h2") and (_dbg_out() or True):
        return
    _tail(0)
    if cut("tail0"):
        return
    _ffn0()
    if cut("ffn0"):
        return
    _exchange()
    if cut("xc"):
        return
    _l1_proj_common()
    if cut("l1p"):
        return
    for p in range(4):
        _l1_pass(p)
        if cut(f"pass{p}"):
            return
    _tail(1)
    if cut("tail1"):
        return
    _ffn1()
    if cut("ffn1"):
        return
    _final_ln_out()


def _dbg_out():
    barrier("dbg")
    for tt in range(16):
        dma("y", B.d["y"][tt * 128:(tt + 1) * 128, :],
            B.AF[:, STRM + tt * 1024: STRM + (tt + 1) * 1024])
    em("y", lambda e, _h=B.qsem["y"].h, _n=B.qsem["y"].n: e.wait_ge(_h, _n))
    em("g", lambda e, _h=B.qsem["g"].h, _n=B.qsem["g"].n: e.wait_ge(_h, _n))



def _rope_unit(kind, idx, h, unit, rope_done, extra_w=()):
    AB, WB, ps = B.AB, B.WB, B.ps
    S = B.S
    w = 512 if kind == "k" else 256
    slot = W_SLOT[h % 2]
    if kind == "k":
        rhs_off = idx * 512
        wofA, wofB = slot + W_KIND["l0_wk"], slot + W_KIND["l0_wk_rot"]
        dst = KTA + idx * 512
        tba, tbb = TKA + idx * 512, TKB + idx * 512
    else:
        rhs_off = (2 * idx) * BLK
        wofA, wofB = slot + W_KIND["l0_wq"], slot + W_KIND["l0_wq_rot"]
        dst = QTA + idx * 256
        tba, tbb = TQA + idx * 256, TQB + idx * 256
    if unit % 2 == 0:
        pA, pB = ps["pA"][:, 0:w], ps["pB"][:, 0:w]
    else:
        pA, pB = ps["pS0"][:, 0:w], ps["pS0"][:, 512:512 + w]
    waits = tuple(extra_w)
    if unit - 2 in rope_done and rope_done[unit - 2] is not None:
        waits += ((S["v"], rope_done[unit - 2]),)
    for mc in range(8):
        t_mm(pA, WB[:, wofA + mc * 128: wofA + mc * 128 + 128],
             AB[:, XT + mc * T + rhs_off: XT + mc * T + rhs_off + w],
             start=(mc == 0), stop=(mc == 7), waits=(waits if mc == 0 else ()))
    for mc in range(8):
        t_mm(pB, WB[:, wofB + mc * 128: wofB + mc * 128 + 128],
             AB[:, XT + mc * T + rhs_off: XT + mc * T + rhs_off + w],
             start=(mc == 0), stop=(mc == 7), inc=(mc == 7))
    tN = S["t"].n

    def f(e, _pA=pA, _pB=pB, _dst=dst, _ta=tba, _tb=tbb, _w=w, _u=unit):
        AFl = B.AF
        fsb = FS + (_u % 2) * 1024
        t1 = AFl[:, fsb:fsb + _w]
        t2 = AFl[:, fsb + 512:fsb + 512 + _w]
        e.tensor_tensor(t1, _pA, AFl[:, _ta:_ta + _w], ALU.mult)
        vc(e, e.tensor_tensor(t2, _pB, AFl[:, _tb:_tb + _w], ALU.mult))
        return e.tensor_tensor(B.AB[:, _dst:_dst + _w], t1, t2, ALU.add)
    rope_done[unit] = e_op("v", f, waits=((S["t"], tN),), inc=True)


def _proj_head_l0(h):
    AB, WB, ps = B.AB, B.WB, B.ps
    S = B.S
    wldw = ((B.qsem["gw0" if h % 2 == 0 else "gw1"], B._l0_wld[h]),) \
        if hasattr(B, "_l0_wld") and h in B._l0_wld else ()
    wldw += ((S["s"], S["s"].n), (S["v"], S["v"].n))
    vpad = e_op("v", lambda e: e.memset(
        AP(B.AB, VA + 64, [[AB_COLS, 128], [VW0, 32], [1, 1]]), 1.0), inc=True)
    rope_done = {}
    for tt in range(8):
        if tt == 0:
            rope_done[-2] = None
        _rope_unit("k", tt, h, tt, rope_done, extra_w=(wldw if tt == 0 else ()))
        wldw = ()
    for jb in range(8):
        _rope_unit("q", jb, h, 8 + jb, rope_done)
    B.l0_rope_last = rope_done[15]

    vev = {}
    for vb in range(8):
        half = vb % 2
        waits = ((S["s"], vev[vb - 2]),) if vb >= 2 else ()
        for i in range(4):
            tl = vb * 4 + i
            for mc in range(8):
                t_mm(ps["pS1"][:, half * 512 + i * 128: half * 512 + i * 128 + 128],
                     AB[:, XT + mc * T + tl * 128: XT + mc * T + tl * 128 + 128],
                     WB[:, W_SLOT[h % 2] + W_KIND["l0_wv"] + mc * 128: W_SLOT[h % 2] + W_KIND["l0_wv"] + mc * 128 + 128],
                     start=(mc == 0), stop=(mc == 7),
                     waits=(waits if (i == 0 and mc == 0) else ()),
                     inc=(i == 3 and mc == 7))
        tN = S["t"].n

        def f_ev(e, _vb=vb, _half=half):
            src1 = AP(B.ps["pS1"], _half * 512, [[1024, 128], [128, 4], [1, 64]])
            dst1 = AP(B.AB, VA + (_vb * 4) * VW0, [[AB_COLS, 128], [VW0, 4], [1, 64]])
            e.activation(dst1, src1, AF.Copy)
            src2 = AP(B.ps["pS1"], _half * 512 + 64, [[1024, 128], [128, 4], [1, 64]])
            dst2 = AP(B.AB, VA + (_vb * 4) * VW0 + 65, [[AB_COLS, 128], [VW0, 4], [1, 64]])
            return e.activation(dst2, src2, AF.Copy)
        vev[vb] = e_op("s", f_ev, waits=((S["t"], tN), (S["v"], vpad)), inc=True)
    B.l0_vev_last = vev[7]
    if h + 2 < L0H:
        if not hasattr(B, "_l0_wld"):
            B._l0_wld = {}
        B._l0_wld[h + 2] = _load_l0_head_w(h + 2, waits=((S["t"], S["t"].n),))



def _bound_l0(h):
    AB, ps, sb = B.AB, B.ps, B.sb
    S = B.S
    units = [("k", tc) for tc in range(8)] + [("q", tc) for tc in range(4)]
    mmN = {}
    for u, (kind, tc) in enumerate(units):
        base = (KTA if kind == "k" else QTA) + tc * 512
        w = ((S["v"], B.l0_rope_last),) if u == 0 else ()
        if u - 2 in mmN:
            w += ((S["t"], mmN[u - 2]),)
        eb = EA + (u % 2) * 1024

        def f_sq(e, _b=base, _e=eb):
            return e.tensor_tensor(B.AB[:, _e:_e + 512], B.AB[:, _b:_b + 512],
                                   B.AB[:, _b:_b + 512], ALU.mult)
        vN = e_op("v", f_sq, waits=w, inc=True)
        row = tc if kind == "k" else 8 + tc
        mmN[u] = t_mm(ps["pA"][0:12, 0:512], sb["selA"][:, row * 12: row * 12 + 12],
                      AB[:, eb:eb + 512], start=(u == 0), stop=(u == len(units) - 1),
                      waits=((S["v"], vN),), inc=True)
    tN = S["t"].n
    def f_mxc(e):
        vc(e, e.reduce_max(B.sb["nscr"][0:12, 0:1], B.ps["pA"][0:12, 0:512], AX.X))
        return e.tensor_copy(B.sb["tinyb"][0:12, 0:1], B.sb["nscr"][0:12, 0:1])
    vN = e_op("v", f_mxc, waits=((S["t"], tN),), inc=True)
    tN = e_op("t", lambda e: e.transpose(B.ps["pT"][0:1, 0:12],
                                         B.sb["tinyb"][0:12, 0:1],
                                         B.sb["ident"][0:12, 0:12]),
              waits=((S["v"], vN),), inc=True)

    def f_mx(e):
        e.reduce_max(B.sb["tiny"][0:1, 0:1], B.ps["pT"][0:1, 0:8], AX.X)
        vc(e, e.reduce_max(B.sb["tiny"][0:1, 1:2], B.ps["pT"][0:1, 8:12], AX.X))
        return e.tensor_tensor(B.sb["tiny"][0:1, 2:3], B.sb["tiny"][0:1, 0:1],
                               B.sb["tiny"][0:1, 1:2], ALU.mult)
    vN = e_op("v", f_mx, waits=((S["t"], tN),), inc=True)
    sN = e_op("s", lambda e: e.activation(B.sb["tiny"][0:1, 3:4],
                                          B.sb["tiny"][0:1, 2:3], AF.Sqrt),
              waits=((S["v"], vN),), inc=True)
    tN = t_mm(ps["pB"][:, 0:1], sb["ones_f"][0:1, :], sb["tiny"][0:1, 3:4],
              start=True, stop=True, waits=((S["s"], sN),), inc=True)
    e_op("v", lambda e, _h=h: e.tensor_scalar(B.sb["cbias"][:, _h:_h + 1],
                                              B.ps["pB"][:, 0:1], -1.0 / 128.0,
                                              None, ALU.mult),
         waits=((S["t"], tN),), inc=True)


def _bound_l1(p):
    AB, ps, sb = B.AB, B.ps, B.sb
    S = B.S
    units = ([("k", cc, tc) for cc in range(2) for tc in range(8)] +
             [("q", cc, tc) for cc in range(2) for tc in range(4)])
    first = True
    mmN = {}
    for nu, (kind, cc, tc) in enumerate(units):
        c = 2 * p + cc
        if kind == "k":
            base = (L1_KRG + tc * 512) if c == 7 else (L1_KT + cc * T + tc * 512)
            arr = B.AB
        else:
            base = (W_QRG + tc * 512) if c == 7 else (L1_QT + cc * TOWN + tc * 512)
            arr = B.WB if c == 7 else B.AB
        w = ((S["s"], S["s"].n),) if first else ()
        if nu - 2 in mmN:
            w += ((S["t"], mmN[nu - 2]),)
        eb = L1_E + (nu % 2) * 1024

        def f_sq(e, _a=arr, _b=base, _e=eb):
            return e.tensor_tensor(B.AB[:, _e:_e + 512], _a[:, _b:_b + 512],
                                   _a[:, _b:_b + 512], ALU.mult)
        vN = e_op("v", f_sq, waits=w, inc=True)
        v = (0 if kind == "k" else 2) + cc
        mmN[nu] = t_mm(ps["pA"][0:8, 0:512], sb["selB"][:, v * 8: v * 8 + 8],
                       AB[:, eb:eb + 512], start=first, stop=(nu == len(units) - 1),
                       waits=((S["v"], vN),), inc=True)
        first = False
    tN = S["t"].n
    def f_mxc(e):
        vc(e, e.reduce_max(B.sb["nscr"][0:8, 0:1], B.ps["pA"][0:8, 0:512], AX.X))
        return e.tensor_copy(B.sb["tinyb"][0:8, 0:1], B.sb["nscr"][0:8, 0:1])
    vN = e_op("v", f_mxc, waits=((S["t"], tN),), inc=True)
    tN = e_op("t", lambda e: e.transpose(B.ps["pT"][0:1, 0:8],
                                         B.sb["tinyb"][0:8, 0:1],
                                         B.sb["ident"][0:8, 0:8]),
              waits=((S["v"], vN),), inc=True)

    def f_pm(e):
        vc(e, e.tensor_copy(B.sb["tiny"][0:1, 8:16], B.ps["pT"][0:1, 0:8]))
        return e.tensor_tensor(B.sb["tiny"][0:1, 0:4], B.sb["tiny"][0:1, 8:12],
                               B.sb["tiny"][0:1, 12:16], ALU.mult)
    vN = e_op("v", f_pm, waits=((S["t"], tN),), inc=True)
    sN = e_op("s", lambda e: e.activation(B.sb["tiny"][0:1, 4:8],
                                          B.sb["tiny"][0:1, 0:4], AF.Sqrt),
              waits=((S["v"], vN),), inc=True)
    tN = t_mm(ps["pB"][:, 0:4], sb["ones_f"][0:1, :], sb["tiny"][0:1, 4:8],
              start=True, stop=True, waits=((S["s"], sN),), inc=True)
    e_op("v", lambda e: e.tensor_scalar(B.sb["cbias"][:, 16:20], B.ps["pB"][:, 0:4],
                                        -1.0 / 8.0, None, ALU.mult),
         waits=((S["t"], tN),), inc=True)



def _attn_head(layer, h, l1ctx=None):
    AB, ps, sb = B.AB, B.ps, B.sb
    S = B.S
    scale = (1.0 / 128.0) if layer == 0 else (1.0 / 8.0)
    col = h if layer == 0 else 16 + (h % 4)
    ebase = EA if layer == 0 else L1_E
    ebuf = [ebase, ebase + 1024]
    ost = OST0 if layer == 0 else L1_OST
    flat = [(j, b) for j in range(NQG) for b in range(j + 1)]
    exp_done = {}
    av_done = {}
    mask_done = {}
    prev_ev = B._attn_prev_ev
    ost_dma = B._attn_ost_dma
    head_w = ((S["s"], S["s"].n), (S["v"], S["v"].n))

    for fi, (j, b) in enumerate(flat):
        nb = j + 1
        pS = ps["pS0"] if fi % 2 == 0 else ps["pS1"]
        bl = 4
        for i in range(bl):
            kt = 4 * b + i
            if layer == 0:
                lhsT = AB[:, KTA + kt * 128: KTA + kt * 128 + 128]
                rhs = AB[:, QTA + j * 256: QTA + j * 256 + 256]
            else:
                lhsT = l1ctx["KT_sl"](h, kt)
                rhs = l1ctx["QT_sl"](h, j)
            w = ()
            if i == 0:
                if fi - 2 in exp_done:
                    w += ((S["s"], exp_done[fi - 2]),)
                    if fi - 2 in mask_done:
                        w += ((S["v"], mask_done[fi - 2]),)
                if fi == 0:
                    w += head_w
            t_mm(pS[:, i * 256:(i + 1) * 256], lhsT, rhs,
                 start=True, stop=True, waits=w, inc=(i == bl - 1))
        tN = S["t"].n
        eb = ebuf[fi % 2]
        w_exp = [(S["t"], tN)]
        if fi - 2 in av_done:
            w_exp.append((S["t"], av_done[fi - 2]))

        def f_exp(e, _pS=pS, _eb=eb, _c=col, _sc=scale):
            return e.activation(B.AB[:, _eb:_eb + 1024], _pS[:, 0:1024],
                                AF.Exp, bias=B.sb["cbias"][:, _c:_c + 1], scale=_sc)
        sN = e_op("s", f_exp, waits=tuple(w_exp), inc=True)
        exp_done[fi] = sN
        if b == nb - 1:
            def f_mask(e, _eb=eb):
                return e.tensor_tensor(B.AB[:, _eb:_eb + 1024],
                                       B.AB[:, _eb:_eb + 1024],
                                       B.sb["masks"][:, 0:1024], ALU.mult)
            vN = e_op("v", f_mask, waits=((S["s"], sN),), inc=True)
            mask_done[fi] = vN
            gate = (S["v"], vN)
        else:
            gate = (S["s"], sN)
        avcol = (j % 2) * 256
        for i in range(bl):
            first = (b == 0 and i == 0)
            last = (b == nb - 1 and i == bl - 1)
            kt = 4 * b + i
            if layer == 0:
                vlo = AB[:, VA + kt * VW0: VA + kt * VW0 + 65]
                vhi = AB[:, VA + kt * VW0 + 65: VA + kt * VW0 + 129]
            else:
                vlo = l1ctx["V_sl"](h, kt)
                vhi = None
            w = (gate,) if i == 0 else ()
            if first and prev_ev is not None:
                w += (prev_ev,)
            t_mm(ps["pAV"][0:65, avcol:avcol + 256], vlo,
                 AB[:, eb + i * 256: eb + (i + 1) * 256],
                 start=first, stop=last, waits=w, inc=(i == bl - 1 and vhi is None))
            if vhi is not None:
                t_mm(ps["pB"][0:64, avcol:avcol + 256], vhi,
                     AB[:, eb + i * 256: eb + (i + 1) * 256],
                     start=first, stop=last, inc=(i == bl - 1))
        av_done[fi] = S["t"].n
        if b != nb - 1:
            continue
        tN_av = S["t"].n
        scol = 0

        def f_ln(e, _sc=scol, _ac=avcol):
            vc(e, e.activation(B.sb["sumb"][64:65, _sc:_sc + 256],
                               B.ps["pAV"][64:65, _ac:_ac + 256], AF.Ln), "s")
            return e.activation(B.sb["sumb"][64:65, _sc + 256:_sc + 512],
                                B.sb["sumb"][64:65, _sc:_sc + 256], AF.Exp,
                                scale=-1.0)
        sN = e_op("s", f_ln, waits=((S["t"], tN_av),), inc=True)
        tN = t_mm(ps["pA"][:, 0:256], sb["ones_f"][64:65, :],
                  sb["sumb"][64:65, scol + 256:scol + 512],
                  start=True, stop=True, waits=((S["s"], sN),), inc=True)
        w_ost = [(S["t"], tN)]
        if ost_dma is not None:
            w_ost.append((B.qsem["go"], ost_dma))

        def f_on(e, _ac=avcol, _h=h, _lay=layer, _o=ost):
            rb = B.sb["sumb"][0:64, 0:256]
            vc(e, e.tensor_copy(rb, B.ps["pA"][0:64, 0:256]))
            if _lay == 0:
                e.tensor_tensor(B.AB[0:64, _o:_o + 256],
                                B.ps["pAV"][0:64, _ac:_ac + 256], rb, ALU.mult)
                return e.tensor_tensor(B.AB[0:64, _o + 256:_o + 512],
                                       B.ps["pB"][0:64, _ac:_ac + 256], rb, ALU.mult)
            vc(e, e.tensor_tensor(B.AB[0:64, _o:_o + 256],
                                  B.ps["pAV"][0:64, _ac:_ac + 256], rb, ALU.mult))
            return e.tensor_scalar(B.AB[0:64, _o:_o + 256],
                                   B.AB[0:64, _o:_o + 256],
                                   B.sb["b_bv"][0:64, _h:_h + 1], None, ALU.add)
        vN = e_op("v", f_on, waits=tuple(w_ost), inc=True)
        prev_ev = (S["v"], vN)
        if layer == 0:
            r0 = h * 128
            dma("go", B.d["spill_o"][r0:r0 + 64, j * 256:(j + 1) * 256],
                AB[0:64, ost:ost + 256], waits=((S["v"], vN),))
            ost_dma = dma("go", B.d["spill_o"][r0 + 64:r0 + 128, j * 256:(j + 1) * 256],
                          AB[0:64, ost + 256:ost + 512])
        else:
            r0 = h * 64
            ost_dma = dma("go", B.d["spill_o"][r0:r0 + 64, j * 256:(j + 1) * 256],
                          AB[0:64, ost:ost + 256], waits=((S["v"], vN),))
    B._attn_prev_ev = prev_ev
    B._attn_ost_dma = ost_dma



def _tail(layer):
    AB, WB, ps, sb, AFl = B.AB, B.WB, B.ps, B.sb, B.AF
    d = B.d
    S = B.S
    barrier(f"tl{layer}")
    B._attn_prev_ev = None
    B._attn_ost_dma = None

    wof = W_WO
    if layer == 0:
        _load_w_mk("g", "l0_wo", 1024, 8, wof)
    em("y", lambda e, _h=B.qsem["go"].h, _n=B.qsem["go"].n: e.wait_ge(_h, _n))
    if layer == 1:
        em("y", lambda e, _h=B.qsem["yx"].h, _n=B.qsem["yx"].n: e.wait_ge(_h, _n))
    for oc in range(8):
        dma("y", AB[:, ORB + oc * 2048: ORB + (oc + 1) * 2048],
            d["spill_o"][oc * 128:(oc + 1) * 128, :])
    for tt in range(16):
        jj = tt // 2
        if layer == 0:
            src = d["x_perm"][(2 * jj) * BLK + (tt % 2) * 128:
                              (2 * jj) * BLK + (tt % 2) * 128 + 128, :]
        else:
            src = d["spill_x"][tt * 128:(tt + 1) * 128, :]
        dma("y", AFl[:, STRM + tt * 1024: STRM + (tt + 1) * 1024], src)
    for k in ("t", "s", "v"):
        for qn in ("y", "g", "gw0", "gw1"):
            em(k, lambda e, _h=B.qsem[qn].h, _n=B.qsem[qn].n: e.wait_ge(_h, _n))

    B._tail_v = {}
    if ATH_TC == "dumpo":
        em("y", lambda e, _h=B.qsem["g"].h, _n=B.qsem["g"].n: e.wait_ge(_h, _n))
        for i in range(8):
            ld = dma("y", AB[:, ABM_TMP: ABM_TMP + 2048] if False else AB[:, 0:2048],
                     d["spill_o"][i * 128:(i + 1) * 128, :],
                     waits=())
            def f_cp(e, _i=i):
                return e.tensor_copy(B.AF[:, STRM + _i * 2048: STRM + (_i + 1) * 2048],
                                     B.AB[:, 0:2048])
            vN = e_op("v", f_cp, waits=((B.qsem["y"], ld),), inc=True)
            em("y", lambda e, _h=S["v"].h, _n=vN: e.wait_ge(_h, _n))
        for tt in range(16):
            dma("y", B.d["y"][tt * 128:(tt + 1) * 128, :],
                B.AF[:, STRM + tt * 1024: STRM + (tt + 1) * 1024])
        em("y", lambda e, _h=B.qsem["y"].h, _n=B.qsem["y"].n: e.wait_ge(_h, _n))
        barrier(f"tle{layer}")
        return
    if ATH_TC in ("w_t", "w_s", "w_v"):
        sem = {"w_t": S["t"], "w_s": S["s"], "w_v": S["v"]}[ATH_TC]
        em("y", lambda e, _h=sem.h, _n=sem.n: e.wait_ge(_h, _n))
        barrier(f"tle{layer}")
        return
    if ATH_TC == "ld":
        barrier(f"tle{layer}")
        return
    ntt = 16
    if ATH_TC.startswith("mm") and ATH_TC[2:].isdigit():
        ntt = int(ATH_TC[2:])
    if ATH_TC == "cponly":
        tN = e_op("t", lambda e: e.nop(), inc=True)
        e_op("v", lambda e: e.tensor_copy(B.AF[:, FS:FS + 512], ps["pS0"][:, 0:512]),
             waits=((S["t"], tN),), inc=True)
        barrier(f"tle{layer}")
        return
    for tt in range(ntt):
        pL = ps["pS0"][:, 0:512] if tt % 2 == 0 else ps["pS1"][:, 0:512]
        pR = ps["pS0"][:, 512:1024] if tt % 2 == 0 else ps["pS1"][:, 512:1024]
        w = ((S["v"], B._tail_v[tt - 2]),) if tt >= 2 else ()
        for half, pp in ((0, pL), (1, pR)):
            for oc in range(8):
                t_mm(pp, AB[:, ORB + oc * 2048 + tt * 128: ORB + oc * 2048 + tt * 128 + 128],
                     WB[:, wof + oc * 1024 + half * 512: wof + oc * 1024 + half * 512 + 512],
                     start=(oc == 0), stop=(oc == 7 and layer == 0),
                     waits=(w if (half == 0 and oc == 0) else ()),
                     inc=(oc == 7 and layer == 0))
            if layer == 1:
                t_mm(pp, sb["ones_b"][0:1, :],
                     WB[0:1, W_BO + half * 512: W_BO + half * 512 + 512],
                     start=False, stop=True, inc=True)
        tN = B.S["t"].n
        if ATH_TC.startswith("mm"):
            if ATH_TC == "mmnc":
                B._tail_v[tt] = e_op("v", lambda e: e.nop(), waits=((S["t"], tN),), inc=True)
            elif ATH_TC == "mmd":
                def f_d(e, _pL=pL, _pR=pR, _tt=tt):
                    stc = STRM + _tt * 1024
                    e.tensor_copy(B.AF[:, stc:stc + 512], _pL)
                    return e.tensor_copy(B.AF[:, stc + 512:stc + 1024], _pR)
                B._tail_v[tt] = e_op("v", f_d, waits=((S["t"], tN),), inc=True)
            else:
                B._tail_v[tt] = e_op("v", lambda e, _pL=pL: e.tensor_copy(
                    B.AF[:, FS:FS + 512], _pL), waits=((S["t"], tN),), inc=True)
        else:
            _ln_tile(tt, tN, pL, pR, mode="attn")
    barrier(f"tle{layer}")


def _ln_tile(tt, tN, pL, pR, mode):
    AB, ps, sb, AFl = B.AB, B.ps, B.sb, B.AF
    S = B.S
    st = STRM + tt * 1024

    if ATH_TC == "lnr":
        def f_r(e, _st=st, _pL=pL, _pR=pR):
            e.tensor_tensor(AFl[:, _st:_st + 512], AFl[:, _st:_st + 512], _pL, ALU.add)
            return e.tensor_tensor(AFl[:, _st + 512:_st + 1024],
                                   AFl[:, _st + 512:_st + 1024], _pR, ALU.add)
        B._tail_v[tt] = e_op("v", f_r, waits=(((S["t"], tN),) if tN else ()), inc=True)
        return
    if ATH_TC == "lnr2":
        def f_r(e, _st=st, _pL=pL, _pR=pR):
            e.tensor_tensor(AFl[:, _st:_st + 512], _pL, AFl[:, _st:_st + 512], ALU.add)
            return e.tensor_tensor(AFl[:, _st + 512:_st + 1024], _pR,
                                   AFl[:, _st + 512:_st + 1024], ALU.add)
        B._tail_v[tt] = e_op("v", f_r, waits=(((S["t"], tN),) if tN else ()), inc=True)
        return
    if ATH_TC == "lnr4":
        if tt % 2 == 0:
            def f_r(e, _st=st):
                return e.tensor_scalar(AFl[:, _st:_st + 1024], AFl[:, _st:_st + 1024],
                                       1.0, None, ALU.mult)
        else:
            def f_r(e, _st=st, _pL=pL):
                e.tensor_copy(AFl[:, FS:FS + 512], _pL)
                return e.tensor_tensor(AFl[:, _st:_st + 512], AFl[:, _st:_st + 512],
                                       AFl[:, FS:FS + 512], ALU.add)
        B._tail_v[tt] = e_op("v", f_r, waits=(((S["t"], tN),) if tN else ()), inc=True)
        return
    if ATH_TC == "lnr3":
        def f_r(e, _st=st, _pL=pL, _pR=pR):
            e.scalar_tensor_tensor(AFl[:, _st:_st + 512], _pL, 1.0,
                                   AFl[:, _st:_st + 512], ALU.mult, ALU.add)
            return e.scalar_tensor_tensor(AFl[:, _st + 512:_st + 1024], _pR, 1.0,
                                          AFl[:, _st + 512:_st + 1024], ALU.mult, ALU.add)
        B._tail_v[tt] = e_op("v", f_r, waits=(((S["t"], tN),) if tN else ()), inc=True)
        return

    def f_res(e, _st=st, _pL=pL, _pR=pR, _m=mode):
        if _m == "attn":
            e.tensor_tensor(AFl[:, _st:_st + 512], AFl[:, _st:_st + 512], _pL, ALU.add)
            vc(e, e.tensor_tensor(AFl[:, _st + 512:_st + 1024],
                                  AFl[:, _st + 512:_st + 1024], _pR, ALU.add))
        vc(e, e.reduce_sum(sb["lnscr"][:, 0:1], AFl[:, _st:_st + 1024], AX.X))
        vc(e, e.tensor_scalar(sb["lnscr"][:, 1:2], sb["lnscr"][:, 0:1], 1.0 / DM,
                              None, ALU.mult))
        return e.tensor_scalar(AFl[:, _st:_st + 1024], AFl[:, _st:_st + 1024],
                               sb["lnscr"][:, 1:2], None, ALU.subtract)
    vN = e_op("v", f_res, waits=(((S["t"], tN),) if tN else ()), inc=True)
    sN = e_op("s", lambda e, _st=st: e.activation(AFl[:, FS:FS + 1024],
                                                  AFl[:, _st:_st + 1024], AF.Square,
                                                  accum_out=sb["lnscr"][:, 2:3]),
              waits=((S["v"], vN),), inc=True)

    def f_rstd(e):
        vc(e, e.tensor_scalar(sb["lnscr"][:, 3:4], sb["lnscr"][:, 2:3], 1.0 / DM,
                              EPS, ALU.mult, ALU.add))
        return e.reciprocal(sb["lnscr"][:, 4:5], sb["lnscr"][:, 3:4])
    vN = e_op("v", f_rstd, waits=((S["s"], sN),), inc=True)
    sN = e_op("s", lambda e: e.activation(sb["lnscr"][:, 5:6], sb["lnscr"][:, 4:5],
                                          AF.Sqrt),
              waits=((S["v"], vN),), inc=True)
    if ATH_TC == "lnp":
        def f_p(e, _st=st):
            return e.tensor_copy(AFl[:, _st:_st + 8], sb["lnscr"][:, 0:8])
        B._tail_v[tt] = e_op("v", f_p, waits=((S["s"], sN),), inc=True)
        return
    vN = e_op("v", lambda e, _st=st: e.tensor_scalar(AFl[:, _st:_st + 1024],
                                                     AFl[:, _st:_st + 1024],
                                                     sb["lnscr"][:, 5:6], None,
                                                     ALU.mult),
              waits=((S["s"], sN),), inc=True)
    B._tail_v[tt] = vN

    if mode == "ffn1":
        dma("y", B.d["y"][tt * 128:(tt + 1) * 128, :], AFl[:, st:st + 1024],
            waits=((S["v"], vN),))
        return
    if mode == "ffn0":
        dma("yx", B.d["spill_x"][tt * 128:(tt + 1) * 128, :], AFl[:, st:st + 1024],
            waits=((S["v"], vN),))
        dma("g", B.d["cc_in"][tt * 128:(tt + 1) * 128, :], AFl[:, st:st + 1024],
            waits=((S["v"], vN),))
        return
    if ATH_TC == "ln":
        return
    stg = EA
    sN = e_op("s", lambda e, _st=st, _sg=stg: e.activation(
        AB[:, _sg:_sg + 1024], AFl[:, _st:_st + 1024], AF.Copy),
        waits=((S["v"], vN),), inc=True)
    jj = tt // 2
    poscol = jj * 256 + (tt % 2) * 128
    for half in range(2):
        tNt = 0
        for mb in range(4):
            mc = half * 4 + mb
            tNt = e_op("t", lambda e, _mc=mc, _mb=mb, _sg=stg, _hf=half:
                       e.transpose(B.ps["pT"][:, _hf * 512 + _mb * 128: _hf * 512 + (_mb + 1) * 128],
                                   B.AB[:, _sg + _mc * 128: _sg + _mc * 128 + 128],
                                   B.sb["ident"][:]),
                       waits=(((S["s"], sN),) if (half == 0 and mb == 0) else
                              ((S["v"], B._pt_ev),) if (half == 1 and mb == 0 and hasattr(B, "_pt_ev")) else ()),
                       inc=(mb == 3))

        def f_tev(e, _half=half, _pc=poscol):
            src = AP(B.ps["pT"], _half * 512, [[1024, 128], [128, 4], [1, 128]])
            dst = AP(B.AB, XTO + (_half * 4) * 2048 + _pc,
                     [[AB_COLS, 128], [2048, 4], [1, 128]])
            return e.tensor_copy(dst, src)
        B._pt_ev = e_op("v", f_tev, waits=((S["t"], tNt),), inc=True)



def _ffn0():
    AB, WB, ps, sb, AFl = B.AB, B.WB, B.ps, B.sb, B.AF
    S = B.S
    d = B.d
    barrier("f0")
    gu_tp = {}
    d_tp = {}
    comp_v = {}
    stt_v = {}
    acc_v = {}
    guu = 0
    duu = 0

    for c in range(6):
        ws = F0_WS[c % 2]
        for wi, nm in ((0, "l0_wg"), (1, "l0_wu")):
            for st_i in range(4):
                gs = F0_GS[guu % 2]
                wld = ()
                if guu - 2 >= 0:
                    wld = ((S["t"], gu_tp[guu - 2]),)
                ld = dma("gu0" if guu % 2 == 0 else "gu1", AB[:, gs:gs + 1024],
                         AP(d[nm].tensor, (c * 512 + st_i * 128) * 1024,
                            [[1024, 128], [1, 1024]]), waits=wld)
                tN = 0
                for mg in range(2):
                    par = (guu * 2 + mg) % 2
                    w0 = ((S["v"], max(B._ptev.values())),)
                    if mg == 0:
                        w0 += ((B.qsem["gu0" if guu % 2 == 0 else "gu1"], ld),)
                    for mb in range(4):
                        mc = mg * 4 + mb
                        tN = e_op("t", lambda e, _gs=gs, _mc=mc, _mb=mb, _p=par:
                                  e.transpose(B.ps["pT"][:, _p * 512 + _mb * 128: _p * 512 + (_mb + 1) * 128],
                                              B.AB[:, _gs + _mc * 128: _gs + _mc * 128 + 128],
                                              B.sb["ident"][:]),
                                  waits=(w0 if mb == 0 else ()),
                                  inc=(mb == 3))

                    def f_ev(e, _ws=ws, _wi=wi, _st=st_i, _mg=mg, _p=par):
                        src = AP(B.ps["pT"], _p * 512, [[1024, 128], [128, 4], [1, 128]])
                        dst = AP(B.AB, _ws + _wi * 4096 + (_mg * 4) * 512 + _st * 128,
                                 [[AB_COLS, 128], [512, 4], [1, 128]])
                        return e.tensor_copy(dst, src)
                    B._ptev[par] = e_op("v", f_ev, waits=((S["t"], tN),), inc=True)
                gu_tp[guu] = tN
                guu += 1
        for dsub in range(8):
            dsb = F0_DS[duu % 2]
            wld = ()
            if duu - 2 >= 0:
                wld = ((S["t"], d_tp[duu - 2]),)
            ld = dma("gd0" if duu % 2 == 0 else "gd1", AB[:, dsb:dsb + 512],
                     AP(d["l0_wd"].tensor, dsub * 128 * FF0 + c * 512,
                        [[FF0, 128], [1, 512]]), waits=wld)
            tN = 0
            par = duu % 2
            w0 = ((S["v"], max(B._ptev.values())),
                  (B.qsem["gd0" if duu % 2 == 0 else "gd1"], ld))
            for hb in range(4):
                tN = e_op("t", lambda e, _dsb=dsb, _hb=hb, _p=par:
                          e.transpose(B.ps["pT"][:, _p * 512 + _hb * 128: _p * 512 + (_hb + 1) * 128],
                                      B.AB[:, _dsb + _hb * 128: _dsb + _hb * 128 + 128],
                                      B.sb["ident"][:]),
                          waits=(w0 if hb == 0 else ()),
                          inc=(hb == 3))

            def f_evd(e, _ws=ws, _dsub=dsub, _p=par):
                src = AP(B.ps["pT"], _p * 512, [[1024, 128], [128, 4], [1, 128]])
                dst = AP(B.AB, _ws + 8192 + _dsub * 128,
                         [[AB_COLS, 128], [1024, 4], [1, 128]])
                return e.tensor_copy(dst, src)
            B._ptev[par] = e_op("v", f_evd, waits=((S["t"], tN),), inc=True)
            d_tp[duu] = tN
            duu += 1
        tp_v = S["v"].n

        for tq in range(8):
            for hs in range(4):
                bank = hs % 2
                pg = ps["pS0"][:, bank * 512: bank * 512 + 256]
                pu = ps["pS0"][:, bank * 512 + 256: bank * 512 + 512]
                w0 = ()
                if tq == 0 and hs == 0:
                    w0 = ((S["v"], tp_v),)
                key = (c, tq, hs - 2) if hs >= 2 else (c, tq - 1, hs + 2)
                if key in stt_v:
                    w0 += ((S["v"], stt_v[key]),)
                for mc in range(8):
                    t_mm(pg, AB[:, ws + (mc * 4 + hs) * 128: ws + (mc * 4 + hs) * 128 + 128],
                         AB[:, XTO + mc * 2048 + tq * 256: XTO + mc * 2048 + tq * 256 + 256],
                         start=(mc == 0), stop=(mc == 7),
                         waits=(w0 if mc == 0 else ()))
                for mc in range(8):
                    t_mm(pu, AB[:, ws + 4096 + (mc * 4 + hs) * 128: ws + 4096 + (mc * 4 + hs) * 128 + 128],
                         AB[:, XTO + mc * 2048 + tq * 256: XTO + mc * 2048 + tq * 256 + 256],
                         start=(mc == 0), stop=(mc == 7), inc=(mc == 7))
                tN = S["t"].n
                w_sil = [(S["t"], tN)]
                if (c, tq, hs, "stt") in stt_v:
                    pass
                prevstt = stt_v.get((c, tq, hs - 1), stt_v.get((c, tq - 1, 3) if hs == 0 else None))
                if prevstt:
                    w_sil.append((S["v"], prevstt))
                sN = e_op("s", lambda e, _pg=pg: e.activation(
                    B.AB[:, F0_SIL:F0_SIL + 256], _pg, AF.Silu, scale=1.0 / 127.0),
                    waits=tuple(w_sil), inc=True)
                vN = e_op("v", lambda e, _pu=pu, _hs=hs: e.scalar_tensor_tensor(
                    B.AB[:, F0_ST + _hs * 256: F0_ST + (_hs + 1) * 256], _pu,
                    1.0 / 127.0, B.AB[:, F0_SIL:F0_SIL + 256], ALU.mult, ALU.mult),
                    waits=((S["s"], sN),), inc=True)
                stt_v[(c, tq, hs)] = vN
            st_v = S["v"].n
            for ti in range(2):
                tt = tq * 2 + ti
                wdn = ((S["v"], st_v),)
                pk = (c, tq, ti - 1) if ti else ((c, tq - 1, 1) if tq else (c - 1, 7, 1))
                if pk in acc_v:
                    wdn += ((S["v"], acc_v[pk]),)
                for half, pp in ((0, ps["pAV"][:, 0:512]), (1, ps["pB"][:, 0:512])):
                    for hs in range(4):
                        t_mm(pp, AB[:, F0_ST + hs * 256 + ti * 128: F0_ST + hs * 256 + ti * 128 + 128],
                             AB[:, ws + 8192 + hs * 1024 + half * 512: ws + 8192 + hs * 1024 + half * 512 + 512],
                             start=(hs == 0), stop=(hs == 3),
                             waits=(wdn if (half == 0 and hs == 0) else ()),
                             inc=(hs == 3 and half == 1))
                tN = S["t"].n

                def f_acc(e, _tt=tt):
                    stc = STRM + _tt * 1024
                    e.scalar_tensor_tensor(B.AF[:, stc:stc + 512], B.ps["pAV"][:, 0:512],
                                           1.0 / 127.0, B.AF[:, stc:stc + 512],
                                           ALU.mult, ALU.add)
                    return e.scalar_tensor_tensor(B.AF[:, stc + 512:stc + 1024],
                                                  B.ps["pB"][:, 0:512], 1.0 / 127.0,
                                                  B.AF[:, stc + 512:stc + 1024],
                                                  ALU.mult, ALU.add)
                acc_v[(c, tq, ti)] = e_op("v", f_acc, waits=((S["t"], tN),), inc=True)
    barrier("f0e")
    B._tail_v = {}
    for tt in range(16):
        _ln_tile(tt, 0, None, None, mode="ffn0")
    barrier("f0z")



def _exchange():
    AB, WB, ps, sb, AFl = B.AB, B.WB, B.ps, B.sb, B.AF
    S = B.S
    d = B.d
    ng = B.qsem["g"].n
    em("g", lambda e, _h=B.qsem["g"].h, _n=ng: e.wait_ge(_h, _n))
    nyx = B.qsem["yx"].n
    em("y", lambda e, _h=B.qsem["yx"].h, _n=nyx: e.wait_ge(_h, _n))
    barrier("xc")

    def f_cc(e):
        e.collective_compute(
            "AllGather", ALU.bypass,
            replica_groups=[[0, 1], [2, 3], [4, 5], [6, 7]],
            ins=[d["cc_in"].opt()],
            outs=[d["cc_out"].opt()]).then_inc(B.ccs.h, 1)
    em("g", f_cc)
    B.ccs.n += 1

    dma("y", AFl[:, TKA:TKA + T], d["tabk_a"])
    dma("y", AFl[:, TKB:TKB + T], d["tabk_b"])
    dma("y", AFl[:, TQA:TQA + TOWN], d["tabq_a"])
    dma("y", AFl[:, TQB:TQB + TOWN], d["tabq_b"])
    _load_w_mk("g", "l1_wq", 256, 8, W_C["l1_wq"])
    _load_w_mk("g", "l1_wqr", 128, 8, W_C["l1_wqr"])
    _load_w_mk("g", "l1_wqr_rot", 128, 8, W_C["l1_wqr_rot"])
    _load_w_mk("g", "l1_wkv", 256, 8, W_C["l1_wkv"])
    _load_w_mk("g", "l1_wkr", 128, 8, W_C["l1_wkr"])
    _load_w_mk("g", "l1_wkr_rot", 128, 8, W_C["l1_wkr_rot"])
    dma("g", AP(B.WB, W_BO, [[WB_COLS, 1], [1, DM]]), d["r_bo"])
    dma("g", AP(B.WB, W_B2, [[WB_COLS, 1], [1, DM]]), d["r_b2"])

    _tern_stats()

    B._xtev = {}
    for tt in range(16):
        stg = PSTG + (tt % 2) * 1024
        wld = ((S["v"], B._xtev[tt - 2]),) if (tt - 2) in B._xtev else ()
        qn = "yc0" if tt % 2 == 0 else "yc1"
        ld = dma(qn, AB[:, stg:stg + 1024], d["cc_in"][tt * 128:(tt + 1) * 128, :],
                 waits=wld)
        _xt_transpose(tt, stg, (qn, ld), own=True)
    for tt in range(16):
        stg = PSTG + (tt % 2) * 1024
        prev_key = (tt - 2 + 100) if tt >= 2 else (14 + tt)
        wv = B._xtev.get(prev_key)
        ld = _peer_tile_dma(tt, stg, wv)
        _xt_transpose(tt, stg, ("gp0" if tt % 2 == 0 else "gp1", ld), own=False)
    barrier("xce")


def _peer_tile_dma(tt, stg, wv):
    sem = B.qsem["gp0" if tt % 2 == 0 else "gp1"]
    S = B.S

    def f(e, _tt=tt, _stg=stg, _h=sem.h, _wv=wv):
        if _tt == 0:
            e.wait_ge(B.ccs.h, B.ccs.n)
        if _wv is not None:
            e.wait_ge(S["v"].h, _wv)
        with e.register(f"po{_tt}") as po:
            e.reg_load(po, B.sb["u32s"][0:1, 0:1])
            e.reg_add(po, po, _tt * 131072)
            e.dma_start(out=B.AB[:, _stg:_stg + 1024],
                        in_=AP(B.d["cc_out"].tensor, po, [[1024, 128], [1, 1024]])
                        ).then_inc(_h, 16)
    ccn = B.ccs.n
    em("g", f)
    sem.n += 16
    return sem.n


def _xt_transpose(tt, stg, ld, own):
    S = B.S
    q, ldn = ld
    jj = tt // 2
    pos = (2 * jj + (0 if own else 1)) * BLK + (tt % 2) * 128
    tN = 0
    vN = 0
    for half in range(2):
        for mb in range(4):
            mc = half * 4 + mb
            w0 = ()
            if mb == 0:
                if half == 0:
                    w0 = ((B.qsem[q], ldn),)
                if hasattr(B, "_pt_ev"):
                    w0 += ((S["v"], B._pt_ev),)
            tN = e_op("t", lambda e, _stg=stg, _mc=mc, _mb=mb, _hf=half:
                      e.transpose(B.ps["pT"][:, _hf * 512 + _mb * 128: _hf * 512 + (_mb + 1) * 128],
                                  B.AB[:, _stg + _mc * 128: _stg + _mc * 128 + 128],
                                  B.sb["ident"][:]),
                      waits=w0, inc=(mb == 3))

        def f_ev(e, _half=half, _pos=pos):
            src = AP(B.ps["pT"], _half * 512, [[1024, 128], [128, 4], [1, 128]])
            dst = AP(B.AB, L1_X2T + (_half * 4) * T + _pos,
                     [[AB_COLS, 128], [T, 4], [1, 128]])
            return e.tensor_copy(dst, src)
        vN = e_op("v", f_ev, waits=((S["t"], tN),), inc=True)
        B._pt_ev = vN
    B._xtev[tt + (0 if own else 100)] = vN


def _tern_stats():
    AFl, sb, ps = B.AF, B.sb, B.ps
    S = B.S
    d = B.d
    msN = e_op("v", lambda e: e.memset(B.sb["asum"][:, 0:4], 0.0), inc=True)
    st_v = {-1: msN}
    uu = 0
    for wi, nm in ((0, "l1_W1"), (1, "l1_W2")):
        for i in range(32):
            gs = F1_GS[0]
            wld = ((S["v"], st_v[uu - 1]),) if uu - 1 in st_v else ()
            if wi == 0:
                src = d[nm][i * 128:(i + 1) * 128, :]
            else:
                rt, ct = i // 4, i % 4
                src = AP(d[nm].tensor, rt * 128 * FF1 + ct * 1024,
                         [[FF1, 128], [1, 1024]])
            ld = dma("ys", AFl[:, gs:gs + 1024], src, waits=wld)

            def f_r(e, _gs=gs, _wi=wi):
                vc(e, e.reduce_sum(B.sb["asum"][:, 2 + _wi:3 + _wi],
                                   B.AF[:, _gs:_gs + 1024], AX.X,
                                   apply_absolute_value=True))
                vc(e, e.tensor_tensor(B.sb["asum"][:, _wi:_wi + 1],
                                      B.sb["asum"][:, _wi:_wi + 1],
                                      B.sb["asum"][:, 2 + _wi:3 + _wi], ALU.add))
                return e.nop()
            st_v[uu] = e_op("v", f_r, waits=((B.qsem["ys"], ld),), inc=True)
            uu += 1
    vN = S["v"].n
    for wi in range(2):
        t_mm(ps["pA"][0:1, wi:wi + 1], sb["asum"][:, wi:wi + 1],
             sb["ones_f"][:, 0:1], start=True, stop=True,
             waits=(((S["v"], vN),) if wi == 0 else ()), inc=(wi == 1))
    tN = S["t"].n

    def f_thr(e):
        vc(e, e.tensor_scalar(B.sb["tiny"][0:1, 6:8], B.ps["pA"][0:1, 0:2],
                              1.0 / (FF1 * DM), None, ALU.mult))
        return e.tensor_scalar(B.sb["tiny"][0:1, 6:8], B.sb["tiny"][0:1, 6:8],
                               1e-8, 0.5, ALU.max, ALU.mult)
    vN = e_op("v", f_thr, waits=((S["t"], tN),), inc=True)
    tN = t_mm(ps["pA"][:, 2:4], sb["ones_f"][0:1, :], sb["tiny"][0:1, 6:8],
              start=True, stop=True, waits=((S["v"], vN),), inc=True)

    def f_b(e):
        e.tensor_copy(B.sb["thr"][:, 0:1], B.ps["pA"][:, 2:3])
        e.tensor_scalar(B.sb["thr"][:, 1:2], B.ps["pA"][:, 2:3], -1.0, None, ALU.mult)
        e.tensor_copy(B.sb["thr"][:, 2:3], B.ps["pA"][:, 3:4])
        return e.tensor_scalar(B.sb["thr"][:, 3:4], B.ps["pA"][:, 3:4], -1.0,
                               None, ALU.mult)
    e_op("v", f_b, waits=((S["t"], tN),), inc=True)



def _l1_proj_common():
    AB, WB, ps, sb = B.AB, B.WB, B.ps, B.sb
    S = B.S
    barrier("l1p")
    prev = {}
    for u in range(16):
        dc, tt = u // 8, u % 8
        pp = ps["pA"][:, 0:512] if u % 2 == 0 else ps["pS0"][:, 0:512]
        w = ((S["s"], prev[u - 2]),) if u - 2 in prev else ()
        for mc in range(8):
            t_mm(pp, WB[:, W_C["l1_wkv"] + mc * 256 + dc * 128: W_C["l1_wkv"] + mc * 256 + dc * 128 + 128],
                 AB[:, L1_X2T + mc * T + tt * 512: L1_X2T + mc * T + tt * 512 + 512],
                 start=(mc == 0), stop=(mc == 7), waits=(w if mc == 0 else ()),
                 inc=(mc == 7))
        tN = S["t"].n
        prev[u] = e_op("s", lambda e, _pp=pp, _dc=dc, _tt=tt: e.activation(
            B.AB[:, L1_CKV + _dc * T + _tt * 512: L1_CKV + _dc * T + _tt * 512 + 512],
            _pp, AF.Gelu, bias=B.sb["b_bkv"][:, _dc:_dc + 1]),
            waits=((S["t"], tN),), inc=True)
    prev_v = {-2: prev[14], -1: prev[15]}
    for tt in range(8):
        if tt % 2 == 0:
            pA, pB = ps["pA"][:, 0:512], ps["pB"][:, 0:512]
        else:
            pA, pB = ps["pS0"][:, 0:512], ps["pS0"][:, 512:1024]
        w = ((S["s"], prev_v[tt - 2]),) if tt - 2 in prev_v else ()
        for mc in range(8):
            t_mm(pA, WB[:, W_C["l1_wkr"] + mc * 128: W_C["l1_wkr"] + mc * 128 + 128],
                 AB[:, L1_X2T + mc * T + tt * 512: L1_X2T + mc * T + tt * 512 + 512],
                 start=(mc == 0), stop=(mc == 7), waits=(w if mc == 0 else ()))
        for mc in range(8):
            t_mm(pB, WB[:, W_C["l1_wkr_rot"] + mc * 128: W_C["l1_wkr_rot"] + mc * 128 + 128],
                 AB[:, L1_X2T + mc * T + tt * 512: L1_X2T + mc * T + tt * 512 + 512],
                 start=(mc == 0), stop=(mc == 7), inc=(mc == 7))
        tN = S["t"].n

        def f_rope(e, _pA=pA, _pB=pB, _tt=tt):
            fsb = FS + (_tt % 2) * 1024
            t1 = B.AF[:, fsb:fsb + 512]
            t2 = B.AF[:, fsb + 512:fsb + 1024]
            e.scalar_tensor_tensor(t1, _pA, B.sb["b_bkr"][:, 0:1],
                                   B.AF[:, TKA + _tt * 512: TKA + _tt * 512 + 512],
                                   ALU.add, ALU.mult)
            vc(e, e.scalar_tensor_tensor(t2, _pB, B.sb["b_bkr_rot"][:, 0:1],
                                         B.AF[:, TKB + _tt * 512: TKB + _tt * 512 + 512],
                                         ALU.add, ALU.mult))
            return e.tensor_tensor(t1, t1, t2, ALU.add)
        vN = e_op("v", f_rope, waits=((S["t"], tN),), inc=True)
        prev_v[tt] = e_op("s", lambda e, _tt=tt: e.activation(
            B.AB[:, L1_KRG + _tt * 512: L1_KRG + _tt * 512 + 512],
            B.AF[:, FS + (_tt % 2) * 1024: FS + (_tt % 2) * 1024 + 512], AF.Gelu),
            waits=((S["v"], vN),), inc=True)
    kr_v_last = vN
    prevq = {}
    for u in range(16):
        dc, jb = u // 8, u % 8
        pp = ps["pA"][:, 0:256] if u % 2 == 0 else ps["pS0"][:, 0:256]
        if u - 2 in prevq:
            w = ((S["s"], prevq[u - 2]),)
        else:
            w = ((S["v"], kr_v_last),)
        pos = (2 * jb) * BLK
        for mc in range(8):
            t_mm(pp, WB[:, W_C["l1_wq"] + mc * 256 + dc * 128: W_C["l1_wq"] + mc * 256 + dc * 128 + 128],
                 AB[:, L1_X2T + mc * T + pos: L1_X2T + mc * T + pos + 256],
                 start=(mc == 0), stop=(mc == 7), waits=(w if mc == 0 else ()),
                 inc=(mc == 7))
        tN = S["t"].n
        prevq[u] = e_op("s", lambda e, _pp=pp, _dc=dc, _jb=jb: e.activation(
            B.WB[:, W_CQ + _dc * TOWN + _jb * 256: W_CQ + _dc * TOWN + _jb * 256 + 256],
            _pp, AF.Gelu, bias=B.sb["b_bq"][:, _dc:_dc + 1]),
            waits=((S["t"], tN),), inc=True)
    prevr = {-2: prevq[14], -1: prevq[15]}
    for jb in range(8):
        if jb % 2 == 0:
            pA, pB = ps["pA"][:, 0:256], ps["pB"][:, 0:256]
        else:
            pA, pB = ps["pS0"][:, 0:256], ps["pS0"][:, 512:768]
        w = ((S["s"], prevr[jb - 2]),) if jb - 2 in prevr else ()
        pos = (2 * jb) * BLK
        for mc in range(8):
            t_mm(pA, WB[:, W_C["l1_wqr"] + mc * 128: W_C["l1_wqr"] + mc * 128 + 128],
                 AB[:, L1_X2T + mc * T + pos: L1_X2T + mc * T + pos + 256],
                 start=(mc == 0), stop=(mc == 7), waits=(w if mc == 0 else ()))
        for mc in range(8):
            t_mm(pB, WB[:, W_C["l1_wqr_rot"] + mc * 128: W_C["l1_wqr_rot"] + mc * 128 + 128],
                 AB[:, L1_X2T + mc * T + pos: L1_X2T + mc * T + pos + 256],
                 start=(mc == 0), stop=(mc == 7), inc=(mc == 7))
        tN = S["t"].n

        def f_rope(e, _pA=pA, _pB=pB, _jb=jb):
            fsb = FS + (_jb % 2) * 1024
            t1 = B.AF[:, fsb:fsb + 256]
            t2 = B.AF[:, fsb + 512:fsb + 768]
            e.scalar_tensor_tensor(t1, _pA, B.sb["b_bqr"][:, 0:1],
                                   B.AF[:, TQA + _jb * 256: TQA + _jb * 256 + 256],
                                   ALU.add, ALU.mult)
            vc(e, e.scalar_tensor_tensor(t2, _pB, B.sb["b_bqr_rot"][:, 0:1],
                                         B.AF[:, TQB + _jb * 256: TQB + _jb * 256 + 256],
                                         ALU.add, ALU.mult))
            return e.tensor_tensor(t1, t1, t2, ALU.add)
        vN = e_op("v", f_rope, waits=((S["t"], tN),), inc=True)
        prevr[jb] = e_op("s", lambda e, _jb=jb: e.activation(
            B.WB[:, W_QRG + _jb * 256: W_QRG + _jb * 256 + 256],
            B.AF[:, FS + (_jb % 2) * 1024: FS + (_jb % 2) * 1024 + 256], AF.Gelu),
            waits=((S["v"], vN),), inc=True)
    barrier("l1pe")


def _l1_pass(p):
    AB, WB, ps, sb = B.AB, B.WB, B.ps, B.sb
    S = B.S
    d = B.d
    barrier(f"l1s{p}")
    pb = W_PS[p % 2]
    qp = "gw0" if p % 2 == 0 else "gw1"
    for cc in range(2):
        c = 2 * p + cc
        if c < 7:
            dma(qp, AP(WB, pb + cc * 256 + 0, [[WB_COLS, 128], [128, 2], [1, 128]]),
                AP(d["l1_wkc"].tensor, c * 128, [[896, 128], [896 * 128, 2], [1, 128]]))
            dma(qp, AP(WB, pb + 512 + cc * 256, [[WB_COLS, 128], [128, 2], [1, 128]]),
                AP(d["l1_wqc"].tensor, c * 128, [[896, 128], [896 * 128, 2], [1, 128]]))
    dma(qp, AP(WB, pb + 1024, [[WB_COLS, 128], [256, 2], [1, 256]]),
        AP(d["l1_wv"].tensor, 4 * p * 64, [[1024, 128], [131072, 2], [1, 256]]))
    if p == 0:
        _load_w_mk(qp, "l1_wo", 1024, 8, W_WO,
                   waits=((S["t"], S["t"].n),))
    em("t", lambda e, _h=B.qsem[qp].h, _n=B.qsem[qp].n: e.wait_ge(_h, _n))
    vpad = e_op("v", lambda e: e.memset(
        AP(B.AB, L1_V + 64, [[AB_COLS, 128], [65, 4 * 32], [1, 1]]), 1.0), inc=True)

    prev = {}
    u = 0
    for cc in range(2):
        c = 2 * p + cc
        if c == 7:
            continue
        for tt in range(8):
            pp = ps["pA"][:, 0:512] if u % 2 == 0 else ps["pS0"][:, 0:512]
            w = ((S["s"], prev[u - 2]),) if u - 2 in prev else ()
            for dc in range(2):
                t_mm(pp, WB[:, pb + cc * 256 + dc * 128: pb + cc * 256 + dc * 128 + 128],
                     AB[:, L1_CKV + dc * T + tt * 512: L1_CKV + dc * T + tt * 512 + 512],
                     start=(dc == 0), stop=(dc == 1), waits=(w if dc == 0 else ()),
                     inc=(dc == 1))
            tN = S["t"].n
            prev[u] = e_op("s", lambda e, _pp=pp, _cc=cc, _tt=tt, _c=c: e.activation(
                B.AB[:, L1_KT + _cc * T + _tt * 512: L1_KT + _cc * T + _tt * 512 + 512],
                _pp, AF.Gelu, bias=B.sb["b_bkc"][:, _c:_c + 1]),
                waits=((S["t"], tN),), inc=True)
            u += 1
    for cc in range(2):
        c = 2 * p + cc
        if c == 7:
            continue
        for jb in range(8):
            pp = ps["pA"][:, 0:256] if u % 2 == 0 else ps["pS0"][:, 0:256]
            w = ((S["s"], prev[u - 2]),) if u - 2 in prev else ()
            for dc in range(2):
                t_mm(pp, WB[:, pb + 512 + cc * 256 + dc * 128: pb + 512 + cc * 256 + dc * 128 + 128],
                     WB[:, W_CQ + dc * TOWN + jb * 256: W_CQ + dc * TOWN + jb * 256 + 256],
                     start=(dc == 0), stop=(dc == 1), waits=(w if dc == 0 else ()),
                     inc=(dc == 1))
            tN = S["t"].n
            prev[u] = e_op("s", lambda e, _pp=pp, _cc=cc, _jb=jb, _c=c: e.activation(
                B.AB[:, L1_QT + _cc * TOWN + _jb * 256: L1_QT + _cc * TOWN + _jb * 256 + 256],
                _pp, AF.Gelu, bias=B.sb["b_bqc"][:, _c:_c + 1]),
                waits=((S["t"], tN),), inc=True)
            u += 1
    prevv = {}
    for kt in range(32):
        if kt % 2 == 0:
            pt, off = ps["pB"], 0
        else:
            pt, off = ps["pS1"], 0
        pp = pt[:, off:off + 256]
        w = ((S["s"], prevv[kt - 2]),) if kt - 2 in prevv else ()
        if kt < 2:
            w += ((S["s"], prev[u - 1] if (u - 1) in prev else S["s"].n),)
        for dc in range(2):
            t_mm(pp, AB[:, L1_CKV + dc * T + kt * 128: L1_CKV + dc * T + kt * 128 + 128],
                 WB[:, pb + 1024 + dc * 256: pb + 1024 + dc * 256 + 256],
                 start=(dc == 0), stop=(dc == 1), waits=(w if dc == 0 else ()),
                 inc=(dc == 1))
        tN = S["t"].n
        rowext = 512 if kt % 2 == 0 else 1024

        def f_ev(e, _pt=pt, _kt=kt, _re=rowext):
            src = AP(_pt, 0, [[_re, 128], [64, 4], [1, 64]])
            dst = AP(B.AB, L1_V + _kt * VW1, [[AB_COLS, 128], [65, 4], [1, 64]])
            return e.activation(dst, src, AF.Copy)
        prevv[kt] = e_op("s", f_ev, waits=((S["t"], tN), (S["v"], vpad)), inc=True)

    _bound_l1(p)

    def KT_sl(h, kt):
        pr = slice((h % 2) * 64, (h % 2) * 64 + 64)
        if h // 2 == 7:
            return B.AB[pr, L1_KRG + kt * 128: L1_KRG + kt * 128 + 128]
        cc = h // 2 - 2 * p
        return B.AB[pr, L1_KT + cc * T + kt * 128: L1_KT + cc * T + kt * 128 + 128]

    def QT_sl(h, j):
        pr = slice((h % 2) * 64, (h % 2) * 64 + 64)
        if h // 2 == 7:
            return B.WB[pr, W_QRG + j * 256: W_QRG + j * 256 + 256]
        cc = h // 2 - 2 * p
        return B.AB[pr, L1_QT + cc * TOWN + j * 256: L1_QT + cc * TOWN + j * 256 + 256]

    def V_sl(h, kt):
        hh = h - 4 * p
        return B.AB[:, L1_V + kt * VW1 + hh * 65: L1_V + kt * VW1 + hh * 65 + 65]

    ctxd = {"KT_sl": KT_sl, "QT_sl": QT_sl, "V_sl": V_sl}
    for hh in range(4):
        _attn_head(1, 4 * p + hh, l1ctx=ctxd)



def _ffn1():
    AB, WB, ps, sb, AFl = B.AB, B.WB, B.ps, B.sb, B.AF
    S = B.S
    d = B.d
    barrier("f1")
    B._f1par = 0
    tern_v = {}
    tp_t = {}
    gelu_s = {}
    acc_v = {}
    uu = 0
    for c in range(8):
        ws = F1_WS[c % 2]
        for wi, nm, nsub in ((0, "l1_W1", 4), (1, "l1_W2", 8)):
            for st_i in range(nsub):
                gs = F1_GS[0]
                wld = ((S["v"], tern_v[uu - 1]),) if uu - 1 in tern_v else ()
                if wi == 0:
                    src = AP(d[nm].tensor, (c * 512 + st_i * 128) * DM,
                             [[DM, 128], [1, DM]])
                    wcols = 1024
                else:
                    src = AP(d[nm].tensor, st_i * 128 * FF1 + c * 512,
                             [[FF1, 128], [1, 512]])
                    wcols = 512
                ld = dma("ys", AFl[:, gs:gs + wcols], src, waits=wld)
                thr_c = 0 if wi == 0 else 2
                wtp = ((S["t"], tp_t[uu - 2]),) if uu - 2 in tp_t else ()

                def f_tern(e, _gs=gs, _wc=wcols, _tc=thr_c):
                    e.tensor_scalar(B.AB[:, F1_TS:F1_TS + _wc], B.AF[:, _gs:_gs + _wc],
                                    B.sb["thr"][:, _tc:_tc + 1], None, ALU.is_ge)
                    vc(e, e.tensor_scalar(B.AB[:, F1_TS + 1024:F1_TS + 1024 + _wc],
                                          B.AF[:, _gs:_gs + _wc],
                                          B.sb["thr"][:, _tc + 1:_tc + 2], None,
                                          ALU.is_le))
                    return e.tensor_tensor(B.AB[:, F1_TS + 2048:F1_TS + 2048 + _wc],
                                           B.AB[:, F1_TS:F1_TS + _wc],
                                           B.AB[:, F1_TS + 1024:F1_TS + 1024 + _wc],
                                           ALU.subtract)
                vN = e_op("v", f_tern, waits=((B.qsem["ys"], ld),) + wtp, inc=True)
                tern_v[uu] = vN
                nb = 8 if wi == 0 else 4
                tN = 0
                for bg in range(nb // 4):
                    par = B._f1par
                    B._f1par = 1 - B._f1par
                    w0 = ((S["v"], max(B._ptev.values())),)
                    if bg == 0:
                        w0 += ((S["v"], vN),)
                    for bb in range(4):
                        blk = bg * 4 + bb
                        tN = e_op("t", lambda e, _blk=blk, _bb=bb, _p=par:
                                  e.transpose(B.ps["pT"][:, _p * 512 + _bb * 128: _p * 512 + (_bb + 1) * 128],
                                              B.AB[:, F1_TS + 2048 + _blk * 128: F1_TS + 2048 + _blk * 128 + 128],
                                              B.sb["ident"][:]),
                                  waits=(w0 if bb == 0 else ()),
                                  inc=(bb == 3))
                    if wi == 0:
                        def f_ev(e, _ws=ws, _st=st_i, _bg=bg, _p=par):
                            src = AP(B.ps["pT"], _p * 512, [[1024, 128], [128, 4], [1, 128]])
                            dst = AP(B.AB, _ws + (_bg * 4) * 512 + _st * 128,
                                     [[AB_COLS, 128], [512, 4], [1, 128]])
                            return e.tensor_copy(dst, src)
                    else:
                        def f_ev(e, _ws=ws, _st=st_i, _bg=bg, _p=par):
                            src = AP(B.ps["pT"], _p * 512, [[1024, 128], [128, 4], [1, 128]])
                            dst = AP(B.AB, _ws + 4096 + _st * 128,
                                     [[AB_COLS, 128], [1024, 4], [1, 128]])
                            return e.tensor_copy(dst, src)
                    B._ptev[par] = e_op("v", f_ev, waits=((S["t"], tN),), inc=True)
                tp_t[uu] = tN
                uu += 1
        tp_v = S["v"].n
        for tq in range(8):
            for fs in range(4):
                bank = fs % 2
                ph = ps["pS0"][:, bank * 512: bank * 512 + 256]
                w0 = ()
                if tq == 0 and fs == 0:
                    w0 = ((S["v"], tp_v),)
                key = (c, tq, fs - 2) if fs >= 2 else (c, tq - 1, fs + 2)
                if key in gelu_s:
                    w0 += ((S["s"], gelu_s[key]),)
                for mc in range(8):
                    t_mm(ph, AB[:, ws + (mc * 4 + fs) * 128: ws + (mc * 4 + fs) * 128 + 128],
                         AB[:, XTO + mc * 2048 + tq * 256: XTO + mc * 2048 + tq * 256 + 256],
                         start=(mc == 0), stop=(mc == 7),
                         waits=(w0 if mc == 0 else ()), inc=(mc == 7))
                tN = S["t"].n
                gelu_s[(c, tq, fs)] = e_op("s", lambda e, _ph=ph, _fs=fs, _c=c: e.activation(
                    B.AB[:, F1_HT + _fs * 256: F1_HT + (_fs + 1) * 256], _ph, AF.Gelu,
                    bias=B.sb["b_b1"][:, _c * 4 + _fs:_c * 4 + _fs + 1]),
                    waits=((S["t"], tN),), inc=True)
            ht_s = S["s"].n
            for ti in range(2):
                tt = tq * 2 + ti
                wdn = ((S["s"], ht_s),)
                pk = (c, tq, ti - 1) if ti else ((c, tq - 1, 1) if tq else (c - 1, 7, 1))
                if pk in acc_v:
                    wdn += ((S["v"], acc_v[pk]),)
                for half, pp in ((0, ps["pAV"][:, 0:512]), (1, ps["pB"][:, 0:512])):
                    for fs in range(4):
                        laststop = (fs == 3 and c < 7)
                        t_mm(pp, AB[:, F1_HT + fs * 256 + ti * 128: F1_HT + fs * 256 + ti * 128 + 128],
                             AB[:, ws + 4096 + fs * 1024 + half * 512: ws + 4096 + fs * 1024 + half * 512 + 512],
                             start=(fs == 0), stop=laststop,
                             waits=(wdn if (half == 0 and fs == 0) else ()),
                             inc=(laststop and half == 1))
                    if c == 7:
                        t_mm(pp, sb["ones_b"][0:1, :],
                             B.WB[0:1, W_B2 + half * 512: W_B2 + half * 512 + 512],
                             start=False, stop=True, inc=(half == 1))
                tN = S["t"].n

                def f_acc(e, _tt=tt):
                    stc = STRM + _tt * 1024
                    e.tensor_tensor(B.AF[:, stc:stc + 512], B.AF[:, stc:stc + 512],
                                    B.ps["pAV"][:, 0:512], ALU.add)
                    return e.tensor_tensor(B.AF[:, stc + 512:stc + 1024],
                                           B.AF[:, stc + 512:stc + 1024],
                                           B.ps["pB"][:, 0:512], ALU.add)
                acc_v[(c, tq, ti)] = e_op("v", f_acc, waits=((S["t"], tN),), inc=True)
    barrier("f1e")


def _final_ln_out():
    B._tail_v = {}
    for tt in range(16):
        _ln_tile(tt, 0, None, None, mode="ffn1")
    em("y", lambda e, _h=B.qsem["y"].h, _n=B.qsem["y"].n: e.wait_ge(_h, _n))
    em("g", lambda e, _h=B.qsem["g"].h, _n=B.qsem["g"].n: e.wait_ge(_h, _n))


def kernel(**inputs):
    nc = build_program()
    in_maps = [make_core_inputs(inputs, c // 2, c % 2) for c in range(8)]
    res = run_bass_kernel_spmd(nc, in_maps, core_ids=list(range(8)))
    out = np.zeros((4, T, DM), dtype=np.float32)
    for c in range(8):
        s, r = c // 2, c % 2
        yc = res.results[c]["y"]
        for j in range(NBLK):
            g = 2 * j + r
            out[s, g * BLK:(g + 1) * BLK, :] = yc[j * BLK:(j + 1) * BLK, :]
    return out


# version 05
# speedup vs baseline: 1.2903x; 1.2903x over previous
import os
import numpy as np
import ml_dtypes
from contextlib import ExitStack

ATH_PH = os.environ.get("ATH_PH", "full")
ATH_TC = os.environ.get("ATH_TC", "full")

import concourse.bass as bass
import concourse.mybir as mybir
from concourse.bass_utils import run_bass_kernel_spmd

F32 = mybir.dt.float32
BF16 = mybir.dt.bfloat16
I32 = mybir.dt.int32
U32 = mybir.dt.uint32
AF = mybir.ActivationFunctionType
ALU = mybir.AluOpType
AX = mybir.AxisListType

T = 4096
DM = 1024
BLK = 256
NBLK = 8
TOWN = 2048
L0H = 8
L1H = 16
DCV, DRP = 256, 128
FF0, FF1 = 3072, 4096
QG = 256
NQG = 8
EPS = 1e-5

AB_COLS = 45600
AF_COLS = 18432
STRM = 0
FS = 16384
TKA, TKB = 0, 4096
TQA, TQB = 8192, 10240

XT = 0
KTA = 32768
QTA = 36864
VA = 38912
VW0 = 129
EA = 43040
OST0 = 45088
TSC = 45600

XTO = 0
ORB = 16384

F0_WS = [16384, 28672]
F0_GS = [40960, 41984]
F0_DS = [43008, 43520]
F0_SIL = 44032
F0_ST = 44544

L1_X2T = 0
L1_CKV = 32768
L1_KRG = 40960
PSTG = 45056
PSTG = 32768

L1_KT = 0
L1_QT = 8192
L1_V = 12288
VW1 = 260
L1_E = 20608
L1_OST = 22656

F1_WS = [16384, 24576]
F1_TS = 32768
F1_HT = 35840
F1_GS = [FS + 1024]



def _perm_blocks(r):
    return [p + r if p % 2 == 0 else p - r for p in range(2 * NBLK)]


def _rope_tabs(pos):
    i = np.arange(64, dtype=np.float64)
    invf = np.power(10000.0, -i / 64.0)
    rad = pos[None, :].astype(np.float64) * invf[:, None]
    cos = np.cos(rad).astype(np.float32)
    sin = np.sin(rad).astype(np.float32)
    return (np.concatenate([cos, cos], axis=0),
            np.concatenate([-sin, sin], axis=0))


def _masks(r):
    k = np.arange(128)[:, None]
    q = np.arange(QG)[None, :]
    tri0 = (k <= q).astype(np.float32)
    tri1 = ((128 + k) <= q).astype(np.float32)
    full = np.full((128, QG), 1.0 if r == 1 else 0.0, np.float32)
    return np.concatenate([tri0, tri1, full, full], axis=1).astype(ml_dtypes.bfloat16)


def _selA():
    m = np.zeros((128, 144), np.float32)
    for tc in range(12):
        m[:, tc * 12 + tc] = 1.0
    return m.astype(ml_dtypes.bfloat16)


def _selB():
    m = np.zeros((128, 32), np.float32)
    for v in range(4):
        base = 2 * v
        m[0:64, v * 8 + base] = 1.0
        m[64:128, v * 8 + base + 1] = 1.0
    return m.astype(ml_dtypes.bfloat16)


def _rot_halves(w, width):
    g = w.shape[1] // width
    return np.ascontiguousarray(
        w.reshape(w.shape[0], g, 2, width // 2)[:, :, ::-1, :].reshape(w.shape))


def make_core_inputs(inputs, s, r):
    x = np.asarray(inputs["L0_attn_x"], dtype=np.float32)
    perm = _perm_blocks(r)
    rows = np.concatenate([np.arange(b * BLK, (b + 1) * BLK) for b in perm])
    x_perm = np.ascontiguousarray(x[s][rows])
    xT_perm = np.ascontiguousarray(x_perm.T)
    own_rows = rows.reshape(2 * NBLK, BLK)[0::2].reshape(-1)
    tka, tkb = _rope_tabs(rows.astype(np.float64))
    tqa, tqb = _rope_tabs(own_rows.astype(np.float64))
    f32 = lambda a: np.ascontiguousarray(np.asarray(a, dtype=np.float32))
    d = {
        "x_perm": x_perm, "xT_perm": xT_perm,
        "tabk_a": tka, "tabk_b": tkb, "tabq_a": tqa, "tabq_b": tqb,
        "masks": _masks(r),
        "ident": np.eye(128, dtype=ml_dtypes.bfloat16),
        "ones_b": np.ones((128, 128), dtype=ml_dtypes.bfloat16),
        "ones_f": np.ones((128, 128), dtype=np.float32),
        "selA": _selA(), "selB": _selB(),
        "peer_off": np.array([[(1 - r) * TOWN * DM]], dtype=np.uint32),
    }
    d["l0_wq"] = f32(np.asarray(inputs["L0_attn_wq"]).reshape(DM, DM))
    d["l0_wk"] = f32(np.asarray(inputs["L0_attn_wk"]).reshape(DM, DM))
    d["l0_wv"] = f32(np.asarray(inputs["L0_attn_wv"]).reshape(DM, DM))
    d["l0_wo"] = f32(np.asarray(inputs["L0_attn_wo"]).reshape(DM, DM))
    d["l0_wq_rot"] = _rot_halves(d["l0_wq"], 128)
    d["l0_wk_rot"] = _rot_halves(d["l0_wk"], 128)
    for nm_src, nm in (("L0_ffn_w_gate", "l0_wg"), ("L0_ffn_w_up", "l0_wu"),
                       ("L0_ffn_w_down", "l0_wd")):
        d[nm] = np.ascontiguousarray(np.asarray(inputs[nm_src], dtype=np.int32))
    for nm_src, nm in (("L1_attn_wq", "l1_wq"), ("L1_attn_wqc", "l1_wqc"),
                       ("L1_attn_wqr", "l1_wqr"), ("L1_attn_wkv", "l1_wkv"),
                       ("L1_attn_wkr", "l1_wkr"), ("L1_attn_wkc", "l1_wkc"),
                       ("L1_attn_wv", "l1_wv"), ("L1_attn_wo", "l1_wo"),
                       ("L1_ffn_W1", "l1_W1"), ("L1_ffn_W2", "l1_W2")):
        d[nm] = f32(inputs[nm_src])
    d["l1_wqr_rot"] = _rot_halves(d["l1_wqr"], 128)
    d["l1_wkr_rot"] = _rot_halves(d["l1_wkr"], 128)

    def cols(v):
        v = np.asarray(v, dtype=np.float32).reshape(-1)
        return np.ascontiguousarray(v.reshape(-1, 128).T)

    def rot_col(v):
        v = np.asarray(v, dtype=np.float32).reshape(128)
        return np.ascontiguousarray(np.concatenate([v[64:], v[:64]]).reshape(128, 1))

    d["b_bq"] = cols(inputs["L1_attn_bq"])
    d["b_bqc"] = cols(inputs["L1_attn_bqc"])
    d["b_bqr"] = cols(inputs["L1_attn_bqr"])
    d["b_bqr_rot"] = rot_col(inputs["L1_attn_bqr"])
    d["b_bkv"] = cols(inputs["L1_attn_bkv"])
    d["b_bkr"] = cols(inputs["L1_attn_bkr"])
    d["b_bkr_rot"] = rot_col(inputs["L1_attn_bkr"])
    d["b_bkc"] = cols(inputs["L1_attn_bkc"])
    d["b_b1"] = cols(inputs["L1_ffn_b1"])
    bv = np.asarray(inputs["L1_attn_bv"], dtype=np.float32).reshape(L1H, 64).T
    d["b_bv"] = np.ascontiguousarray(np.concatenate([bv, np.zeros((64, L1H), np.float32)]))
    d["r_bo"] = np.asarray(inputs["L1_attn_bo"], dtype=np.float32).reshape(1, DM)
    d["r_b2"] = np.asarray(inputs["L1_ffn_b2"], dtype=np.float32).reshape(1, DM)
    return d



class Sem:
    def __init__(self, ctx, nc, name):
        self.h = ctx.enter_context(nc.semaphore(name))
        self.n = 0


class B:
    pass


ENGS = ("t", "s", "v", "g", "y")


def em(eng, fn):
    B.streams[eng].append(fn)


def barrier(name):
    if not hasattr(B, "barsem"):
        B.barsem = {k: Sem(B.ctx, B.nc, "bar_" + k) for k in ENGS}
        B.barn = 0
    B.barn += 1
    n = B.barn
    for k in ENGS:
        qs = ()
        if hasattr(B, "qsem"):
            qs = tuple((B.qsem[q].h, B.qsem[q].n) for q in B.qsem
                       if q.startswith(k) and B.qsem[q].n > 0)
        def _f(e, _h=B.barsem[k].h, _k=k, _qs=qs):
            for (_qh, _qn) in _qs:
                e.wait_ge(_qh, _qn)
            if _k != "y":
                e.drain()
            e.nop().then_inc(_h, 1)
        em(k, _f)
    for k in ENGS:
        for k2 in ENGS:
            if k2 != k:
                em(k, lambda e, _h=B.barsem[k2].h, _n=n: e.wait_ge(_h, _n))


def dma(q, dst, src, waits=()):
    sem = B.qsem[q]
    eng = "y" if q.startswith("y") else "g"

    def f(e, _d=dst, _s=src, _h=sem.h, _w=tuple(waits)):
        for (sm, val) in _w:
            e.wait_ge(sm.h, val)
        e.dma_start(out=_d, in_=_s).then_inc(_h, 16)
    em(eng, f)
    sem.n += 16
    return sem.n


def t_mm(out, lhsT, rhs, start, stop, waits=(), inc=False):
    s_t = B.S["t"]

    def f(e, _o=out, _l=lhsT, _r=rhs, _a=start, _b=stop, _w=tuple(waits), _i=inc):
        for (sm, val) in _w:
            e.wait_ge(sm.h, val)
        ins = e.matmul(_o, _l, _r, start=_a, stop=_b)
        if _i:
            ins.then_inc(s_t.h, 1)
    em("t", f)
    if inc:
        s_t.n += 1
    return s_t.n


def vc(e, ins, eng="v"):
    sem = B.chain[eng]
    B.chain_n[eng] += 1
    ins.then_inc(sem.h, 1)
    e.wait_ge(sem.h, B.chain_n[eng])


def e_op(eng, fn, waits=(), inc=False):
    s = B.S[eng]

    def f(e, _f=fn, _w=tuple(waits), _i=inc):
        for (sm, val) in _w:
            e.wait_ge(sm.h, val)
        ins = _f(e)
        if _i:
            ins.then_inc(s.h, 1)
    em(eng, f)
    if inc:
        s.n += 1
    return s.n


def AP(t, off, pat):
    return bass.AP(t, off, pat)



_CACHE = {}

WB_COLS = 20480
W_SLOT = [0, 5120]
W_KIND = {"l0_wq": 0, "l0_wq_rot": 1024, "l0_wk": 2048, "l0_wk_rot": 3072,
          "l0_wv": 4096}
W_WO = 0
W_C = {"l1_wq": 0, "l1_wqr": 2048, "l1_wqr_rot": 3072, "l1_wkv": 4096,
       "l1_wkr": 6144, "l1_wkr_rot": 7168}
W_PS = [8192, 9728]
W_BO = 11264
W_B2 = 12288
W_CQ = 13312
W_QRG = 17408


def build_program():
    if "nc" in _CACHE:
        return _CACHE["nc"]
    nc = bass.Bass("TRN2", target_bir_lowering=False)
    ctx = ExitStack()
    B.nc, B.ctx = nc, ctx
    B.streams = {k: [] for k in ENGS}
    B.S = {k: Sem(ctx, nc, "p_" + k) for k in ("t", "s", "v", "g")}
    B.qsem = {k: Sem(ctx, nc, "q_" + k) for k in
              ("y", "g", "gw0", "gw1", "gu0", "gu1", "gd0", "gd1", "go",
               "ys", "yc0", "yc1", "gp0", "gp1", "yx")}
    B.ccs = Sem(ctx, nc, "q_cc")
    B.chain = {"v": Sem(ctx, nc, "ch_v"), "s": Sem(ctx, nc, "ch_s")}
    B.chain_n = {"v": 0, "s": 0}
    B._ptev = {0: 0, 1: 0}

    d = {}
    B.d = d

    def din(name, shape, dt):
        d[name] = nc.dram_tensor(name, shape, dt, kind="ExternalInput").ap()

    din("x_perm", [T, DM], F32)
    din("xT_perm", [DM, T], F32)
    din("tabk_a", [128, T], F32)
    din("tabk_b", [128, T], F32)
    din("tabq_a", [128, TOWN], F32)
    din("tabq_b", [128, TOWN], F32)
    din("masks", [128, 4 * QG], BF16)
    din("ident", [128, 128], BF16)
    din("ones_b", [128, 128], BF16)
    din("ones_f", [128, 128], F32)
    din("selA", [128, 144], BF16)
    din("selB", [128, 32], BF16)
    din("peer_off", [1, 1], U32)
    for nm in ("l0_wq", "l0_wk", "l0_wv", "l0_wo", "l0_wq_rot", "l0_wk_rot", "l1_wo"):
        din(nm, [DM, DM], F32)
    din("l0_wg", [FF0, DM], I32)
    din("l0_wu", [FF0, DM], I32)
    din("l0_wd", [DM, FF0], I32)
    din("l1_wq", [DM, DCV], F32)
    din("l1_wqc", [DCV, DM - DRP], F32)
    din("l1_wqr", [DM, DRP], F32)
    din("l1_wqr_rot", [DM, DRP], F32)
    din("l1_wkv", [DM, DCV], F32)
    din("l1_wkr", [DM, DRP], F32)
    din("l1_wkr_rot", [DM, DRP], F32)
    din("l1_wkc", [DCV, DM - DRP], F32)
    din("l1_wv", [DCV, DM], F32)
    din("l1_W1", [FF1, DM], F32)
    din("l1_W2", [DM, FF1], F32)
    for nm, n in (("b_bq", 2), ("b_bqc", 7), ("b_bqr", 1), ("b_bqr_rot", 1),
                  ("b_bkv", 2), ("b_bkr", 1), ("b_bkr_rot", 1), ("b_bkc", 7),
                  ("b_b1", 32), ("b_bv", 16)):
        din(nm, [128, n], F32)
    din("r_bo", [1, DM], F32)
    din("r_b2", [1, DM], F32)
    d["y"] = nc.dram_tensor("y", [TOWN, DM], F32, kind="ExternalOutput").ap()
    d["cc_in"] = nc.dram_tensor("cc_in", [TOWN, DM], BF16).ap()
    d["cc_out"] = nc.dram_tensor("cc_out", [2 * TOWN, DM], BF16).ap()
    d["spill_x"] = nc.dram_tensor("spill_x", [TOWN, DM], F32).ap()
    d["spill_o"] = nc.dram_tensor("spill_o", [DM, TOWN], BF16).ap()

    sb = {}
    B.sb = sb

    def sbuf(name, shape, dt):
        sb[name] = ctx.enter_context(nc.sbuf_tensor("s_" + name, shape, dt))
        return sb[name]

    B.AB = sbuf("ABar", [128, AB_COLS], BF16)
    B.AF = sbuf("AFar", [128, AF_COLS], F32)
    B.WB = sbuf("WBar", [128, WB_COLS], BF16)
    for nm, shape, dt in (
            ("ident", [128, 128], BF16), ("ones_b", [128, 128], BF16),
            ("ones_f", [128, 128], F32), ("selA", [128, 144], BF16),
            ("selB", [128, 32], BF16), ("masks", [128, 4 * QG], BF16),
            ("cbias", [128, 32], F32), ("nscr", [128, 8], F32),
            ("tiny", [128, 16], F32), ("tinyb", [128, 16], BF16),
            ("sumb", [128, 512], F32), ("lnscr", [128, 8], F32),
            ("u32s", [1, 1], U32), ("thr", [128, 4], F32),
            ("asum", [128, 4], F32)):
        sbuf(nm, shape, dt)
    for nm, n in (("b_bq", 2), ("b_bqc", 7), ("b_bqr", 1), ("b_bqr_rot", 1),
                  ("b_bkv", 2), ("b_bkr", 1), ("b_bkr_rot", 1), ("b_bkc", 7),
                  ("b_b1", 32), ("b_bv", 16)):
        sbuf(nm, [128, n], F32)

    ps = {}
    B.ps = ps
    ps["pS0"] = ctx.enter_context(nc.psum_tensor("pS0", [128, 1024], F32))
    ps["pS1"] = ctx.enter_context(nc.psum_tensor("pS1", [128, 1024], F32))
    ps["pAV"] = ctx.enter_context(nc.psum_tensor("pAV", [128, 512], F32))
    ps["pA"] = ctx.enter_context(nc.psum_tensor("pA", [128, 512], F32))
    ps["pB"] = ctx.enter_context(nc.psum_tensor("pB", [128, 512], F32))
    ps["pT"] = ctx.enter_context(nc.psum_tensor("pT", [128, 1024], BF16))

    _emit()

    with nc.Block() as block:
        @block.tensor
        def _(e):
            for fn in B.streams["t"]:
                fn(e)

        @block.scalar
        def _(e):
            for fn in B.streams["s"]:
                fn(e)

        @block.vector
        def _(e):
            for fn in B.streams["v"]:
                fn(e)

        @block.gpsimd
        def _(e):
            for fn in B.streams["g"]:
                fn(e)

        @block.sync
        def _(e):
            for fn in B.streams["y"]:
                fn(e)

    _CACHE["nc"] = nc
    return nc


def _load_l0_head_w(h, waits=()):
    slot = W_SLOT[h % 2]
    q = "gw0" if h % 2 == 0 else "gw1"
    n = 0
    for nm, ko in W_KIND.items():
        src = AP(B.d[nm].tensor, h * 128, [[1024, 128], [131072, 8], [1, 128]])
        dst = AP(B.WB, slot + ko, [[WB_COLS, 128], [128, 8], [1, 128]])
        n = dma(q, dst, src, waits=waits)
        waits = ()
    return n


def _load_w_mk(q, nm, ncols, nchunk, woff, waits=()):
    src = B.d[nm]
    dst = AP(B.WB, woff, [[WB_COLS, 128], [ncols, nchunk], [1, ncols]])
    sap = AP(src.tensor, 0, [[ncols, 128], [128 * ncols, nchunk], [1, ncols]])
    return dma(q, dst, sap, waits=waits)


def _emit():
    d, sb, AB, AFt = B.d, B.sb, B.AB, B.AF

    for nm in ("ident", "ones_b", "ones_f", "selA", "selB", "masks",
               "b_bq", "b_bqc", "b_bqr", "b_bqr_rot", "b_bkv", "b_bkr",
               "b_bkr_rot", "b_bkc", "b_b1", "b_bv"):
        dma("y", sb[nm][:], d[nm])
    dma("y", sb["u32s"][:], d["peer_off"])
    dma("y", AFt[:, TKA:TKA + T], d["tabk_a"])
    dma("y", AFt[:, TKB:TKB + T], d["tabk_b"])
    dma("y", AFt[:, TQA:TQA + TOWN], d["tabq_a"])
    dma("y", AFt[:, TQB:TQB + TOWN], d["tabq_b"])
    for mc in range(8):
        dma("g", AB[:, XT + mc * T:XT + (mc + 1) * T],
            d["xT_perm"][mc * 128:(mc + 1) * 128, :])
    n0 = _load_l0_head_w(0)
    n1 = _load_l0_head_w(1)

    nld, ngl = B.qsem["y"].n, B.qsem["g"].n
    for k in ENGS:
        em(k, lambda e, _h=B.qsem["y"].h, _n=nld: e.wait_ge(_h, _n))
        em(k, lambda e, _h=B.qsem["g"].h, _n=ngl: e.wait_ge(_h, _n))

    B._attn_prev_ev = None
    B._attn_ost_dma = None
    B._l0_wld = {0: n0, 1: n1}

    def cut(ph):
        if ATH_PH == ph:
            _dbg_out()
            return True
        return False

    if cut("none"):
        return
    nheads = L0H if ATH_PH not in ("h1", "h2") else int(ATH_PH[1])
    for h in range(nheads):
        _proj_head_l0(h)
        _bound_l0(h)
        _attn_head(0, h)
    if cut("l0a") or ATH_PH in ("h1", "h2") and (_dbg_out() or True):
        return
    _tail(0)
    if cut("tail0"):
        return
    _ffn0()
    if cut("ffn0"):
        return
    _exchange()
    if cut("xc"):
        return
    _l1_proj_common()
    if cut("l1p"):
        return
    for p in range(4):
        _l1_pass(p)
        if cut(f"pass{p}"):
            return
    _tail(1)
    if cut("tail1"):
        return
    _ffn1()
    if cut("ffn1"):
        return
    _final_ln_out()


def _dbg_out():
    barrier("dbg")
    for tt in range(16):
        dma("y", B.d["y"][tt * 128:(tt + 1) * 128, :],
            B.AF[:, STRM + tt * 1024: STRM + (tt + 1) * 1024])
    em("y", lambda e, _h=B.qsem["y"].h, _n=B.qsem["y"].n: e.wait_ge(_h, _n))
    em("g", lambda e, _h=B.qsem["g"].h, _n=B.qsem["g"].n: e.wait_ge(_h, _n))



def _rope_unit(kind, idx, h, unit, rope_done, extra_w=()):
    AB, WB, ps = B.AB, B.WB, B.ps
    S = B.S
    w = 512 if kind == "k" else 256
    slot = W_SLOT[h % 2]
    if kind == "k":
        rhs_off = idx * 512
        wofA, wofB = slot + W_KIND["l0_wk"], slot + W_KIND["l0_wk_rot"]
        dst = KTA + idx * 512
        tba, tbb = TKA + idx * 512, TKB + idx * 512
    else:
        rhs_off = (2 * idx) * BLK
        wofA, wofB = slot + W_KIND["l0_wq"], slot + W_KIND["l0_wq_rot"]
        dst = QTA + idx * 256
        tba, tbb = TQA + idx * 256, TQB + idx * 256
    if unit % 2 == 0:
        pA, pB = ps["pA"][:, 0:w], ps["pB"][:, 0:w]
    else:
        pA, pB = ps["pS0"][:, 0:w], ps["pS0"][:, 512:512 + w]
    waits = tuple(extra_w)
    if unit - 2 in rope_done and rope_done[unit - 2] is not None:
        waits += ((S["v"], rope_done[unit - 2]),)
    for mc in range(8):
        t_mm(pA, WB[:, wofA + mc * 128: wofA + mc * 128 + 128],
             AB[:, XT + mc * T + rhs_off: XT + mc * T + rhs_off + w],
             start=(mc == 0), stop=(mc == 7), waits=(waits if mc == 0 else ()))
    for mc in range(8):
        t_mm(pB, WB[:, wofB + mc * 128: wofB + mc * 128 + 128],
             AB[:, XT + mc * T + rhs_off: XT + mc * T + rhs_off + w],
             start=(mc == 0), stop=(mc == 7), inc=(mc == 7))
    tN = S["t"].n

    def f(e, _pA=pA, _pB=pB, _dst=dst, _ta=tba, _tb=tbb, _w=w, _u=unit):
        AFl = B.AF
        fsb = FS + (_u % 2) * 1024
        t1 = AFl[:, fsb:fsb + _w]
        t2 = AFl[:, fsb + 512:fsb + 512 + _w]
        e.tensor_tensor(t1, _pA, AFl[:, _ta:_ta + _w], ALU.mult)
        vc(e, e.tensor_tensor(t2, _pB, AFl[:, _tb:_tb + _w], ALU.mult))
        return e.tensor_tensor(B.AB[:, _dst:_dst + _w], t1, t2, ALU.add)
    rope_done[unit] = e_op("v", f, waits=((S["t"], tN),), inc=True)


def _proj_head_l0(h):
    AB, WB, ps = B.AB, B.WB, B.ps
    S = B.S
    wldw = ((B.qsem["gw0" if h % 2 == 0 else "gw1"], B._l0_wld[h]),) \
        if hasattr(B, "_l0_wld") and h in B._l0_wld else ()
    wldw += ((S["s"], S["s"].n), (S["v"], S["v"].n))
    vpad = e_op("v", lambda e: e.memset(
        AP(B.AB, VA + 64, [[AB_COLS, 128], [VW0, 32], [1, 1]]), 1.0), inc=True)
    rope_done = {}
    for tt in range(8):
        if tt == 0:
            rope_done[-2] = None
        _rope_unit("k", tt, h, tt, rope_done, extra_w=(wldw if tt == 0 else ()))
        wldw = ()
    for jb in range(8):
        _rope_unit("q", jb, h, 8 + jb, rope_done)
    B.l0_rope_last = rope_done[15]

    vev = {}
    for vb in range(8):
        half = vb % 2
        waits = ((S["s"], vev[vb - 2]),) if vb >= 2 else ()
        for i in range(4):
            tl = vb * 4 + i
            for mc in range(8):
                t_mm(ps["pS1"][:, half * 512 + i * 128: half * 512 + i * 128 + 128],
                     AB[:, XT + mc * T + tl * 128: XT + mc * T + tl * 128 + 128],
                     WB[:, W_SLOT[h % 2] + W_KIND["l0_wv"] + mc * 128: W_SLOT[h % 2] + W_KIND["l0_wv"] + mc * 128 + 128],
                     start=(mc == 0), stop=(mc == 7),
                     waits=(waits if (i == 0 and mc == 0) else ()),
                     inc=(i == 3 and mc == 7))
        tN = S["t"].n

        def f_ev(e, _vb=vb, _half=half):
            src1 = AP(B.ps["pS1"], _half * 512, [[1024, 128], [128, 4], [1, 64]])
            dst1 = AP(B.AB, VA + (_vb * 4) * VW0, [[AB_COLS, 128], [VW0, 4], [1, 64]])
            e.activation(dst1, src1, AF.Copy)
            src2 = AP(B.ps["pS1"], _half * 512 + 64, [[1024, 128], [128, 4], [1, 64]])
            dst2 = AP(B.AB, VA + (_vb * 4) * VW0 + 65, [[AB_COLS, 128], [VW0, 4], [1, 64]])
            return e.activation(dst2, src2, AF.Copy)
        vev[vb] = e_op("s", f_ev, waits=((S["t"], tN), (S["v"], vpad)), inc=True)
    B.l0_vev_last = vev[7]
    B._l0_projt = S["t"].n
    if h + 2 < L0H:
        if not hasattr(B, "_l0_wld"):
            B._l0_wld = {}
        B._l0_wld[h + 2] = _load_l0_head_w(h + 2, waits=((S["t"], S["t"].n),))



def _bound_l0(h):
    AB, ps, sb = B.AB, B.ps, B.sb
    S = B.S
    units = [("k", tc) for tc in range(8)] + [("q", tc) for tc in range(4)]
    mmN = {}
    for u, (kind, tc) in enumerate(units):
        base = (KTA if kind == "k" else QTA) + tc * 512
        w = ((S["v"], B.l0_rope_last),) if u == 0 else ()
        if u - 2 in mmN:
            w += ((S["t"], mmN[u - 2]),)
        eb = EA + (u % 2) * 1024

        def f_sq(e, _b=base, _e=eb):
            return e.tensor_tensor(B.AB[:, _e:_e + 512], B.AB[:, _b:_b + 512],
                                   B.AB[:, _b:_b + 512], ALU.mult)
        vN = e_op("v", f_sq, waits=w, inc=True)
        row = tc if kind == "k" else 8 + tc
        mmN[u] = t_mm(ps["pA"][0:12, 0:512], sb["selA"][:, row * 12: row * 12 + 12],
                      AB[:, eb:eb + 512], start=(u == 0), stop=(u == len(units) - 1),
                      waits=((S["v"], vN),), inc=True)
    tN = S["t"].n
    def f_mxc(e):
        vc(e, e.reduce_max(B.sb["nscr"][0:12, 0:1], B.ps["pA"][0:12, 0:512], AX.X))
        return e.tensor_copy(B.sb["tinyb"][0:12, 0:1], B.sb["nscr"][0:12, 0:1])
    vN = e_op("v", f_mxc, waits=((S["t"], tN),), inc=True)
    tN = e_op("t", lambda e: e.transpose(B.ps["pT"][0:1, 0:12],
                                         B.sb["tinyb"][0:12, 0:1],
                                         B.sb["ident"][0:12, 0:12]),
              waits=((S["v"], vN),), inc=True)

    def f_mx(e):
        e.reduce_max(B.sb["tiny"][0:1, 0:1], B.ps["pT"][0:1, 0:8], AX.X)
        vc(e, e.reduce_max(B.sb["tiny"][0:1, 1:2], B.ps["pT"][0:1, 8:12], AX.X))
        return e.tensor_tensor(B.sb["tiny"][0:1, 2:3], B.sb["tiny"][0:1, 0:1],
                               B.sb["tiny"][0:1, 1:2], ALU.mult)
    vN = e_op("v", f_mx, waits=((S["t"], tN),), inc=True)
    sN = e_op("s", lambda e: e.activation(B.sb["tiny"][0:1, 3:4],
                                          B.sb["tiny"][0:1, 2:3], AF.Sqrt),
              waits=((S["v"], vN),), inc=True)
    tN = t_mm(ps["pB"][:, 0:1], sb["ones_f"][0:1, :], sb["tiny"][0:1, 3:4],
              start=True, stop=True, waits=((S["s"], sN),), inc=True)
    e_op("v", lambda e, _h=h: e.tensor_scalar(B.sb["cbias"][:, _h:_h + 1],
                                              B.ps["pB"][:, 0:1], -1.0 / 128.0,
                                              None, ALU.mult),
         waits=((S["t"], tN),), inc=True)


def _bound_l1(p):
    AB, ps, sb = B.AB, B.ps, B.sb
    S = B.S
    units = ([("k", cc, tc) for cc in range(2) for tc in range(8)] +
             [("q", cc, tc) for cc in range(2) for tc in range(4)])
    first = True
    mmN = {}
    for nu, (kind, cc, tc) in enumerate(units):
        c = 2 * p + cc
        if kind == "k":
            base = (L1_KRG + tc * 512) if c == 7 else (L1_KT + cc * T + tc * 512)
            arr = B.AB
        else:
            base = (W_QRG + tc * 512) if c == 7 else (L1_QT + cc * TOWN + tc * 512)
            arr = B.WB if c == 7 else B.AB
        w = ((S["s"], S["s"].n),) if first else ()
        if nu - 2 in mmN:
            w += ((S["t"], mmN[nu - 2]),)
        eb = L1_E + (nu % 2) * 1024

        def f_sq(e, _a=arr, _b=base, _e=eb):
            return e.tensor_tensor(B.AB[:, _e:_e + 512], _a[:, _b:_b + 512],
                                   _a[:, _b:_b + 512], ALU.mult)
        vN = e_op("v", f_sq, waits=w, inc=True)
        v = (0 if kind == "k" else 2) + cc
        mmN[nu] = t_mm(ps["pA"][0:8, 0:512], sb["selB"][:, v * 8: v * 8 + 8],
                       AB[:, eb:eb + 512], start=first, stop=(nu == len(units) - 1),
                       waits=((S["v"], vN),), inc=True)
        first = False
    tN = S["t"].n
    def f_mxc(e):
        vc(e, e.reduce_max(B.sb["nscr"][0:8, 0:1], B.ps["pA"][0:8, 0:512], AX.X))
        return e.tensor_copy(B.sb["tinyb"][0:8, 0:1], B.sb["nscr"][0:8, 0:1])
    vN = e_op("v", f_mxc, waits=((S["t"], tN),), inc=True)
    tN = e_op("t", lambda e: e.transpose(B.ps["pT"][0:1, 0:8],
                                         B.sb["tinyb"][0:8, 0:1],
                                         B.sb["ident"][0:8, 0:8]),
              waits=((S["v"], vN),), inc=True)

    def f_pm(e):
        vc(e, e.tensor_copy(B.sb["tiny"][0:1, 8:16], B.ps["pT"][0:1, 0:8]))
        return e.tensor_tensor(B.sb["tiny"][0:1, 0:4], B.sb["tiny"][0:1, 8:12],
                               B.sb["tiny"][0:1, 12:16], ALU.mult)
    vN = e_op("v", f_pm, waits=((S["t"], tN),), inc=True)
    sN = e_op("s", lambda e: e.activation(B.sb["tiny"][0:1, 4:8],
                                          B.sb["tiny"][0:1, 0:4], AF.Sqrt),
              waits=((S["v"], vN),), inc=True)
    tN = t_mm(ps["pB"][:, 0:4], sb["ones_f"][0:1, :], sb["tiny"][0:1, 4:8],
              start=True, stop=True, waits=((S["s"], sN),), inc=True)
    e_op("v", lambda e: e.tensor_scalar(B.sb["cbias"][:, 16:20], B.ps["pB"][:, 0:4],
                                        -1.0 / 8.0, None, ALU.mult),
         waits=((S["t"], tN),), inc=True)



def _attn_head(layer, h, l1ctx=None):
    AB, ps, sb = B.AB, B.ps, B.sb
    S = B.S
    scale = (1.0 / 128.0) if layer == 0 else (1.0 / 8.0)
    col = h if layer == 0 else 16 + (h % 4)
    ebase = EA if layer == 0 else L1_E
    ebuf = [ebase, ebase + 1024]
    ost = OST0 if layer == 0 else L1_OST
    flat = [(j, b) for j in range(NQG) for b in range(j + 1)]
    exp_done = {}
    av_done = {}
    mask_done = {}
    prev_ev = B._attn_prev_ev
    ost_dma = B._attn_ost_dma
    head_w = ((S["s"], S["s"].n), (S["v"], S["v"].n))

    state = {"prev_ev": prev_ev, "ost_dma": ost_dma}
    defq = []

    def flush():
        for f in defq:
            f()
        del defq[:]

    def emit_av(fi, j, b, eb, gate):
        nb = j + 1
        bl = 4
        avcol = (j % 2) * 256
        if layer == 0:
            for half in range(2):
                for i in range(bl):
                    first = (b == 0 and i == 0)
                    last = (b == nb - 1 and i == bl - 1)
                    kt = 4 * b + i
                    co = kt * VW0 + (65 if half else 0)
                    vt = AB[:, VA + co: VA + co + (64 if half else 65)]
                    pp = (ps["pB"][0:64, avcol:avcol + 256] if half else
                          ps["pAV"][0:65, avcol:avcol + 256])
                    w = (gate,) if (i == 0 and half == 0) else ()
                    if first and half == 0 and state["prev_ev"] is not None:
                        w += (state["prev_ev"],)
                    t_mm(pp, vt, AB[:, eb + i * 256: eb + (i + 1) * 256],
                         start=first, stop=last, waits=w,
                         inc=(i == bl - 1 and half == 1))
        else:
            for i in range(bl):
                first = (b == 0 and i == 0)
                last = (b == nb - 1 and i == bl - 1)
                kt = 4 * b + i
                vlo = l1ctx["V_sl"](h, kt)
                w = (gate,) if i == 0 else ()
                if first and state["prev_ev"] is not None:
                    w += (state["prev_ev"],)
                t_mm(ps["pAV"][0:65, avcol:avcol + 256], vlo,
                     AB[:, eb + i * 256: eb + (i + 1) * 256],
                     start=first, stop=last, waits=w, inc=(i == bl - 1))
        av_done[fi] = S["t"].n
        if b != nb - 1:
            return
        tN_av = S["t"].n
        scol = 0

        def f_ln(e, _sc=scol, _ac=avcol):
            vc(e, e.activation(B.sb["sumb"][64:65, _sc:_sc + 256],
                               B.ps["pAV"][64:65, _ac:_ac + 256], AF.Ln), "s")
            return e.activation(B.sb["sumb"][64:65, _sc + 256:_sc + 512],
                                B.sb["sumb"][64:65, _sc:_sc + 256], AF.Exp,
                                scale=-1.0)
        sN = e_op("s", f_ln, waits=((S["t"], tN_av),), inc=True)
        tN = t_mm(ps["pA"][:, 0:256], sb["ones_f"][64:65, :],
                  sb["sumb"][64:65, scol + 256:scol + 512],
                  start=True, stop=True, waits=((S["s"], sN),), inc=True)
        w_ost = [(S["t"], tN)]
        if state["ost_dma"] is not None:
            w_ost.append((B.qsem["go"], state["ost_dma"]))

        def f_on(e, _ac=avcol, _h=h, _lay=layer, _o=ost):
            rb = B.sb["sumb"][0:64, 0:256]
            vc(e, e.tensor_copy(rb, B.ps["pA"][0:64, 0:256]))
            if _lay == 0:
                e.tensor_tensor(B.AB[0:64, _o:_o + 256],
                                B.ps["pAV"][0:64, _ac:_ac + 256], rb, ALU.mult)
                return e.tensor_tensor(B.AB[0:64, _o + 256:_o + 512],
                                       B.ps["pB"][0:64, _ac:_ac + 256], rb, ALU.mult)
            vc(e, e.tensor_tensor(B.AB[0:64, _o:_o + 256],
                                  B.ps["pAV"][0:64, _ac:_ac + 256], rb, ALU.mult))
            return e.tensor_scalar(B.AB[0:64, _o:_o + 256],
                                   B.AB[0:64, _o:_o + 256],
                                   B.sb["b_bv"][0:64, _h:_h + 1], None, ALU.add)
        vN = e_op("v", f_on, waits=tuple(w_ost), inc=True)
        state["prev_ev"] = (S["v"], vN)
        if layer == 0:
            r0 = h * 128
            dma("go", B.d["spill_o"][r0:r0 + 64, j * 256:(j + 1) * 256],
                AB[0:64, ost:ost + 256], waits=((S["v"], vN),))
            state["ost_dma"] = dma("go",
                                   B.d["spill_o"][r0 + 64:r0 + 128, j * 256:(j + 1) * 256],
                                   AB[0:64, ost + 256:ost + 512])
        else:
            r0 = h * 64
            state["ost_dma"] = dma("go",
                                   B.d["spill_o"][r0:r0 + 64, j * 256:(j + 1) * 256],
                                   AB[0:64, ost:ost + 256], waits=((S["v"], vN),))

    for fi, (j, b) in enumerate(flat):
        nb = j + 1
        pS = ps["pS0"] if fi % 2 == 0 else ps["pS1"]
        bl = 4
        for i in range(bl):
            kt = 4 * b + i
            if layer == 0:
                lhsT = AB[:, KTA + kt * 128: KTA + kt * 128 + 128]
                rhs = AB[:, QTA + j * 256: QTA + j * 256 + 256]
            else:
                lhsT = l1ctx["KT_sl"](h, kt)
                rhs = l1ctx["QT_sl"](h, j)
            w = ()
            if i == 0:
                if fi - 2 in exp_done:
                    w += ((S["s"], exp_done[fi - 2]),)
                    if fi - 2 in mask_done:
                        w += ((S["v"], mask_done[fi - 2]),)
                if fi == 0:
                    w += head_w
            t_mm(pS[:, i * 256:(i + 1) * 256], lhsT, rhs,
                 start=True, stop=True, waits=w, inc=(i == bl - 1))
        tN = S["t"].n
        flush()
        eb = ebuf[fi % 2]
        w_exp = [(S["t"], tN)]
        if fi - 2 in av_done:
            w_exp.append((S["t"], av_done[fi - 2]))

        def f_exp(e, _pS=pS, _eb=eb, _c=col, _sc=scale):
            return e.activation(B.AB[:, _eb:_eb + 1024], _pS[:, 0:1024],
                                AF.Exp, bias=B.sb["cbias"][:, _c:_c + 1], scale=_sc)
        sN = e_op("s", f_exp, waits=tuple(w_exp), inc=True)
        exp_done[fi] = sN
        if b == nb - 1:
            def f_mask(e, _eb=eb):
                return e.tensor_tensor(B.AB[:, _eb:_eb + 1024],
                                       B.AB[:, _eb:_eb + 1024],
                                       B.sb["masks"][:, 0:1024], ALU.mult)
            vN = e_op("v", f_mask, waits=((S["s"], sN),), inc=True)
            mask_done[fi] = vN
            gate = (S["v"], vN)
        else:
            gate = (S["s"], sN)
        defq.append(lambda _fi=fi, _j=j, _b=b, _eb=eb, _g=gate:
                    emit_av(_fi, _j, _b, _eb, _g))
    flush()
    if layer == 0:
        if not hasattr(B, "_l0_go_n"):
            B._l0_go_n = {}
        B._l0_go_n[h] = B.qsem["go"].n
    B._attn_prev_ev = state["prev_ev"]
    B._attn_ost_dma = state["ost_dma"]



def _tail(layer):
    AB, WB, ps, sb, AFl = B.AB, B.WB, B.ps, B.sb, B.AF
    d = B.d
    S = B.S
    if layer == 0:
        for tt in range(16):
            jj = tt // 2
            src = d["x_perm"][(2 * jj) * BLK + (tt % 2) * 128:
                              (2 * jj) * BLK + (tt % 2) * 128 + 128, :]
            w = ((S["v"], B.l0_rope_last),) if tt == 0 else ()
            dma("y" if tt % 2 == 0 else "ys",
                AFl[:, STRM + tt * 1024: STRM + (tt + 1) * 1024], src, waits=w)
        wT = ((S["t"], B._l0_projt),)
        for oc in range(8):
            dma("y" if oc % 2 == 0 else "ys",
                AB[:, ORB + oc * 2048: ORB + (oc + 1) * 2048],
                d["spill_o"][oc * 128:(oc + 1) * 128, :],
                waits=wT + ((B.qsem["go"], B._l0_go_n[oc]),))
            wT = ()
    else:
        em("y", lambda e, _h=B.qsem["yx"].h, _n=B.qsem["yx"].n: e.wait_ge(_h, _n))
        for tt in range(16):
            src = d["spill_x"][tt * 128:(tt + 1) * 128, :]
            w = ((S["v"], B._l1p_v_end),) if tt == 0 else ()
            dma("y" if tt % 2 == 0 else "ys",
                AFl[:, STRM + tt * 1024: STRM + (tt + 1) * 1024], src, waits=w)
    barrier(f"tl{layer}")
    B._attn_prev_ev = None
    B._attn_ost_dma = None

    wof = W_WO
    if layer == 0:
        _load_w_mk("g", "l0_wo", 1024, 8, wof)
    if layer == 1:
        em("y", lambda e, _h=B.qsem["go"].h, _n=B.qsem["go"].n: e.wait_ge(_h, _n))
        for oc in range(8):
            dma("y" if oc % 2 == 0 else "ys",
                AB[:, ORB + oc * 2048: ORB + (oc + 1) * 2048],
                d["spill_o"][oc * 128:(oc + 1) * 128, :])
    for k in ("t", "s", "v"):
        for qn in ("y", "ys", "g", "gw0", "gw1"):
            em(k, lambda e, _h=B.qsem[qn].h, _n=B.qsem[qn].n: e.wait_ge(_h, _n))

    B._tail_v = {}
    if ATH_TC == "dumpo":
        em("y", lambda e, _h=B.qsem["g"].h, _n=B.qsem["g"].n: e.wait_ge(_h, _n))
        for i in range(8):
            ld = dma("y", AB[:, ABM_TMP: ABM_TMP + 2048] if False else AB[:, 0:2048],
                     d["spill_o"][i * 128:(i + 1) * 128, :],
                     waits=())
            def f_cp(e, _i=i):
                return e.tensor_copy(B.AF[:, STRM + _i * 2048: STRM + (_i + 1) * 2048],
                                     B.AB[:, 0:2048])
            vN = e_op("v", f_cp, waits=((B.qsem["y"], ld),), inc=True)
            em("y", lambda e, _h=S["v"].h, _n=vN: e.wait_ge(_h, _n))
        for tt in range(16):
            dma("y", B.d["y"][tt * 128:(tt + 1) * 128, :],
                B.AF[:, STRM + tt * 1024: STRM + (tt + 1) * 1024])
        em("y", lambda e, _h=B.qsem["y"].h, _n=B.qsem["y"].n: e.wait_ge(_h, _n))
        barrier(f"tle{layer}")
        return
    if ATH_TC in ("w_t", "w_s", "w_v"):
        sem = {"w_t": S["t"], "w_s": S["s"], "w_v": S["v"]}[ATH_TC]
        em("y", lambda e, _h=sem.h, _n=sem.n: e.wait_ge(_h, _n))
        barrier(f"tle{layer}")
        return
    if ATH_TC == "ld":
        barrier(f"tle{layer}")
        return
    ntt = 16
    if ATH_TC.startswith("mm") and ATH_TC[2:].isdigit():
        ntt = int(ATH_TC[2:])
    if ATH_TC == "cponly":
        tN = e_op("t", lambda e: e.nop(), inc=True)
        e_op("v", lambda e: e.tensor_copy(B.AF[:, FS:FS + 512], ps["pS0"][:, 0:512]),
             waits=((S["t"], tN),), inc=True)
        barrier(f"tle{layer}")
        return
    for tt in range(ntt):
        pL = ps["pS0"][:, 0:512] if tt % 2 == 0 else ps["pS1"][:, 0:512]
        pR = ps["pS0"][:, 512:1024] if tt % 2 == 0 else ps["pS1"][:, 512:1024]
        w = ((S["v"], B._tail_v[tt - 2]),) if tt >= 2 else ()
        for half, pp in ((0, pL), (1, pR)):
            for oc in range(8):
                t_mm(pp, AB[:, ORB + oc * 2048 + tt * 128: ORB + oc * 2048 + tt * 128 + 128],
                     WB[:, wof + oc * 1024 + half * 512: wof + oc * 1024 + half * 512 + 512],
                     start=(oc == 0), stop=(oc == 7 and layer == 0),
                     waits=(w if (half == 0 and oc == 0) else ()),
                     inc=(oc == 7 and layer == 0))
            if layer == 1:
                t_mm(pp, sb["ones_b"][0:1, :],
                     WB[0:1, W_BO + half * 512: W_BO + half * 512 + 512],
                     start=False, stop=True, inc=True)
        tN = B.S["t"].n
        if ATH_TC.startswith("mm"):
            if ATH_TC == "mmnc":
                B._tail_v[tt] = e_op("v", lambda e: e.nop(), waits=((S["t"], tN),), inc=True)
            elif ATH_TC == "mmd":
                def f_d(e, _pL=pL, _pR=pR, _tt=tt):
                    stc = STRM + _tt * 1024
                    e.tensor_copy(B.AF[:, stc:stc + 512], _pL)
                    return e.tensor_copy(B.AF[:, stc + 512:stc + 1024], _pR)
                B._tail_v[tt] = e_op("v", f_d, waits=((S["t"], tN),), inc=True)
            else:
                B._tail_v[tt] = e_op("v", lambda e, _pL=pL: e.tensor_copy(
                    B.AF[:, FS:FS + 512], _pL), waits=((S["t"], tN),), inc=True)
        else:
            _ln_tile(tt, tN, pL, pR, mode="attn")
    barrier(f"tle{layer}")


def _ln_tile(tt, tN, pL, pR, mode):
    AB, ps, sb, AFl = B.AB, B.ps, B.sb, B.AF
    S = B.S
    st = STRM + tt * 1024

    if ATH_TC == "lnr":
        def f_r(e, _st=st, _pL=pL, _pR=pR):
            e.tensor_tensor(AFl[:, _st:_st + 512], AFl[:, _st:_st + 512], _pL, ALU.add)
            return e.tensor_tensor(AFl[:, _st + 512:_st + 1024],
                                   AFl[:, _st + 512:_st + 1024], _pR, ALU.add)
        B._tail_v[tt] = e_op("v", f_r, waits=(((S["t"], tN),) if tN else ()), inc=True)
        return
    if ATH_TC == "lnr2":
        def f_r(e, _st=st, _pL=pL, _pR=pR):
            e.tensor_tensor(AFl[:, _st:_st + 512], _pL, AFl[:, _st:_st + 512], ALU.add)
            return e.tensor_tensor(AFl[:, _st + 512:_st + 1024], _pR,
                                   AFl[:, _st + 512:_st + 1024], ALU.add)
        B._tail_v[tt] = e_op("v", f_r, waits=(((S["t"], tN),) if tN else ()), inc=True)
        return
    if ATH_TC == "lnr4":
        if tt % 2 == 0:
            def f_r(e, _st=st):
                return e.tensor_scalar(AFl[:, _st:_st + 1024], AFl[:, _st:_st + 1024],
                                       1.0, None, ALU.mult)
        else:
            def f_r(e, _st=st, _pL=pL):
                e.tensor_copy(AFl[:, FS:FS + 512], _pL)
                return e.tensor_tensor(AFl[:, _st:_st + 512], AFl[:, _st:_st + 512],
                                       AFl[:, FS:FS + 512], ALU.add)
        B._tail_v[tt] = e_op("v", f_r, waits=(((S["t"], tN),) if tN else ()), inc=True)
        return
    if ATH_TC == "lnr3":
        def f_r(e, _st=st, _pL=pL, _pR=pR):
            e.scalar_tensor_tensor(AFl[:, _st:_st + 512], _pL, 1.0,
                                   AFl[:, _st:_st + 512], ALU.mult, ALU.add)
            return e.scalar_tensor_tensor(AFl[:, _st + 512:_st + 1024], _pR, 1.0,
                                          AFl[:, _st + 512:_st + 1024], ALU.mult, ALU.add)
        B._tail_v[tt] = e_op("v", f_r, waits=(((S["t"], tN),) if tN else ()), inc=True)
        return

    def f_res(e, _st=st, _pL=pL, _pR=pR, _m=mode):
        if _m == "attn":
            e.tensor_tensor(AFl[:, _st:_st + 512], AFl[:, _st:_st + 512], _pL, ALU.add)
            vc(e, e.tensor_tensor(AFl[:, _st + 512:_st + 1024],
                                  AFl[:, _st + 512:_st + 1024], _pR, ALU.add))
        vc(e, e.reduce_sum(sb["lnscr"][:, 0:1], AFl[:, _st:_st + 1024], AX.X))
        vc(e, e.tensor_scalar(sb["lnscr"][:, 1:2], sb["lnscr"][:, 0:1], 1.0 / DM,
                              None, ALU.mult))
        return e.tensor_scalar(AFl[:, _st:_st + 1024], AFl[:, _st:_st + 1024],
                               sb["lnscr"][:, 1:2], None, ALU.subtract)
    vN = e_op("v", f_res, waits=(((S["t"], tN),) if tN else ()), inc=True)
    sN = e_op("s", lambda e, _st=st: e.activation(AFl[:, FS:FS + 1024],
                                                  AFl[:, _st:_st + 1024], AF.Square,
                                                  accum_out=sb["lnscr"][:, 2:3]),
              waits=((S["v"], vN),), inc=True)

    def f_rstd(e):
        vc(e, e.tensor_scalar(sb["lnscr"][:, 3:4], sb["lnscr"][:, 2:3], 1.0 / DM,
                              EPS, ALU.mult, ALU.add))
        return e.reciprocal(sb["lnscr"][:, 4:5], sb["lnscr"][:, 3:4])
    vN = e_op("v", f_rstd, waits=((S["s"], sN),), inc=True)
    sN = e_op("s", lambda e: e.activation(sb["lnscr"][:, 5:6], sb["lnscr"][:, 4:5],
                                          AF.Sqrt),
              waits=((S["v"], vN),), inc=True)
    if ATH_TC == "lnp":
        def f_p(e, _st=st):
            return e.tensor_copy(AFl[:, _st:_st + 8], sb["lnscr"][:, 0:8])
        B._tail_v[tt] = e_op("v", f_p, waits=((S["s"], sN),), inc=True)
        return
    vN = e_op("v", lambda e, _st=st: e.tensor_scalar(AFl[:, _st:_st + 1024],
                                                     AFl[:, _st:_st + 1024],
                                                     sb["lnscr"][:, 5:6], None,
                                                     ALU.mult),
              waits=((S["s"], sN),), inc=True)
    B._tail_v[tt] = vN

    if mode == "ffn1":
        dma("y", B.d["y"][tt * 128:(tt + 1) * 128, :], AFl[:, st:st + 1024],
            waits=((S["v"], vN),))
        return
    if mode == "ffn0":
        dma("yx", B.d["spill_x"][tt * 128:(tt + 1) * 128, :], AFl[:, st:st + 1024],
            waits=((S["v"], vN),))
        dma("g", B.d["cc_in"][tt * 128:(tt + 1) * 128, :], AFl[:, st:st + 1024],
            waits=((S["v"], vN),))
        return
    if ATH_TC == "ln":
        return
    stg = EA
    sN = e_op("s", lambda e, _st=st, _sg=stg: e.activation(
        AB[:, _sg:_sg + 1024], AFl[:, _st:_st + 1024], AF.Copy),
        waits=((S["v"], vN),), inc=True)
    jj = tt // 2
    poscol = jj * 256 + (tt % 2) * 128
    for half in range(2):
        tNt = 0
        for mb in range(4):
            mc = half * 4 + mb
            tNt = e_op("t", lambda e, _mc=mc, _mb=mb, _sg=stg, _hf=half:
                       e.transpose(B.ps["pT"][:, _hf * 512 + _mb * 128: _hf * 512 + (_mb + 1) * 128],
                                   B.AB[:, _sg + _mc * 128: _sg + _mc * 128 + 128],
                                   B.sb["ident"][:]),
                       waits=(((S["s"], sN),) if (half == 0 and mb == 0) else
                              ((S["v"], B._pt_ev),) if (half == 1 and mb == 0 and hasattr(B, "_pt_ev")) else ()),
                       inc=(mb == 3))

        def f_tev(e, _half=half, _pc=poscol):
            src = AP(B.ps["pT"], _half * 512, [[1024, 128], [128, 4], [1, 128]])
            dst = AP(B.AB, XTO + (_half * 4) * 2048 + _pc,
                     [[AB_COLS, 128], [2048, 4], [1, 128]])
            return e.tensor_copy(dst, src)
        B._pt_ev = e_op("v", f_tev, waits=((S["t"], tNt),), inc=True)



def _ffn0():
    AB, WB, ps, sb, AFl = B.AB, B.WB, B.ps, B.sb, B.AF
    S = B.S
    d = B.d
    barrier("f0")
    stats = _tern_stats_units()
    _statn = [0]

    def stats_slot():
        if _statn[0] < len(stats):
            stats[_statn[0]]()
            _statn[0] += 1
    gu_tp = {}
    d_tp = {}
    comp_v = {}
    stt_v = {}
    acc_v = {}
    guu = 0
    duu = 0

    for c in range(6):
        ws = F0_WS[c % 2]
        for wi, nm in ((0, "l0_wg"), (1, "l0_wu")):
            for st_i in range(4):
                gs = F0_GS[guu % 2]
                wld = ()
                if guu - 2 >= 0:
                    wld = ((S["t"], gu_tp[guu - 2]),)
                ld = dma("gu0" if guu % 2 == 0 else "gu1", AB[:, gs:gs + 1024],
                         AP(d[nm].tensor, (c * 512 + st_i * 128) * 1024,
                            [[1024, 128], [1, 1024]]), waits=wld)
                tN = 0
                for mg in range(2):
                    par = (guu * 2 + mg) % 2
                    w0 = ((S["v"], max(B._ptev.values())),)
                    if mg == 0:
                        w0 += ((B.qsem["gu0" if guu % 2 == 0 else "gu1"], ld),)
                    for mb in range(4):
                        mc = mg * 4 + mb
                        tN = e_op("t", lambda e, _gs=gs, _mc=mc, _mb=mb, _p=par:
                                  e.transpose(B.ps["pT"][:, _p * 512 + _mb * 128: _p * 512 + (_mb + 1) * 128],
                                              B.AB[:, _gs + _mc * 128: _gs + _mc * 128 + 128],
                                              B.sb["ident"][:]),
                                  waits=(w0 if mb == 0 else ()),
                                  inc=(mb == 3))

                    def f_ev(e, _ws=ws, _wi=wi, _st=st_i, _mg=mg, _p=par):
                        src = AP(B.ps["pT"], _p * 512, [[1024, 128], [128, 4], [1, 128]])
                        dst = AP(B.AB, _ws + _wi * 4096 + (_mg * 4) * 512 + _st * 128,
                                 [[AB_COLS, 128], [512, 4], [1, 128]])
                        return e.tensor_copy(dst, src)
                    B._ptev[par] = e_op("v", f_ev, waits=((S["t"], tN),), inc=True)
                gu_tp[guu] = tN
                guu += 1
                stats_slot()
        for dsub in range(8):
            dsb = F0_DS[duu % 2]
            wld = ()
            if duu - 2 >= 0:
                wld = ((S["t"], d_tp[duu - 2]),)
            ld = dma("gd0" if duu % 2 == 0 else "gd1", AB[:, dsb:dsb + 512],
                     AP(d["l0_wd"].tensor, dsub * 128 * FF0 + c * 512,
                        [[FF0, 128], [1, 512]]), waits=wld)
            tN = 0
            par = duu % 2
            w0 = ((S["v"], max(B._ptev.values())),
                  (B.qsem["gd0" if duu % 2 == 0 else "gd1"], ld))
            for hb in range(4):
                tN = e_op("t", lambda e, _dsb=dsb, _hb=hb, _p=par:
                          e.transpose(B.ps["pT"][:, _p * 512 + _hb * 128: _p * 512 + (_hb + 1) * 128],
                                      B.AB[:, _dsb + _hb * 128: _dsb + _hb * 128 + 128],
                                      B.sb["ident"][:]),
                          waits=(w0 if hb == 0 else ()),
                          inc=(hb == 3))

            def f_evd(e, _ws=ws, _dsub=dsub, _p=par):
                src = AP(B.ps["pT"], _p * 512, [[1024, 128], [128, 4], [1, 128]])
                dst = AP(B.AB, _ws + 8192 + _dsub * 128,
                         [[AB_COLS, 128], [1024, 4], [1, 128]])
                return e.tensor_copy(dst, src)
            B._ptev[par] = e_op("v", f_evd, waits=((S["t"], tN),), inc=True)
            d_tp[duu] = tN
            duu += 1
            stats_slot()
        tp_v = S["v"].n

        for tq in range(8):
            for hs in range(4):
                bank = hs % 2
                pg = ps["pS0"][:, bank * 512: bank * 512 + 256]
                pu = ps["pS0"][:, bank * 512 + 256: bank * 512 + 512]
                w0 = ()
                if tq == 0 and hs == 0:
                    w0 = ((S["v"], tp_v),)
                key = (c, tq, hs - 2) if hs >= 2 else (c, tq - 1, hs + 2)
                if key in stt_v:
                    w0 += ((S["v"], stt_v[key]),)
                for mc in range(8):
                    t_mm(pg, AB[:, ws + (mc * 4 + hs) * 128: ws + (mc * 4 + hs) * 128 + 128],
                         AB[:, XTO + mc * 2048 + tq * 256: XTO + mc * 2048 + tq * 256 + 256],
                         start=(mc == 0), stop=(mc == 7),
                         waits=(w0 if mc == 0 else ()))
                for mc in range(8):
                    t_mm(pu, AB[:, ws + 4096 + (mc * 4 + hs) * 128: ws + 4096 + (mc * 4 + hs) * 128 + 128],
                         AB[:, XTO + mc * 2048 + tq * 256: XTO + mc * 2048 + tq * 256 + 256],
                         start=(mc == 0), stop=(mc == 7), inc=(mc == 7))
                tN = S["t"].n
                w_sil = [(S["t"], tN)]
                if (c, tq, hs, "stt") in stt_v:
                    pass
                prevstt = stt_v.get((c, tq, hs - 1), stt_v.get((c, tq - 1, 3) if hs == 0 else None))
                if prevstt:
                    w_sil.append((S["v"], prevstt))
                sN = e_op("s", lambda e, _pg=pg: e.activation(
                    B.AB[:, F0_SIL:F0_SIL + 256], _pg, AF.Silu, scale=1.0 / 127.0),
                    waits=tuple(w_sil), inc=True)
                vN = e_op("v", lambda e, _pu=pu, _hs=hs: e.scalar_tensor_tensor(
                    B.AB[:, F0_ST + _hs * 256: F0_ST + (_hs + 1) * 256], _pu,
                    1.0 / 127.0, B.AB[:, F0_SIL:F0_SIL + 256], ALU.mult, ALU.mult),
                    waits=((S["s"], sN),), inc=True)
                stt_v[(c, tq, hs)] = vN
            st_v = S["v"].n
            for ti in range(2):
                tt = tq * 2 + ti
                wdn = ((S["v"], st_v),)
                pk = (c, tq, ti - 1) if ti else ((c, tq - 1, 1) if tq else (c - 1, 7, 1))
                if pk in acc_v:
                    wdn += ((S["v"], acc_v[pk]),)
                for half, pp in ((0, ps["pAV"][:, 0:512]), (1, ps["pB"][:, 0:512])):
                    for hs in range(4):
                        t_mm(pp, AB[:, F0_ST + hs * 256 + ti * 128: F0_ST + hs * 256 + ti * 128 + 128],
                             AB[:, ws + 8192 + hs * 1024 + half * 512: ws + 8192 + hs * 1024 + half * 512 + 512],
                             start=(hs == 0), stop=(hs == 3),
                             waits=(wdn if (half == 0 and hs == 0) else ()),
                             inc=(hs == 3 and half == 1))
                tN = S["t"].n

                def f_acc(e, _tt=tt):
                    stc = STRM + _tt * 1024
                    e.scalar_tensor_tensor(B.AF[:, stc:stc + 512], B.ps["pAV"][:, 0:512],
                                           1.0 / 127.0, B.AF[:, stc:stc + 512],
                                           ALU.mult, ALU.add)
                    return e.scalar_tensor_tensor(B.AF[:, stc + 512:stc + 1024],
                                                  B.ps["pB"][:, 0:512], 1.0 / 127.0,
                                                  B.AF[:, stc + 512:stc + 1024],
                                                  ALU.mult, ALU.add)
                acc_v[(c, tq, ti)] = e_op("v", f_acc, waits=((S["t"], tN),), inc=True)
            stats_slot()
    while _statn[0] < len(stats):
        stats_slot()
    barrier("f0e")
    B._tail_v = {}
    for tt in range(16):
        _ln_tile(tt, 0, None, None, mode="ffn0")
    ng = B.qsem["g"].n
    em("g", lambda e, _h=B.qsem["g"].h, _n=ng: e.wait_ge(_h, _n))

    def f_cc(e):
        e.collective_compute(
            "AllGather", ALU.bypass,
            replica_groups=[[0, 1], [2, 3], [4, 5], [6, 7]],
            ins=[B.d["cc_in"].opt()],
            outs=[B.d["cc_out"].opt()]).then_inc(B.ccs.h, 1)
    em("g", f_cc)
    B.ccs.n += 1
    barrier("f0z")



def _exchange():
    AB, WB, ps, sb, AFl = B.AB, B.WB, B.ps, B.sb, B.AF
    S = B.S
    d = B.d
    nyx = B.qsem["yx"].n
    em("y", lambda e, _h=B.qsem["yx"].h, _n=nyx: e.wait_ge(_h, _n))
    barrier("xc")

    dma("y", AFl[:, TKA:TKA + T], d["tabk_a"])
    dma("y", AFl[:, TKB:TKB + T], d["tabk_b"])
    dma("y", AFl[:, TQA:TQA + TOWN], d["tabq_a"])
    dma("y", AFl[:, TQB:TQB + TOWN], d["tabq_b"])
    _load_w_mk("g", "l1_wq", 256, 8, W_C["l1_wq"])
    _load_w_mk("g", "l1_wqr", 128, 8, W_C["l1_wqr"])
    _load_w_mk("g", "l1_wqr_rot", 128, 8, W_C["l1_wqr_rot"])
    _load_w_mk("g", "l1_wkv", 256, 8, W_C["l1_wkv"])
    _load_w_mk("g", "l1_wkr", 128, 8, W_C["l1_wkr"])
    _load_w_mk("g", "l1_wkr_rot", 128, 8, W_C["l1_wkr_rot"])
    dma("g", AP(B.WB, W_BO, [[WB_COLS, 1], [1, DM]]), d["r_bo"])
    dma("g", AP(B.WB, W_B2, [[WB_COLS, 1], [1, DM]]), d["r_b2"])

    _tern_stats()

    B._xtev = {}
    for tt in range(16):
        stg = PSTG + (tt % 2) * 1024
        wld = ((S["v"], B._xtev[tt - 2]),) if (tt - 2) in B._xtev else ()
        qn = "yc0" if tt % 2 == 0 else "yc1"
        ld = dma(qn, AB[:, stg:stg + 1024], d["cc_in"][tt * 128:(tt + 1) * 128, :],
                 waits=wld)
        _xt_transpose(tt, stg, (qn, ld), own=True)
    for tt in range(16):
        stg = PSTG + (tt % 2) * 1024
        prev_key = (tt - 2 + 100) if tt >= 2 else (14 + tt)
        wv = B._xtev.get(prev_key)
        ld = _peer_tile_dma(tt, stg, wv)
        _xt_transpose(tt, stg, ("gp0" if tt % 2 == 0 else "gp1", ld), own=False)
    barrier("xce")


def _peer_tile_dma(tt, stg, wv):
    sem = B.qsem["gp0" if tt % 2 == 0 else "gp1"]
    S = B.S

    def f(e, _tt=tt, _stg=stg, _h=sem.h, _wv=wv):
        if _tt == 0:
            e.wait_ge(B.ccs.h, B.ccs.n)
        if _wv is not None:
            e.wait_ge(S["v"].h, _wv)
        with e.register(f"po{_tt}") as po:
            e.reg_load(po, B.sb["u32s"][0:1, 0:1])
            e.reg_add(po, po, _tt * 131072)
            e.dma_start(out=B.AB[:, _stg:_stg + 1024],
                        in_=AP(B.d["cc_out"].tensor, po, [[1024, 128], [1, 1024]])
                        ).then_inc(_h, 16)
    ccn = B.ccs.n
    em("g", f)
    sem.n += 16
    return sem.n


def _xt_transpose(tt, stg, ld, own):
    S = B.S
    q, ldn = ld
    jj = tt // 2
    pos = (2 * jj + (0 if own else 1)) * BLK + (tt % 2) * 128
    tN = 0
    vN = 0
    for half in range(2):
        for mb in range(4):
            mc = half * 4 + mb
            w0 = ()
            if mb == 0:
                if half == 0:
                    w0 = ((B.qsem[q], ldn),)
                if hasattr(B, "_pt_ev"):
                    w0 += ((S["v"], B._pt_ev),)
            tN = e_op("t", lambda e, _stg=stg, _mc=mc, _mb=mb, _hf=half:
                      e.transpose(B.ps["pT"][:, _hf * 512 + _mb * 128: _hf * 512 + (_mb + 1) * 128],
                                  B.AB[:, _stg + _mc * 128: _stg + _mc * 128 + 128],
                                  B.sb["ident"][:]),
                      waits=w0, inc=(mb == 3))

        def f_ev(e, _half=half, _pos=pos):
            src = AP(B.ps["pT"], _half * 512, [[1024, 128], [128, 4], [1, 128]])
            dst = AP(B.AB, L1_X2T + (_half * 4) * T + _pos,
                     [[AB_COLS, 128], [T, 4], [1, 128]])
            return e.tensor_copy(dst, src)
        vN = e_op("v", f_ev, waits=((S["t"], tN),), inc=True)
        B._pt_ev = vN
    B._xtev[tt + (0 if own else 100)] = vN


def _tern_stats_units():
    AFl, sb = B.AF, B.sb
    S = B.S
    d = B.d
    msN = e_op("v", lambda e: e.memset(B.sb["asum"][:, 0:4], 0.0), inc=True)
    B._stv = {-1: msN}
    units = []
    uu = 0
    for wi, nm in ((0, "l1_W1"), (1, "l1_W2")):
        for i in range(32):
            def emit(_uu=uu, _wi=wi, _nm=nm, _i=i):
                gs = F1_GS[0]
                wld = ((S["v"], B._stv[_uu - 1]),)
                if _wi == 0:
                    src = d[_nm][_i * 128:(_i + 1) * 128, :]
                else:
                    rt, ct = _i // 4, _i % 4
                    src = AP(d[_nm].tensor, rt * 128 * FF1 + ct * 1024,
                             [[FF1, 128], [1, 1024]])
                ld = dma("ys", AFl[:, gs:gs + 1024], src, waits=wld)

                def f_r(e, _gs=gs, _w=_wi):
                    vc(e, e.reduce_sum(B.sb["asum"][:, 2 + _w:3 + _w],
                                       B.AF[:, _gs:_gs + 1024], AX.X,
                                       apply_absolute_value=True))
                    vc(e, e.tensor_tensor(B.sb["asum"][:, _w:_w + 1],
                                          B.sb["asum"][:, _w:_w + 1],
                                          B.sb["asum"][:, 2 + _w:3 + _w], ALU.add))
                    return e.nop()
                B._stv[_uu] = e_op("v", f_r, waits=((B.qsem["ys"], ld),), inc=True)
            units.append(emit)
            uu += 1
    return units


def _tern_stats():
    sb, ps = B.sb, B.ps
    S = B.S
    vN = S["v"].n
    for wi in range(2):
        t_mm(ps["pA"][0:1, wi:wi + 1], sb["asum"][:, wi:wi + 1],
             sb["ones_f"][:, 0:1], start=True, stop=True,
             waits=(((S["v"], vN),) if wi == 0 else ()), inc=(wi == 1))
    tN = S["t"].n

    def f_thr(e):
        vc(e, e.tensor_scalar(B.sb["tiny"][0:1, 6:8], B.ps["pA"][0:1, 0:2],
                              1.0 / (FF1 * DM), None, ALU.mult))
        return e.tensor_scalar(B.sb["tiny"][0:1, 6:8], B.sb["tiny"][0:1, 6:8],
                               1e-8, 0.5, ALU.max, ALU.mult)
    vN = e_op("v", f_thr, waits=((S["t"], tN),), inc=True)
    tN = t_mm(ps["pA"][:, 2:4], sb["ones_f"][0:1, :], sb["tiny"][0:1, 6:8],
              start=True, stop=True, waits=((S["v"], vN),), inc=True)

    def f_b(e):
        e.tensor_copy(B.sb["thr"][:, 0:1], B.ps["pA"][:, 2:3])
        e.tensor_scalar(B.sb["thr"][:, 1:2], B.ps["pA"][:, 2:3], -1.0, None, ALU.mult)
        e.tensor_copy(B.sb["thr"][:, 2:3], B.ps["pA"][:, 3:4])
        return e.tensor_scalar(B.sb["thr"][:, 3:4], B.ps["pA"][:, 3:4], -1.0,
                               None, ALU.mult)
    e_op("v", f_b, waits=((S["t"], tN),), inc=True)



def _l1_proj_common():
    AB, WB, ps, sb = B.AB, B.WB, B.ps, B.sb
    S = B.S
    barrier("l1p")
    prev = {}
    for u in range(16):
        dc, tt = u // 8, u % 8
        pp = ps["pA"][:, 0:512] if u % 2 == 0 else ps["pS0"][:, 0:512]
        w = ((S["s"], prev[u - 2]),) if u - 2 in prev else ()
        for mc in range(8):
            t_mm(pp, WB[:, W_C["l1_wkv"] + mc * 256 + dc * 128: W_C["l1_wkv"] + mc * 256 + dc * 128 + 128],
                 AB[:, L1_X2T + mc * T + tt * 512: L1_X2T + mc * T + tt * 512 + 512],
                 start=(mc == 0), stop=(mc == 7), waits=(w if mc == 0 else ()),
                 inc=(mc == 7))
        tN = S["t"].n
        prev[u] = e_op("s", lambda e, _pp=pp, _dc=dc, _tt=tt: e.activation(
            B.AB[:, L1_CKV + _dc * T + _tt * 512: L1_CKV + _dc * T + _tt * 512 + 512],
            _pp, AF.Gelu, bias=B.sb["b_bkv"][:, _dc:_dc + 1]),
            waits=((S["t"], tN),), inc=True)
    prev_v = {-2: prev[14], -1: prev[15]}
    for tt in range(8):
        if tt % 2 == 0:
            pA, pB = ps["pA"][:, 0:512], ps["pB"][:, 0:512]
        else:
            pA, pB = ps["pS0"][:, 0:512], ps["pS0"][:, 512:1024]
        w = ((S["s"], prev_v[tt - 2]),) if tt - 2 in prev_v else ()
        for mc in range(8):
            t_mm(pA, WB[:, W_C["l1_wkr"] + mc * 128: W_C["l1_wkr"] + mc * 128 + 128],
                 AB[:, L1_X2T + mc * T + tt * 512: L1_X2T + mc * T + tt * 512 + 512],
                 start=(mc == 0), stop=(mc == 7), waits=(w if mc == 0 else ()))
        for mc in range(8):
            t_mm(pB, WB[:, W_C["l1_wkr_rot"] + mc * 128: W_C["l1_wkr_rot"] + mc * 128 + 128],
                 AB[:, L1_X2T + mc * T + tt * 512: L1_X2T + mc * T + tt * 512 + 512],
                 start=(mc == 0), stop=(mc == 7), inc=(mc == 7))
        tN = S["t"].n

        def f_rope(e, _pA=pA, _pB=pB, _tt=tt):
            fsb = FS + (_tt % 2) * 1024
            t1 = B.AF[:, fsb:fsb + 512]
            t2 = B.AF[:, fsb + 512:fsb + 1024]
            e.scalar_tensor_tensor(t1, _pA, B.sb["b_bkr"][:, 0:1],
                                   B.AF[:, TKA + _tt * 512: TKA + _tt * 512 + 512],
                                   ALU.add, ALU.mult)
            vc(e, e.scalar_tensor_tensor(t2, _pB, B.sb["b_bkr_rot"][:, 0:1],
                                         B.AF[:, TKB + _tt * 512: TKB + _tt * 512 + 512],
                                         ALU.add, ALU.mult))
            return e.tensor_tensor(t1, t1, t2, ALU.add)
        vN = e_op("v", f_rope, waits=((S["t"], tN),), inc=True)
        prev_v[tt] = e_op("s", lambda e, _tt=tt: e.activation(
            B.AB[:, L1_KRG + _tt * 512: L1_KRG + _tt * 512 + 512],
            B.AF[:, FS + (_tt % 2) * 1024: FS + (_tt % 2) * 1024 + 512], AF.Gelu),
            waits=((S["v"], vN),), inc=True)
    kr_v_last = vN
    prevq = {}
    for u in range(16):
        dc, jb = u // 8, u % 8
        pp = ps["pA"][:, 0:256] if u % 2 == 0 else ps["pS0"][:, 0:256]
        if u - 2 in prevq:
            w = ((S["s"], prevq[u - 2]),)
        else:
            w = ((S["v"], kr_v_last),)
        pos = (2 * jb) * BLK
        for mc in range(8):
            t_mm(pp, WB[:, W_C["l1_wq"] + mc * 256 + dc * 128: W_C["l1_wq"] + mc * 256 + dc * 128 + 128],
                 AB[:, L1_X2T + mc * T + pos: L1_X2T + mc * T + pos + 256],
                 start=(mc == 0), stop=(mc == 7), waits=(w if mc == 0 else ()),
                 inc=(mc == 7))
        tN = S["t"].n
        prevq[u] = e_op("s", lambda e, _pp=pp, _dc=dc, _jb=jb: e.activation(
            B.WB[:, W_CQ + _dc * TOWN + _jb * 256: W_CQ + _dc * TOWN + _jb * 256 + 256],
            _pp, AF.Gelu, bias=B.sb["b_bq"][:, _dc:_dc + 1]),
            waits=((S["t"], tN),), inc=True)
    prevr = {-2: prevq[14], -1: prevq[15]}
    for jb in range(8):
        if jb % 2 == 0:
            pA, pB = ps["pA"][:, 0:256], ps["pB"][:, 0:256]
        else:
            pA, pB = ps["pS0"][:, 0:256], ps["pS0"][:, 512:768]
        w = ((S["s"], prevr[jb - 2]),) if jb - 2 in prevr else ()
        pos = (2 * jb) * BLK
        for mc in range(8):
            t_mm(pA, WB[:, W_C["l1_wqr"] + mc * 128: W_C["l1_wqr"] + mc * 128 + 128],
                 AB[:, L1_X2T + mc * T + pos: L1_X2T + mc * T + pos + 256],
                 start=(mc == 0), stop=(mc == 7), waits=(w if mc == 0 else ()))
        for mc in range(8):
            t_mm(pB, WB[:, W_C["l1_wqr_rot"] + mc * 128: W_C["l1_wqr_rot"] + mc * 128 + 128],
                 AB[:, L1_X2T + mc * T + pos: L1_X2T + mc * T + pos + 256],
                 start=(mc == 0), stop=(mc == 7), inc=(mc == 7))
        tN = S["t"].n

        def f_rope(e, _pA=pA, _pB=pB, _jb=jb):
            fsb = FS + (_jb % 2) * 1024
            t1 = B.AF[:, fsb:fsb + 256]
            t2 = B.AF[:, fsb + 512:fsb + 768]
            e.scalar_tensor_tensor(t1, _pA, B.sb["b_bqr"][:, 0:1],
                                   B.AF[:, TQA + _jb * 256: TQA + _jb * 256 + 256],
                                   ALU.add, ALU.mult)
            vc(e, e.scalar_tensor_tensor(t2, _pB, B.sb["b_bqr_rot"][:, 0:1],
                                         B.AF[:, TQB + _jb * 256: TQB + _jb * 256 + 256],
                                         ALU.add, ALU.mult))
            return e.tensor_tensor(t1, t1, t2, ALU.add)
        vN = e_op("v", f_rope, waits=((S["t"], tN),), inc=True)
        prevr[jb] = e_op("s", lambda e, _jb=jb: e.activation(
            B.WB[:, W_QRG + _jb * 256: W_QRG + _jb * 256 + 256],
            B.AF[:, FS + (_jb % 2) * 1024: FS + (_jb % 2) * 1024 + 256], AF.Gelu),
            waits=((S["v"], vN),), inc=True)
    B._l1p_v_end = S["v"].n
    barrier("l1pe")


def _l1_pass(p):
    AB, WB, ps, sb = B.AB, B.WB, B.ps, B.sb
    S = B.S
    d = B.d
    barrier(f"l1s{p}")
    pb = W_PS[p % 2]
    qp = "gw0" if p % 2 == 0 else "gw1"
    for cc in range(2):
        c = 2 * p + cc
        if c < 7:
            dma(qp, AP(WB, pb + cc * 256 + 0, [[WB_COLS, 128], [128, 2], [1, 128]]),
                AP(d["l1_wkc"].tensor, c * 128, [[896, 128], [896 * 128, 2], [1, 128]]))
            dma(qp, AP(WB, pb + 512 + cc * 256, [[WB_COLS, 128], [128, 2], [1, 128]]),
                AP(d["l1_wqc"].tensor, c * 128, [[896, 128], [896 * 128, 2], [1, 128]]))
    dma(qp, AP(WB, pb + 1024, [[WB_COLS, 128], [256, 2], [1, 256]]),
        AP(d["l1_wv"].tensor, 4 * p * 64, [[1024, 128], [131072, 2], [1, 256]]))
    if p == 0:
        _load_w_mk(qp, "l1_wo", 1024, 8, W_WO,
                   waits=((S["t"], S["t"].n),))
    em("t", lambda e, _h=B.qsem[qp].h, _n=B.qsem[qp].n: e.wait_ge(_h, _n))
    vpad = e_op("v", lambda e: e.memset(
        AP(B.AB, L1_V + 64, [[AB_COLS, 128], [65, 4 * 32], [1, 1]]), 1.0), inc=True)

    prev = {}
    u = 0
    for cc in range(2):
        c = 2 * p + cc
        if c == 7:
            continue
        for tt in range(8):
            pp = ps["pA"][:, 0:512] if u % 2 == 0 else ps["pS0"][:, 0:512]
            w = ((S["s"], prev[u - 2]),) if u - 2 in prev else ()
            for dc in range(2):
                t_mm(pp, WB[:, pb + cc * 256 + dc * 128: pb + cc * 256 + dc * 128 + 128],
                     AB[:, L1_CKV + dc * T + tt * 512: L1_CKV + dc * T + tt * 512 + 512],
                     start=(dc == 0), stop=(dc == 1), waits=(w if dc == 0 else ()),
                     inc=(dc == 1))
            tN = S["t"].n
            prev[u] = e_op("s", lambda e, _pp=pp, _cc=cc, _tt=tt, _c=c: e.activation(
                B.AB[:, L1_KT + _cc * T + _tt * 512: L1_KT + _cc * T + _tt * 512 + 512],
                _pp, AF.Gelu, bias=B.sb["b_bkc"][:, _c:_c + 1]),
                waits=((S["t"], tN),), inc=True)
            u += 1
    for cc in range(2):
        c = 2 * p + cc
        if c == 7:
            continue
        for jb in range(8):
            pp = ps["pA"][:, 0:256] if u % 2 == 0 else ps["pS0"][:, 0:256]
            w = ((S["s"], prev[u - 2]),) if u - 2 in prev else ()
            for dc in range(2):
                t_mm(pp, WB[:, pb + 512 + cc * 256 + dc * 128: pb + 512 + cc * 256 + dc * 128 + 128],
                     WB[:, W_CQ + dc * TOWN + jb * 256: W_CQ + dc * TOWN + jb * 256 + 256],
                     start=(dc == 0), stop=(dc == 1), waits=(w if dc == 0 else ()),
                     inc=(dc == 1))
            tN = S["t"].n
            prev[u] = e_op("s", lambda e, _pp=pp, _cc=cc, _jb=jb, _c=c: e.activation(
                B.AB[:, L1_QT + _cc * TOWN + _jb * 256: L1_QT + _cc * TOWN + _jb * 256 + 256],
                _pp, AF.Gelu, bias=B.sb["b_bqc"][:, _c:_c + 1]),
                waits=((S["t"], tN),), inc=True)
            u += 1
    prevv = {}
    for kt in range(32):
        if kt % 2 == 0:
            pt, off = ps["pB"], 0
        else:
            pt, off = ps["pS1"], 0
        pp = pt[:, off:off + 256]
        w = ((S["s"], prevv[kt - 2]),) if kt - 2 in prevv else ()
        if kt < 2:
            w += ((S["s"], prev[u - 1] if (u - 1) in prev else S["s"].n),)
        for dc in range(2):
            t_mm(pp, AB[:, L1_CKV + dc * T + kt * 128: L1_CKV + dc * T + kt * 128 + 128],
                 WB[:, pb + 1024 + dc * 256: pb + 1024 + dc * 256 + 256],
                 start=(dc == 0), stop=(dc == 1), waits=(w if dc == 0 else ()),
                 inc=(dc == 1))
        tN = S["t"].n
        rowext = 512 if kt % 2 == 0 else 1024

        def f_ev(e, _pt=pt, _kt=kt, _re=rowext):
            src = AP(_pt, 0, [[_re, 128], [64, 4], [1, 64]])
            dst = AP(B.AB, L1_V + _kt * VW1, [[AB_COLS, 128], [65, 4], [1, 64]])
            return e.activation(dst, src, AF.Copy)
        prevv[kt] = e_op("s", f_ev, waits=((S["t"], tN), (S["v"], vpad)), inc=True)

    _bound_l1(p)

    def KT_sl(h, kt):
        pr = slice((h % 2) * 64, (h % 2) * 64 + 64)
        if h // 2 == 7:
            return B.AB[pr, L1_KRG + kt * 128: L1_KRG + kt * 128 + 128]
        cc = h // 2 - 2 * p
        return B.AB[pr, L1_KT + cc * T + kt * 128: L1_KT + cc * T + kt * 128 + 128]

    def QT_sl(h, j):
        pr = slice((h % 2) * 64, (h % 2) * 64 + 64)
        if h // 2 == 7:
            return B.WB[pr, W_QRG + j * 256: W_QRG + j * 256 + 256]
        cc = h // 2 - 2 * p
        return B.AB[pr, L1_QT + cc * TOWN + j * 256: L1_QT + cc * TOWN + j * 256 + 256]

    def V_sl(h, kt):
        hh = h - 4 * p
        return B.AB[:, L1_V + kt * VW1 + hh * 65: L1_V + kt * VW1 + hh * 65 + 65]

    ctxd = {"KT_sl": KT_sl, "QT_sl": QT_sl, "V_sl": V_sl}
    for hh in range(4):
        _attn_head(1, 4 * p + hh, l1ctx=ctxd)



def _ffn1():
    AB, WB, ps, sb, AFl = B.AB, B.WB, B.ps, B.sb, B.AF
    S = B.S
    d = B.d
    barrier("f1")
    B._f1par = 0
    tern_v = {}
    tp_t = {}
    gelu_s = {}
    acc_v = {}
    uu = 0
    for c in range(8):
        ws = F1_WS[c % 2]
        for wi, nm, nsub in ((0, "l1_W1", 4), (1, "l1_W2", 8)):
            for st_i in range(nsub):
                gs = F1_GS[0]
                wld = ((S["v"], tern_v[uu - 1]),) if uu - 1 in tern_v else ()
                if wi == 0:
                    src = AP(d[nm].tensor, (c * 512 + st_i * 128) * DM,
                             [[DM, 128], [1, DM]])
                    wcols = 1024
                else:
                    src = AP(d[nm].tensor, st_i * 128 * FF1 + c * 512,
                             [[FF1, 128], [1, 512]])
                    wcols = 512
                ld = dma("ys", AFl[:, gs:gs + wcols], src, waits=wld)
                thr_c = 0 if wi == 0 else 2
                wtp = ((S["t"], tp_t[uu - 2]),) if uu - 2 in tp_t else ()

                def f_tern(e, _gs=gs, _wc=wcols, _tc=thr_c):
                    e.tensor_scalar(B.AB[:, F1_TS:F1_TS + _wc], B.AF[:, _gs:_gs + _wc],
                                    B.sb["thr"][:, _tc:_tc + 1], None, ALU.is_ge)
                    vc(e, e.tensor_scalar(B.AB[:, F1_TS + 1024:F1_TS + 1024 + _wc],
                                          B.AF[:, _gs:_gs + _wc],
                                          B.sb["thr"][:, _tc + 1:_tc + 2], None,
                                          ALU.is_le))
                    return e.tensor_tensor(B.AB[:, F1_TS + 2048:F1_TS + 2048 + _wc],
                                           B.AB[:, F1_TS:F1_TS + _wc],
                                           B.AB[:, F1_TS + 1024:F1_TS + 1024 + _wc],
                                           ALU.subtract)
                vN = e_op("v", f_tern, waits=((B.qsem["ys"], ld),) + wtp, inc=True)
                tern_v[uu] = vN
                nb = 8 if wi == 0 else 4
                tN = 0
                for bg in range(nb // 4):
                    par = B._f1par
                    B._f1par = 1 - B._f1par
                    w0 = ((S["v"], max(B._ptev.values())),)
                    if bg == 0:
                        w0 += ((S["v"], vN),)
                    for bb in range(4):
                        blk = bg * 4 + bb
                        tN = e_op("t", lambda e, _blk=blk, _bb=bb, _p=par:
                                  e.transpose(B.ps["pT"][:, _p * 512 + _bb * 128: _p * 512 + (_bb + 1) * 128],
                                              B.AB[:, F1_TS + 2048 + _blk * 128: F1_TS + 2048 + _blk * 128 + 128],
                                              B.sb["ident"][:]),
                                  waits=(w0 if bb == 0 else ()),
                                  inc=(bb == 3))
                    if wi == 0:
                        def f_ev(e, _ws=ws, _st=st_i, _bg=bg, _p=par):
                            src = AP(B.ps["pT"], _p * 512, [[1024, 128], [128, 4], [1, 128]])
                            dst = AP(B.AB, _ws + (_bg * 4) * 512 + _st * 128,
                                     [[AB_COLS, 128], [512, 4], [1, 128]])
                            return e.tensor_copy(dst, src)
                    else:
                        def f_ev(e, _ws=ws, _st=st_i, _bg=bg, _p=par):
                            src = AP(B.ps["pT"], _p * 512, [[1024, 128], [128, 4], [1, 128]])
                            dst = AP(B.AB, _ws + 4096 + _st * 128,
                                     [[AB_COLS, 128], [1024, 4], [1, 128]])
                            return e.tensor_copy(dst, src)
                    B._ptev[par] = e_op("v", f_ev, waits=((S["t"], tN),), inc=True)
                tp_t[uu] = tN
                uu += 1
        tp_v = S["v"].n
        for tq in range(8):
            for fs in range(4):
                bank = fs % 2
                ph = ps["pS0"][:, bank * 512: bank * 512 + 256]
                w0 = ()
                if tq == 0 and fs == 0:
                    w0 = ((S["v"], tp_v),)
                key = (c, tq, fs - 2) if fs >= 2 else (c, tq - 1, fs + 2)
                if key in gelu_s:
                    w0 += ((S["s"], gelu_s[key]),)
                for mc in range(8):
                    t_mm(ph, AB[:, ws + (mc * 4 + fs) * 128: ws + (mc * 4 + fs) * 128 + 128],
                         AB[:, XTO + mc * 2048 + tq * 256: XTO + mc * 2048 + tq * 256 + 256],
                         start=(mc == 0), stop=(mc == 7),
                         waits=(w0 if mc == 0 else ()), inc=(mc == 7))
                tN = S["t"].n
                gelu_s[(c, tq, fs)] = e_op("s", lambda e, _ph=ph, _fs=fs, _c=c: e.activation(
                    B.AB[:, F1_HT + _fs * 256: F1_HT + (_fs + 1) * 256], _ph, AF.Gelu,
                    bias=B.sb["b_b1"][:, _c * 4 + _fs:_c * 4 + _fs + 1]),
                    waits=((S["t"], tN),), inc=True)
            ht_s = S["s"].n
            for ti in range(2):
                tt = tq * 2 + ti
                wdn = ((S["s"], ht_s),)
                pk = (c, tq, ti - 1) if ti else ((c, tq - 1, 1) if tq else (c - 1, 7, 1))
                if pk in acc_v:
                    wdn += ((S["v"], acc_v[pk]),)
                for half, pp in ((0, ps["pAV"][:, 0:512]), (1, ps["pB"][:, 0:512])):
                    for fs in range(4):
                        laststop = (fs == 3 and c < 7)
                        t_mm(pp, AB[:, F1_HT + fs * 256 + ti * 128: F1_HT + fs * 256 + ti * 128 + 128],
                             AB[:, ws + 4096 + fs * 1024 + half * 512: ws + 4096 + fs * 1024 + half * 512 + 512],
                             start=(fs == 0), stop=laststop,
                             waits=(wdn if (half == 0 and fs == 0) else ()),
                             inc=(laststop and half == 1))
                    if c == 7:
                        t_mm(pp, sb["ones_b"][0:1, :],
                             B.WB[0:1, W_B2 + half * 512: W_B2 + half * 512 + 512],
                             start=False, stop=True, inc=(half == 1))
                tN = S["t"].n

                def f_acc(e, _tt=tt):
                    stc = STRM + _tt * 1024
                    e.tensor_tensor(B.AF[:, stc:stc + 512], B.AF[:, stc:stc + 512],
                                    B.ps["pAV"][:, 0:512], ALU.add)
                    return e.tensor_tensor(B.AF[:, stc + 512:stc + 1024],
                                           B.AF[:, stc + 512:stc + 1024],
                                           B.ps["pB"][:, 0:512], ALU.add)
                acc_v[(c, tq, ti)] = e_op("v", f_acc, waits=((S["t"], tN),), inc=True)
    barrier("f1e")


def _final_ln_out():
    B._tail_v = {}
    for tt in range(16):
        _ln_tile(tt, 0, None, None, mode="ffn1")
    em("y", lambda e, _h=B.qsem["y"].h, _n=B.qsem["y"].n: e.wait_ge(_h, _n))
    em("g", lambda e, _h=B.qsem["g"].h, _n=B.qsem["g"].n: e.wait_ge(_h, _n))


def kernel(**inputs):
    nc = build_program()
    in_maps = [make_core_inputs(inputs, c // 2, c % 2) for c in range(8)]
    res = run_bass_kernel_spmd(nc, in_maps, core_ids=list(range(8)))
    out = np.zeros((4, T, DM), dtype=np.float32)
    for c in range(8):
        s, r = c // 2, c % 2
        yc = res.results[c]["y"]
        for j in range(NBLK):
            g = 2 * j + r
            out[s, g * BLK:(g + 1) * BLK, :] = yc[j * BLK:(j + 1) * BLK, :]
    return out


# version 07
# speedup vs baseline: 1.4737x; 1.1421x over previous
import os
import numpy as np
import ml_dtypes
from contextlib import ExitStack

ATH_PH = os.environ.get("ATH_PH", "full")
ATH_TC = os.environ.get("ATH_TC", "full")

import concourse.bass as bass
import concourse.mybir as mybir
from concourse.bass_utils import run_bass_kernel_spmd

F32 = mybir.dt.float32
BF16 = mybir.dt.bfloat16
I32 = mybir.dt.int32
U32 = mybir.dt.uint32
AF = mybir.ActivationFunctionType
ALU = mybir.AluOpType
AX = mybir.AxisListType

T = 4096
DM = 1024
BLK = 256
NBLK = 8
TOWN = 2048
L0H = 8
L1H = 16
DCV, DRP = 256, 128
FF0, FF1 = 3072, 4096
QG = 256
NQG = 8
EPS = 1e-5

AB_COLS = 45600
AF_COLS = 18432
STRM = 0
FS = 16384
TKA, TKB = 0, 4096
TQA, TQB = 8192, 10240

XT = 0
KTA = 32768
QTA = 36864
VA = 38912
VW0 = 129
EA = 43040
OST0 = 45088
TSC = 45600

XTO = 0
ORB = 16384

F0_WS = [16384, 28672]
F0_GS = [40960, 41984]
F0_DS = [43008, 43520]
F0_SIL = 44032
F0_ST = 44544

L1_X2T = 0
L1_CKV = 32768
L1_KRG = 40960
PSTG = 45056
PSTG = 32768

L1_KT = 0
L1_QT = 8192
L1_V = 12288
VW1 = 260
L1_E = 20608
L1_OST = 24704

F1_WS = [16384, 24576]
F1_TS = 32768
F1_HT = 35840
F1_GS = [FS + 1024]



def _perm_blocks(r):
    return [p + r if p % 2 == 0 else p - r for p in range(2 * NBLK)]


def _rope_tabs(pos):
    i = np.arange(64, dtype=np.float64)
    invf = np.power(10000.0, -i / 64.0)
    rad = pos[None, :].astype(np.float64) * invf[:, None]
    cos = np.cos(rad).astype(np.float32)
    sin = np.sin(rad).astype(np.float32)
    return (np.concatenate([cos, cos], axis=0),
            np.concatenate([-sin, sin], axis=0))


def _masks(r):
    k = np.arange(128)[:, None]
    q = np.arange(QG)[None, :]
    tri0 = (k <= q).astype(np.float32)
    tri1 = ((128 + k) <= q).astype(np.float32)
    full = np.full((128, QG), 1.0 if r == 1 else 0.0, np.float32)
    return np.concatenate([tri0, tri1, full, full], axis=1).astype(ml_dtypes.bfloat16)


def _selA():
    m = np.zeros((128, 144), np.float32)
    for tc in range(12):
        m[:, tc * 12 + tc] = 1.0
    return m.astype(ml_dtypes.bfloat16)


def _selB():
    m = np.zeros((128, 32), np.float32)
    for v in range(4):
        base = 2 * v
        m[0:64, v * 8 + base] = 1.0
        m[64:128, v * 8 + base + 1] = 1.0
    return m.astype(ml_dtypes.bfloat16)


def _rot_halves(w, width):
    g = w.shape[1] // width
    return np.ascontiguousarray(
        w.reshape(w.shape[0], g, 2, width // 2)[:, :, ::-1, :].reshape(w.shape))


def make_core_inputs(inputs, s, r):
    x = np.asarray(inputs["L0_attn_x"], dtype=np.float32)
    perm = _perm_blocks(r)
    rows = np.concatenate([np.arange(b * BLK, (b + 1) * BLK) for b in perm])
    x_perm = np.ascontiguousarray(x[s][rows])
    xT_perm = np.ascontiguousarray(x_perm.T)
    own_rows = rows.reshape(2 * NBLK, BLK)[0::2].reshape(-1)
    tka, tkb = _rope_tabs(rows.astype(np.float64))
    tqa, tqb = _rope_tabs(own_rows.astype(np.float64))
    f32 = lambda a: np.ascontiguousarray(np.asarray(a, dtype=np.float32))
    d = {
        "x_perm": x_perm, "xT_perm": xT_perm,
        "tabk_a": tka, "tabk_b": tkb, "tabq_a": tqa, "tabq_b": tqb,
        "masks": _masks(r),
        "ident": np.eye(128, dtype=ml_dtypes.bfloat16),
        "ones_b": np.ones((128, 128), dtype=ml_dtypes.bfloat16),
        "ones_f": np.ones((128, 128), dtype=np.float32),
        "selA": _selA(), "selB": _selB(),
        "peer_off": np.array([[(1 - r) * TOWN * DM]], dtype=np.uint32),
    }
    d["l0_wq"] = f32(np.asarray(inputs["L0_attn_wq"]).reshape(DM, DM))
    d["l0_wk"] = f32(np.asarray(inputs["L0_attn_wk"]).reshape(DM, DM))
    d["l0_wv"] = f32(np.asarray(inputs["L0_attn_wv"]).reshape(DM, DM))
    d["l0_wo"] = f32(np.asarray(inputs["L0_attn_wo"]).reshape(DM, DM))
    d["l0_wq_rot"] = _rot_halves(d["l0_wq"], 128)
    d["l0_wk_rot"] = _rot_halves(d["l0_wk"], 128)
    for nm_src, nm in (("L0_ffn_w_gate", "l0_wg"), ("L0_ffn_w_up", "l0_wu"),
                       ("L0_ffn_w_down", "l0_wd")):
        d[nm] = np.ascontiguousarray(np.asarray(inputs[nm_src], dtype=np.int32))
    for nm_src, nm in (("L1_attn_wq", "l1_wq"), ("L1_attn_wqc", "l1_wqc"),
                       ("L1_attn_wqr", "l1_wqr"), ("L1_attn_wkv", "l1_wkv"),
                       ("L1_attn_wkr", "l1_wkr"), ("L1_attn_wkc", "l1_wkc"),
                       ("L1_attn_wv", "l1_wv"), ("L1_attn_wo", "l1_wo"),
                       ("L1_ffn_W1", "l1_W1"), ("L1_ffn_W2", "l1_W2")):
        d[nm] = f32(inputs[nm_src])
    d["l1_wqr_rot"] = _rot_halves(d["l1_wqr"], 128)
    d["l1_wkr_rot"] = _rot_halves(d["l1_wkr"], 128)

    def cols(v):
        v = np.asarray(v, dtype=np.float32).reshape(-1)
        return np.ascontiguousarray(v.reshape(-1, 128).T)

    def rot_col(v):
        v = np.asarray(v, dtype=np.float32).reshape(128)
        return np.ascontiguousarray(np.concatenate([v[64:], v[:64]]).reshape(128, 1))

    d["b_bq"] = cols(inputs["L1_attn_bq"])
    d["b_bqc"] = cols(inputs["L1_attn_bqc"])
    d["b_bqr"] = cols(inputs["L1_attn_bqr"])
    d["b_bqr_rot"] = rot_col(inputs["L1_attn_bqr"])
    d["b_bkv"] = cols(inputs["L1_attn_bkv"])
    d["b_bkr"] = cols(inputs["L1_attn_bkr"])
    d["b_bkr_rot"] = rot_col(inputs["L1_attn_bkr"])
    d["b_bkc"] = cols(inputs["L1_attn_bkc"])
    d["b_b1"] = cols(inputs["L1_ffn_b1"])
    bv = np.asarray(inputs["L1_attn_bv"], dtype=np.float32).reshape(L1H, 64).T
    d["b_bv"] = np.ascontiguousarray(np.concatenate([bv, np.zeros((64, L1H), np.float32)]))
    d["r_bo"] = np.asarray(inputs["L1_attn_bo"], dtype=np.float32).reshape(1, DM)
    d["r_b2"] = np.asarray(inputs["L1_ffn_b2"], dtype=np.float32).reshape(1, DM)
    return d



class Sem:
    def __init__(self, ctx, nc, name):
        self.h = ctx.enter_context(nc.semaphore(name))
        self.n = 0


class B:
    pass


ENGS = ("t", "s", "v", "g", "y")


def em(eng, fn):
    B.streams[eng].append(fn)


def barrier(name):
    if not hasattr(B, "barsem"):
        B.barsem = {k: Sem(B.ctx, B.nc, "bar_" + k) for k in ENGS}
        B.barn = 0
    B.barn += 1
    n = B.barn
    for k in ENGS:
        qs = ()
        if hasattr(B, "qsem"):
            qs = tuple((B.qsem[q].h, B.qsem[q].n) for q in B.qsem
                       if q.startswith(k) and B.qsem[q].n > 0)
        def _f(e, _h=B.barsem[k].h, _k=k, _qs=qs):
            for (_qh, _qn) in _qs:
                e.wait_ge(_qh, _qn)
            if _k != "y":
                e.drain()
            e.nop().then_inc(_h, 1)
        em(k, _f)
    for k in ENGS:
        for k2 in ENGS:
            if k2 != k:
                em(k, lambda e, _h=B.barsem[k2].h, _n=n: e.wait_ge(_h, _n))


def dma(q, dst, src, waits=()):
    sem = B.qsem[q]
    eng = "y" if q.startswith("y") else "g"

    def f(e, _d=dst, _s=src, _h=sem.h, _w=tuple(waits)):
        for (sm, val) in _w:
            e.wait_ge(sm.h, val)
        e.dma_start(out=_d, in_=_s).then_inc(_h, 16)
    em(eng, f)
    sem.n += 16
    return sem.n


def t_mm(out, lhsT, rhs, start, stop, waits=(), inc=False):
    s_t = B.S["t"]

    def f(e, _o=out, _l=lhsT, _r=rhs, _a=start, _b=stop, _w=tuple(waits), _i=inc):
        for (sm, val) in _w:
            e.wait_ge(sm.h, val)
        ins = e.matmul(_o, _l, _r, start=_a, stop=_b)
        if _i:
            ins.then_inc(s_t.h, 1)
    em("t", f)
    if inc:
        s_t.n += 1
    return s_t.n


def vc(e, ins, eng="v"):
    sem = B.chain[eng]
    B.chain_n[eng] += 1
    ins.then_inc(sem.h, 1)
    e.wait_ge(sem.h, B.chain_n[eng])


def e_op(eng, fn, waits=(), inc=False):
    s = B.S[eng]

    def f(e, _f=fn, _w=tuple(waits), _i=inc):
        for (sm, val) in _w:
            e.wait_ge(sm.h, val)
        ins = _f(e)
        if _i:
            ins.then_inc(s.h, 1)
    em(eng, f)
    if inc:
        s.n += 1
    return s.n


def AP(t, off, pat):
    return bass.AP(t, off, pat)



_CACHE = {}

WB_COLS = 20480
W_SLOT = [0, 5120]
W_KIND = {"l0_wq": 0, "l0_wq_rot": 1024, "l0_wk": 2048, "l0_wk_rot": 3072,
          "l0_wv": 4096}
W_WO = 0
W_C = {"l1_wq": 0, "l1_wqr": 2048, "l1_wqr_rot": 3072, "l1_wkv": 4096,
       "l1_wkr": 6144, "l1_wkr_rot": 7168}
W_PS = [8192, 9728]
W_BO = 11264
W_B2 = 12288
W_CQ = 13312
W_QRG = 17408


def build_program():
    if "nc" in _CACHE:
        return _CACHE["nc"]
    nc = bass.Bass("TRN2", target_bir_lowering=False)
    ctx = ExitStack()
    B.nc, B.ctx = nc, ctx
    B.streams = {k: [] for k in ENGS}
    B.S = {k: Sem(ctx, nc, "p_" + k) for k in ("t", "s", "v", "g")}
    B.qsem = {k: Sem(ctx, nc, "q_" + k) for k in
              ("y", "g", "gw0", "gw1", "gu0", "gu1", "gd0", "gd1", "go",
               "ys", "yc0", "yc1", "gp0", "gp1", "yx")}
    B.ccs = Sem(ctx, nc, "q_cc")
    B.chain = {"v": Sem(ctx, nc, "ch_v"), "s": Sem(ctx, nc, "ch_s")}
    B.chain_n = {"v": 0, "s": 0}
    B._ptev = {0: 0, 1: 0}

    d = {}
    B.d = d

    def din(name, shape, dt):
        d[name] = nc.dram_tensor(name, shape, dt, kind="ExternalInput").ap()

    din("x_perm", [T, DM], F32)
    din("xT_perm", [DM, T], F32)
    din("tabk_a", [128, T], F32)
    din("tabk_b", [128, T], F32)
    din("tabq_a", [128, TOWN], F32)
    din("tabq_b", [128, TOWN], F32)
    din("masks", [128, 4 * QG], BF16)
    din("ident", [128, 128], BF16)
    din("ones_b", [128, 128], BF16)
    din("ones_f", [128, 128], F32)
    din("selA", [128, 144], BF16)
    din("selB", [128, 32], BF16)
    din("peer_off", [1, 1], U32)
    for nm in ("l0_wq", "l0_wk", "l0_wv", "l0_wo", "l0_wq_rot", "l0_wk_rot", "l1_wo"):
        din(nm, [DM, DM], F32)
    din("l0_wg", [FF0, DM], I32)
    din("l0_wu", [FF0, DM], I32)
    din("l0_wd", [DM, FF0], I32)
    din("l1_wq", [DM, DCV], F32)
    din("l1_wqc", [DCV, DM - DRP], F32)
    din("l1_wqr", [DM, DRP], F32)
    din("l1_wqr_rot", [DM, DRP], F32)
    din("l1_wkv", [DM, DCV], F32)
    din("l1_wkr", [DM, DRP], F32)
    din("l1_wkr_rot", [DM, DRP], F32)
    din("l1_wkc", [DCV, DM - DRP], F32)
    din("l1_wv", [DCV, DM], F32)
    din("l1_W1", [FF1, DM], F32)
    din("l1_W2", [DM, FF1], F32)
    for nm, n in (("b_bq", 2), ("b_bqc", 7), ("b_bqr", 1), ("b_bqr_rot", 1),
                  ("b_bkv", 2), ("b_bkr", 1), ("b_bkr_rot", 1), ("b_bkc", 7),
                  ("b_b1", 32), ("b_bv", 16)):
        din(nm, [128, n], F32)
    din("r_bo", [1, DM], F32)
    din("r_b2", [1, DM], F32)
    d["y"] = nc.dram_tensor("y", [TOWN, DM], F32, kind="ExternalOutput").ap()
    d["cc_in"] = nc.dram_tensor("cc_in", [TOWN, DM], BF16).ap()
    d["cc_out"] = nc.dram_tensor("cc_out", [2 * TOWN, DM], BF16).ap()
    d["spill_x"] = nc.dram_tensor("spill_x", [TOWN, DM], F32).ap()
    d["spill_o"] = nc.dram_tensor("spill_o", [DM, TOWN], BF16).ap()

    sb = {}
    B.sb = sb

    def sbuf(name, shape, dt):
        sb[name] = ctx.enter_context(nc.sbuf_tensor("s_" + name, shape, dt))
        return sb[name]

    B.AB = sbuf("ABar", [128, AB_COLS], BF16)
    B.AF = sbuf("AFar", [128, AF_COLS], F32)
    B.WB = sbuf("WBar", [128, WB_COLS], BF16)
    for nm, shape, dt in (
            ("ident", [128, 128], BF16), ("ones_b", [128, 128], BF16),
            ("ones_f", [128, 128], F32), ("selA", [128, 144], BF16),
            ("selB", [128, 32], BF16), ("masks", [128, 4 * QG], BF16),
            ("cbias", [128, 32], F32), ("nscr", [128, 8], F32),
            ("tiny", [128, 16], F32), ("tinyb", [128, 16], BF16),
            ("sumb", [128, 512], F32), ("lnscr", [128, 8], F32),
            ("u32s", [1, 1], U32), ("thr", [128, 4], F32),
            ("asum", [128, 66], F32)):
        sbuf(nm, shape, dt)
    for nm, n in (("b_bq", 2), ("b_bqc", 7), ("b_bqr", 1), ("b_bqr_rot", 1),
                  ("b_bkv", 2), ("b_bkr", 1), ("b_bkr_rot", 1), ("b_bkc", 7),
                  ("b_b1", 32), ("b_bv", 16)):
        sbuf(nm, [128, n], F32)

    ps = {}
    B.ps = ps
    ps["pS0"] = ctx.enter_context(nc.psum_tensor("pS0", [128, 1024], F32))
    ps["pS1"] = ctx.enter_context(nc.psum_tensor("pS1", [128, 1024], F32))
    ps["pAV"] = ctx.enter_context(nc.psum_tensor("pAV", [128, 512], F32))
    ps["pA"] = ctx.enter_context(nc.psum_tensor("pA", [128, 512], F32))
    ps["pB"] = ctx.enter_context(nc.psum_tensor("pB", [128, 512], F32))
    ps["pT"] = ctx.enter_context(nc.psum_tensor("pT", [128, 1024], BF16))

    _emit()

    with nc.Block() as block:
        @block.tensor
        def _(e):
            for fn in B.streams["t"]:
                fn(e)

        @block.scalar
        def _(e):
            for fn in B.streams["s"]:
                fn(e)

        @block.vector
        def _(e):
            for fn in B.streams["v"]:
                fn(e)

        @block.gpsimd
        def _(e):
            for fn in B.streams["g"]:
                fn(e)

        @block.sync
        def _(e):
            for fn in B.streams["y"]:
                fn(e)

    _CACHE["nc"] = nc
    return nc


def _load_l0_head_w(h, waits=()):
    slot = W_SLOT[h % 2]
    q = "gw0" if h % 2 == 0 else "gw1"
    n = 0
    for nm, ko in W_KIND.items():
        src = AP(B.d[nm].tensor, h * 128, [[1024, 128], [131072, 8], [1, 128]])
        dst = AP(B.WB, slot + ko, [[WB_COLS, 128], [128, 8], [1, 128]])
        n = dma(q, dst, src, waits=waits)
        waits = ()
    return n


def _load_w_mk(q, nm, ncols, nchunk, woff, waits=()):
    src = B.d[nm]
    dst = AP(B.WB, woff, [[WB_COLS, 128], [ncols, nchunk], [1, ncols]])
    sap = AP(src.tensor, 0, [[ncols, 128], [128 * ncols, nchunk], [1, ncols]])
    return dma(q, dst, sap, waits=waits)


def _emit():
    d, sb, AB, AFt = B.d, B.sb, B.AB, B.AF

    for nm in ("ident", "ones_b", "ones_f", "selA", "selB", "masks",
               "b_bq", "b_bqc", "b_bqr", "b_bqr_rot", "b_bkv", "b_bkr",
               "b_bkr_rot", "b_bkc", "b_b1", "b_bv"):
        dma("y", sb[nm][:], d[nm])
    dma("y", sb["u32s"][:], d["peer_off"])
    dma("y", AFt[:, TKA:TKA + T], d["tabk_a"])
    dma("y", AFt[:, TKB:TKB + T], d["tabk_b"])
    dma("y", AFt[:, TQA:TQA + TOWN], d["tabq_a"])
    dma("y", AFt[:, TQB:TQB + TOWN], d["tabq_b"])
    for mc in range(8):
        dma("g", AB[:, XT + mc * T:XT + (mc + 1) * T],
            d["xT_perm"][mc * 128:(mc + 1) * 128, :])
    n0 = _load_l0_head_w(0)
    n1 = _load_l0_head_w(1)

    nld, ngl = B.qsem["y"].n, B.qsem["g"].n
    for k in ENGS:
        em(k, lambda e, _h=B.qsem["y"].h, _n=nld: e.wait_ge(_h, _n))
        em(k, lambda e, _h=B.qsem["g"].h, _n=ngl: e.wait_ge(_h, _n))

    B._attn_prev_ev = None
    B._attn_ost_dma = None
    B._l0_wld = {0: n0, 1: n1}

    def cut(ph):
        if ATH_PH == ph:
            _dbg_out()
            return True
        return False

    if cut("none"):
        return
    nheads = L0H if ATH_PH not in ("h1", "h2") else int(ATH_PH[1])
    for h in range(nheads):
        _proj_head_l0(h)
        _bound_l0(h)
        _attn_head(0, h)
    if cut("l0a") or ATH_PH in ("h1", "h2") and (_dbg_out() or True):
        return
    _tail(0)
    if cut("tail0"):
        return
    _ffn0()
    if cut("ffn0"):
        return
    _exchange()
    if cut("xc"):
        return
    _l1_proj_common()
    if cut("l1p"):
        return
    for p in range(4):
        _l1_pass(p)
        if cut(f"pass{p}"):
            return
    _tail(1)
    if cut("tail1"):
        return
    _ffn1()
    if cut("ffn1"):
        return
    _final_ln_out()


def _dbg_out():
    barrier("dbg")
    for tt in range(16):
        dma("y", B.d["y"][tt * 128:(tt + 1) * 128, :],
            B.AF[:, STRM + tt * 1024: STRM + (tt + 1) * 1024])
    em("y", lambda e, _h=B.qsem["y"].h, _n=B.qsem["y"].n: e.wait_ge(_h, _n))
    em("g", lambda e, _h=B.qsem["g"].h, _n=B.qsem["g"].n: e.wait_ge(_h, _n))



def _rope_unit(kind, idx, h, unit, rope_done, extra_w=()):
    AB, WB, ps = B.AB, B.WB, B.ps
    S = B.S
    w = 512 if kind == "k" else 256
    slot = W_SLOT[h % 2]
    if kind == "k":
        rhs_off = idx * 512
        wofA, wofB = slot + W_KIND["l0_wk"], slot + W_KIND["l0_wk_rot"]
        dst = KTA + idx * 512
        tba, tbb = TKA + idx * 512, TKB + idx * 512
    else:
        rhs_off = (2 * idx) * BLK
        wofA, wofB = slot + W_KIND["l0_wq"], slot + W_KIND["l0_wq_rot"]
        dst = QTA + idx * 256
        tba, tbb = TQA + idx * 256, TQB + idx * 256
    if unit % 2 == 0:
        pA, pB = ps["pA"][:, 0:w], ps["pB"][:, 0:w]
    else:
        pA, pB = ps["pS0"][:, 0:w], ps["pS0"][:, 512:512 + w]
    waits = tuple(extra_w)
    if unit - 2 in rope_done and rope_done[unit - 2] is not None:
        waits += ((S["v"], rope_done[unit - 2]),)
    for mc in range(8):
        t_mm(pA, WB[:, wofA + mc * 128: wofA + mc * 128 + 128],
             AB[:, XT + mc * T + rhs_off: XT + mc * T + rhs_off + w],
             start=(mc == 0), stop=(mc == 7), waits=(waits if mc == 0 else ()))
    for mc in range(8):
        t_mm(pB, WB[:, wofB + mc * 128: wofB + mc * 128 + 128],
             AB[:, XT + mc * T + rhs_off: XT + mc * T + rhs_off + w],
             start=(mc == 0), stop=(mc == 7), inc=(mc == 7))
    tN = S["t"].n

    def f(e, _pA=pA, _pB=pB, _dst=dst, _ta=tba, _tb=tbb, _w=w, _u=unit):
        AFl = B.AF
        fsb = FS + (_u % 2) * 1024
        t1 = AFl[:, fsb:fsb + _w]
        t2 = AFl[:, fsb + 512:fsb + 512 + _w]
        e.tensor_tensor(t1, _pA, AFl[:, _ta:_ta + _w], ALU.mult)
        vc(e, e.tensor_tensor(t2, _pB, AFl[:, _tb:_tb + _w], ALU.mult))
        return e.tensor_tensor(B.AB[:, _dst:_dst + _w], t1, t2, ALU.add)
    rope_done[unit] = e_op("v", f, waits=((S["t"], tN),), inc=True)


def _proj_head_l0(h):
    AB, WB, ps = B.AB, B.WB, B.ps
    S = B.S
    wldw = ((B.qsem["gw0" if h % 2 == 0 else "gw1"], B._l0_wld[h]),) \
        if hasattr(B, "_l0_wld") and h in B._l0_wld else ()
    wldw += ((S["s"], S["s"].n), (S["v"], S["v"].n))
    vpad = e_op("v", lambda e: e.memset(
        AP(B.AB, VA + 64, [[AB_COLS, 128], [VW0, 32], [1, 1]]), 1.0), inc=True)
    rope_done = {}
    for tt in range(8):
        if tt == 0:
            rope_done[-2] = None
        _rope_unit("k", tt, h, tt, rope_done, extra_w=(wldw if tt == 0 else ()))
        wldw = ()
    for jb in range(8):
        _rope_unit("q", jb, h, 8 + jb, rope_done)
    B.l0_rope_last = rope_done[15]

    vev = {}
    for vb in range(8):
        half = vb % 2
        waits = ((S["s"], vev[vb - 2]),) if vb >= 2 else ()
        for i in range(4):
            tl = vb * 4 + i
            for mc in range(8):
                t_mm(ps["pS1"][:, half * 512 + i * 128: half * 512 + i * 128 + 128],
                     AB[:, XT + mc * T + tl * 128: XT + mc * T + tl * 128 + 128],
                     WB[:, W_SLOT[h % 2] + W_KIND["l0_wv"] + mc * 128: W_SLOT[h % 2] + W_KIND["l0_wv"] + mc * 128 + 128],
                     start=(mc == 0), stop=(mc == 7),
                     waits=(waits if (i == 0 and mc == 0) else ()),
                     inc=(i == 3 and mc == 7))
        tN = S["t"].n

        def f_ev(e, _vb=vb, _half=half):
            src1 = AP(B.ps["pS1"], _half * 512, [[1024, 128], [128, 4], [1, 64]])
            dst1 = AP(B.AB, VA + (_vb * 4) * VW0, [[AB_COLS, 128], [VW0, 4], [1, 64]])
            e.activation(dst1, src1, AF.Copy)
            src2 = AP(B.ps["pS1"], _half * 512 + 64, [[1024, 128], [128, 4], [1, 64]])
            dst2 = AP(B.AB, VA + (_vb * 4) * VW0 + 65, [[AB_COLS, 128], [VW0, 4], [1, 64]])
            return e.activation(dst2, src2, AF.Copy)
        vev[vb] = e_op("s", f_ev, waits=((S["t"], tN), (S["v"], vpad)), inc=True)
    B.l0_vev_last = vev[7]
    B._l0_projt = S["t"].n
    if h + 2 < L0H:
        if not hasattr(B, "_l0_wld"):
            B._l0_wld = {}
        B._l0_wld[h + 2] = _load_l0_head_w(h + 2, waits=((S["t"], S["t"].n),))



def _bound_l0(h):
    AB, ps, sb = B.AB, B.ps, B.sb
    S = B.S
    units = [("k", tc) for tc in range(8)] + [("q", tc) for tc in range(4)]
    mmN = {}
    for u, (kind, tc) in enumerate(units):
        base = (KTA if kind == "k" else QTA) + tc * 512
        w = ((S["v"], B.l0_rope_last),) if u == 0 else ()
        if u - 2 in mmN:
            w += ((S["t"], mmN[u - 2]),)
        eb = EA + (u % 2) * 1024

        def f_sq(e, _b=base, _e=eb):
            return e.tensor_tensor(B.AB[:, _e:_e + 512], B.AB[:, _b:_b + 512],
                                   B.AB[:, _b:_b + 512], ALU.mult)
        vN = e_op("v", f_sq, waits=w, inc=True)
        row = tc if kind == "k" else 8 + tc
        mmN[u] = t_mm(ps["pA"][0:12, 0:512], sb["selA"][:, row * 12: row * 12 + 12],
                      AB[:, eb:eb + 512], start=(u == 0), stop=(u == len(units) - 1),
                      waits=((S["v"], vN),), inc=True)
    tN = S["t"].n
    def f_mxc(e):
        vc(e, e.reduce_max(B.sb["nscr"][0:12, 0:1], B.ps["pA"][0:12, 0:512], AX.X))
        return e.tensor_copy(B.sb["tinyb"][0:12, 0:1], B.sb["nscr"][0:12, 0:1])
    vN = e_op("v", f_mxc, waits=((S["t"], tN),), inc=True)
    tN = e_op("t", lambda e: e.transpose(B.ps["pT"][0:1, 0:12],
                                         B.sb["tinyb"][0:12, 0:1],
                                         B.sb["ident"][0:12, 0:12]),
              waits=((S["v"], vN),), inc=True)

    def f_mx(e):
        e.reduce_max(B.sb["tiny"][0:1, 0:1], B.ps["pT"][0:1, 0:8], AX.X)
        vc(e, e.reduce_max(B.sb["tiny"][0:1, 1:2], B.ps["pT"][0:1, 8:12], AX.X))
        return e.tensor_tensor(B.sb["tiny"][0:1, 2:3], B.sb["tiny"][0:1, 0:1],
                               B.sb["tiny"][0:1, 1:2], ALU.mult)
    vN = e_op("v", f_mx, waits=((S["t"], tN),), inc=True)
    sN = e_op("s", lambda e: e.activation(B.sb["tiny"][0:1, 3:4],
                                          B.sb["tiny"][0:1, 2:3], AF.Sqrt),
              waits=((S["v"], vN),), inc=True)
    tN = t_mm(ps["pB"][:, 0:1], sb["ones_f"][0:1, :], sb["tiny"][0:1, 3:4],
              start=True, stop=True, waits=((S["s"], sN),), inc=True)
    e_op("v", lambda e, _h=h: e.tensor_scalar(B.sb["cbias"][:, _h:_h + 1],
                                              B.ps["pB"][:, 0:1], -1.0 / 128.0,
                                              None, ALU.mult),
         waits=((S["t"], tN),), inc=True)


def _bound_l1(p):
    AB, ps, sb = B.AB, B.ps, B.sb
    S = B.S
    units = ([("k", cc, tc) for cc in range(2) for tc in range(8)] +
             [("q", cc, tc) for cc in range(2) for tc in range(4)])
    first = True
    mmN = {}
    for nu, (kind, cc, tc) in enumerate(units):
        c = 2 * p + cc
        if kind == "k":
            base = (L1_KRG + tc * 512) if c == 7 else (L1_KT + cc * T + tc * 512)
            arr = B.AB
        else:
            base = (W_QRG + tc * 512) if c == 7 else (L1_QT + cc * TOWN + tc * 512)
            arr = B.WB if c == 7 else B.AB
        w = ((S["s"], S["s"].n),) if first else ()
        if nu - 2 in mmN:
            w += ((S["t"], mmN[nu - 2]),)
        eb = L1_E + (nu % 2) * 1024

        def f_sq(e, _a=arr, _b=base, _e=eb):
            return e.tensor_tensor(B.AB[:, _e:_e + 512], _a[:, _b:_b + 512],
                                   _a[:, _b:_b + 512], ALU.mult)
        vN = e_op("v", f_sq, waits=w, inc=True)
        v = (0 if kind == "k" else 2) + cc
        mmN[nu] = t_mm(ps["pA"][0:8, 0:512], sb["selB"][:, v * 8: v * 8 + 8],
                       AB[:, eb:eb + 512], start=first, stop=(nu == len(units) - 1),
                       waits=((S["v"], vN),), inc=True)
        first = False
    tN = S["t"].n
    def f_mxc(e):
        vc(e, e.reduce_max(B.sb["nscr"][0:8, 0:1], B.ps["pA"][0:8, 0:512], AX.X))
        return e.tensor_copy(B.sb["tinyb"][0:8, 0:1], B.sb["nscr"][0:8, 0:1])
    vN = e_op("v", f_mxc, waits=((S["t"], tN),), inc=True)
    tN = e_op("t", lambda e: e.transpose(B.ps["pT"][0:1, 0:8],
                                         B.sb["tinyb"][0:8, 0:1],
                                         B.sb["ident"][0:8, 0:8]),
              waits=((S["v"], vN),), inc=True)

    def f_pm(e):
        vc(e, e.tensor_copy(B.sb["tiny"][0:1, 8:16], B.ps["pT"][0:1, 0:8]))
        return e.tensor_tensor(B.sb["tiny"][0:1, 0:4], B.sb["tiny"][0:1, 8:12],
                               B.sb["tiny"][0:1, 12:16], ALU.mult)
    vN = e_op("v", f_pm, waits=((S["t"], tN),), inc=True)
    sN = e_op("s", lambda e: e.activation(B.sb["tiny"][0:1, 4:8],
                                          B.sb["tiny"][0:1, 0:4], AF.Sqrt),
              waits=((S["v"], vN),), inc=True)
    tN = t_mm(ps["pB"][:, 0:4], sb["ones_f"][0:1, :], sb["tiny"][0:1, 4:8],
              start=True, stop=True, waits=((S["s"], sN),), inc=True)
    e_op("v", lambda e: e.tensor_scalar(B.sb["cbias"][:, 16:20], B.ps["pB"][:, 0:4],
                                        -1.0 / 8.0, None, ALU.mult),
         waits=((S["t"], tN),), inc=True)



def _attn_head(layer, h, l1ctx=None):
    AB, ps, sb = B.AB, B.ps, B.sb
    S = B.S
    scale = (1.0 / 128.0) if layer == 0 else (1.0 / 8.0)
    col = h if layer == 0 else 16 + (h % 4)
    ebase = EA if layer == 0 else L1_E
    ebuf = [ebase, ebase + 1024]
    ost = OST0 if layer == 0 else L1_OST
    flat = [(j, b) for j in range(NQG) for b in range(j + 1)]
    exp_done = {}
    av_done = {}
    mask_done = {}
    prev_ev = B._attn_prev_ev
    ost_dma = B._attn_ost_dma
    head_w = ((S["s"], S["s"].n), (S["v"], S["v"].n))

    state = {"prev_ev": prev_ev, "ost_dma": ost_dma}
    defq = []

    def flush():
        for f in defq:
            f()
        del defq[:]

    def emit_av(fi, j, b, eb, gate):
        nb = j + 1
        bl = 4
        avcol = (j % 2) * 256
        if layer == 0:
            for half in range(2):
                for i in range(bl):
                    first = (b == 0 and i == 0)
                    last = (b == nb - 1 and i == bl - 1)
                    kt = 4 * b + i
                    co = kt * VW0 + (65 if half else 0)
                    vt = AB[:, VA + co: VA + co + (64 if half else 65)]
                    pp = (ps["pB"][0:64, avcol:avcol + 256] if half else
                          ps["pAV"][0:65, avcol:avcol + 256])
                    w = (gate,) if (i == 0 and half == 0) else ()
                    if first and half == 0 and state["prev_ev"] is not None:
                        w += (state["prev_ev"],)
                    t_mm(pp, vt, AB[:, eb + i * 256: eb + (i + 1) * 256],
                         start=first, stop=last, waits=w,
                         inc=(i == bl - 1 and half == 1))
        else:
            for i in range(bl):
                first = (b == 0 and i == 0)
                last = (b == nb - 1 and i == bl - 1)
                kt = 4 * b + i
                vlo = l1ctx["V_sl"](h, kt)
                w = (gate,) if i == 0 else ()
                if first and state["prev_ev"] is not None:
                    w += (state["prev_ev"],)
                t_mm(ps["pAV"][0:65, avcol:avcol + 256], vlo,
                     AB[:, eb + i * 256: eb + (i + 1) * 256],
                     start=first, stop=last, waits=w, inc=(i == bl - 1))
        av_done[fi] = S["t"].n
        if b != nb - 1:
            return
        tN_av = S["t"].n
        scol = 0

        def f_ln(e, _sc=scol, _ac=avcol):
            vc(e, e.activation(B.sb["sumb"][64:65, _sc:_sc + 256],
                               B.ps["pAV"][64:65, _ac:_ac + 256], AF.Ln), "s")
            return e.activation(B.sb["sumb"][64:65, _sc + 256:_sc + 512],
                                B.sb["sumb"][64:65, _sc:_sc + 256], AF.Exp,
                                scale=-1.0)
        sN = e_op("s", f_ln, waits=((S["t"], tN_av),), inc=True)
        tN = t_mm(ps["pA"][:, 0:256], sb["ones_f"][64:65, :],
                  sb["sumb"][64:65, scol + 256:scol + 512],
                  start=True, stop=True, waits=((S["s"], sN),), inc=True)
        w_ost = [(S["t"], tN)]
        if state["ost_dma"] is not None:
            w_ost.append((B.qsem["go"], state["ost_dma"]))

        def f_on(e, _ac=avcol, _h=h, _lay=layer, _o=ost):
            rb = B.sb["sumb"][0:64, 0:256]
            vc(e, e.tensor_copy(rb, B.ps["pA"][0:64, 0:256]))
            if _lay == 0:
                e.tensor_tensor(B.AB[0:64, _o:_o + 256],
                                B.ps["pAV"][0:64, _ac:_ac + 256], rb, ALU.mult)
                return e.tensor_tensor(B.AB[0:64, _o + 256:_o + 512],
                                       B.ps["pB"][0:64, _ac:_ac + 256], rb, ALU.mult)
            vc(e, e.tensor_tensor(B.AB[0:64, _o:_o + 256],
                                  B.ps["pAV"][0:64, _ac:_ac + 256], rb, ALU.mult))
            return e.tensor_scalar(B.AB[0:64, _o:_o + 256],
                                   B.AB[0:64, _o:_o + 256],
                                   B.sb["b_bv"][0:64, _h:_h + 1], None, ALU.add)
        vN = e_op("v", f_on, waits=tuple(w_ost), inc=True)
        state["prev_ev"] = (S["v"], vN)
        if layer == 0:
            r0 = h * 128
            dma("go", B.d["spill_o"][r0:r0 + 64, j * 256:(j + 1) * 256],
                AB[0:64, ost:ost + 256], waits=((S["v"], vN),))
            state["ost_dma"] = dma("go",
                                   B.d["spill_o"][r0 + 64:r0 + 128, j * 256:(j + 1) * 256],
                                   AB[0:64, ost + 256:ost + 512])
        else:
            r0 = h * 64
            state["ost_dma"] = dma("go",
                                   B.d["spill_o"][r0:r0 + 64, j * 256:(j + 1) * 256],
                                   AB[0:64, ost:ost + 256], waits=((S["v"], vN),))

    for fi, (j, b) in enumerate(flat):
        nb = j + 1
        pS = ps["pS0"] if fi % 2 == 0 else ps["pS1"]
        bl = 4
        for i in range(bl):
            kt = 4 * b + i
            if layer == 0:
                lhsT = AB[:, KTA + kt * 128: KTA + kt * 128 + 128]
                rhs = AB[:, QTA + j * 256: QTA + j * 256 + 256]
            else:
                lhsT = l1ctx["KT_sl"](h, kt)
                rhs = l1ctx["QT_sl"](h, j)
            w = ()
            if i == 0:
                if fi - 2 in exp_done:
                    w += ((S["s"], exp_done[fi - 2]),)
                    if fi - 2 in mask_done:
                        w += ((S["v"], mask_done[fi - 2]),)
                if fi == 0:
                    w += head_w
            t_mm(pS[:, i * 256:(i + 1) * 256], lhsT, rhs,
                 start=True, stop=True, waits=w, inc=(i == bl - 1))
        tN = S["t"].n
        flush()
        eb = ebuf[fi % 2]
        w_exp = [(S["t"], tN)]
        if fi - 2 in av_done:
            w_exp.append((S["t"], av_done[fi - 2]))

        def f_exp(e, _pS=pS, _eb=eb, _c=col, _sc=scale):
            return e.activation(B.AB[:, _eb:_eb + 1024], _pS[:, 0:1024],
                                AF.Exp, bias=B.sb["cbias"][:, _c:_c + 1], scale=_sc)
        sN = e_op("s", f_exp, waits=tuple(w_exp), inc=True)
        exp_done[fi] = sN
        if b == nb - 1:
            def f_mask(e, _eb=eb):
                return e.tensor_tensor(B.AB[:, _eb:_eb + 1024],
                                       B.AB[:, _eb:_eb + 1024],
                                       B.sb["masks"][:, 0:1024], ALU.mult)
            vN = e_op("v", f_mask, waits=((S["s"], sN),), inc=True)
            mask_done[fi] = vN
            gate = (S["v"], vN)
        else:
            gate = (S["s"], sN)
        defq.append(lambda _fi=fi, _j=j, _b=b, _eb=eb, _g=gate:
                    emit_av(_fi, _j, _b, _eb, _g))
    flush()
    if layer == 0:
        if not hasattr(B, "_l0_go_n"):
            B._l0_go_n = {}
        B._l0_go_n[h] = B.qsem["go"].n
    B._attn_prev_ev = state["prev_ev"]
    B._attn_ost_dma = state["ost_dma"]


def _attn_head_pair(h0, l1ctx):
    AB, ps, sb = B.AB, B.ps, B.sb
    S = B.S
    scale = 1.0 / 8.0
    flat = [(j, b) for j in range(NQG) for b in range(j + 1)]
    exp_done = {}
    av_done = {}
    mask_done = {}
    state = B._l1_pair_state
    head_w = ((S["s"], S["s"].n), (S["v"], S["v"].n))
    pSs = (ps["pS0"], ps["pS1"])
    pAVs = (ps["pAV"], ps["pB"])
    defq = []

    def flush():
        for f in defq:
            f()
        del defq[:]

    def emit_av(fi, j, b, hp, gate):
        h = h0 + hp
        nb = j + 1
        avcol = (j % 2) * 256
        eb = L1_E + ((fi % 2) * 2 + hp) * 1024
        pp = pAVs[hp]
        for i in range(4):
            first = (b == 0 and i == 0)
            last = (b == nb - 1 and i == 3)
            kt = 4 * b + i
            vlo = l1ctx["V_sl"](h, kt)
            w = (gate,) if i == 0 else ()
            if first and state["prev_ev"][hp] is not None:
                w += (state["prev_ev"][hp],)
            t_mm(pp[0:65, avcol:avcol + 256], vlo,
                 AB[:, eb + i * 256: eb + (i + 1) * 256],
                 start=first, stop=last, waits=w, inc=(i == 3))
        av_done[(fi, hp)] = S["t"].n
        if b != nb - 1:
            return
        tN_av = S["t"].n
        if hp == 0:
            ln_t = sb["sumb"][64:65, 0:256]
            rc_t = sb["sumb"][64:65, 256:512]
            rb_t = sb["sumb"][0:64, 0:256]
        else:
            ln_t = B.AF[64:65, FS:FS + 256]
            rc_t = B.AF[64:65, FS + 256:FS + 512]
            rb_t = B.AF[0:64, FS + 512:FS + 768]

        def f_ln(e, _ac=avcol, _pp=pp, _ln=ln_t, _rc=rc_t):
            vc(e, e.activation(_ln, _pp[64:65, _ac:_ac + 256], AF.Ln), "s")
            return e.activation(_rc, _ln, AF.Exp, scale=-1.0)
        sN = e_op("s", f_ln, waits=((S["t"], tN_av),), inc=True)
        wb = [(S["s"], sN)]
        if state["fon"] is not None:
            wb.append((S["v"], state["fon"]))
        tN = t_mm(ps["pA"][:, hp * 256: hp * 256 + 256],
                  sb["ones_f"][64:65, :], rc_t,
                  start=True, stop=True, waits=tuple(wb), inc=True)
        qn = "go" if hp == 0 else "gp0"
        w_ost = [(S["t"], tN)]
        if state["ost_dma"][hp] is not None:
            w_ost.append((B.qsem[qn], state["ost_dma"][hp]))
        ost = L1_OST + hp * 512

        def f_on(e, _ac=avcol, _h=h, _o=ost, _hp=hp, _pp=pp, _rb=rb_t):
            vc(e, e.tensor_copy(_rb, B.ps["pA"][0:64, _hp * 256:_hp * 256 + 256]))
            vc(e, e.tensor_tensor(B.AB[0:64, _o:_o + 256],
                                  _pp[0:64, _ac:_ac + 256], _rb, ALU.mult))
            return e.tensor_scalar(B.AB[0:64, _o:_o + 256],
                                   B.AB[0:64, _o:_o + 256],
                                   B.sb["b_bv"][0:64, _h:_h + 1], None, ALU.add)
        vN = e_op("v", f_on, waits=tuple(w_ost), inc=True)
        state["prev_ev"][hp] = (S["v"], vN)
        state["fon"] = vN
        r0 = h * 64
        state["ost_dma"][hp] = dma(
            qn, B.d["spill_o"][r0:r0 + 64, j * 256:(j + 1) * 256],
            AB[0:64, ost:ost + 256], waits=((S["v"], vN),))

    for fi, (j, b) in enumerate(flat):
        nb = j + 1
        newq = []
        for hp in range(2):
            h = h0 + hp
            pS = pSs[hp]
            for i in range(4):
                kt = 4 * b + i
                lhsT = l1ctx["KT_sl"](h, kt)
                rhs = l1ctx["QT_sl"](h, j)
                w = ()
                if i == 0:
                    if (fi - 1, hp) in exp_done:
                        w += ((S["s"], exp_done[(fi - 1, hp)]),)
                        if (fi - 1, hp) in mask_done:
                            w += ((S["v"], mask_done[(fi - 1, hp)]),)
                    if fi == 0:
                        w += head_w
                t_mm(pS[:, i * 256:(i + 1) * 256], lhsT, rhs,
                     start=True, stop=True, waits=w, inc=(i == 3))
            tNs = S["t"].n
            eb = L1_E + ((fi % 2) * 2 + hp) * 1024
            col = 16 + (h % 4)
            w_exp = [(S["t"], tNs)]
            if (fi - 2, hp) in av_done:
                w_exp.append((S["t"], av_done[(fi - 2, hp)]))

            def f_exp(e, _pS=pS, _eb=eb, _c=col):
                return e.activation(B.AB[:, _eb:_eb + 1024], _pS[:, 0:1024],
                                    AF.Exp, bias=B.sb["cbias"][:, _c:_c + 1],
                                    scale=scale)
            sN = e_op("s", f_exp, waits=tuple(w_exp), inc=True)
            exp_done[(fi, hp)] = sN
            if b == nb - 1:
                def f_mask(e, _eb=eb):
                    return e.tensor_tensor(B.AB[:, _eb:_eb + 1024],
                                           B.AB[:, _eb:_eb + 1024],
                                           B.sb["masks"][:, 0:1024], ALU.mult)
                vN = e_op("v", f_mask, waits=((S["s"], sN),), inc=True)
                mask_done[(fi, hp)] = vN
                gate = (S["v"], vN)
            else:
                gate = (S["s"], sN)
            newq.append(lambda _fi=fi, _j=j, _b=b, _hp=hp, _g=gate:
                        emit_av(_fi, _j, _b, _hp, _g))
        flush()
        defq.extend(newq)
    flush()



def _tail(layer):
    AB, WB, ps, sb, AFl = B.AB, B.WB, B.ps, B.sb, B.AF
    d = B.d
    S = B.S
    if layer == 0:
        for tt in range(16):
            jj = tt // 2
            src = d["x_perm"][(2 * jj) * BLK + (tt % 2) * 128:
                              (2 * jj) * BLK + (tt % 2) * 128 + 128, :]
            w = ((S["v"], B.l0_rope_last),) if tt == 0 else ()
            dma("y" if tt % 2 == 0 else "ys",
                AFl[:, STRM + tt * 1024: STRM + (tt + 1) * 1024], src, waits=w)
        wT = ((S["t"], B._l0_projt),)
        for oc in range(8):
            dma("y" if oc % 2 == 0 else "ys",
                AB[:, ORB + oc * 2048: ORB + (oc + 1) * 2048],
                d["spill_o"][oc * 128:(oc + 1) * 128, :],
                waits=wT + ((B.qsem["go"], B._l0_go_n[oc]),))
            wT = ()
    else:
        em("y", lambda e, _h=B.qsem["yx"].h, _n=B.qsem["yx"].n: e.wait_ge(_h, _n))
        for tt in range(16):
            src = d["spill_x"][tt * 128:(tt + 1) * 128, :]
            w = ((S["v"], B._l1p_v_end),) if tt == 0 else ()
            dma("y" if tt % 2 == 0 else "ys",
                AFl[:, STRM + tt * 1024: STRM + (tt + 1) * 1024], src, waits=w)
    barrier(f"tl{layer}")
    B._attn_prev_ev = None
    B._attn_ost_dma = None

    wof = W_WO
    if layer == 0:
        _load_w_mk("g", "l0_wo", 1024, 8, wof)
    if layer == 1:
        em("y", lambda e, _h=B.qsem["go"].h, _n=B.qsem["go"].n: e.wait_ge(_h, _n))
        em("y", lambda e, _h=B.qsem["gp0"].h, _n=B.qsem["gp0"].n: e.wait_ge(_h, _n))
        for oc in range(8):
            dma("y" if oc % 2 == 0 else "ys",
                AB[:, ORB + oc * 2048: ORB + (oc + 1) * 2048],
                d["spill_o"][oc * 128:(oc + 1) * 128, :])
    for k in ("t", "s", "v"):
        for qn in ("y", "ys", "g", "gw0", "gw1"):
            em(k, lambda e, _h=B.qsem[qn].h, _n=B.qsem[qn].n: e.wait_ge(_h, _n))

    B._tail_v = {}
    if ATH_TC == "dumpo":
        em("y", lambda e, _h=B.qsem["g"].h, _n=B.qsem["g"].n: e.wait_ge(_h, _n))
        for i in range(8):
            ld = dma("y", AB[:, ABM_TMP: ABM_TMP + 2048] if False else AB[:, 0:2048],
                     d["spill_o"][i * 128:(i + 1) * 128, :],
                     waits=())
            def f_cp(e, _i=i):
                return e.tensor_copy(B.AF[:, STRM + _i * 2048: STRM + (_i + 1) * 2048],
                                     B.AB[:, 0:2048])
            vN = e_op("v", f_cp, waits=((B.qsem["y"], ld),), inc=True)
            em("y", lambda e, _h=S["v"].h, _n=vN: e.wait_ge(_h, _n))
        for tt in range(16):
            dma("y", B.d["y"][tt * 128:(tt + 1) * 128, :],
                B.AF[:, STRM + tt * 1024: STRM + (tt + 1) * 1024])
        em("y", lambda e, _h=B.qsem["y"].h, _n=B.qsem["y"].n: e.wait_ge(_h, _n))
        barrier(f"tle{layer}")
        return
    if ATH_TC in ("w_t", "w_s", "w_v"):
        sem = {"w_t": S["t"], "w_s": S["s"], "w_v": S["v"]}[ATH_TC]
        em("y", lambda e, _h=sem.h, _n=sem.n: e.wait_ge(_h, _n))
        barrier(f"tle{layer}")
        return
    if ATH_TC == "ld":
        barrier(f"tle{layer}")
        return
    ntt = 16
    if ATH_TC.startswith("mm") and ATH_TC[2:].isdigit():
        ntt = int(ATH_TC[2:])
    if ATH_TC == "cponly":
        tN = e_op("t", lambda e: e.nop(), inc=True)
        e_op("v", lambda e: e.tensor_copy(B.AF[:, FS:FS + 512], ps["pS0"][:, 0:512]),
             waits=((S["t"], tN),), inc=True)
        barrier(f"tle{layer}")
        return
    B._tail_defer = []
    for tt in range(ntt):
        pL = ps["pS0"][:, 0:512] if tt % 2 == 0 else ps["pS1"][:, 0:512]
        pR = ps["pS0"][:, 512:1024] if tt % 2 == 0 else ps["pS1"][:, 512:1024]
        w = ((S["v"], B._tail_v[tt - 2]),) if tt >= 2 else ()
        for half, pp in ((0, pL), (1, pR)):
            for oc in range(8):
                t_mm(pp, AB[:, ORB + oc * 2048 + tt * 128: ORB + oc * 2048 + tt * 128 + 128],
                     WB[:, wof + oc * 1024 + half * 512: wof + oc * 1024 + half * 512 + 512],
                     start=(oc == 0), stop=(oc == 7 and layer == 0),
                     waits=(w if (half == 0 and oc == 0) else ()),
                     inc=(oc == 7 and layer == 0))
            if layer == 1:
                t_mm(pp, sb["ones_b"][0:1, :],
                     WB[0:1, W_BO + half * 512: W_BO + half * 512 + 512],
                     start=False, stop=True, inc=True)
        tN = B.S["t"].n
        for f in B._tail_defer:
            f()
        del B._tail_defer[:]
        if ATH_TC.startswith("mm"):
            if ATH_TC == "mmnc":
                B._tail_v[tt] = e_op("v", lambda e: e.nop(), waits=((S["t"], tN),), inc=True)
            elif ATH_TC == "mmd":
                def f_d(e, _pL=pL, _pR=pR, _tt=tt):
                    stc = STRM + _tt * 1024
                    e.tensor_copy(B.AF[:, stc:stc + 512], _pL)
                    return e.tensor_copy(B.AF[:, stc + 512:stc + 1024], _pR)
                B._tail_v[tt] = e_op("v", f_d, waits=((S["t"], tN),), inc=True)
            else:
                B._tail_v[tt] = e_op("v", lambda e, _pL=pL: e.tensor_copy(
                    B.AF[:, FS:FS + 512], _pL), waits=((S["t"], tN),), inc=True)
        else:
            _ln_tile(tt, tN, pL, pR, mode="attn")
    for f in B._tail_defer:
        f()
    del B._tail_defer[:]
    barrier(f"tle{layer}")


def _ln_tile(tt, tN, pL, pR, mode):
    AB, ps, sb, AFl = B.AB, B.ps, B.sb, B.AF
    S = B.S
    st = STRM + tt * 1024

    if ATH_TC == "lnr":
        def f_r(e, _st=st, _pL=pL, _pR=pR):
            e.tensor_tensor(AFl[:, _st:_st + 512], AFl[:, _st:_st + 512], _pL, ALU.add)
            return e.tensor_tensor(AFl[:, _st + 512:_st + 1024],
                                   AFl[:, _st + 512:_st + 1024], _pR, ALU.add)
        B._tail_v[tt] = e_op("v", f_r, waits=(((S["t"], tN),) if tN else ()), inc=True)
        return
    if ATH_TC == "lnr2":
        def f_r(e, _st=st, _pL=pL, _pR=pR):
            e.tensor_tensor(AFl[:, _st:_st + 512], _pL, AFl[:, _st:_st + 512], ALU.add)
            return e.tensor_tensor(AFl[:, _st + 512:_st + 1024], _pR,
                                   AFl[:, _st + 512:_st + 1024], ALU.add)
        B._tail_v[tt] = e_op("v", f_r, waits=(((S["t"], tN),) if tN else ()), inc=True)
        return
    if ATH_TC == "lnr4":
        if tt % 2 == 0:
            def f_r(e, _st=st):
                return e.tensor_scalar(AFl[:, _st:_st + 1024], AFl[:, _st:_st + 1024],
                                       1.0, None, ALU.mult)
        else:
            def f_r(e, _st=st, _pL=pL):
                e.tensor_copy(AFl[:, FS:FS + 512], _pL)
                return e.tensor_tensor(AFl[:, _st:_st + 512], AFl[:, _st:_st + 512],
                                       AFl[:, FS:FS + 512], ALU.add)
        B._tail_v[tt] = e_op("v", f_r, waits=(((S["t"], tN),) if tN else ()), inc=True)
        return
    if ATH_TC == "lnr3":
        def f_r(e, _st=st, _pL=pL, _pR=pR):
            e.scalar_tensor_tensor(AFl[:, _st:_st + 512], _pL, 1.0,
                                   AFl[:, _st:_st + 512], ALU.mult, ALU.add)
            return e.scalar_tensor_tensor(AFl[:, _st + 512:_st + 1024], _pR, 1.0,
                                          AFl[:, _st + 512:_st + 1024], ALU.mult, ALU.add)
        B._tail_v[tt] = e_op("v", f_r, waits=(((S["t"], tN),) if tN else ()), inc=True)
        return

    def f_res(e, _st=st, _pL=pL, _pR=pR, _m=mode):
        if _m == "attn":
            e.tensor_tensor(AFl[:, _st:_st + 512], AFl[:, _st:_st + 512], _pL, ALU.add)
            vc(e, e.tensor_tensor(AFl[:, _st + 512:_st + 1024],
                                  AFl[:, _st + 512:_st + 1024], _pR, ALU.add))
        vc(e, e.reduce_sum(sb["lnscr"][:, 0:1], AFl[:, _st:_st + 1024], AX.X))
        vc(e, e.tensor_scalar(sb["lnscr"][:, 1:2], sb["lnscr"][:, 0:1], 1.0 / DM,
                              None, ALU.mult))
        return e.tensor_scalar(AFl[:, _st:_st + 1024], AFl[:, _st:_st + 1024],
                               sb["lnscr"][:, 1:2], None, ALU.subtract)
    vN = e_op("v", f_res, waits=(((S["t"], tN),) if tN else ()), inc=True)
    sN = e_op("s", lambda e, _st=st: e.activation(AFl[:, FS:FS + 1024],
                                                  AFl[:, _st:_st + 1024], AF.Square,
                                                  accum_out=sb["lnscr"][:, 2:3]),
              waits=((S["v"], vN),), inc=True)

    def f_rstd(e):
        vc(e, e.tensor_scalar(sb["lnscr"][:, 3:4], sb["lnscr"][:, 2:3], 1.0 / DM,
                              EPS, ALU.mult, ALU.add))
        return e.reciprocal(sb["lnscr"][:, 4:5], sb["lnscr"][:, 3:4])
    vN = e_op("v", f_rstd, waits=((S["s"], sN),), inc=True)
    sN = e_op("s", lambda e: e.activation(sb["lnscr"][:, 5:6], sb["lnscr"][:, 4:5],
                                          AF.Sqrt),
              waits=((S["v"], vN),), inc=True)
    if ATH_TC == "lnp":
        def f_p(e, _st=st):
            return e.tensor_copy(AFl[:, _st:_st + 8], sb["lnscr"][:, 0:8])
        B._tail_v[tt] = e_op("v", f_p, waits=((S["s"], sN),), inc=True)
        return
    vN = e_op("v", lambda e, _st=st: e.tensor_scalar(AFl[:, _st:_st + 1024],
                                                     AFl[:, _st:_st + 1024],
                                                     sb["lnscr"][:, 5:6], None,
                                                     ALU.mult),
              waits=((S["s"], sN),), inc=True)
    B._tail_v[tt] = vN

    if mode == "ffn1":
        dma("y", B.d["y"][tt * 128:(tt + 1) * 128, :], AFl[:, st:st + 1024],
            waits=((S["v"], vN),))
        return
    if mode == "ffn0":
        dma("yx", B.d["spill_x"][tt * 128:(tt + 1) * 128, :], AFl[:, st:st + 1024],
            waits=((S["v"], vN),))
        dma("g", B.d["cc_in"][tt * 128:(tt + 1) * 128, :], AFl[:, st:st + 1024],
            waits=((S["v"], vN),))
        return
    if ATH_TC == "ln":
        return
    stg = EA + (tt % 2) * 1024
    sN = e_op("s", lambda e, _st=st, _sg=stg: e.activation(
        AB[:, _sg:_sg + 1024], AFl[:, _st:_st + 1024], AF.Copy),
        waits=((S["v"], vN),), inc=True)
    jj = tt // 2
    poscol = jj * 256 + (tt % 2) * 128

    def emit_tp(_sN=sN, _stg=stg, _pc=poscol):
        for half in range(2):
            tNt = 0
            for mb in range(4):
                mc = half * 4 + mb
                w0 = ()
                if mb == 0:
                    if hasattr(B, "_pt_ev"):
                        w0 = ((S["v"], B._pt_ev),)
                    if half == 0:
                        w0 += ((S["s"], _sN),)
                tNt = e_op("t", lambda e, _mc=mc, _mb=mb, _sg=_stg, _hf=half:
                           e.transpose(B.ps["pT"][:, _hf * 512 + _mb * 128: _hf * 512 + (_mb + 1) * 128],
                                       B.AB[:, _sg + _mc * 128: _sg + _mc * 128 + 128],
                                       B.sb["ident"][:]),
                           waits=w0, inc=(mb == 3))

            def f_tev(e, _half=half, _pc2=_pc):
                src = AP(B.ps["pT"], _half * 512, [[1024, 128], [128, 4], [1, 128]])
                dst = AP(B.AB, XTO + (_half * 4) * 2048 + _pc2,
                         [[AB_COLS, 128], [2048, 4], [1, 128]])
                return e.tensor_copy(dst, src)
            B._pt_ev = e_op("v", f_tev, waits=((S["t"], tNt),), inc=True)
    B._tail_defer.append(emit_tp)



def _ffn0():
    AB, WB, ps, sb, AFl = B.AB, B.WB, B.ps, B.sb, B.AF
    S = B.S
    d = B.d
    barrier("f0")
    stats = _tern_stats_units()
    _statn = [0]

    def stats_slot():
        if _statn[0] < len(stats):
            stats[_statn[0]]()
            _statn[0] += 1
    gu_tp = {}
    d_tp = {}
    comp_v = {}
    stt_v = {}
    acc_v = {}
    guu = 0
    duu = 0

    for c in range(6):
        ws = F0_WS[c % 2]
        for wi, nm in ((0, "l0_wg"), (1, "l0_wu")):
            for st_i in range(4):
                gs = F0_GS[guu % 2]
                wld = ()
                if guu - 2 >= 0:
                    wld = ((S["t"], gu_tp[guu - 2]),)
                ld = dma("gu0" if guu % 2 == 0 else "gu1", AB[:, gs:gs + 1024],
                         AP(d[nm].tensor, (c * 512 + st_i * 128) * 1024,
                            [[1024, 128], [1, 1024]]), waits=wld)
                tN = 0
                for mg in range(2):
                    par = (guu * 2 + mg) % 2
                    w0 = ((S["v"], max(B._ptev.values())),)
                    if mg == 0:
                        w0 += ((B.qsem["gu0" if guu % 2 == 0 else "gu1"], ld),)
                    for mb in range(4):
                        mc = mg * 4 + mb
                        tN = e_op("t", lambda e, _gs=gs, _mc=mc, _mb=mb, _p=par:
                                  e.transpose(B.ps["pT"][:, _p * 512 + _mb * 128: _p * 512 + (_mb + 1) * 128],
                                              B.AB[:, _gs + _mc * 128: _gs + _mc * 128 + 128],
                                              B.sb["ident"][:]),
                                  waits=(w0 if mb == 0 else ()),
                                  inc=(mb == 3))

                    def f_ev(e, _ws=ws, _wi=wi, _st=st_i, _mg=mg, _p=par):
                        src = AP(B.ps["pT"], _p * 512, [[1024, 128], [128, 4], [1, 128]])
                        dst = AP(B.AB, _ws + _wi * 4096 + (_mg * 4) * 512 + _st * 128,
                                 [[AB_COLS, 128], [512, 4], [1, 128]])
                        return e.tensor_copy(dst, src)
                    B._ptev[par] = e_op("v", f_ev, waits=((S["t"], tN),), inc=True)
                gu_tp[guu] = tN
                guu += 1
                stats_slot()
        for dsub in range(8):
            dsb = F0_DS[duu % 2]
            wld = ()
            if duu - 2 >= 0:
                wld = ((S["t"], d_tp[duu - 2]),)
            ld = dma("gd0" if duu % 2 == 0 else "gd1", AB[:, dsb:dsb + 512],
                     AP(d["l0_wd"].tensor, dsub * 128 * FF0 + c * 512,
                        [[FF0, 128], [1, 512]]), waits=wld)
            tN = 0
            par = duu % 2
            w0 = ((S["v"], max(B._ptev.values())),
                  (B.qsem["gd0" if duu % 2 == 0 else "gd1"], ld))
            for hb in range(4):
                tN = e_op("t", lambda e, _dsb=dsb, _hb=hb, _p=par:
                          e.transpose(B.ps["pT"][:, _p * 512 + _hb * 128: _p * 512 + (_hb + 1) * 128],
                                      B.AB[:, _dsb + _hb * 128: _dsb + _hb * 128 + 128],
                                      B.sb["ident"][:]),
                          waits=(w0 if hb == 0 else ()),
                          inc=(hb == 3))

            def f_evd(e, _ws=ws, _dsub=dsub, _p=par):
                src = AP(B.ps["pT"], _p * 512, [[1024, 128], [128, 4], [1, 128]])
                dst = AP(B.AB, _ws + 8192 + _dsub * 128,
                         [[AB_COLS, 128], [1024, 4], [1, 128]])
                return e.tensor_copy(dst, src)
            B._ptev[par] = e_op("v", f_evd, waits=((S["t"], tN),), inc=True)
            d_tp[duu] = tN
            duu += 1
            stats_slot()
        tp_v = S["v"].n

        for tq in range(8):
            for hs in range(4):
                bank = hs % 2
                pg = ps["pS0"][:, bank * 512: bank * 512 + 256]
                pu = ps["pS0"][:, bank * 512 + 256: bank * 512 + 512]
                w0 = ()
                if tq == 0 and hs == 0:
                    w0 = ((S["v"], tp_v),)
                key = (c, tq, hs - 2) if hs >= 2 else (c, tq - 1, hs + 2)
                if key in stt_v:
                    w0 += ((S["v"], stt_v[key]),)
                for mc in range(8):
                    t_mm(pg, AB[:, ws + (mc * 4 + hs) * 128: ws + (mc * 4 + hs) * 128 + 128],
                         AB[:, XTO + mc * 2048 + tq * 256: XTO + mc * 2048 + tq * 256 + 256],
                         start=(mc == 0), stop=(mc == 7),
                         waits=(w0 if mc == 0 else ()))
                for mc in range(8):
                    t_mm(pu, AB[:, ws + 4096 + (mc * 4 + hs) * 128: ws + 4096 + (mc * 4 + hs) * 128 + 128],
                         AB[:, XTO + mc * 2048 + tq * 256: XTO + mc * 2048 + tq * 256 + 256],
                         start=(mc == 0), stop=(mc == 7), inc=(mc == 7))
                tN = S["t"].n
                w_sil = [(S["t"], tN)]
                if (c, tq, hs, "stt") in stt_v:
                    pass
                prevstt = stt_v.get((c, tq, hs - 1), stt_v.get((c, tq - 1, 3) if hs == 0 else None))
                if prevstt:
                    w_sil.append((S["v"], prevstt))
                sN = e_op("s", lambda e, _pg=pg: e.activation(
                    B.AB[:, F0_SIL:F0_SIL + 256], _pg, AF.Silu, scale=1.0 / 127.0),
                    waits=tuple(w_sil), inc=True)
                vN = e_op("v", lambda e, _pu=pu, _hs=hs: e.scalar_tensor_tensor(
                    B.AB[:, F0_ST + _hs * 256: F0_ST + (_hs + 1) * 256], _pu,
                    1.0 / 127.0, B.AB[:, F0_SIL:F0_SIL + 256], ALU.mult, ALU.mult),
                    waits=((S["s"], sN),), inc=True)
                stt_v[(c, tq, hs)] = vN
            st_v = S["v"].n
            for ti in range(2):
                tt = tq * 2 + ti
                wdn = ((S["v"], st_v),)
                pk = (c, tq, ti - 1) if ti else ((c, tq - 1, 1) if tq else (c - 1, 7, 1))
                if pk in acc_v:
                    wdn += ((S["v"], acc_v[pk]),)
                for half, pp in ((0, ps["pAV"][:, 0:512]), (1, ps["pB"][:, 0:512])):
                    for hs in range(4):
                        t_mm(pp, AB[:, F0_ST + hs * 256 + ti * 128: F0_ST + hs * 256 + ti * 128 + 128],
                             AB[:, ws + 8192 + hs * 1024 + half * 512: ws + 8192 + hs * 1024 + half * 512 + 512],
                             start=(hs == 0), stop=(hs == 3),
                             waits=(wdn if (half == 0 and hs == 0) else ()),
                             inc=(hs == 3 and half == 1))
                tN = S["t"].n

                def f_acc(e, _tt=tt):
                    stc = STRM + _tt * 1024
                    e.scalar_tensor_tensor(B.AF[:, stc:stc + 512], B.ps["pAV"][:, 0:512],
                                           1.0 / 127.0, B.AF[:, stc:stc + 512],
                                           ALU.mult, ALU.add)
                    return e.scalar_tensor_tensor(B.AF[:, stc + 512:stc + 1024],
                                                  B.ps["pB"][:, 0:512], 1.0 / 127.0,
                                                  B.AF[:, stc + 512:stc + 1024],
                                                  ALU.mult, ALU.add)
                acc_v[(c, tq, ti)] = e_op("v", f_acc, waits=((S["t"], tN),), inc=True)
            stats_slot()
    while _statn[0] < len(stats):
        stats_slot()
    barrier("f0e")
    B._tail_v = {}
    for tt in range(16):
        _ln_tile(tt, 0, None, None, mode="ffn0")
    ng = B.qsem["g"].n
    em("g", lambda e, _h=B.qsem["g"].h, _n=ng: e.wait_ge(_h, _n))

    def f_cc(e):
        e.collective_compute(
            "AllGather", ALU.bypass,
            replica_groups=[[0, 1], [2, 3], [4, 5], [6, 7]],
            ins=[B.d["cc_in"].opt()],
            outs=[B.d["cc_out"].opt()]).then_inc(B.ccs.h, 1)
    em("g", f_cc)
    B.ccs.n += 1
    barrier("f0z")



def _exchange():
    AB, WB, ps, sb, AFl = B.AB, B.WB, B.ps, B.sb, B.AF
    S = B.S
    d = B.d
    nyx = B.qsem["yx"].n
    em("y", lambda e, _h=B.qsem["yx"].h, _n=nyx: e.wait_ge(_h, _n))
    barrier("xc")

    dma("y", AFl[:, TKA:TKA + T], d["tabk_a"])
    dma("y", AFl[:, TKB:TKB + T], d["tabk_b"])
    dma("y", AFl[:, TQA:TQA + TOWN], d["tabq_a"])
    dma("y", AFl[:, TQB:TQB + TOWN], d["tabq_b"])
    _load_w_mk("g", "l1_wq", 256, 8, W_C["l1_wq"])
    _load_w_mk("g", "l1_wqr", 128, 8, W_C["l1_wqr"])
    _load_w_mk("g", "l1_wqr_rot", 128, 8, W_C["l1_wqr_rot"])
    _load_w_mk("g", "l1_wkv", 256, 8, W_C["l1_wkv"])
    _load_w_mk("g", "l1_wkr", 128, 8, W_C["l1_wkr"])
    _load_w_mk("g", "l1_wkr_rot", 128, 8, W_C["l1_wkr_rot"])
    dma("g", AP(B.WB, W_BO, [[WB_COLS, 1], [1, DM]]), d["r_bo"])
    dma("g", AP(B.WB, W_B2, [[WB_COLS, 1], [1, DM]]), d["r_b2"])

    _tern_stats()

    B._xtev = {}
    for tt in range(16):
        stg = PSTG + (tt % 2) * 1024
        wld = ((S["v"], B._xtev[tt - 2]),) if (tt - 2) in B._xtev else ()
        qn = "yc0" if tt % 2 == 0 else "yc1"
        ld = dma(qn, AB[:, stg:stg + 1024], d["cc_in"][tt * 128:(tt + 1) * 128, :],
                 waits=wld)
        _xt_transpose(tt, stg, (qn, ld), own=True)
    for tt in range(16):
        stg = PSTG + (tt % 2) * 1024
        prev_key = (tt - 2 + 100) if tt >= 2 else (14 + tt)
        wv = B._xtev.get(prev_key)
        ld = _peer_tile_dma(tt, stg, wv)
        _xt_transpose(tt, stg, ("gp0" if tt % 2 == 0 else "gp1", ld), own=False)
    barrier("xce")
    e_op("v", lambda e: e.tensor_scalar(B.WB[0:1, W_B2:W_B2 + DM],
                                        B.WB[0:1, W_B2:W_B2 + DM], -1.0,
                                        None, ALU.mult), inc=True)


def _peer_tile_dma(tt, stg, wv):
    sem = B.qsem["gp0" if tt % 2 == 0 else "gp1"]
    S = B.S

    def f(e, _tt=tt, _stg=stg, _h=sem.h, _wv=wv):
        if _tt == 0:
            e.wait_ge(B.ccs.h, B.ccs.n)
        if _wv is not None:
            e.wait_ge(S["v"].h, _wv)
        with e.register(f"po{_tt}") as po:
            e.reg_load(po, B.sb["u32s"][0:1, 0:1])
            e.reg_add(po, po, _tt * 131072)
            e.dma_start(out=B.AB[:, _stg:_stg + 1024],
                        in_=AP(B.d["cc_out"].tensor, po, [[1024, 128], [1, 1024]])
                        ).then_inc(_h, 16)
    ccn = B.ccs.n
    em("g", f)
    sem.n += 16
    return sem.n


def _xt_transpose(tt, stg, ld, own):
    S = B.S
    q, ldn = ld
    jj = tt // 2
    pos = (2 * jj + (0 if own else 1)) * BLK + (tt % 2) * 128
    tN = 0
    vN = 0
    for half in range(2):
        for mb in range(4):
            mc = half * 4 + mb
            w0 = ()
            if mb == 0:
                if half == 0:
                    w0 = ((B.qsem[q], ldn),)
                if hasattr(B, "_pt_ev"):
                    w0 += ((S["v"], B._pt_ev),)
            tN = e_op("t", lambda e, _stg=stg, _mc=mc, _mb=mb, _hf=half:
                      e.transpose(B.ps["pT"][:, _hf * 512 + _mb * 128: _hf * 512 + (_mb + 1) * 128],
                                  B.AB[:, _stg + _mc * 128: _stg + _mc * 128 + 128],
                                  B.sb["ident"][:]),
                      waits=w0, inc=(mb == 3))

        def f_ev(e, _half=half, _pos=pos):
            src = AP(B.ps["pT"], _half * 512, [[1024, 128], [128, 4], [1, 128]])
            dst = AP(B.AB, L1_X2T + (_half * 4) * T + _pos,
                     [[AB_COLS, 128], [T, 4], [1, 128]])
            return e.tensor_copy(dst, src)
        vN = e_op("v", f_ev, waits=((S["t"], tN),), inc=True)
        B._pt_ev = vN
    B._xtev[tt + (0 if own else 100)] = vN


def _tern_stats_units():
    AFl, sb = B.AF, B.sb
    S = B.S
    d = B.d
    B._stv = {-1: (S["s"], S["s"].n)}
    units = []
    uu = 0
    for wi, nm in ((0, "l1_W1"), (1, "l1_W2")):
        for i in range(32):
            def emit(_uu=uu, _wi=wi, _nm=nm, _i=i):
                gs = F1_GS[0]
                wld = (B._stv[_uu - 1],)
                if _wi == 0:
                    src = d[_nm][_i * 128:(_i + 1) * 128, :]
                else:
                    rt, ct = _i // 4, _i % 4
                    src = AP(d[_nm].tensor, rt * 128 * FF1 + ct * 1024,
                             [[FF1, 128], [1, 1024]])
                ld = dma("ys", AFl[:, gs:gs + 1024], src, waits=wld)

                def f_r(e, _gs=gs, _u=_uu):
                    return e.activation(B.AF[:, _gs:_gs + 1024],
                                        B.AF[:, _gs:_gs + 1024], AF.Abs,
                                        accum_out=B.sb["asum"][:, _u:_u + 1])
                sN = e_op("s", f_r, waits=((B.qsem["ys"], ld),), inc=True)
                B._stv[_uu] = (S["s"], sN)
            units.append(emit)
            uu += 1
    return units


def _tern_stats():
    sb, ps = B.sb, B.ps
    S = B.S

    def f_colsum(e):
        vc(e, e.reduce_sum(B.sb["asum"][:, 64:65], B.sb["asum"][:, 0:32], AX.X))
        return e.reduce_sum(B.sb["asum"][:, 65:66], B.sb["asum"][:, 32:64], AX.X)
    vN = e_op("v", f_colsum, waits=(B._stv[63],), inc=True)
    for wi in range(2):
        t_mm(ps["pA"][0:1, wi:wi + 1], sb["asum"][:, 64 + wi:65 + wi],
             sb["ones_f"][:, 0:1], start=True, stop=True,
             waits=(((S["v"], vN),) if wi == 0 else ()), inc=(wi == 1))
    tN = S["t"].n

    def f_thr(e):
        vc(e, e.tensor_scalar(B.sb["tiny"][0:1, 6:8], B.ps["pA"][0:1, 0:2],
                              1.0 / (FF1 * DM), None, ALU.mult))
        return e.tensor_scalar(B.sb["tiny"][0:1, 6:8], B.sb["tiny"][0:1, 6:8],
                               1e-8, 0.5, ALU.max, ALU.mult)
    vN = e_op("v", f_thr, waits=((S["t"], tN),), inc=True)
    tN = t_mm(ps["pA"][:, 2:4], sb["ones_f"][0:1, :], sb["tiny"][0:1, 6:8],
              start=True, stop=True, waits=((S["v"], vN),), inc=True)

    def f_b(e):
        e.tensor_copy(B.sb["thr"][:, 0:1], B.ps["pA"][:, 2:3])
        e.tensor_scalar(B.sb["thr"][:, 1:2], B.ps["pA"][:, 2:3], -1.0, None, ALU.mult)
        e.tensor_copy(B.sb["thr"][:, 2:3], B.ps["pA"][:, 3:4])
        return e.tensor_scalar(B.sb["thr"][:, 3:4], B.ps["pA"][:, 3:4], -1.0,
                               None, ALU.mult)
    e_op("v", f_b, waits=((S["t"], tN),), inc=True)



def _l1_proj_common():
    AB, WB, ps, sb = B.AB, B.WB, B.ps, B.sb
    S = B.S
    barrier("l1p")
    prev = {}
    for u in range(16):
        dc, tt = u // 8, u % 8
        pp = ps["pA"][:, 0:512] if u % 2 == 0 else ps["pS0"][:, 0:512]
        w = ((S["s"], prev[u - 2]),) if u - 2 in prev else ()
        for mc in range(8):
            t_mm(pp, WB[:, W_C["l1_wkv"] + mc * 256 + dc * 128: W_C["l1_wkv"] + mc * 256 + dc * 128 + 128],
                 AB[:, L1_X2T + mc * T + tt * 512: L1_X2T + mc * T + tt * 512 + 512],
                 start=(mc == 0), stop=(mc == 7), waits=(w if mc == 0 else ()),
                 inc=(mc == 7))
        tN = S["t"].n
        prev[u] = e_op("s", lambda e, _pp=pp, _dc=dc, _tt=tt: e.activation(
            B.AB[:, L1_CKV + _dc * T + _tt * 512: L1_CKV + _dc * T + _tt * 512 + 512],
            _pp, AF.Gelu, bias=B.sb["b_bkv"][:, _dc:_dc + 1]),
            waits=((S["t"], tN),), inc=True)
    prev_v = {-2: prev[14], -1: prev[15]}
    for tt in range(8):
        if tt % 2 == 0:
            pA, pB = ps["pA"][:, 0:512], ps["pB"][:, 0:512]
        else:
            pA, pB = ps["pS0"][:, 0:512], ps["pS0"][:, 512:1024]
        w = ((S["s"], prev_v[tt - 2]),) if tt - 2 in prev_v else ()
        for mc in range(8):
            t_mm(pA, WB[:, W_C["l1_wkr"] + mc * 128: W_C["l1_wkr"] + mc * 128 + 128],
                 AB[:, L1_X2T + mc * T + tt * 512: L1_X2T + mc * T + tt * 512 + 512],
                 start=(mc == 0), stop=(mc == 7), waits=(w if mc == 0 else ()))
        for mc in range(8):
            t_mm(pB, WB[:, W_C["l1_wkr_rot"] + mc * 128: W_C["l1_wkr_rot"] + mc * 128 + 128],
                 AB[:, L1_X2T + mc * T + tt * 512: L1_X2T + mc * T + tt * 512 + 512],
                 start=(mc == 0), stop=(mc == 7), inc=(mc == 7))
        tN = S["t"].n

        def f_rope(e, _pA=pA, _pB=pB, _tt=tt):
            fsb = FS + (_tt % 2) * 1024
            t1 = B.AF[:, fsb:fsb + 512]
            t2 = B.AF[:, fsb + 512:fsb + 1024]
            e.scalar_tensor_tensor(t1, _pA, B.sb["b_bkr"][:, 0:1],
                                   B.AF[:, TKA + _tt * 512: TKA + _tt * 512 + 512],
                                   ALU.add, ALU.mult)
            vc(e, e.scalar_tensor_tensor(t2, _pB, B.sb["b_bkr_rot"][:, 0:1],
                                         B.AF[:, TKB + _tt * 512: TKB + _tt * 512 + 512],
                                         ALU.add, ALU.mult))
            return e.tensor_tensor(t1, t1, t2, ALU.add)
        vN = e_op("v", f_rope, waits=((S["t"], tN),), inc=True)
        prev_v[tt] = e_op("s", lambda e, _tt=tt: e.activation(
            B.AB[:, L1_KRG + _tt * 512: L1_KRG + _tt * 512 + 512],
            B.AF[:, FS + (_tt % 2) * 1024: FS + (_tt % 2) * 1024 + 512], AF.Gelu),
            waits=((S["v"], vN),), inc=True)
    kr_v_last = vN
    prevq = {}
    for u in range(16):
        dc, jb = u // 8, u % 8
        pp = ps["pA"][:, 0:256] if u % 2 == 0 else ps["pS0"][:, 0:256]
        if u - 2 in prevq:
            w = ((S["s"], prevq[u - 2]),)
        else:
            w = ((S["v"], kr_v_last),)
        pos = (2 * jb) * BLK
        for mc in range(8):
            t_mm(pp, WB[:, W_C["l1_wq"] + mc * 256 + dc * 128: W_C["l1_wq"] + mc * 256 + dc * 128 + 128],
                 AB[:, L1_X2T + mc * T + pos: L1_X2T + mc * T + pos + 256],
                 start=(mc == 0), stop=(mc == 7), waits=(w if mc == 0 else ()),
                 inc=(mc == 7))
        tN = S["t"].n
        prevq[u] = e_op("s", lambda e, _pp=pp, _dc=dc, _jb=jb: e.activation(
            B.WB[:, W_CQ + _dc * TOWN + _jb * 256: W_CQ + _dc * TOWN + _jb * 256 + 256],
            _pp, AF.Gelu, bias=B.sb["b_bq"][:, _dc:_dc + 1]),
            waits=((S["t"], tN),), inc=True)
    prevr = {-2: prevq[14], -1: prevq[15]}
    for jb in range(8):
        if jb % 2 == 0:
            pA, pB = ps["pA"][:, 0:256], ps["pB"][:, 0:256]
        else:
            pA, pB = ps["pS0"][:, 0:256], ps["pS0"][:, 512:768]
        w = ((S["s"], prevr[jb - 2]),) if jb - 2 in prevr else ()
        pos = (2 * jb) * BLK
        for mc in range(8):
            t_mm(pA, WB[:, W_C["l1_wqr"] + mc * 128: W_C["l1_wqr"] + mc * 128 + 128],
                 AB[:, L1_X2T + mc * T + pos: L1_X2T + mc * T + pos + 256],
                 start=(mc == 0), stop=(mc == 7), waits=(w if mc == 0 else ()))
        for mc in range(8):
            t_mm(pB, WB[:, W_C["l1_wqr_rot"] + mc * 128: W_C["l1_wqr_rot"] + mc * 128 + 128],
                 AB[:, L1_X2T + mc * T + pos: L1_X2T + mc * T + pos + 256],
                 start=(mc == 0), stop=(mc == 7), inc=(mc == 7))
        tN = S["t"].n

        def f_rope(e, _pA=pA, _pB=pB, _jb=jb):
            fsb = FS + (_jb % 2) * 1024
            t1 = B.AF[:, fsb:fsb + 256]
            t2 = B.AF[:, fsb + 512:fsb + 768]
            e.scalar_tensor_tensor(t1, _pA, B.sb["b_bqr"][:, 0:1],
                                   B.AF[:, TQA + _jb * 256: TQA + _jb * 256 + 256],
                                   ALU.add, ALU.mult)
            vc(e, e.scalar_tensor_tensor(t2, _pB, B.sb["b_bqr_rot"][:, 0:1],
                                         B.AF[:, TQB + _jb * 256: TQB + _jb * 256 + 256],
                                         ALU.add, ALU.mult))
            return e.tensor_tensor(t1, t1, t2, ALU.add)
        vN = e_op("v", f_rope, waits=((S["t"], tN),), inc=True)
        prevr[jb] = e_op("s", lambda e, _jb=jb: e.activation(
            B.WB[:, W_QRG + _jb * 256: W_QRG + _jb * 256 + 256],
            B.AF[:, FS + (_jb % 2) * 1024: FS + (_jb % 2) * 1024 + 256], AF.Gelu),
            waits=((S["v"], vN),), inc=True)
    B._l1p_v_end = S["v"].n
    barrier("l1pe")


def _l1_pass(p):
    AB, WB, ps, sb = B.AB, B.WB, B.ps, B.sb
    S = B.S
    d = B.d
    barrier(f"l1s{p}")
    pb = W_PS[p % 2]
    qp = "gw0" if p % 2 == 0 else "gw1"
    for cc in range(2):
        c = 2 * p + cc
        if c < 7:
            dma(qp, AP(WB, pb + cc * 256 + 0, [[WB_COLS, 128], [128, 2], [1, 128]]),
                AP(d["l1_wkc"].tensor, c * 128, [[896, 128], [896 * 128, 2], [1, 128]]))
            dma(qp, AP(WB, pb + 512 + cc * 256, [[WB_COLS, 128], [128, 2], [1, 128]]),
                AP(d["l1_wqc"].tensor, c * 128, [[896, 128], [896 * 128, 2], [1, 128]]))
    dma(qp, AP(WB, pb + 1024, [[WB_COLS, 128], [256, 2], [1, 256]]),
        AP(d["l1_wv"].tensor, 4 * p * 64, [[1024, 128], [131072, 2], [1, 256]]))
    if p == 0:
        _load_w_mk(qp, "l1_wo", 1024, 8, W_WO,
                   waits=((S["t"], S["t"].n),))
    em("t", lambda e, _h=B.qsem[qp].h, _n=B.qsem[qp].n: e.wait_ge(_h, _n))
    vpad = e_op("v", lambda e: e.memset(
        AP(B.AB, L1_V + 64, [[AB_COLS, 128], [65, 4 * 32], [1, 1]]), 1.0), inc=True)

    prev = {}
    u = 0
    for cc in range(2):
        c = 2 * p + cc
        if c == 7:
            continue
        for tt in range(8):
            pp = ps["pA"][:, 0:512] if u % 2 == 0 else ps["pS0"][:, 0:512]
            w = ((S["s"], prev[u - 2]),) if u - 2 in prev else ()
            for dc in range(2):
                t_mm(pp, WB[:, pb + cc * 256 + dc * 128: pb + cc * 256 + dc * 128 + 128],
                     AB[:, L1_CKV + dc * T + tt * 512: L1_CKV + dc * T + tt * 512 + 512],
                     start=(dc == 0), stop=(dc == 1), waits=(w if dc == 0 else ()),
                     inc=(dc == 1))
            tN = S["t"].n
            prev[u] = e_op("s", lambda e, _pp=pp, _cc=cc, _tt=tt, _c=c: e.activation(
                B.AB[:, L1_KT + _cc * T + _tt * 512: L1_KT + _cc * T + _tt * 512 + 512],
                _pp, AF.Gelu, bias=B.sb["b_bkc"][:, _c:_c + 1]),
                waits=((S["t"], tN),), inc=True)
            u += 1
    for cc in range(2):
        c = 2 * p + cc
        if c == 7:
            continue
        for jb in range(8):
            pp = ps["pA"][:, 0:256] if u % 2 == 0 else ps["pS0"][:, 0:256]
            w = ((S["s"], prev[u - 2]),) if u - 2 in prev else ()
            for dc in range(2):
                t_mm(pp, WB[:, pb + 512 + cc * 256 + dc * 128: pb + 512 + cc * 256 + dc * 128 + 128],
                     WB[:, W_CQ + dc * TOWN + jb * 256: W_CQ + dc * TOWN + jb * 256 + 256],
                     start=(dc == 0), stop=(dc == 1), waits=(w if dc == 0 else ()),
                     inc=(dc == 1))
            tN = S["t"].n
            prev[u] = e_op("s", lambda e, _pp=pp, _cc=cc, _jb=jb, _c=c: e.activation(
                B.AB[:, L1_QT + _cc * TOWN + _jb * 256: L1_QT + _cc * TOWN + _jb * 256 + 256],
                _pp, AF.Gelu, bias=B.sb["b_bqc"][:, _c:_c + 1]),
                waits=((S["t"], tN),), inc=True)
            u += 1
    prevv = {}
    for kt in range(32):
        if kt % 2 == 0:
            pt, off = ps["pB"], 0
        else:
            pt, off = ps["pS1"], 0
        pp = pt[:, off:off + 256]
        w = ((S["s"], prevv[kt - 2]),) if kt - 2 in prevv else ()
        if kt < 2:
            w += ((S["s"], prev[u - 1] if (u - 1) in prev else S["s"].n),)
        for dc in range(2):
            t_mm(pp, AB[:, L1_CKV + dc * T + kt * 128: L1_CKV + dc * T + kt * 128 + 128],
                 WB[:, pb + 1024 + dc * 256: pb + 1024 + dc * 256 + 256],
                 start=(dc == 0), stop=(dc == 1), waits=(w if dc == 0 else ()),
                 inc=(dc == 1))
        tN = S["t"].n
        rowext = 512 if kt % 2 == 0 else 1024

        def f_ev(e, _pt=pt, _kt=kt, _re=rowext):
            src = AP(_pt, 0, [[_re, 128], [64, 4], [1, 64]])
            dst = AP(B.AB, L1_V + _kt * VW1, [[AB_COLS, 128], [65, 4], [1, 64]])
            return e.activation(dst, src, AF.Copy)
        prevv[kt] = e_op("s", f_ev, waits=((S["t"], tN), (S["v"], vpad)), inc=True)

    _bound_l1(p)

    def KT_sl(h, kt):
        pr = slice((h % 2) * 64, (h % 2) * 64 + 64)
        if h // 2 == 7:
            return B.AB[pr, L1_KRG + kt * 128: L1_KRG + kt * 128 + 128]
        cc = h // 2 - 2 * p
        return B.AB[pr, L1_KT + cc * T + kt * 128: L1_KT + cc * T + kt * 128 + 128]

    def QT_sl(h, j):
        pr = slice((h % 2) * 64, (h % 2) * 64 + 64)
        if h // 2 == 7:
            return B.WB[pr, W_QRG + j * 256: W_QRG + j * 256 + 256]
        cc = h // 2 - 2 * p
        return B.AB[pr, L1_QT + cc * TOWN + j * 256: L1_QT + cc * TOWN + j * 256 + 256]

    def V_sl(h, kt):
        hh = h - 4 * p
        return B.AB[:, L1_V + kt * VW1 + hh * 65: L1_V + kt * VW1 + hh * 65 + 65]

    ctxd = {"KT_sl": KT_sl, "QT_sl": QT_sl, "V_sl": V_sl}
    if not hasattr(B, "_l1_pair_state"):
        B._l1_pair_state = {"prev_ev": {0: None, 1: None},
                            "ost_dma": {0: None, 1: None}, "fon": None}
    for hh in (0, 2):
        _attn_head_pair(4 * p + hh, ctxd)



def _ffn1():
    AB, WB, ps, sb, AFl = B.AB, B.WB, B.ps, B.sb, B.AF
    S = B.S
    d = B.d
    barrier("f1")
    B._f1par = 0
    tern_v = {}
    tp_t = {}
    gelu_s = {}
    acc_v = {}
    uu = 0
    for c in range(8):
        ws = F1_WS[c % 2]
        for wi, nm, nsub in ((0, "l1_W1", 4), (1, "l1_W2", 8)):
            for st_i in range(nsub):
                gs = F1_GS[0] if uu % 2 == 0 else FS
                wld = ((S["v"], tern_v[uu - 2]),) if uu - 2 in tern_v else ()
                if wi == 0:
                    src = AP(d[nm].tensor, (c * 512 + st_i * 128) * DM,
                             [[DM, 128], [1, DM]])
                    wcols = 1024
                else:
                    src = AP(d[nm].tensor, st_i * 128 * FF1 + c * 512,
                             [[FF1, 128], [1, 512]])
                    wcols = 512
                ld = dma("ys" if uu % 2 == 0 else "yc0", AFl[:, gs:gs + wcols],
                         src, waits=wld)
                thr_c = 0 if wi == 0 else 2
                wtp = ((S["t"], tp_t[uu - 2]),) if uu - 2 in tp_t else ()

                def f_tern(e, _gs=gs, _wc=wcols, _tc=thr_c):
                    vc(e, e.tensor_scalar(B.AB[:, F1_TS:F1_TS + _wc],
                                          B.AF[:, _gs:_gs + _wc],
                                          B.sb["thr"][:, _tc:_tc + 1], None,
                                          ALU.is_ge))
                    return e.scalar_tensor_tensor(
                        B.AB[:, F1_TS + 2048:F1_TS + 2048 + _wc],
                        B.AF[:, _gs:_gs + _wc], B.sb["thr"][:, _tc + 1:_tc + 2],
                        B.AB[:, F1_TS:F1_TS + _wc], ALU.is_le, ALU.subtract)
                vN = e_op("v", f_tern, waits=((B.qsem["ys" if uu % 2 == 0 else "yc0"], ld),) + wtp, inc=True)
                tern_v[uu] = vN
                nb = 8 if wi == 0 else 4
                tN = 0
                for bg in range(nb // 4):
                    par = B._f1par
                    B._f1par = 1 - B._f1par
                    w0 = ((S["v"], max(B._ptev.values())),)
                    if bg == 0:
                        w0 += ((S["v"], vN),)
                    for bb in range(4):
                        blk = bg * 4 + bb
                        tN = e_op("t", lambda e, _blk=blk, _bb=bb, _p=par:
                                  e.transpose(B.ps["pT"][:, _p * 512 + _bb * 128: _p * 512 + (_bb + 1) * 128],
                                              B.AB[:, F1_TS + 2048 + _blk * 128: F1_TS + 2048 + _blk * 128 + 128],
                                              B.sb["ident"][:]),
                                  waits=(w0 if bb == 0 else ()),
                                  inc=(bb == 3))
                    if wi == 0:
                        def f_ev(e, _ws=ws, _st=st_i, _bg=bg, _p=par):
                            src = AP(B.ps["pT"], _p * 512, [[1024, 128], [128, 4], [1, 128]])
                            dst = AP(B.AB, _ws + (_bg * 4) * 512 + _st * 128,
                                     [[AB_COLS, 128], [512, 4], [1, 128]])
                            return e.tensor_copy(dst, src)
                    else:
                        def f_ev(e, _ws=ws, _st=st_i, _bg=bg, _p=par):
                            src = AP(B.ps["pT"], _p * 512, [[1024, 128], [128, 4], [1, 128]])
                            dst = AP(B.AB, _ws + 4096 + _st * 128,
                                     [[AB_COLS, 128], [1024, 4], [1, 128]])
                            return e.tensor_copy(dst, src)
                    B._ptev[par] = e_op("v", f_ev, waits=((S["t"], tN),), inc=True)
                tp_t[uu] = tN
                uu += 1
        tp_v = S["v"].n
        for tq in range(8):
            for fs in range(4):
                bank = fs % 2
                ph = ps["pS0"][:, bank * 512: bank * 512 + 256]
                w0 = ()
                if tq == 0 and fs == 0:
                    w0 = ((S["v"], tp_v),)
                key = (c, tq, fs - 2) if fs >= 2 else (c, tq - 1, fs + 2)
                if key in gelu_s:
                    w0 += ((S["s"], gelu_s[key]),)
                for mc in range(8):
                    t_mm(ph, AB[:, ws + (mc * 4 + fs) * 128: ws + (mc * 4 + fs) * 128 + 128],
                         AB[:, XTO + mc * 2048 + tq * 256: XTO + mc * 2048 + tq * 256 + 256],
                         start=(mc == 0), stop=(mc == 7),
                         waits=(w0 if mc == 0 else ()), inc=(mc == 7))
                tN = S["t"].n
                gelu_s[(c, tq, fs)] = e_op("s", lambda e, _ph=ph, _fs=fs, _c=c: e.activation(
                    B.AB[:, F1_HT + _fs * 256: F1_HT + (_fs + 1) * 256], _ph, AF.Gelu,
                    bias=B.sb["b_b1"][:, _c * 4 + _fs:_c * 4 + _fs + 1], scale=-1.0),
                    waits=((S["t"], tN),), inc=True)
            ht_s = S["s"].n
            for ti in range(2):
                tt = tq * 2 + ti
                wdn = ((S["s"], ht_s),)
                pk = (c, tq, ti - 1) if ti else ((c, tq - 1, 1) if tq else (c - 1, 7, 1))
                if pk in acc_v:
                    wdn += ((S["v"], acc_v[pk]),)
                for half, pp in ((0, ps["pAV"][:, 0:512]), (1, ps["pB"][:, 0:512])):
                    for fs in range(4):
                        laststop = (fs == 3 and c < 7)
                        t_mm(pp, AB[:, F1_HT + fs * 256 + ti * 128: F1_HT + fs * 256 + ti * 128 + 128],
                             AB[:, ws + 4096 + fs * 1024 + half * 512: ws + 4096 + fs * 1024 + half * 512 + 512],
                             start=(fs == 0), stop=laststop,
                             waits=(wdn if (half == 0 and fs == 0) else ()),
                             inc=(laststop and half == 1))
                    if c == 7:
                        t_mm(pp, sb["ones_b"][0:1, :],
                             B.WB[0:1, W_B2 + half * 512: W_B2 + half * 512 + 512],
                             start=False, stop=True, inc=(half == 1))
                tN = S["t"].n

                def f_acc(e, _tt=tt):
                    stc = STRM + _tt * 1024
                    e.tensor_tensor(B.AF[:, stc:stc + 512], B.AF[:, stc:stc + 512],
                                    B.ps["pAV"][:, 0:512], ALU.subtract)
                    return e.tensor_tensor(B.AF[:, stc + 512:stc + 1024],
                                           B.AF[:, stc + 512:stc + 1024],
                                           B.ps["pB"][:, 0:512], ALU.subtract)
                acc_v[(c, tq, ti)] = e_op("v", f_acc, waits=((S["t"], tN),), inc=True)
    barrier("f1e")


def _final_ln_out():
    B._tail_v = {}
    for tt in range(16):
        _ln_tile(tt, 0, None, None, mode="ffn1")
    em("y", lambda e, _h=B.qsem["y"].h, _n=B.qsem["y"].n: e.wait_ge(_h, _n))
    em("g", lambda e, _h=B.qsem["g"].h, _n=B.qsem["g"].n: e.wait_ge(_h, _n))


def kernel(**inputs):
    nc = build_program()
    in_maps = [make_core_inputs(inputs, c // 2, c % 2) for c in range(8)]
    res = run_bass_kernel_spmd(nc, in_maps, core_ids=list(range(8)))
    out = np.zeros((4, T, DM), dtype=np.float32)
    for c in range(8):
        s, r = c // 2, c % 2
        yc = res.results[c]["y"]
        for j in range(NBLK):
            g = 2 * j + r
            out[s, g * BLK:(g + 1) * BLK, :] = yc[j * BLK:(j + 1) * BLK, :]
    return out


# version 08
# speedup vs baseline: 1.5337x; 1.0407x over previous
import os
import numpy as np
import ml_dtypes
from contextlib import ExitStack

ATH_PH = os.environ.get("ATH_PH", "full")
ATH_TC = os.environ.get("ATH_TC", "full")

import concourse.bass as bass
import concourse.mybir as mybir
from concourse.bass_utils import run_bass_kernel_spmd

F32 = mybir.dt.float32
BF16 = mybir.dt.bfloat16
I32 = mybir.dt.int32
U32 = mybir.dt.uint32
AF = mybir.ActivationFunctionType
ALU = mybir.AluOpType
AX = mybir.AxisListType

T = 4096
DM = 1024
BLK = 256
NBLK = 8
TOWN = 2048
L0H = 8
L1H = 16
DCV, DRP = 256, 128
FF0, FF1 = 3072, 4096
QG = 256
NQG = 8
EPS = 1e-5

AB_COLS = 45600
AF_COLS = 18432
STRM = 0
FS = 16384
TKA, TKB = 0, 4096
TQA, TQB = 8192, 10240

XT = 0
KTA = 32768
QTA = 36864
VA = 38912
VW0 = 129
EA = 43040
OST0 = 45088
TSC = 45600

XTO = 0
ORB = 16384

F0_WS = [16384, 28672]
F0_GS = [40960, 41984]
F0_DS = [43008, 43520]
F0_SIL = 44032
F0_ST = 44544

L1_X2T = 0
L1_CKV = 32768
L1_KRG = 40960
PSTG = 45056
PSTG = 32768

L1_KT = 0
L1_QT = 8192
L1_V = 12288
VW1 = 260
L1_E = 20608
L1_OST = 24704

F1_WS = [16384, 24576]
F1_TS = 32768
F1_HT = 35840
F1_GS = [FS + 1024]



def _perm_blocks(r):
    return [p + r if p % 2 == 0 else p - r for p in range(2 * NBLK)]


def _rope_tabs(pos):
    i = np.arange(64, dtype=np.float64)
    invf = np.power(10000.0, -i / 64.0)
    rad = pos[None, :].astype(np.float64) * invf[:, None]
    cos = np.cos(rad).astype(np.float32)
    sin = np.sin(rad).astype(np.float32)
    return (np.concatenate([cos, cos], axis=0),
            np.concatenate([-sin, sin], axis=0))


def _masks(r):
    k = np.arange(128)[:, None]
    q = np.arange(QG)[None, :]
    tri0 = (k <= q).astype(np.float32)
    tri1 = ((128 + k) <= q).astype(np.float32)
    full = np.full((128, QG), 1.0 if r == 1 else 0.0, np.float32)
    return np.concatenate([tri0, tri1, full, full], axis=1).astype(ml_dtypes.bfloat16)


def _selA():
    m = np.zeros((128, 144), np.float32)
    for tc in range(12):
        m[:, tc * 12 + tc] = 1.0
    return m.astype(ml_dtypes.bfloat16)


def _selB():
    m = np.zeros((128, 32), np.float32)
    for v in range(4):
        base = 2 * v
        m[0:64, v * 8 + base] = 1.0
        m[64:128, v * 8 + base + 1] = 1.0
    return m.astype(ml_dtypes.bfloat16)


def _rot_halves(w, width):
    g = w.shape[1] // width
    return np.ascontiguousarray(
        w.reshape(w.shape[0], g, 2, width // 2)[:, :, ::-1, :].reshape(w.shape))


def make_core_inputs(inputs, s, r):
    x = np.asarray(inputs["L0_attn_x"], dtype=np.float32)
    perm = _perm_blocks(r)
    rows = np.concatenate([np.arange(b * BLK, (b + 1) * BLK) for b in perm])
    x_perm = np.ascontiguousarray(x[s][rows])
    xT_perm = np.ascontiguousarray(x_perm.T)
    own_rows = rows.reshape(2 * NBLK, BLK)[0::2].reshape(-1)
    tka, tkb = _rope_tabs(rows.astype(np.float64))
    tqa, tqb = _rope_tabs(own_rows.astype(np.float64))
    f32 = lambda a: np.ascontiguousarray(np.asarray(a, dtype=np.float32))
    d = {
        "x_perm": x_perm, "xT_perm": xT_perm,
        "tabk_a": tka, "tabk_b": tkb, "tabq_a": tqa, "tabq_b": tqb,
        "masks": _masks(r),
        "ident": np.eye(128, dtype=ml_dtypes.bfloat16),
        "ones_b": np.ones((128, 128), dtype=ml_dtypes.bfloat16),
        "ones_f": np.ones((128, 128), dtype=np.float32),
        "selA": _selA(), "selB": _selB(),
        "peer_off": np.array([[(1 - r) * TOWN * DM]], dtype=np.uint32),
    }
    d["l0_wq"] = f32(np.asarray(inputs["L0_attn_wq"]).reshape(DM, DM))
    d["l0_wk"] = f32(np.asarray(inputs["L0_attn_wk"]).reshape(DM, DM))
    d["l0_wv"] = f32(np.asarray(inputs["L0_attn_wv"]).reshape(DM, DM))
    d["l0_wo"] = f32(np.asarray(inputs["L0_attn_wo"]).reshape(DM, DM))
    d["l0_wq_rot"] = _rot_halves(d["l0_wq"], 128)
    d["l0_wk_rot"] = _rot_halves(d["l0_wk"], 128)
    for nm_src, nm in (("L0_ffn_w_gate", "l0_wg"), ("L0_ffn_w_up", "l0_wu"),
                       ("L0_ffn_w_down", "l0_wd")):
        d[nm] = np.ascontiguousarray(np.asarray(inputs[nm_src], dtype=np.int32))
    for nm_src, nm in (("L1_attn_wq", "l1_wq"), ("L1_attn_wqc", "l1_wqc"),
                       ("L1_attn_wqr", "l1_wqr"), ("L1_attn_wkv", "l1_wkv"),
                       ("L1_attn_wkr", "l1_wkr"), ("L1_attn_wkc", "l1_wkc"),
                       ("L1_attn_wv", "l1_wv"), ("L1_attn_wo", "l1_wo"),
                       ("L1_ffn_W1", "l1_W1"), ("L1_ffn_W2", "l1_W2")):
        d[nm] = f32(inputs[nm_src])
    d["l1_wqr_rot"] = _rot_halves(d["l1_wqr"], 128)
    d["l1_wkr_rot"] = _rot_halves(d["l1_wkr"], 128)

    def cols(v):
        v = np.asarray(v, dtype=np.float32).reshape(-1)
        return np.ascontiguousarray(v.reshape(-1, 128).T)

    def rot_col(v):
        v = np.asarray(v, dtype=np.float32).reshape(128)
        return np.ascontiguousarray(np.concatenate([v[64:], v[:64]]).reshape(128, 1))

    d["b_bq"] = cols(inputs["L1_attn_bq"])
    d["b_bqc"] = cols(inputs["L1_attn_bqc"])
    d["b_bqr"] = cols(inputs["L1_attn_bqr"])
    d["b_bqr_rot"] = rot_col(inputs["L1_attn_bqr"])
    d["b_bkv"] = cols(inputs["L1_attn_bkv"])
    d["b_bkr"] = cols(inputs["L1_attn_bkr"])
    d["b_bkr_rot"] = rot_col(inputs["L1_attn_bkr"])
    d["b_bkc"] = cols(inputs["L1_attn_bkc"])
    d["b_b1"] = cols(inputs["L1_ffn_b1"])
    bv = np.asarray(inputs["L1_attn_bv"], dtype=np.float32).reshape(L1H, 64).T
    d["b_bv"] = np.ascontiguousarray(np.concatenate([bv, np.zeros((64, L1H), np.float32)]))
    d["r_bo"] = np.asarray(inputs["L1_attn_bo"], dtype=np.float32).reshape(1, DM)
    d["r_b2"] = np.asarray(inputs["L1_ffn_b2"], dtype=np.float32).reshape(1, DM)
    return d



class Sem:
    def __init__(self, ctx, nc, name):
        self.h = ctx.enter_context(nc.semaphore(name))
        self.n = 0


class B:
    pass


ENGS = ("t", "s", "v", "g", "y")


def em(eng, fn):
    B.streams[eng].append(fn)


def barrier(name):
    if not hasattr(B, "barsem"):
        B.barsem = {k: Sem(B.ctx, B.nc, "bar_" + k) for k in ENGS}
        B.barn = 0
    B.barn += 1
    n = B.barn
    for k in ENGS:
        qs = ()
        if hasattr(B, "qsem"):
            qs = tuple((B.qsem[q].h, B.qsem[q].n) for q in B.qsem
                       if q.startswith(k) and B.qsem[q].n > 0)
        def _f(e, _h=B.barsem[k].h, _k=k, _qs=qs):
            for (_qh, _qn) in _qs:
                e.wait_ge(_qh, _qn)
            if _k != "y":
                e.drain()
            e.nop().then_inc(_h, 1)
        em(k, _f)
    for k in ENGS:
        for k2 in ENGS:
            if k2 != k:
                em(k, lambda e, _h=B.barsem[k2].h, _n=n: e.wait_ge(_h, _n))


def dma(q, dst, src, waits=()):
    sem = B.qsem[q]
    eng = "y" if q.startswith("y") else "g"

    def f(e, _d=dst, _s=src, _h=sem.h, _w=tuple(waits)):
        for (sm, val) in _w:
            e.wait_ge(sm.h, val)
        e.dma_start(out=_d, in_=_s).then_inc(_h, 16)
    em(eng, f)
    sem.n += 16
    return sem.n


def t_mm(out, lhsT, rhs, start, stop, waits=(), inc=False):
    s_t = B.S["t"]

    def f(e, _o=out, _l=lhsT, _r=rhs, _a=start, _b=stop, _w=tuple(waits), _i=inc):
        for (sm, val) in _w:
            e.wait_ge(sm.h, val)
        ins = e.matmul(_o, _l, _r, start=_a, stop=_b)
        if _i:
            ins.then_inc(s_t.h, 1)
    em("t", f)
    if inc:
        s_t.n += 1
    return s_t.n


def vc(e, ins, eng="v"):
    sem = B.chain[eng]
    B.chain_n[eng] += 1
    ins.then_inc(sem.h, 1)
    e.wait_ge(sem.h, B.chain_n[eng])


def e_op(eng, fn, waits=(), inc=False):
    s = B.S[eng]

    def f(e, _f=fn, _w=tuple(waits), _i=inc):
        for (sm, val) in _w:
            e.wait_ge(sm.h, val)
        ins = _f(e)
        if _i:
            ins.then_inc(s.h, 1)
    em(eng, f)
    if inc:
        s.n += 1
    return s.n


def AP(t, off, pat):
    return bass.AP(t, off, pat)



_CACHE = {}

WB_COLS = 20480
W_SLOT = [0, 5120]
W_KIND = {"l0_wq": 0, "l0_wq_rot": 1024, "l0_wk": 2048, "l0_wk_rot": 3072,
          "l0_wv": 4096}
W_WO = 0
W_C = {"l1_wq": 0, "l1_wqr": 2048, "l1_wqr_rot": 3072, "l1_wkv": 4096,
       "l1_wkr": 6144, "l1_wkr_rot": 7168}
W_PS = [8192, 9728]
W_BO = 11264
W_B2 = 12288
W_CQ = 13312
W_QRG = 17408


def build_program():
    if "nc" in _CACHE:
        return _CACHE["nc"]
    nc = bass.Bass("TRN2", target_bir_lowering=False)
    ctx = ExitStack()
    B.nc, B.ctx = nc, ctx
    B.streams = {k: [] for k in ENGS}
    B.S = {k: Sem(ctx, nc, "p_" + k) for k in ("t", "s", "v", "g")}
    B.qsem = {k: Sem(ctx, nc, "q_" + k) for k in
              ("y", "g", "gw0", "gw1", "gu0", "gu1", "gd0", "gd1", "go",
               "ys", "yc0", "yc1", "gp0", "gp1", "yx")}
    B.ccs = Sem(ctx, nc, "q_cc")
    B.chain = {"v": Sem(ctx, nc, "ch_v"), "s": Sem(ctx, nc, "ch_s")}
    B.chain_n = {"v": 0, "s": 0}
    B._ptev = {0: 0, 1: 0}

    d = {}
    B.d = d

    def din(name, shape, dt):
        d[name] = nc.dram_tensor(name, shape, dt, kind="ExternalInput").ap()

    din("x_perm", [T, DM], F32)
    din("xT_perm", [DM, T], F32)
    din("tabk_a", [128, T], F32)
    din("tabk_b", [128, T], F32)
    din("tabq_a", [128, TOWN], F32)
    din("tabq_b", [128, TOWN], F32)
    din("masks", [128, 4 * QG], BF16)
    din("ident", [128, 128], BF16)
    din("ones_b", [128, 128], BF16)
    din("ones_f", [128, 128], F32)
    din("selA", [128, 144], BF16)
    din("selB", [128, 32], BF16)
    din("peer_off", [1, 1], U32)
    for nm in ("l0_wq", "l0_wk", "l0_wv", "l0_wo", "l0_wq_rot", "l0_wk_rot", "l1_wo"):
        din(nm, [DM, DM], F32)
    din("l0_wg", [FF0, DM], I32)
    din("l0_wu", [FF0, DM], I32)
    din("l0_wd", [DM, FF0], I32)
    din("l1_wq", [DM, DCV], F32)
    din("l1_wqc", [DCV, DM - DRP], F32)
    din("l1_wqr", [DM, DRP], F32)
    din("l1_wqr_rot", [DM, DRP], F32)
    din("l1_wkv", [DM, DCV], F32)
    din("l1_wkr", [DM, DRP], F32)
    din("l1_wkr_rot", [DM, DRP], F32)
    din("l1_wkc", [DCV, DM - DRP], F32)
    din("l1_wv", [DCV, DM], F32)
    din("l1_W1", [FF1, DM], F32)
    din("l1_W2", [DM, FF1], F32)
    for nm, n in (("b_bq", 2), ("b_bqc", 7), ("b_bqr", 1), ("b_bqr_rot", 1),
                  ("b_bkv", 2), ("b_bkr", 1), ("b_bkr_rot", 1), ("b_bkc", 7),
                  ("b_b1", 32), ("b_bv", 16)):
        din(nm, [128, n], F32)
    din("r_bo", [1, DM], F32)
    din("r_b2", [1, DM], F32)
    d["y"] = nc.dram_tensor("y", [TOWN, DM], F32, kind="ExternalOutput").ap()
    d["cc_in"] = nc.dram_tensor("cc_in", [TOWN, DM], BF16).ap()
    d["cc_out"] = nc.dram_tensor("cc_out", [2 * TOWN, DM], BF16).ap()
    d["spill_x"] = nc.dram_tensor("spill_x", [TOWN, DM], F32).ap()
    d["spill_o"] = nc.dram_tensor("spill_o", [DM, TOWN], BF16).ap()

    sb = {}
    B.sb = sb

    def sbuf(name, shape, dt):
        sb[name] = ctx.enter_context(nc.sbuf_tensor("s_" + name, shape, dt))
        return sb[name]

    B.AB = sbuf("ABar", [128, AB_COLS], BF16)
    B.AF = sbuf("AFar", [128, AF_COLS], F32)
    B.WB = sbuf("WBar", [128, WB_COLS], BF16)
    for nm, shape, dt in (
            ("ident", [128, 128], BF16), ("ones_b", [128, 128], BF16),
            ("ones_f", [128, 128], F32), ("selA", [128, 144], BF16),
            ("selB", [128, 32], BF16), ("masks", [128, 4 * QG], BF16),
            ("cbias", [128, 32], F32), ("nscr", [128, 8], F32),
            ("tiny", [128, 16], F32), ("tinyb", [128, 16], BF16),
            ("sumb", [128, 512], F32), ("lnscr", [128, 8], F32),
            ("u32s", [1, 1], U32), ("thr", [128, 4], F32),
            ("asum", [128, 66], F32)):
        sbuf(nm, shape, dt)
    for nm, n in (("b_bq", 2), ("b_bqc", 7), ("b_bqr", 1), ("b_bqr_rot", 1),
                  ("b_bkv", 2), ("b_bkr", 1), ("b_bkr_rot", 1), ("b_bkc", 7),
                  ("b_b1", 32), ("b_bv", 16)):
        sbuf(nm, [128, n], F32)

    ps = {}
    B.ps = ps
    ps["pS0"] = ctx.enter_context(nc.psum_tensor("pS0", [128, 1024], F32))
    ps["pS1"] = ctx.enter_context(nc.psum_tensor("pS1", [128, 1024], F32))
    ps["pAV"] = ctx.enter_context(nc.psum_tensor("pAV", [128, 512], F32))
    ps["pA"] = ctx.enter_context(nc.psum_tensor("pA", [128, 512], F32))
    ps["pB"] = ctx.enter_context(nc.psum_tensor("pB", [128, 512], F32))
    ps["pT"] = ctx.enter_context(nc.psum_tensor("pT", [128, 1024], BF16))

    _emit()

    with nc.Block() as block:
        @block.tensor
        def _(e):
            for fn in B.streams["t"]:
                fn(e)

        @block.scalar
        def _(e):
            for fn in B.streams["s"]:
                fn(e)

        @block.vector
        def _(e):
            for fn in B.streams["v"]:
                fn(e)

        @block.gpsimd
        def _(e):
            for fn in B.streams["g"]:
                fn(e)

        @block.sync
        def _(e):
            for fn in B.streams["y"]:
                fn(e)

    _CACHE["nc"] = nc
    return nc


def _load_l0_head_w(h, waits=()):
    slot = W_SLOT[h % 2]
    q = "gw0" if h % 2 == 0 else "gw1"
    n = 0
    for nm, ko in W_KIND.items():
        src = AP(B.d[nm].tensor, h * 128, [[1024, 128], [131072, 8], [1, 128]])
        dst = AP(B.WB, slot + ko, [[WB_COLS, 128], [128, 8], [1, 128]])
        n = dma(q, dst, src, waits=waits)
        waits = ()
    return n


def _load_w_mk(q, nm, ncols, nchunk, woff, waits=()):
    src = B.d[nm]
    dst = AP(B.WB, woff, [[WB_COLS, 128], [ncols, nchunk], [1, ncols]])
    sap = AP(src.tensor, 0, [[ncols, 128], [128 * ncols, nchunk], [1, ncols]])
    return dma(q, dst, sap, waits=waits)


def _emit():
    d, sb, AB, AFt = B.d, B.sb, B.AB, B.AF

    for nm in ("ident", "ones_b", "ones_f", "selA", "selB", "masks",
               "b_bq", "b_bqc", "b_bqr", "b_bqr_rot", "b_bkv", "b_bkr",
               "b_bkr_rot", "b_bkc", "b_b1", "b_bv"):
        dma("y", sb[nm][:], d[nm])
    dma("y", sb["u32s"][:], d["peer_off"])
    dma("y", AFt[:, TKA:TKA + T], d["tabk_a"])
    dma("y", AFt[:, TKB:TKB + T], d["tabk_b"])
    dma("y", AFt[:, TQA:TQA + TOWN], d["tabq_a"])
    dma("y", AFt[:, TQB:TQB + TOWN], d["tabq_b"])
    for mc in range(8):
        dma("g", AB[:, XT + mc * T:XT + (mc + 1) * T],
            d["xT_perm"][mc * 128:(mc + 1) * 128, :])
    n0 = _load_l0_head_w(0)
    n1 = _load_l0_head_w(1)

    nld, ngl = B.qsem["y"].n, B.qsem["g"].n
    for k in ENGS:
        em(k, lambda e, _h=B.qsem["y"].h, _n=nld: e.wait_ge(_h, _n))
        em(k, lambda e, _h=B.qsem["g"].h, _n=ngl: e.wait_ge(_h, _n))

    B._attn_prev_ev = None
    B._attn_ost_dma = None
    B._l0_wld = {0: n0, 1: n1}

    def cut(ph):
        if ATH_PH == ph:
            _dbg_out()
            return True
        return False

    if cut("none"):
        return
    nheads = L0H if ATH_PH not in ("h1", "h2") else int(ATH_PH[1])
    for h in range(nheads):
        _proj_head_l0(h)
        _bound_l0(h)
        _attn_head(0, h)
    if cut("l0a") or ATH_PH in ("h1", "h2") and (_dbg_out() or True):
        return
    _tail(0)
    if cut("tail0"):
        return
    _ffn0()
    if cut("ffn0"):
        return
    _exchange()
    if cut("xc"):
        return
    _l1_proj_common()
    if cut("l1p"):
        return
    for p in range(4):
        _l1_pass(p)
        if cut(f"pass{p}"):
            return
    _tail(1)
    if cut("tail1"):
        return
    _ffn1()
    if cut("ffn1"):
        return
    _final_ln_out()


def _dbg_out():
    barrier("dbg")
    for tt in range(16):
        dma("y", B.d["y"][tt * 128:(tt + 1) * 128, :],
            B.AF[:, STRM + tt * 1024: STRM + (tt + 1) * 1024])
    em("y", lambda e, _h=B.qsem["y"].h, _n=B.qsem["y"].n: e.wait_ge(_h, _n))
    em("g", lambda e, _h=B.qsem["g"].h, _n=B.qsem["g"].n: e.wait_ge(_h, _n))



def _rope_unit(kind, idx, h, unit, rope_done, extra_w=()):
    AB, WB, ps = B.AB, B.WB, B.ps
    S = B.S
    w = 512 if kind == "k" else 256
    slot = W_SLOT[h % 2]
    if kind == "k":
        rhs_off = idx * 512
        wofA, wofB = slot + W_KIND["l0_wk"], slot + W_KIND["l0_wk_rot"]
        dst = KTA + idx * 512
        tba, tbb = TKA + idx * 512, TKB + idx * 512
    else:
        rhs_off = (2 * idx) * BLK
        wofA, wofB = slot + W_KIND["l0_wq"], slot + W_KIND["l0_wq_rot"]
        dst = QTA + idx * 256
        tba, tbb = TQA + idx * 256, TQB + idx * 256
    if unit % 2 == 0:
        pA, pB = ps["pA"][:, 0:w], ps["pB"][:, 0:w]
    else:
        pA, pB = ps["pS0"][:, 0:w], ps["pS0"][:, 512:512 + w]
    waits = tuple(extra_w)
    if unit - 2 in rope_done and rope_done[unit - 2] is not None:
        waits += ((S["v"], rope_done[unit - 2]),)
    for mc in range(8):
        t_mm(pA, WB[:, wofA + mc * 128: wofA + mc * 128 + 128],
             AB[:, XT + mc * T + rhs_off: XT + mc * T + rhs_off + w],
             start=(mc == 0), stop=(mc == 7), waits=(waits if mc == 0 else ()))
    for mc in range(8):
        t_mm(pB, WB[:, wofB + mc * 128: wofB + mc * 128 + 128],
             AB[:, XT + mc * T + rhs_off: XT + mc * T + rhs_off + w],
             start=(mc == 0), stop=(mc == 7), inc=(mc == 7))
    tN = S["t"].n

    def f(e, _pA=pA, _pB=pB, _dst=dst, _ta=tba, _tb=tbb, _w=w, _u=unit):
        AFl = B.AF
        fsb = FS + (_u % 2) * 1024
        t1 = AFl[:, fsb:fsb + _w]
        t2 = AFl[:, fsb + 512:fsb + 512 + _w]
        e.tensor_tensor(t1, _pA, AFl[:, _ta:_ta + _w], ALU.mult)
        vc(e, e.tensor_tensor(t2, _pB, AFl[:, _tb:_tb + _w], ALU.mult))
        return e.tensor_tensor(B.AB[:, _dst:_dst + _w], t1, t2, ALU.add)
    rope_done[unit] = e_op("v", f, waits=((S["t"], tN),), inc=True)


def _proj_head_l0(h):
    AB, WB, ps = B.AB, B.WB, B.ps
    S = B.S
    wldw = ((B.qsem["gw0" if h % 2 == 0 else "gw1"], B._l0_wld[h]),) \
        if hasattr(B, "_l0_wld") and h in B._l0_wld else ()
    wldw += ((S["s"], S["s"].n), (S["v"], S["v"].n))
    vpad = e_op("v", lambda e: e.memset(
        AP(B.AB, VA + 64, [[AB_COLS, 128], [VW0, 32], [1, 1]]), 1.0), inc=True)
    rope_done = {}
    for tt in range(8):
        if tt == 0:
            rope_done[-2] = None
        _rope_unit("k", tt, h, tt, rope_done, extra_w=(wldw if tt == 0 else ()))
        wldw = ()
    for jb in range(8):
        _rope_unit("q", jb, h, 8 + jb, rope_done)
    B.l0_rope_last = rope_done[15]

    vev = {}
    for vb in range(8):
        half = vb % 2
        waits = ((S["s"], vev[vb - 2]),) if vb >= 2 else ()
        for i in range(4):
            tl = vb * 4 + i
            for mc in range(8):
                t_mm(ps["pS1"][:, half * 512 + i * 128: half * 512 + i * 128 + 128],
                     AB[:, XT + mc * T + tl * 128: XT + mc * T + tl * 128 + 128],
                     WB[:, W_SLOT[h % 2] + W_KIND["l0_wv"] + mc * 128: W_SLOT[h % 2] + W_KIND["l0_wv"] + mc * 128 + 128],
                     start=(mc == 0), stop=(mc == 7),
                     waits=(waits if (i == 0 and mc == 0) else ()),
                     inc=(i == 3 and mc == 7))
        tN = S["t"].n

        def f_ev(e, _vb=vb, _half=half):
            src1 = AP(B.ps["pS1"], _half * 512, [[1024, 128], [128, 4], [1, 64]])
            dst1 = AP(B.AB, VA + (_vb * 4) * VW0, [[AB_COLS, 128], [VW0, 4], [1, 64]])
            e.activation(dst1, src1, AF.Copy)
            src2 = AP(B.ps["pS1"], _half * 512 + 64, [[1024, 128], [128, 4], [1, 64]])
            dst2 = AP(B.AB, VA + (_vb * 4) * VW0 + 65, [[AB_COLS, 128], [VW0, 4], [1, 64]])
            return e.activation(dst2, src2, AF.Copy)
        vev[vb] = e_op("s", f_ev, waits=((S["t"], tN), (S["v"], vpad)), inc=True)
    B.l0_vev_last = vev[7]
    B._l0_projt = S["t"].n
    if h + 2 < L0H:
        if not hasattr(B, "_l0_wld"):
            B._l0_wld = {}
        B._l0_wld[h + 2] = _load_l0_head_w(h + 2, waits=((S["t"], S["t"].n),))



def _bound_l0(h):
    AB, ps, sb = B.AB, B.ps, B.sb
    S = B.S
    units = [("k", tc) for tc in range(8)] + [("q", tc) for tc in range(4)]
    mmN = {}
    for u, (kind, tc) in enumerate(units):
        base = (KTA if kind == "k" else QTA) + tc * 512
        w = ((S["v"], B.l0_rope_last),) if u == 0 else ()
        if u - 2 in mmN:
            w += ((S["t"], mmN[u - 2]),)
        eb = EA + (u % 2) * 1024

        def f_sq(e, _b=base, _e=eb):
            return e.tensor_tensor(B.AB[:, _e:_e + 512], B.AB[:, _b:_b + 512],
                                   B.AB[:, _b:_b + 512], ALU.mult)
        vN = e_op("v", f_sq, waits=w, inc=True)
        row = tc if kind == "k" else 8 + tc
        mmN[u] = t_mm(ps["pA"][0:12, 0:512], sb["selA"][:, row * 12: row * 12 + 12],
                      AB[:, eb:eb + 512], start=(u == 0), stop=(u == len(units) - 1),
                      waits=((S["v"], vN),), inc=True)
    tN = S["t"].n
    def f_mxc(e):
        vc(e, e.reduce_max(B.sb["nscr"][0:12, 0:1], B.ps["pA"][0:12, 0:512], AX.X))
        return e.tensor_copy(B.sb["tinyb"][0:12, 0:1], B.sb["nscr"][0:12, 0:1])
    vN = e_op("v", f_mxc, waits=((S["t"], tN),), inc=True)
    tN = e_op("t", lambda e: e.transpose(B.ps["pT"][0:1, 0:12],
                                         B.sb["tinyb"][0:12, 0:1],
                                         B.sb["ident"][0:12, 0:12]),
              waits=((S["v"], vN),), inc=True)

    def f_mx(e):
        e.reduce_max(B.sb["tiny"][0:1, 0:1], B.ps["pT"][0:1, 0:8], AX.X)
        vc(e, e.reduce_max(B.sb["tiny"][0:1, 1:2], B.ps["pT"][0:1, 8:12], AX.X))
        return e.tensor_tensor(B.sb["tiny"][0:1, 2:3], B.sb["tiny"][0:1, 0:1],
                               B.sb["tiny"][0:1, 1:2], ALU.mult)
    vN = e_op("v", f_mx, waits=((S["t"], tN),), inc=True)
    sN = e_op("s", lambda e: e.activation(B.sb["tiny"][0:1, 3:4],
                                          B.sb["tiny"][0:1, 2:3], AF.Sqrt),
              waits=((S["v"], vN),), inc=True)
    tN = t_mm(ps["pB"][:, 0:1], sb["ones_f"][0:1, :], sb["tiny"][0:1, 3:4],
              start=True, stop=True, waits=((S["s"], sN),), inc=True)
    e_op("v", lambda e, _h=h: e.tensor_scalar(B.sb["cbias"][:, _h:_h + 1],
                                              B.ps["pB"][:, 0:1], -1.0 / 128.0,
                                              None, ALU.mult),
         waits=((S["t"], tN),), inc=True)


def _bound_l1(p):
    AB, ps, sb = B.AB, B.ps, B.sb
    S = B.S
    units = ([("k", cc, tc) for cc in range(2) for tc in range(8)] +
             [("q", cc, tc) for cc in range(2) for tc in range(4)])
    first = True
    mmN = {}
    for nu, (kind, cc, tc) in enumerate(units):
        c = 2 * p + cc
        if kind == "k":
            base = (L1_KRG + tc * 512) if c == 7 else (L1_KT + cc * T + tc * 512)
            arr = B.AB
        else:
            base = (W_QRG + tc * 512) if c == 7 else (L1_QT + cc * TOWN + tc * 512)
            arr = B.WB if c == 7 else B.AB
        w = ((S["s"], S["s"].n),) if first else ()
        if nu - 2 in mmN:
            w += ((S["t"], mmN[nu - 2]),)
        eb = L1_E + (nu % 2) * 1024

        def f_sq(e, _a=arr, _b=base, _e=eb):
            return e.tensor_tensor(B.AB[:, _e:_e + 512], _a[:, _b:_b + 512],
                                   _a[:, _b:_b + 512], ALU.mult)
        vN = e_op("v", f_sq, waits=w, inc=True)
        v = (0 if kind == "k" else 2) + cc
        mmN[nu] = t_mm(ps["pA"][0:8, 0:512], sb["selB"][:, v * 8: v * 8 + 8],
                       AB[:, eb:eb + 512], start=first, stop=(nu == len(units) - 1),
                       waits=((S["v"], vN),), inc=True)
        first = False
    tN = S["t"].n
    def f_mxc(e):
        vc(e, e.reduce_max(B.sb["nscr"][0:8, 0:1], B.ps["pA"][0:8, 0:512], AX.X))
        return e.tensor_copy(B.sb["tinyb"][0:8, 0:1], B.sb["nscr"][0:8, 0:1])
    vN = e_op("v", f_mxc, waits=((S["t"], tN),), inc=True)
    tN = e_op("t", lambda e: e.transpose(B.ps["pT"][0:1, 0:8],
                                         B.sb["tinyb"][0:8, 0:1],
                                         B.sb["ident"][0:8, 0:8]),
              waits=((S["v"], vN),), inc=True)

    def f_pm(e):
        vc(e, e.tensor_copy(B.sb["tiny"][0:1, 8:16], B.ps["pT"][0:1, 0:8]))
        return e.tensor_tensor(B.sb["tiny"][0:1, 0:4], B.sb["tiny"][0:1, 8:12],
                               B.sb["tiny"][0:1, 12:16], ALU.mult)
    vN = e_op("v", f_pm, waits=((S["t"], tN),), inc=True)
    sN = e_op("s", lambda e: e.activation(B.sb["tiny"][0:1, 4:8],
                                          B.sb["tiny"][0:1, 0:4], AF.Sqrt),
              waits=((S["v"], vN),), inc=True)
    tN = t_mm(ps["pB"][:, 0:4], sb["ones_f"][0:1, :], sb["tiny"][0:1, 4:8],
              start=True, stop=True, waits=((S["s"], sN),), inc=True)
    e_op("v", lambda e: e.tensor_scalar(B.sb["cbias"][:, 16:20], B.ps["pB"][:, 0:4],
                                        -1.0 / 8.0, None, ALU.mult),
         waits=((S["t"], tN),), inc=True)



def _attn_head(layer, h, l1ctx=None):
    AB, ps, sb = B.AB, B.ps, B.sb
    S = B.S
    scale = (1.0 / 128.0) if layer == 0 else (1.0 / 8.0)
    col = h if layer == 0 else 16 + (h % 4)
    ebase = EA if layer == 0 else L1_E
    ebuf = [ebase, ebase + 1024]
    ost = OST0 if layer == 0 else L1_OST
    flat = [(j, b) for j in range(NQG) for b in range(j + 1)]
    exp_done = {}
    av_done = {}
    mask_done = {}
    prev_ev = B._attn_prev_ev
    ost_dma = B._attn_ost_dma
    head_w = ((S["s"], S["s"].n), (S["v"], S["v"].n))

    state = {"prev_ev": prev_ev, "ost_dma": ost_dma}
    defq = []

    def flush():
        for f in defq:
            f()
        del defq[:]

    def emit_av(fi, j, b, eb, gate):
        nb = j + 1
        bl = 4
        avcol = (j % 2) * 256
        if layer == 0:
            for half in range(2):
                for i in range(bl):
                    first = (b == 0 and i == 0)
                    last = (b == nb - 1 and i == bl - 1)
                    kt = 4 * b + i
                    co = kt * VW0 + (65 if half else 0)
                    vt = AB[:, VA + co: VA + co + (64 if half else 65)]
                    pp = (ps["pB"][0:64, avcol:avcol + 256] if half else
                          ps["pAV"][0:65, avcol:avcol + 256])
                    w = (gate,) if (i == 0 and half == 0) else ()
                    if first and half == 0 and state["prev_ev"] is not None:
                        w += (state["prev_ev"],)
                    t_mm(pp, vt, AB[:, eb + i * 256: eb + (i + 1) * 256],
                         start=first, stop=last, waits=w,
                         inc=(i == bl - 1 and half == 1))
        else:
            for i in range(bl):
                first = (b == 0 and i == 0)
                last = (b == nb - 1 and i == bl - 1)
                kt = 4 * b + i
                vlo = l1ctx["V_sl"](h, kt)
                w = (gate,) if i == 0 else ()
                if first and state["prev_ev"] is not None:
                    w += (state["prev_ev"],)
                t_mm(ps["pAV"][0:65, avcol:avcol + 256], vlo,
                     AB[:, eb + i * 256: eb + (i + 1) * 256],
                     start=first, stop=last, waits=w, inc=(i == bl - 1))
        av_done[fi] = S["t"].n
        if b != nb - 1:
            return
        tN_av = S["t"].n
        scol = 0

        def f_ln(e, _sc=scol, _ac=avcol):
            vc(e, e.activation(B.sb["sumb"][64:65, _sc:_sc + 256],
                               B.ps["pAV"][64:65, _ac:_ac + 256], AF.Ln), "s")
            return e.activation(B.sb["sumb"][64:65, _sc + 256:_sc + 512],
                                B.sb["sumb"][64:65, _sc:_sc + 256], AF.Exp,
                                scale=-1.0)
        sN = e_op("s", f_ln, waits=((S["t"], tN_av),), inc=True)
        tN = t_mm(ps["pA"][:, 0:256], sb["ones_f"][64:65, :],
                  sb["sumb"][64:65, scol + 256:scol + 512],
                  start=True, stop=True, waits=((S["s"], sN),), inc=True)
        w_ost = [(S["t"], tN)]
        if state["ost_dma"] is not None:
            w_ost.append((B.qsem["go"], state["ost_dma"]))

        def f_on(e, _ac=avcol, _h=h, _lay=layer, _o=ost):
            rb = B.sb["sumb"][0:64, 0:256]
            vc(e, e.tensor_copy(rb, B.ps["pA"][0:64, 0:256]))
            if _lay == 0:
                e.tensor_tensor(B.AB[0:64, _o:_o + 256],
                                B.ps["pAV"][0:64, _ac:_ac + 256], rb, ALU.mult)
                return e.tensor_tensor(B.AB[0:64, _o + 256:_o + 512],
                                       B.ps["pB"][0:64, _ac:_ac + 256], rb, ALU.mult)
            vc(e, e.tensor_tensor(B.AB[0:64, _o:_o + 256],
                                  B.ps["pAV"][0:64, _ac:_ac + 256], rb, ALU.mult))
            return e.tensor_scalar(B.AB[0:64, _o:_o + 256],
                                   B.AB[0:64, _o:_o + 256],
                                   B.sb["b_bv"][0:64, _h:_h + 1], None, ALU.add)
        vN = e_op("v", f_on, waits=tuple(w_ost), inc=True)
        state["prev_ev"] = (S["v"], vN)
        if layer == 0:
            r0 = h * 128
            dma("go", B.d["spill_o"][r0:r0 + 64, j * 256:(j + 1) * 256],
                AB[0:64, ost:ost + 256], waits=((S["v"], vN),))
            state["ost_dma"] = dma("go",
                                   B.d["spill_o"][r0 + 64:r0 + 128, j * 256:(j + 1) * 256],
                                   AB[0:64, ost + 256:ost + 512])
        else:
            r0 = h * 64
            state["ost_dma"] = dma("go",
                                   B.d["spill_o"][r0:r0 + 64, j * 256:(j + 1) * 256],
                                   AB[0:64, ost:ost + 256], waits=((S["v"], vN),))

    for fi, (j, b) in enumerate(flat):
        nb = j + 1
        pS = ps["pS0"] if fi % 2 == 0 else ps["pS1"]
        bl = 4
        for i in range(bl):
            kt = 4 * b + i
            if layer == 0:
                lhsT = AB[:, KTA + kt * 128: KTA + kt * 128 + 128]
                rhs = AB[:, QTA + j * 256: QTA + j * 256 + 256]
            else:
                lhsT = l1ctx["KT_sl"](h, kt)
                rhs = l1ctx["QT_sl"](h, j)
            w = ()
            if i == 0:
                if fi - 2 in exp_done:
                    w += ((S["s"], exp_done[fi - 2]),)
                    if fi - 2 in mask_done:
                        w += ((S["v"], mask_done[fi - 2]),)
                if fi == 0:
                    w += head_w
            t_mm(pS[:, i * 256:(i + 1) * 256], lhsT, rhs,
                 start=True, stop=True, waits=w, inc=(i == bl - 1))
        tN = S["t"].n
        flush()
        eb = ebuf[fi % 2]
        w_exp = [(S["t"], tN)]
        if fi - 2 in av_done:
            w_exp.append((S["t"], av_done[fi - 2]))

        def f_exp(e, _pS=pS, _eb=eb, _c=col, _sc=scale):
            return e.activation(B.AB[:, _eb:_eb + 1024], _pS[:, 0:1024],
                                AF.Exp, bias=B.sb["cbias"][:, _c:_c + 1], scale=_sc)
        sN = e_op("s", f_exp, waits=tuple(w_exp), inc=True)
        exp_done[fi] = sN
        if b == nb - 1:
            def f_mask(e, _eb=eb):
                return e.tensor_tensor(B.AB[:, _eb:_eb + 1024],
                                       B.AB[:, _eb:_eb + 1024],
                                       B.sb["masks"][:, 0:1024], ALU.mult)
            vN = e_op("v", f_mask, waits=((S["s"], sN),), inc=True)
            mask_done[fi] = vN
            gate = (S["v"], vN)
        else:
            gate = (S["s"], sN)
        defq.append(lambda _fi=fi, _j=j, _b=b, _eb=eb, _g=gate:
                    emit_av(_fi, _j, _b, _eb, _g))
    flush()
    if layer == 0:
        if not hasattr(B, "_l0_go_n"):
            B._l0_go_n = {}
        B._l0_go_n[h] = B.qsem["go"].n
    B._attn_prev_ev = state["prev_ev"]
    B._attn_ost_dma = state["ost_dma"]


def _attn_head_pair(h0, l1ctx):
    AB, ps, sb = B.AB, B.ps, B.sb
    S = B.S
    scale = 1.0 / 8.0
    flat = [(j, b) for j in range(NQG) for b in range(j + 1)]
    exp_done = {}
    av_done = {}
    mask_done = {}
    state = B._l1_pair_state
    head_w = ((S["s"], S["s"].n), (S["v"], S["v"].n))
    pSs = (ps["pS0"], ps["pS1"])
    pAVs = (ps["pAV"], ps["pB"])
    defq = []

    def flush():
        for f in defq:
            f()
        del defq[:]

    def emit_av(fi, j, b, hp, gate):
        h = h0 + hp
        nb = j + 1
        avcol = (j % 2) * 256
        eb = L1_E + ((fi % 2) * 2 + hp) * 1024
        pp = pAVs[hp]
        for i in range(4):
            first = (b == 0 and i == 0)
            last = (b == nb - 1 and i == 3)
            kt = 4 * b + i
            vlo = l1ctx["V_sl"](h, kt)
            w = (gate,) if i == 0 else ()
            if first and state["prev_ev"][hp] is not None:
                w += (state["prev_ev"][hp],)
            t_mm(pp[0:65, avcol:avcol + 256], vlo,
                 AB[:, eb + i * 256: eb + (i + 1) * 256],
                 start=first, stop=last, waits=w, inc=(i == 3))
        av_done[(fi, hp)] = S["t"].n
        if b != nb - 1:
            return
        tN_av = S["t"].n
        if hp == 0:
            ln_t = sb["sumb"][64:65, 0:256]
            rc_t = sb["sumb"][64:65, 256:512]
            rb_t = sb["sumb"][0:64, 0:256]
        else:
            ln_t = B.AF[64:65, FS:FS + 256]
            rc_t = B.AF[64:65, FS + 256:FS + 512]
            rb_t = B.AF[0:64, FS + 512:FS + 768]

        def f_ln(e, _ac=avcol, _pp=pp, _ln=ln_t, _rc=rc_t):
            vc(e, e.activation(_ln, _pp[64:65, _ac:_ac + 256], AF.Ln), "s")
            return e.activation(_rc, _ln, AF.Exp, scale=-1.0)
        sN = e_op("s", f_ln, waits=((S["t"], tN_av),), inc=True)
        wb = [(S["s"], sN)]
        if state["fon"] is not None:
            wb.append((S["v"], state["fon"]))
        tN = t_mm(ps["pA"][:, hp * 256: hp * 256 + 256],
                  sb["ones_f"][64:65, :], rc_t,
                  start=True, stop=True, waits=tuple(wb), inc=True)
        qn = "go" if hp == 0 else "gp0"
        w_ost = [(S["t"], tN)]
        if state["ost_dma"][hp] is not None:
            w_ost.append((B.qsem[qn], state["ost_dma"][hp]))
        ost = L1_OST + hp * 512

        def f_on(e, _ac=avcol, _h=h, _o=ost, _hp=hp, _pp=pp, _rb=rb_t):
            vc(e, e.tensor_copy(_rb, B.ps["pA"][0:64, _hp * 256:_hp * 256 + 256]))
            vc(e, e.tensor_tensor(B.AB[0:64, _o:_o + 256],
                                  _pp[0:64, _ac:_ac + 256], _rb, ALU.mult))
            return e.tensor_scalar(B.AB[0:64, _o:_o + 256],
                                   B.AB[0:64, _o:_o + 256],
                                   B.sb["b_bv"][0:64, _h:_h + 1], None, ALU.add)
        vN = e_op("v", f_on, waits=tuple(w_ost), inc=True)
        state["prev_ev"][hp] = (S["v"], vN)
        state["fon"] = vN
        r0 = h * 64
        state["ost_dma"][hp] = dma(
            qn, B.d["spill_o"][r0:r0 + 64, j * 256:(j + 1) * 256],
            AB[0:64, ost:ost + 256], waits=((S["v"], vN),))

    for fi, (j, b) in enumerate(flat):
        nb = j + 1
        newq = []
        for hp in range(2):
            h = h0 + hp
            pS = pSs[hp]
            for i in range(4):
                kt = 4 * b + i
                lhsT = l1ctx["KT_sl"](h, kt)
                rhs = l1ctx["QT_sl"](h, j)
                w = ()
                if i == 0:
                    if (fi - 1, hp) in exp_done:
                        w += ((S["s"], exp_done[(fi - 1, hp)]),)
                        if (fi - 1, hp) in mask_done:
                            w += ((S["v"], mask_done[(fi - 1, hp)]),)
                    if fi == 0:
                        w += head_w
                t_mm(pS[:, i * 256:(i + 1) * 256], lhsT, rhs,
                     start=True, stop=True, waits=w, inc=(i == 3))
            tNs = S["t"].n
            eb = L1_E + ((fi % 2) * 2 + hp) * 1024
            col = 16 + (h % 4)
            w_exp = [(S["t"], tNs)]
            if (fi - 2, hp) in av_done:
                w_exp.append((S["t"], av_done[(fi - 2, hp)]))

            def f_exp(e, _pS=pS, _eb=eb, _c=col):
                return e.activation(B.AB[:, _eb:_eb + 1024], _pS[:, 0:1024],
                                    AF.Exp, bias=B.sb["cbias"][:, _c:_c + 1],
                                    scale=scale)
            sN = e_op("s", f_exp, waits=tuple(w_exp), inc=True)
            exp_done[(fi, hp)] = sN
            if b == nb - 1:
                def f_mask(e, _eb=eb):
                    return e.tensor_tensor(B.AB[:, _eb:_eb + 1024],
                                           B.AB[:, _eb:_eb + 1024],
                                           B.sb["masks"][:, 0:1024], ALU.mult)
                vN = e_op("v", f_mask, waits=((S["s"], sN),), inc=True)
                mask_done[(fi, hp)] = vN
                gate = (S["v"], vN)
            else:
                gate = (S["s"], sN)
            newq.append(lambda _fi=fi, _j=j, _b=b, _hp=hp, _g=gate:
                        emit_av(_fi, _j, _b, _hp, _g))
        flush()
        defq.extend(newq)
    flush()



def _tail(layer):
    AB, WB, ps, sb, AFl = B.AB, B.WB, B.ps, B.sb, B.AF
    d = B.d
    S = B.S
    if layer == 0:
        for tt in range(16):
            jj = tt // 2
            src = d["x_perm"][(2 * jj) * BLK + (tt % 2) * 128:
                              (2 * jj) * BLK + (tt % 2) * 128 + 128, :]
            w = ((S["v"], B.l0_rope_last),) if tt == 0 else ()
            dma("y" if tt % 2 == 0 else "ys",
                AFl[:, STRM + tt * 1024: STRM + (tt + 1) * 1024], src, waits=w)
        wT = ((S["t"], B._l0_projt),)
        for oc in range(8):
            dma("y" if oc % 2 == 0 else "ys",
                AB[:, ORB + oc * 2048: ORB + (oc + 1) * 2048],
                d["spill_o"][oc * 128:(oc + 1) * 128, :],
                waits=wT + ((B.qsem["go"], B._l0_go_n[oc]),))
            wT = ()
    else:
        em("y", lambda e, _h=B.qsem["yx"].h, _n=B.qsem["yx"].n: e.wait_ge(_h, _n))
        for tt in range(16):
            src = d["spill_x"][tt * 128:(tt + 1) * 128, :]
            w = ((S["v"], B._l1p_v_end),) if tt == 0 else ()
            dma("y" if tt % 2 == 0 else "ys",
                AFl[:, STRM + tt * 1024: STRM + (tt + 1) * 1024], src, waits=w)
    barrier(f"tl{layer}")
    B._attn_prev_ev = None
    B._attn_ost_dma = None

    wof = W_WO
    if layer == 0:
        _load_w_mk("g", "l0_wo", 1024, 8, wof)
    if layer == 1:
        em("y", lambda e, _h=B.qsem["go"].h, _n=B.qsem["go"].n: e.wait_ge(_h, _n))
        em("y", lambda e, _h=B.qsem["gp0"].h, _n=B.qsem["gp0"].n: e.wait_ge(_h, _n))
        for oc in range(8):
            dma("y" if oc % 2 == 0 else "ys",
                AB[:, ORB + oc * 2048: ORB + (oc + 1) * 2048],
                d["spill_o"][oc * 128:(oc + 1) * 128, :])
    for k in ("t", "s", "v"):
        for qn in ("y", "ys", "g", "gw0", "gw1"):
            em(k, lambda e, _h=B.qsem[qn].h, _n=B.qsem[qn].n: e.wait_ge(_h, _n))

    B._tail_v = {}
    if ATH_TC == "dumpo":
        em("y", lambda e, _h=B.qsem["g"].h, _n=B.qsem["g"].n: e.wait_ge(_h, _n))
        for i in range(8):
            ld = dma("y", AB[:, ABM_TMP: ABM_TMP + 2048] if False else AB[:, 0:2048],
                     d["spill_o"][i * 128:(i + 1) * 128, :],
                     waits=())
            def f_cp(e, _i=i):
                return e.tensor_copy(B.AF[:, STRM + _i * 2048: STRM + (_i + 1) * 2048],
                                     B.AB[:, 0:2048])
            vN = e_op("v", f_cp, waits=((B.qsem["y"], ld),), inc=True)
            em("y", lambda e, _h=S["v"].h, _n=vN: e.wait_ge(_h, _n))
        for tt in range(16):
            dma("y", B.d["y"][tt * 128:(tt + 1) * 128, :],
                B.AF[:, STRM + tt * 1024: STRM + (tt + 1) * 1024])
        em("y", lambda e, _h=B.qsem["y"].h, _n=B.qsem["y"].n: e.wait_ge(_h, _n))
        barrier(f"tle{layer}")
        return
    if ATH_TC in ("w_t", "w_s", "w_v"):
        sem = {"w_t": S["t"], "w_s": S["s"], "w_v": S["v"]}[ATH_TC]
        em("y", lambda e, _h=sem.h, _n=sem.n: e.wait_ge(_h, _n))
        barrier(f"tle{layer}")
        return
    if ATH_TC == "ld":
        barrier(f"tle{layer}")
        return
    ntt = 16
    if ATH_TC.startswith("mm") and ATH_TC[2:].isdigit():
        ntt = int(ATH_TC[2:])
    if ATH_TC == "cponly":
        tN = e_op("t", lambda e: e.nop(), inc=True)
        e_op("v", lambda e: e.tensor_copy(B.AF[:, FS:FS + 512], ps["pS0"][:, 0:512]),
             waits=((S["t"], tN),), inc=True)
        barrier(f"tle{layer}")
        return
    B._tail_defer = []
    for tt in range(ntt):
        pL = ps["pS0"][:, 0:512] if tt % 2 == 0 else ps["pS1"][:, 0:512]
        pR = ps["pS0"][:, 512:1024] if tt % 2 == 0 else ps["pS1"][:, 512:1024]
        w = ((S["v"], B._tail_v[tt - 2]),) if tt >= 2 else ()
        for half, pp in ((0, pL), (1, pR)):
            for oc in range(8):
                t_mm(pp, AB[:, ORB + oc * 2048 + tt * 128: ORB + oc * 2048 + tt * 128 + 128],
                     WB[:, wof + oc * 1024 + half * 512: wof + oc * 1024 + half * 512 + 512],
                     start=(oc == 0), stop=(oc == 7 and layer == 0),
                     waits=(w if (half == 0 and oc == 0) else ()),
                     inc=(oc == 7 and layer == 0))
            if layer == 1:
                t_mm(pp, sb["ones_b"][0:1, :],
                     WB[0:1, W_BO + half * 512: W_BO + half * 512 + 512],
                     start=False, stop=True, inc=True)
        tN = B.S["t"].n
        for f in B._tail_defer:
            f()
        del B._tail_defer[:]
        if ATH_TC.startswith("mm"):
            if ATH_TC == "mmnc":
                B._tail_v[tt] = e_op("v", lambda e: e.nop(), waits=((S["t"], tN),), inc=True)
            elif ATH_TC == "mmd":
                def f_d(e, _pL=pL, _pR=pR, _tt=tt):
                    stc = STRM + _tt * 1024
                    e.tensor_copy(B.AF[:, stc:stc + 512], _pL)
                    return e.tensor_copy(B.AF[:, stc + 512:stc + 1024], _pR)
                B._tail_v[tt] = e_op("v", f_d, waits=((S["t"], tN),), inc=True)
            else:
                B._tail_v[tt] = e_op("v", lambda e, _pL=pL: e.tensor_copy(
                    B.AF[:, FS:FS + 512], _pL), waits=((S["t"], tN),), inc=True)
        else:
            _ln_tile(tt, tN, pL, pR, mode="attn")
    for f in B._tail_defer:
        f()
    del B._tail_defer[:]
    barrier(f"tle{layer}")


def _ln_tile(tt, tN, pL, pR, mode):
    AB, ps, sb, AFl = B.AB, B.ps, B.sb, B.AF
    S = B.S
    st = STRM + tt * 1024

    if ATH_TC == "lnr":
        def f_r(e, _st=st, _pL=pL, _pR=pR):
            e.tensor_tensor(AFl[:, _st:_st + 512], AFl[:, _st:_st + 512], _pL, ALU.add)
            return e.tensor_tensor(AFl[:, _st + 512:_st + 1024],
                                   AFl[:, _st + 512:_st + 1024], _pR, ALU.add)
        B._tail_v[tt] = e_op("v", f_r, waits=(((S["t"], tN),) if tN else ()), inc=True)
        return
    if ATH_TC == "lnr2":
        def f_r(e, _st=st, _pL=pL, _pR=pR):
            e.tensor_tensor(AFl[:, _st:_st + 512], _pL, AFl[:, _st:_st + 512], ALU.add)
            return e.tensor_tensor(AFl[:, _st + 512:_st + 1024], _pR,
                                   AFl[:, _st + 512:_st + 1024], ALU.add)
        B._tail_v[tt] = e_op("v", f_r, waits=(((S["t"], tN),) if tN else ()), inc=True)
        return
    if ATH_TC == "lnr4":
        if tt % 2 == 0:
            def f_r(e, _st=st):
                return e.tensor_scalar(AFl[:, _st:_st + 1024], AFl[:, _st:_st + 1024],
                                       1.0, None, ALU.mult)
        else:
            def f_r(e, _st=st, _pL=pL):
                e.tensor_copy(AFl[:, FS:FS + 512], _pL)
                return e.tensor_tensor(AFl[:, _st:_st + 512], AFl[:, _st:_st + 512],
                                       AFl[:, FS:FS + 512], ALU.add)
        B._tail_v[tt] = e_op("v", f_r, waits=(((S["t"], tN),) if tN else ()), inc=True)
        return
    if ATH_TC == "lnr3":
        def f_r(e, _st=st, _pL=pL, _pR=pR):
            e.scalar_tensor_tensor(AFl[:, _st:_st + 512], _pL, 1.0,
                                   AFl[:, _st:_st + 512], ALU.mult, ALU.add)
            return e.scalar_tensor_tensor(AFl[:, _st + 512:_st + 1024], _pR, 1.0,
                                          AFl[:, _st + 512:_st + 1024], ALU.mult, ALU.add)
        B._tail_v[tt] = e_op("v", f_r, waits=(((S["t"], tN),) if tN else ()), inc=True)
        return

    def f_res(e, _st=st, _pL=pL, _pR=pR, _m=mode):
        if _m == "attn":
            e.tensor_tensor(AFl[:, _st:_st + 512], AFl[:, _st:_st + 512], _pL, ALU.add)
            vc(e, e.tensor_tensor(AFl[:, _st + 512:_st + 1024],
                                  AFl[:, _st + 512:_st + 1024], _pR, ALU.add))
        vc(e, e.reduce_sum(sb["lnscr"][:, 0:1], AFl[:, _st:_st + 1024], AX.X))
        vc(e, e.tensor_scalar(sb["lnscr"][:, 1:2], sb["lnscr"][:, 0:1], 1.0 / DM,
                              None, ALU.mult))
        return e.tensor_scalar(AFl[:, _st:_st + 1024], AFl[:, _st:_st + 1024],
                               sb["lnscr"][:, 1:2], None, ALU.subtract)
    vN = e_op("v", f_res, waits=(((S["t"], tN),) if tN else ()), inc=True)
    sN = e_op("s", lambda e, _st=st: e.activation(AFl[:, FS:FS + 1024],
                                                  AFl[:, _st:_st + 1024], AF.Square,
                                                  accum_out=sb["lnscr"][:, 2:3]),
              waits=((S["v"], vN),), inc=True)

    def f_rstd(e):
        vc(e, e.tensor_scalar(sb["lnscr"][:, 3:4], sb["lnscr"][:, 2:3], 1.0 / DM,
                              EPS, ALU.mult, ALU.add))
        return e.reciprocal(sb["lnscr"][:, 4:5], sb["lnscr"][:, 3:4])
    vN = e_op("v", f_rstd, waits=((S["s"], sN),), inc=True)
    sN = e_op("s", lambda e: e.activation(sb["lnscr"][:, 5:6], sb["lnscr"][:, 4:5],
                                          AF.Sqrt),
              waits=((S["v"], vN),), inc=True)
    if ATH_TC == "lnp":
        def f_p(e, _st=st):
            return e.tensor_copy(AFl[:, _st:_st + 8], sb["lnscr"][:, 0:8])
        B._tail_v[tt] = e_op("v", f_p, waits=((S["s"], sN),), inc=True)
        return
    vN = e_op("v", lambda e, _st=st: e.tensor_scalar(AFl[:, _st:_st + 1024],
                                                     AFl[:, _st:_st + 1024],
                                                     sb["lnscr"][:, 5:6], None,
                                                     ALU.mult),
              waits=((S["s"], sN),), inc=True)
    B._tail_v[tt] = vN

    if mode == "ffn1":
        dma("y", B.d["y"][tt * 128:(tt + 1) * 128, :], AFl[:, st:st + 1024],
            waits=((S["v"], vN),))
        return
    if mode == "ffn0":
        dma("yx", B.d["spill_x"][tt * 128:(tt + 1) * 128, :], AFl[:, st:st + 1024],
            waits=((S["v"], vN),))
        dma("g", B.d["cc_in"][tt * 128:(tt + 1) * 128, :], AFl[:, st:st + 1024],
            waits=((S["v"], vN),))
        return
    if ATH_TC == "ln":
        return
    stg = EA + (tt % 2) * 1024
    sN = e_op("s", lambda e, _st=st, _sg=stg: e.activation(
        AB[:, _sg:_sg + 1024], AFl[:, _st:_st + 1024], AF.Copy),
        waits=((S["v"], vN),), inc=True)
    jj = tt // 2
    poscol = jj * 256 + (tt % 2) * 128

    def emit_tp(_sN=sN, _stg=stg, _pc=poscol):
        for half in range(2):
            tNt = 0
            for mb in range(4):
                mc = half * 4 + mb
                w0 = ()
                if mb == 0:
                    if hasattr(B, "_pt_ev"):
                        w0 = ((S["v"], B._pt_ev),)
                    if half == 0:
                        w0 += ((S["s"], _sN),)
                tNt = e_op("t", lambda e, _mc=mc, _mb=mb, _sg=_stg, _hf=half:
                           e.transpose(B.ps["pT"][:, _hf * 512 + _mb * 128: _hf * 512 + (_mb + 1) * 128],
                                       B.AB[:, _sg + _mc * 128: _sg + _mc * 128 + 128],
                                       B.sb["ident"][:]),
                           waits=w0, inc=(mb == 3))

            def f_tev(e, _half=half, _pc2=_pc):
                src = AP(B.ps["pT"], _half * 512, [[1024, 128], [128, 4], [1, 128]])
                dst = AP(B.AB, XTO + (_half * 4) * 2048 + _pc2,
                         [[AB_COLS, 128], [2048, 4], [1, 128]])
                return e.tensor_copy(dst, src)
            B._pt_ev = e_op("v", f_tev, waits=((S["t"], tNt),), inc=True)
    B._tail_defer.append(emit_tp)



def _ffn0():
    AB, WB, ps, sb, AFl = B.AB, B.WB, B.ps, B.sb, B.AF
    S = B.S
    d = B.d
    barrier("f0")
    stats = _tern_stats_units()
    _statn = [0]

    def stats_slot():
        if _statn[0] < len(stats):
            stats[_statn[0]]()
            _statn[0] += 1
    gu_tp = {}
    d_tp = {}
    comp_v = {}
    stt_v = {}
    acc_v = {}
    guu = 0
    duu = 0

    def mk_prep(c):
        ws = F0_WS[c % 2]
        units = []
        for wi, nm in ((0, "l0_wg"), (1, "l0_wu")):
            for st_i in range(4):
                g_ = c * 8 + wi * 4 + st_i

                def u_gu(_c=c, _ws=ws, _wi=wi, _st=st_i, _g=g_, _nm=nm):
                    gs = F0_GS[_g % 2]
                    wld = ((S["t"], gu_tp[_g - 2]),) if _g - 2 >= 0 else ()
                    ld = dma("gu0" if _g % 2 == 0 else "gu1", AB[:, gs:gs + 1024],
                             AP(d[_nm].tensor, (_c * 512 + _st * 128) * 1024,
                                [[1024, 128], [1, 1024]]), waits=wld)
                    tN = 0
                    for mg in range(2):
                        par = (_g * 2 + mg) % 2
                        w0 = ((S["v"], max(B._ptev.values())),)
                        if mg == 0:
                            w0 += ((B.qsem["gu0" if _g % 2 == 0 else "gu1"], ld),)
                        for mb in range(4):
                            mc = mg * 4 + mb
                            tN = e_op("t", lambda e, _gs=gs, _mc=mc, _mb=mb, _p=par:
                                      e.transpose(B.ps["pT"][:, _p * 512 + _mb * 128: _p * 512 + (_mb + 1) * 128],
                                                  B.AB[:, _gs + _mc * 128: _gs + _mc * 128 + 128],
                                                  B.sb["ident"][:]),
                                      waits=(w0 if mb == 0 else ()),
                                      inc=(mb == 3))

                        def f_ev(e, _ws2=_ws, _wi2=_wi, _st2=_st, _mg=mg, _p=par):
                            src = AP(B.ps["pT"], _p * 512, [[1024, 128], [128, 4], [1, 128]])
                            dst = AP(B.AB, _ws2 + _wi2 * 4096 + (_mg * 4) * 512 + _st2 * 128,
                                     [[AB_COLS, 128], [512, 4], [1, 128]])
                            return e.tensor_copy(dst, src)
                        B._ptev[par] = e_op("v", f_ev, waits=((S["t"], tN),), inc=True)
                    gu_tp[_g] = tN
                    stats_slot()
                units.append(u_gu)
        for dsub in range(8):
            d_ = c * 8 + dsub

            def u_d(_c=c, _ws=ws, _dsub=dsub, _d=d_):
                dsb = F0_DS[_d % 2]
                wld = ((S["t"], d_tp[_d - 2]),) if _d - 2 >= 0 else ()
                ld = dma("gd0" if _d % 2 == 0 else "gd1", AB[:, dsb:dsb + 512],
                         AP(d["l0_wd"].tensor, _dsub * 128 * FF0 + _c * 512,
                            [[FF0, 128], [1, 512]]), waits=wld)
                tN = 0
                par = _d % 2
                w0 = ((S["v"], max(B._ptev.values())),
                      (B.qsem["gd0" if _d % 2 == 0 else "gd1"], ld))
                for hb in range(4):
                    tN = e_op("t", lambda e, _dsb=dsb, _hb=hb, _p=par:
                              e.transpose(B.ps["pT"][:, _p * 512 + _hb * 128: _p * 512 + (_hb + 1) * 128],
                                          B.AB[:, _dsb + _hb * 128: _dsb + _hb * 128 + 128],
                                          B.sb["ident"][:]),
                              waits=(w0 if hb == 0 else ()),
                              inc=(hb == 3))

                def f_evd(e, _ws2=_ws, _dsub2=_dsub, _p=par):
                    src = AP(B.ps["pT"], _p * 512, [[1024, 128], [128, 4], [1, 128]])
                    dst = AP(B.AB, _ws2 + 8192 + _dsub2 * 128,
                             [[AB_COLS, 128], [1024, 4], [1, 128]])
                    return e.tensor_copy(dst, src)
                B._ptev[par] = e_op("v", f_evd, waits=((S["t"], tN),), inc=True)
                d_tp[_d] = tN
                stats_slot()
            units.append(u_d)
        return units

    prep = [mk_prep(c) for c in range(6)]
    prep_i = [0] * 6
    f0pd = {}

    def prep_step(c):
        if c < 6 and prep_i[c] < len(prep[c]):
            prep[c][prep_i[c]]()
            prep_i[c] += 1
            if prep_i[c] == len(prep[c]):
                f0pd[c] = S["v"].n

    def prep_finish(c):
        while c < 6 and prep_i[c] < len(prep[c]):
            prep_step(c)

    for c in range(6):
        ws = F0_WS[c % 2]
        prep_finish(c)
        tp_v = f0pd[c]

        for tq in range(8):
            for hs in range(4):
                bank = hs % 2
                pg = ps["pS0"][:, bank * 512: bank * 512 + 256]
                pu = ps["pS0"][:, bank * 512 + 256: bank * 512 + 512]
                w0 = ()
                if tq == 0 and hs == 0:
                    w0 = ((S["v"], tp_v),)
                key = (c, tq, hs - 2) if hs >= 2 else (c, tq - 1, hs + 2)
                if key in stt_v:
                    w0 += ((S["v"], stt_v[key]),)
                for mc in range(8):
                    t_mm(pg, AB[:, ws + (mc * 4 + hs) * 128: ws + (mc * 4 + hs) * 128 + 128],
                         AB[:, XTO + mc * 2048 + tq * 256: XTO + mc * 2048 + tq * 256 + 256],
                         start=(mc == 0), stop=(mc == 7),
                         waits=(w0 if mc == 0 else ()))
                for mc in range(8):
                    t_mm(pu, AB[:, ws + 4096 + (mc * 4 + hs) * 128: ws + 4096 + (mc * 4 + hs) * 128 + 128],
                         AB[:, XTO + mc * 2048 + tq * 256: XTO + mc * 2048 + tq * 256 + 256],
                         start=(mc == 0), stop=(mc == 7), inc=(mc == 7))
                tN = S["t"].n
                w_sil = [(S["t"], tN)]
                if (c, tq, hs, "stt") in stt_v:
                    pass
                prevstt = stt_v.get((c, tq, hs - 1), stt_v.get((c, tq - 1, 3) if hs == 0 else None))
                if prevstt:
                    w_sil.append((S["v"], prevstt))
                sN = e_op("s", lambda e, _pg=pg: e.activation(
                    B.AB[:, F0_SIL:F0_SIL + 256], _pg, AF.Silu, scale=1.0 / 127.0),
                    waits=tuple(w_sil), inc=True)
                vN = e_op("v", lambda e, _pu=pu, _hs=hs: e.scalar_tensor_tensor(
                    B.AB[:, F0_ST + _hs * 256: F0_ST + (_hs + 1) * 256], _pu,
                    1.0 / 127.0, B.AB[:, F0_SIL:F0_SIL + 256], ALU.mult, ALU.mult),
                    waits=((S["s"], sN),), inc=True)
                stt_v[(c, tq, hs)] = vN
                prep_step(c + 1)
            st_v = S["v"].n
            for ti in range(2):
                tt = tq * 2 + ti
                wdn = ((S["v"], st_v),)
                pk = (c, tq, ti - 1) if ti else ((c, tq - 1, 1) if tq else (c - 1, 7, 1))
                if pk in acc_v:
                    wdn += ((S["v"], acc_v[pk]),)
                for half, pp in ((0, ps["pAV"][:, 0:512]), (1, ps["pB"][:, 0:512])):
                    for hs in range(4):
                        t_mm(pp, AB[:, F0_ST + hs * 256 + ti * 128: F0_ST + hs * 256 + ti * 128 + 128],
                             AB[:, ws + 8192 + hs * 1024 + half * 512: ws + 8192 + hs * 1024 + half * 512 + 512],
                             start=(hs == 0), stop=(hs == 3),
                             waits=(wdn if (half == 0 and hs == 0) else ()),
                             inc=(hs == 3 and half == 1))
                tN = S["t"].n

                def f_acc(e, _tt=tt):
                    stc = STRM + _tt * 1024
                    e.scalar_tensor_tensor(B.AF[:, stc:stc + 512], B.ps["pAV"][:, 0:512],
                                           1.0 / 127.0, B.AF[:, stc:stc + 512],
                                           ALU.mult, ALU.add)
                    return e.scalar_tensor_tensor(B.AF[:, stc + 512:stc + 1024],
                                                  B.ps["pB"][:, 0:512], 1.0 / 127.0,
                                                  B.AF[:, stc + 512:stc + 1024],
                                                  ALU.mult, ALU.add)
                acc_v[(c, tq, ti)] = e_op("v", f_acc, waits=((S["t"], tN),), inc=True)
            stats_slot()
    while _statn[0] < len(stats):
        stats_slot()
    barrier("f0e")
    B._tail_v = {}
    for tt in range(16):
        _ln_tile(tt, 0, None, None, mode="ffn0")
    ng = B.qsem["g"].n
    em("g", lambda e, _h=B.qsem["g"].h, _n=ng: e.wait_ge(_h, _n))

    def f_cc(e):
        e.collective_compute(
            "AllGather", ALU.bypass,
            replica_groups=[[0, 1], [2, 3], [4, 5], [6, 7]],
            ins=[B.d["cc_in"].opt()],
            outs=[B.d["cc_out"].opt()]).then_inc(B.ccs.h, 1)
    em("g", f_cc)
    B.ccs.n += 1
    barrier("f0z")



def _exchange():
    AB, WB, ps, sb, AFl = B.AB, B.WB, B.ps, B.sb, B.AF
    S = B.S
    d = B.d
    nyx = B.qsem["yx"].n
    em("y", lambda e, _h=B.qsem["yx"].h, _n=nyx: e.wait_ge(_h, _n))
    barrier("xc")

    dma("y", AFl[:, TKA:TKA + T], d["tabk_a"])
    dma("y", AFl[:, TKB:TKB + T], d["tabk_b"])
    dma("y", AFl[:, TQA:TQA + TOWN], d["tabq_a"])
    dma("y", AFl[:, TQB:TQB + TOWN], d["tabq_b"])
    _load_w_mk("g", "l1_wq", 256, 8, W_C["l1_wq"])
    _load_w_mk("g", "l1_wqr", 128, 8, W_C["l1_wqr"])
    _load_w_mk("g", "l1_wqr_rot", 128, 8, W_C["l1_wqr_rot"])
    _load_w_mk("g", "l1_wkv", 256, 8, W_C["l1_wkv"])
    _load_w_mk("g", "l1_wkr", 128, 8, W_C["l1_wkr"])
    _load_w_mk("g", "l1_wkr_rot", 128, 8, W_C["l1_wkr_rot"])
    dma("g", AP(B.WB, W_BO, [[WB_COLS, 1], [1, DM]]), d["r_bo"])
    dma("g", AP(B.WB, W_B2, [[WB_COLS, 1], [1, DM]]), d["r_b2"])

    _tern_stats()

    B._xtev = {}
    for tt in range(16):
        stg = PSTG + (tt % 2) * 1024
        wld = ((S["v"], B._xtev[tt - 2]),) if (tt - 2) in B._xtev else ()
        qn = "yc0" if tt % 2 == 0 else "yc1"
        ld = dma(qn, AB[:, stg:stg + 1024], d["cc_in"][tt * 128:(tt + 1) * 128, :],
                 waits=wld)
        _xt_transpose(tt, stg, (qn, ld), own=True)
    for tt in range(16):
        stg = PSTG + (tt % 2) * 1024
        prev_key = (tt - 2 + 100) if tt >= 2 else (14 + tt)
        wv = B._xtev.get(prev_key)
        ld = _peer_tile_dma(tt, stg, wv)
        _xt_transpose(tt, stg, ("gp0" if tt % 2 == 0 else "gp1", ld), own=False)
    barrier("xce")
    e_op("v", lambda e: e.tensor_scalar(B.WB[0:1, W_B2:W_B2 + DM],
                                        B.WB[0:1, W_B2:W_B2 + DM], -1.0,
                                        None, ALU.mult), inc=True)


def _peer_tile_dma(tt, stg, wv):
    sem = B.qsem["gp0" if tt % 2 == 0 else "gp1"]
    S = B.S

    def f(e, _tt=tt, _stg=stg, _h=sem.h, _wv=wv):
        if _tt == 0:
            e.wait_ge(B.ccs.h, B.ccs.n)
        if _wv is not None:
            e.wait_ge(S["v"].h, _wv)
        with e.register(f"po{_tt}") as po:
            e.reg_load(po, B.sb["u32s"][0:1, 0:1])
            e.reg_add(po, po, _tt * 131072)
            e.dma_start(out=B.AB[:, _stg:_stg + 1024],
                        in_=AP(B.d["cc_out"].tensor, po, [[1024, 128], [1, 1024]])
                        ).then_inc(_h, 16)
    ccn = B.ccs.n
    em("g", f)
    sem.n += 16
    return sem.n


def _xt_transpose(tt, stg, ld, own):
    S = B.S
    q, ldn = ld
    jj = tt // 2
    pos = (2 * jj + (0 if own else 1)) * BLK + (tt % 2) * 128
    tN = 0
    vN = 0
    for half in range(2):
        for mb in range(4):
            mc = half * 4 + mb
            w0 = ()
            if mb == 0:
                if half == 0:
                    w0 = ((B.qsem[q], ldn),)
                if hasattr(B, "_pt_ev"):
                    w0 += ((S["v"], B._pt_ev),)
            tN = e_op("t", lambda e, _stg=stg, _mc=mc, _mb=mb, _hf=half:
                      e.transpose(B.ps["pT"][:, _hf * 512 + _mb * 128: _hf * 512 + (_mb + 1) * 128],
                                  B.AB[:, _stg + _mc * 128: _stg + _mc * 128 + 128],
                                  B.sb["ident"][:]),
                      waits=w0, inc=(mb == 3))

        def f_ev(e, _half=half, _pos=pos):
            src = AP(B.ps["pT"], _half * 512, [[1024, 128], [128, 4], [1, 128]])
            dst = AP(B.AB, L1_X2T + (_half * 4) * T + _pos,
                     [[AB_COLS, 128], [T, 4], [1, 128]])
            return e.tensor_copy(dst, src)
        vN = e_op("v", f_ev, waits=((S["t"], tN),), inc=True)
        B._pt_ev = vN
    B._xtev[tt + (0 if own else 100)] = vN


def _tern_stats_units():
    AFl, sb = B.AF, B.sb
    S = B.S
    d = B.d
    B._stv = {-1: (S["s"], S["s"].n)}
    units = []
    uu = 0
    for wi, nm in ((0, "l1_W1"), (1, "l1_W2")):
        for i in range(32):
            def emit(_uu=uu, _wi=wi, _nm=nm, _i=i):
                gs = F1_GS[0]
                wld = (B._stv[_uu - 1],)
                if _wi == 0:
                    src = d[_nm][_i * 128:(_i + 1) * 128, :]
                else:
                    rt, ct = _i // 4, _i % 4
                    src = AP(d[_nm].tensor, rt * 128 * FF1 + ct * 1024,
                             [[FF1, 128], [1, 1024]])
                ld = dma("ys", AFl[:, gs:gs + 1024], src, waits=wld)

                def f_r(e, _gs=gs, _u=_uu):
                    return e.activation(B.AF[:, _gs:_gs + 1024],
                                        B.AF[:, _gs:_gs + 1024], AF.Abs,
                                        accum_out=B.sb["asum"][:, _u:_u + 1])
                sN = e_op("s", f_r, waits=((B.qsem["ys"], ld),), inc=True)
                B._stv[_uu] = (S["s"], sN)
            units.append(emit)
            uu += 1
    return units


def _tern_stats():
    sb, ps = B.sb, B.ps
    S = B.S

    def f_colsum(e):
        vc(e, e.reduce_sum(B.sb["asum"][:, 64:65], B.sb["asum"][:, 0:32], AX.X))
        return e.reduce_sum(B.sb["asum"][:, 65:66], B.sb["asum"][:, 32:64], AX.X)
    vN = e_op("v", f_colsum, waits=(B._stv[63],), inc=True)
    for wi in range(2):
        t_mm(ps["pA"][0:1, wi:wi + 1], sb["asum"][:, 64 + wi:65 + wi],
             sb["ones_f"][:, 0:1], start=True, stop=True,
             waits=(((S["v"], vN),) if wi == 0 else ()), inc=(wi == 1))
    tN = S["t"].n

    def f_thr(e):
        vc(e, e.tensor_scalar(B.sb["tiny"][0:1, 6:8], B.ps["pA"][0:1, 0:2],
                              1.0 / (FF1 * DM), None, ALU.mult))
        return e.tensor_scalar(B.sb["tiny"][0:1, 6:8], B.sb["tiny"][0:1, 6:8],
                               1e-8, 0.5, ALU.max, ALU.mult)
    vN = e_op("v", f_thr, waits=((S["t"], tN),), inc=True)
    tN = t_mm(ps["pA"][:, 2:4], sb["ones_f"][0:1, :], sb["tiny"][0:1, 6:8],
              start=True, stop=True, waits=((S["v"], vN),), inc=True)

    def f_b(e):
        e.tensor_copy(B.sb["thr"][:, 0:1], B.ps["pA"][:, 2:3])
        e.tensor_scalar(B.sb["thr"][:, 1:2], B.ps["pA"][:, 2:3], -1.0, None, ALU.mult)
        e.tensor_copy(B.sb["thr"][:, 2:3], B.ps["pA"][:, 3:4])
        return e.tensor_scalar(B.sb["thr"][:, 3:4], B.ps["pA"][:, 3:4], -1.0,
                               None, ALU.mult)
    e_op("v", f_b, waits=((S["t"], tN),), inc=True)



def _l1_proj_common():
    AB, WB, ps, sb = B.AB, B.WB, B.ps, B.sb
    S = B.S
    barrier("l1p")
    prev = {}
    for u in range(16):
        dc, tt = u // 8, u % 8
        pp = ps["pA"][:, 0:512] if u % 2 == 0 else ps["pS0"][:, 0:512]
        w = ((S["s"], prev[u - 2]),) if u - 2 in prev else ()
        for mc in range(8):
            t_mm(pp, WB[:, W_C["l1_wkv"] + mc * 256 + dc * 128: W_C["l1_wkv"] + mc * 256 + dc * 128 + 128],
                 AB[:, L1_X2T + mc * T + tt * 512: L1_X2T + mc * T + tt * 512 + 512],
                 start=(mc == 0), stop=(mc == 7), waits=(w if mc == 0 else ()),
                 inc=(mc == 7))
        tN = S["t"].n
        prev[u] = e_op("s", lambda e, _pp=pp, _dc=dc, _tt=tt: e.activation(
            B.AB[:, L1_CKV + _dc * T + _tt * 512: L1_CKV + _dc * T + _tt * 512 + 512],
            _pp, AF.Gelu, bias=B.sb["b_bkv"][:, _dc:_dc + 1]),
            waits=((S["t"], tN),), inc=True)
    prev_v = {-2: prev[14], -1: prev[15]}
    for tt in range(8):
        if tt % 2 == 0:
            pA, pB = ps["pA"][:, 0:512], ps["pB"][:, 0:512]
        else:
            pA, pB = ps["pS0"][:, 0:512], ps["pS0"][:, 512:1024]
        w = ((S["s"], prev_v[tt - 2]),) if tt - 2 in prev_v else ()
        for mc in range(8):
            t_mm(pA, WB[:, W_C["l1_wkr"] + mc * 128: W_C["l1_wkr"] + mc * 128 + 128],
                 AB[:, L1_X2T + mc * T + tt * 512: L1_X2T + mc * T + tt * 512 + 512],
                 start=(mc == 0), stop=(mc == 7), waits=(w if mc == 0 else ()))
        for mc in range(8):
            t_mm(pB, WB[:, W_C["l1_wkr_rot"] + mc * 128: W_C["l1_wkr_rot"] + mc * 128 + 128],
                 AB[:, L1_X2T + mc * T + tt * 512: L1_X2T + mc * T + tt * 512 + 512],
                 start=(mc == 0), stop=(mc == 7), inc=(mc == 7))
        tN = S["t"].n

        def f_rope(e, _pA=pA, _pB=pB, _tt=tt):
            fsb = FS + (_tt % 2) * 1024
            t1 = B.AF[:, fsb:fsb + 512]
            t2 = B.AF[:, fsb + 512:fsb + 1024]
            e.scalar_tensor_tensor(t1, _pA, B.sb["b_bkr"][:, 0:1],
                                   B.AF[:, TKA + _tt * 512: TKA + _tt * 512 + 512],
                                   ALU.add, ALU.mult)
            vc(e, e.scalar_tensor_tensor(t2, _pB, B.sb["b_bkr_rot"][:, 0:1],
                                         B.AF[:, TKB + _tt * 512: TKB + _tt * 512 + 512],
                                         ALU.add, ALU.mult))
            return e.tensor_tensor(t1, t1, t2, ALU.add)
        vN = e_op("v", f_rope, waits=((S["t"], tN),), inc=True)
        prev_v[tt] = e_op("s", lambda e, _tt=tt: e.activation(
            B.AB[:, L1_KRG + _tt * 512: L1_KRG + _tt * 512 + 512],
            B.AF[:, FS + (_tt % 2) * 1024: FS + (_tt % 2) * 1024 + 512], AF.Gelu),
            waits=((S["v"], vN),), inc=True)
    kr_v_last = vN
    prevq = {}
    for u in range(16):
        dc, jb = u // 8, u % 8
        pp = ps["pA"][:, 0:256] if u % 2 == 0 else ps["pS0"][:, 0:256]
        if u - 2 in prevq:
            w = ((S["s"], prevq[u - 2]),)
        else:
            w = ((S["v"], kr_v_last),)
        pos = (2 * jb) * BLK
        for mc in range(8):
            t_mm(pp, WB[:, W_C["l1_wq"] + mc * 256 + dc * 128: W_C["l1_wq"] + mc * 256 + dc * 128 + 128],
                 AB[:, L1_X2T + mc * T + pos: L1_X2T + mc * T + pos + 256],
                 start=(mc == 0), stop=(mc == 7), waits=(w if mc == 0 else ()),
                 inc=(mc == 7))
        tN = S["t"].n
        prevq[u] = e_op("s", lambda e, _pp=pp, _dc=dc, _jb=jb: e.activation(
            B.WB[:, W_CQ + _dc * TOWN + _jb * 256: W_CQ + _dc * TOWN + _jb * 256 + 256],
            _pp, AF.Gelu, bias=B.sb["b_bq"][:, _dc:_dc + 1]),
            waits=((S["t"], tN),), inc=True)
    prevr = {-2: prevq[14], -1: prevq[15]}
    for jb in range(8):
        if jb % 2 == 0:
            pA, pB = ps["pA"][:, 0:256], ps["pB"][:, 0:256]
        else:
            pA, pB = ps["pS0"][:, 0:256], ps["pS0"][:, 512:768]
        w = ((S["s"], prevr[jb - 2]),) if jb - 2 in prevr else ()
        pos = (2 * jb) * BLK
        for mc in range(8):
            t_mm(pA, WB[:, W_C["l1_wqr"] + mc * 128: W_C["l1_wqr"] + mc * 128 + 128],
                 AB[:, L1_X2T + mc * T + pos: L1_X2T + mc * T + pos + 256],
                 start=(mc == 0), stop=(mc == 7), waits=(w if mc == 0 else ()))
        for mc in range(8):
            t_mm(pB, WB[:, W_C["l1_wqr_rot"] + mc * 128: W_C["l1_wqr_rot"] + mc * 128 + 128],
                 AB[:, L1_X2T + mc * T + pos: L1_X2T + mc * T + pos + 256],
                 start=(mc == 0), stop=(mc == 7), inc=(mc == 7))
        tN = S["t"].n

        def f_rope(e, _pA=pA, _pB=pB, _jb=jb):
            fsb = FS + (_jb % 2) * 1024
            t1 = B.AF[:, fsb:fsb + 256]
            t2 = B.AF[:, fsb + 512:fsb + 768]
            e.scalar_tensor_tensor(t1, _pA, B.sb["b_bqr"][:, 0:1],
                                   B.AF[:, TQA + _jb * 256: TQA + _jb * 256 + 256],
                                   ALU.add, ALU.mult)
            vc(e, e.scalar_tensor_tensor(t2, _pB, B.sb["b_bqr_rot"][:, 0:1],
                                         B.AF[:, TQB + _jb * 256: TQB + _jb * 256 + 256],
                                         ALU.add, ALU.mult))
            return e.tensor_tensor(t1, t1, t2, ALU.add)
        vN = e_op("v", f_rope, waits=((S["t"], tN),), inc=True)
        prevr[jb] = e_op("s", lambda e, _jb=jb: e.activation(
            B.WB[:, W_QRG + _jb * 256: W_QRG + _jb * 256 + 256],
            B.AF[:, FS + (_jb % 2) * 1024: FS + (_jb % 2) * 1024 + 256], AF.Gelu),
            waits=((S["v"], vN),), inc=True)
    B._l1p_v_end = S["v"].n
    barrier("l1pe")


def _l1_pass(p):
    AB, WB, ps, sb = B.AB, B.WB, B.ps, B.sb
    S = B.S
    d = B.d
    barrier(f"l1s{p}")
    pb = W_PS[p % 2]
    qp = "gw0" if p % 2 == 0 else "gw1"
    for cc in range(2):
        c = 2 * p + cc
        if c < 7:
            dma(qp, AP(WB, pb + cc * 256 + 0, [[WB_COLS, 128], [128, 2], [1, 128]]),
                AP(d["l1_wkc"].tensor, c * 128, [[896, 128], [896 * 128, 2], [1, 128]]))
            dma(qp, AP(WB, pb + 512 + cc * 256, [[WB_COLS, 128], [128, 2], [1, 128]]),
                AP(d["l1_wqc"].tensor, c * 128, [[896, 128], [896 * 128, 2], [1, 128]]))
    dma(qp, AP(WB, pb + 1024, [[WB_COLS, 128], [256, 2], [1, 256]]),
        AP(d["l1_wv"].tensor, 4 * p * 64, [[1024, 128], [131072, 2], [1, 256]]))
    if p == 0:
        _load_w_mk(qp, "l1_wo", 1024, 8, W_WO,
                   waits=((S["t"], S["t"].n),))
    em("t", lambda e, _h=B.qsem[qp].h, _n=B.qsem[qp].n: e.wait_ge(_h, _n))
    vpad = e_op("v", lambda e: e.memset(
        AP(B.AB, L1_V + 64, [[AB_COLS, 128], [65, 4 * 32], [1, 1]]), 1.0), inc=True)

    prev = {}
    u = 0
    for cc in range(2):
        c = 2 * p + cc
        if c == 7:
            continue
        for tt in range(8):
            pp = ps["pA"][:, 0:512] if u % 2 == 0 else ps["pS0"][:, 0:512]
            w = ((S["s"], prev[u - 2]),) if u - 2 in prev else ()
            for dc in range(2):
                t_mm(pp, WB[:, pb + cc * 256 + dc * 128: pb + cc * 256 + dc * 128 + 128],
                     AB[:, L1_CKV + dc * T + tt * 512: L1_CKV + dc * T + tt * 512 + 512],
                     start=(dc == 0), stop=(dc == 1), waits=(w if dc == 0 else ()),
                     inc=(dc == 1))
            tN = S["t"].n
            prev[u] = e_op("s", lambda e, _pp=pp, _cc=cc, _tt=tt, _c=c: e.activation(
                B.AB[:, L1_KT + _cc * T + _tt * 512: L1_KT + _cc * T + _tt * 512 + 512],
                _pp, AF.Gelu, bias=B.sb["b_bkc"][:, _c:_c + 1]),
                waits=((S["t"], tN),), inc=True)
            u += 1
    for cc in range(2):
        c = 2 * p + cc
        if c == 7:
            continue
        for jb in range(8):
            pp = ps["pA"][:, 0:256] if u % 2 == 0 else ps["pS0"][:, 0:256]
            w = ((S["s"], prev[u - 2]),) if u - 2 in prev else ()
            for dc in range(2):
                t_mm(pp, WB[:, pb + 512 + cc * 256 + dc * 128: pb + 512 + cc * 256 + dc * 128 + 128],
                     WB[:, W_CQ + dc * TOWN + jb * 256: W_CQ + dc * TOWN + jb * 256 + 256],
                     start=(dc == 0), stop=(dc == 1), waits=(w if dc == 0 else ()),
                     inc=(dc == 1))
            tN = S["t"].n
            prev[u] = e_op("s", lambda e, _pp=pp, _cc=cc, _jb=jb, _c=c: e.activation(
                B.AB[:, L1_QT + _cc * TOWN + _jb * 256: L1_QT + _cc * TOWN + _jb * 256 + 256],
                _pp, AF.Gelu, bias=B.sb["b_bqc"][:, _c:_c + 1]),
                waits=((S["t"], tN),), inc=True)
            u += 1
    prevv = {}
    for kt in range(32):
        if kt % 2 == 0:
            pt, off = ps["pB"], 0
        else:
            pt, off = ps["pS1"], 0
        pp = pt[:, off:off + 256]
        w = ((S["s"], prevv[kt - 2]),) if kt - 2 in prevv else ()
        if kt < 2:
            w += ((S["s"], prev[u - 1] if (u - 1) in prev else S["s"].n),)
        for dc in range(2):
            t_mm(pp, AB[:, L1_CKV + dc * T + kt * 128: L1_CKV + dc * T + kt * 128 + 128],
                 WB[:, pb + 1024 + dc * 256: pb + 1024 + dc * 256 + 256],
                 start=(dc == 0), stop=(dc == 1), waits=(w if dc == 0 else ()),
                 inc=(dc == 1))
        tN = S["t"].n
        rowext = 512 if kt % 2 == 0 else 1024

        def f_ev(e, _pt=pt, _kt=kt, _re=rowext):
            src = AP(_pt, 0, [[_re, 128], [64, 4], [1, 64]])
            dst = AP(B.AB, L1_V + _kt * VW1, [[AB_COLS, 128], [65, 4], [1, 64]])
            return e.activation(dst, src, AF.Copy)
        prevv[kt] = e_op("s", f_ev, waits=((S["t"], tN), (S["v"], vpad)), inc=True)

    _bound_l1(p)

    def KT_sl(h, kt):
        pr = slice((h % 2) * 64, (h % 2) * 64 + 64)
        if h // 2 == 7:
            return B.AB[pr, L1_KRG + kt * 128: L1_KRG + kt * 128 + 128]
        cc = h // 2 - 2 * p
        return B.AB[pr, L1_KT + cc * T + kt * 128: L1_KT + cc * T + kt * 128 + 128]

    def QT_sl(h, j):
        pr = slice((h % 2) * 64, (h % 2) * 64 + 64)
        if h // 2 == 7:
            return B.WB[pr, W_QRG + j * 256: W_QRG + j * 256 + 256]
        cc = h // 2 - 2 * p
        return B.AB[pr, L1_QT + cc * TOWN + j * 256: L1_QT + cc * TOWN + j * 256 + 256]

    def V_sl(h, kt):
        hh = h - 4 * p
        return B.AB[:, L1_V + kt * VW1 + hh * 65: L1_V + kt * VW1 + hh * 65 + 65]

    ctxd = {"KT_sl": KT_sl, "QT_sl": QT_sl, "V_sl": V_sl}
    if not hasattr(B, "_l1_pair_state"):
        B._l1_pair_state = {"prev_ev": {0: None, 1: None},
                            "ost_dma": {0: None, 1: None}, "fon": None}
    for hh in (0, 2):
        _attn_head_pair(4 * p + hh, ctxd)



def _ffn1():
    AB, WB, ps, sb, AFl = B.AB, B.WB, B.ps, B.sb, B.AF
    S = B.S
    d = B.d
    barrier("f1")
    B._f1par = 0
    tern_v = {}
    tp_t = {}
    gelu_s = {}
    acc_v = {}

    def mk_prep(c):
        ws = F1_WS[c % 2]
        units = []
        base = c * 12
        for wi, nm, nsub in ((0, "l1_W1", 4), (1, "l1_W2", 8)):
            for st_i in range(nsub):
                u_ = base + (st_i if wi == 0 else 4 + st_i)

                def unit(_c=c, _ws=ws, _wi=wi, _st=st_i, _u=u_, _nm=nm):
                    gs = F1_GS[0] if _u % 2 == 0 else FS
                    wld = ((S["v"], tern_v[_u - 2]),) if _u - 2 in tern_v else ()
                    if _wi == 0:
                        src = AP(d[_nm].tensor, (_c * 512 + _st * 128) * DM,
                                 [[DM, 128], [1, DM]])
                        wcols = 1024
                    else:
                        src = AP(d[_nm].tensor, _st * 128 * FF1 + _c * 512,
                                 [[FF1, 128], [1, 512]])
                        wcols = 512
                    qn = "ys" if _u % 2 == 0 else "yc0"
                    ld = dma(qn, AFl[:, gs:gs + wcols], src, waits=wld)
                    thr_c = 0 if _wi == 0 else 2
                    wtp = ((S["t"], tp_t[_u - 2]),) if _u - 2 in tp_t else ()

                    def f_tern(e, _gs=gs, _wc=wcols, _tc=thr_c):
                        vc(e, e.tensor_scalar(B.AB[:, F1_TS:F1_TS + _wc],
                                              B.AF[:, _gs:_gs + _wc],
                                              B.sb["thr"][:, _tc:_tc + 1], None,
                                              ALU.is_ge))
                        return e.scalar_tensor_tensor(
                            B.AB[:, F1_TS + 2048:F1_TS + 2048 + _wc],
                            B.AF[:, _gs:_gs + _wc], B.sb["thr"][:, _tc + 1:_tc + 2],
                            B.AB[:, F1_TS:F1_TS + _wc], ALU.is_le, ALU.subtract)
                    vN = e_op("v", f_tern, waits=((B.qsem[qn], ld),) + wtp, inc=True)
                    tern_v[_u] = vN
                    nb = 8 if _wi == 0 else 4
                    tN = 0
                    for bg in range(nb // 4):
                        par = B._f1par
                        B._f1par = 1 - B._f1par
                        w0 = ((S["v"], max(B._ptev.values())),)
                        if bg == 0:
                            w0 += ((S["v"], vN),)
                        for bb in range(4):
                            blk = bg * 4 + bb
                            tN = e_op("t", lambda e, _blk=blk, _bb=bb, _p=par:
                                      e.transpose(B.ps["pT"][:, _p * 512 + _bb * 128: _p * 512 + (_bb + 1) * 128],
                                                  B.AB[:, F1_TS + 2048 + _blk * 128: F1_TS + 2048 + _blk * 128 + 128],
                                                  B.sb["ident"][:]),
                                      waits=(w0 if bb == 0 else ()),
                                      inc=(bb == 3))
                        if _wi == 0:
                            def f_ev(e, _ws2=_ws, _st2=_st, _bg=bg, _p=par):
                                src = AP(B.ps["pT"], _p * 512, [[1024, 128], [128, 4], [1, 128]])
                                dst = AP(B.AB, _ws2 + (_bg * 4) * 512 + _st2 * 128,
                                         [[AB_COLS, 128], [512, 4], [1, 128]])
                                return e.tensor_copy(dst, src)
                        else:
                            def f_ev(e, _ws2=_ws, _st2=_st, _bg=bg, _p=par):
                                src = AP(B.ps["pT"], _p * 512, [[1024, 128], [128, 4], [1, 128]])
                                dst = AP(B.AB, _ws2 + 4096 + _st2 * 128,
                                         [[AB_COLS, 128], [1024, 4], [1, 128]])
                                return e.tensor_copy(dst, src)
                        B._ptev[par] = e_op("v", f_ev, waits=((S["t"], tN),), inc=True)
                    tp_t[_u] = tN
                units.append(unit)
        return units

    prep = [mk_prep(c) for c in range(8)]
    prep_i = [0] * 8
    f1pd = {}

    def prep_step(c):
        if c < 8 and prep_i[c] < len(prep[c]):
            prep[c][prep_i[c]]()
            prep_i[c] += 1
            if prep_i[c] == len(prep[c]):
                f1pd[c] = S["v"].n

    def prep_finish(c):
        while c < 8 and prep_i[c] < len(prep[c]):
            prep_step(c)

    for c in range(8):
        ws = F1_WS[c % 2]
        prep_finish(c)
        tp_v = f1pd[c]
        for tq in range(8):
            for fs in range(4):
                bank = fs % 2
                ph = ps["pS0"][:, bank * 512: bank * 512 + 256]
                w0 = ()
                if tq == 0 and fs == 0:
                    w0 = ((S["v"], tp_v),)
                key = (c, tq, fs - 2) if fs >= 2 else (c, tq - 1, fs + 2)
                if key in gelu_s:
                    w0 += ((S["s"], gelu_s[key]),)
                for mc in range(8):
                    t_mm(ph, AB[:, ws + (mc * 4 + fs) * 128: ws + (mc * 4 + fs) * 128 + 128],
                         AB[:, XTO + mc * 2048 + tq * 256: XTO + mc * 2048 + tq * 256 + 256],
                         start=(mc == 0), stop=(mc == 7),
                         waits=(w0 if mc == 0 else ()), inc=(mc == 7))
                tN = S["t"].n
                gelu_s[(c, tq, fs)] = e_op("s", lambda e, _ph=ph, _fs=fs, _c=c: e.activation(
                    B.AB[:, F1_HT + _fs * 256: F1_HT + (_fs + 1) * 256], _ph, AF.Gelu,
                    bias=B.sb["b_b1"][:, _c * 4 + _fs:_c * 4 + _fs + 1], scale=-1.0),
                    waits=((S["t"], tN),), inc=True)
                prep_step(c + 1)
            ht_s = S["s"].n
            for ti in range(2):
                tt = tq * 2 + ti
                wdn = ((S["s"], ht_s),)
                pk = (c, tq, ti - 1) if ti else ((c, tq - 1, 1) if tq else (c - 1, 7, 1))
                if pk in acc_v:
                    wdn += ((S["v"], acc_v[pk]),)
                for half, pp in ((0, ps["pAV"][:, 0:512]), (1, ps["pB"][:, 0:512])):
                    for fs in range(4):
                        laststop = (fs == 3 and c < 7)
                        t_mm(pp, AB[:, F1_HT + fs * 256 + ti * 128: F1_HT + fs * 256 + ti * 128 + 128],
                             AB[:, ws + 4096 + fs * 1024 + half * 512: ws + 4096 + fs * 1024 + half * 512 + 512],
                             start=(fs == 0), stop=laststop,
                             waits=(wdn if (half == 0 and fs == 0) else ()),
                             inc=(laststop and half == 1))
                    if c == 7:
                        t_mm(pp, sb["ones_b"][0:1, :],
                             B.WB[0:1, W_B2 + half * 512: W_B2 + half * 512 + 512],
                             start=False, stop=True, inc=(half == 1))
                tN = S["t"].n

                def f_acc(e, _tt=tt):
                    stc = STRM + _tt * 1024
                    e.tensor_tensor(B.AF[:, stc:stc + 512], B.AF[:, stc:stc + 512],
                                    B.ps["pAV"][:, 0:512], ALU.subtract)
                    return e.tensor_tensor(B.AF[:, stc + 512:stc + 1024],
                                           B.AF[:, stc + 512:stc + 1024],
                                           B.ps["pB"][:, 0:512], ALU.subtract)
                acc_v[(c, tq, ti)] = e_op("v", f_acc, waits=((S["t"], tN),), inc=True)
    barrier("f1e")


def _final_ln_out():
    B._tail_v = {}
    for tt in range(16):
        _ln_tile(tt, 0, None, None, mode="ffn1")
    em("y", lambda e, _h=B.qsem["y"].h, _n=B.qsem["y"].n: e.wait_ge(_h, _n))
    em("g", lambda e, _h=B.qsem["g"].h, _n=B.qsem["g"].n: e.wait_ge(_h, _n))


def kernel(**inputs):
    nc = build_program()
    in_maps = [make_core_inputs(inputs, c // 2, c % 2) for c in range(8)]
    res = run_bass_kernel_spmd(nc, in_maps, core_ids=list(range(8)))
    out = np.zeros((4, T, DM), dtype=np.float32)
    for c in range(8):
        s, r = c // 2, c % 2
        yc = res.results[c]["y"]
        for j in range(NBLK):
            g = 2 * j + r
            out[s, g * BLK:(g + 1) * BLK, :] = yc[j * BLK:(j + 1) * BLK, :]
    return out


# version 09
# speedup vs baseline: 1.5510x; 1.0113x over previous
import os
import numpy as np
import ml_dtypes
from contextlib import ExitStack

ATH_PH = os.environ.get("ATH_PH", "full")
ATH_TC = os.environ.get("ATH_TC", "full")

import concourse.bass as bass
import concourse.mybir as mybir
from concourse.bass_utils import run_bass_kernel_spmd

F32 = mybir.dt.float32
BF16 = mybir.dt.bfloat16
I32 = mybir.dt.int32
U32 = mybir.dt.uint32
AF = mybir.ActivationFunctionType
ALU = mybir.AluOpType
AX = mybir.AxisListType

T = 4096
DM = 1024
BLK = 256
NBLK = 8
TOWN = 2048
L0H = 8
L1H = 16
DCV, DRP = 256, 128
FF0, FF1 = 3072, 4096
QG = 256
NQG = 8
EPS = 1e-5

AB_COLS = 45600
AF_COLS = 18432
STRM = 0
FS = 16384
TKA, TKB = 0, 4096
TQA, TQB = 8192, 10240

XT = 0
KTA = 32768
QTA = 36864
VA = 38912
VW0 = 129
EA = 43040
OST0 = 45088
TSC = 45600

XTO = 0
ORB = 16384

F0_WS = [16384, 28672]
F0_GS = [40960, 41984]
F0_DS = [43008, 43520]
F0_SIL = 44032
F0_ST = 44544

L1_X2T = 0
L1_CKV = 32768
L1_KRG = 40960
PSTG = 45056
PSTG = 32768

L1_KT = 0
L1_QT = 8192
L1_V = 12288
VW1 = 260
L1_E = 20608
L1_OST = 24704

F1_WS = [16384, 24576]
F1_TS = 32768
F1_HT = 35840
F1_GS = [FS + 1024]



def _perm_blocks(r):
    return [p + r if p % 2 == 0 else p - r for p in range(2 * NBLK)]


def _rope_tabs(pos):
    i = np.arange(64, dtype=np.float64)
    invf = np.power(10000.0, -i / 64.0)
    rad = pos[None, :].astype(np.float64) * invf[:, None]
    cos = np.cos(rad).astype(np.float32)
    sin = np.sin(rad).astype(np.float32)
    return (np.concatenate([cos, cos], axis=0),
            np.concatenate([-sin, sin], axis=0))


def _masks(r):
    k = np.arange(128)[:, None]
    q = np.arange(QG)[None, :]
    tri0 = (k <= q).astype(np.float32)
    tri1 = ((128 + k) <= q).astype(np.float32)
    full = np.full((128, QG), 1.0 if r == 1 else 0.0, np.float32)
    return np.concatenate([tri0, tri1, full, full], axis=1).astype(ml_dtypes.bfloat16)


def _selA():
    m = np.zeros((128, 144), np.float32)
    for tc in range(12):
        m[:, tc * 12 + tc] = 1.0
    return m.astype(ml_dtypes.bfloat16)


def _selB():
    m = np.zeros((128, 32), np.float32)
    for v in range(4):
        base = 2 * v
        m[0:64, v * 8 + base] = 1.0
        m[64:128, v * 8 + base + 1] = 1.0
    return m.astype(ml_dtypes.bfloat16)


def _rot_halves(w, width):
    g = w.shape[1] // width
    return np.ascontiguousarray(
        w.reshape(w.shape[0], g, 2, width // 2)[:, :, ::-1, :].reshape(w.shape))


def make_core_inputs(inputs, s, r):
    x = np.asarray(inputs["L0_attn_x"], dtype=np.float32)
    perm = _perm_blocks(r)
    rows = np.concatenate([np.arange(b * BLK, (b + 1) * BLK) for b in perm])
    x_perm = np.ascontiguousarray(x[s][rows])
    xT_perm = np.ascontiguousarray(x_perm.T)
    own_rows = rows.reshape(2 * NBLK, BLK)[0::2].reshape(-1)
    tka, tkb = _rope_tabs(rows.astype(np.float64))
    tqa, tqb = _rope_tabs(own_rows.astype(np.float64))
    f32 = lambda a: np.ascontiguousarray(np.asarray(a, dtype=np.float32))
    d = {
        "x_perm": x_perm, "xT_perm": xT_perm,
        "tabk_a": tka, "tabk_b": tkb, "tabq_a": tqa, "tabq_b": tqb,
        "masks": _masks(r),
        "ident": np.eye(128, dtype=ml_dtypes.bfloat16),
        "ones_b": np.ones((128, 128), dtype=ml_dtypes.bfloat16),
        "ones_f": np.ones((128, 128), dtype=np.float32),
        "selA": _selA(), "selB": _selB(),
        "peer_off": np.array([[(1 - r) * TOWN * DM]], dtype=np.uint32),
    }
    d["l0_wq"] = f32(np.asarray(inputs["L0_attn_wq"]).reshape(DM, DM))
    d["l0_wk"] = f32(np.asarray(inputs["L0_attn_wk"]).reshape(DM, DM))
    d["l0_wv"] = f32(np.asarray(inputs["L0_attn_wv"]).reshape(DM, DM))
    d["l0_wo"] = f32(np.asarray(inputs["L0_attn_wo"]).reshape(DM, DM))
    d["l0_wq_rot"] = _rot_halves(d["l0_wq"], 128)
    d["l0_wk_rot"] = _rot_halves(d["l0_wk"], 128)
    for nm_src, nm in (("L0_ffn_w_gate", "l0_wg"), ("L0_ffn_w_up", "l0_wu"),
                       ("L0_ffn_w_down", "l0_wd")):
        d[nm] = np.ascontiguousarray(np.asarray(inputs[nm_src], dtype=np.int32))
    for nm_src, nm in (("L1_attn_wq", "l1_wq"), ("L1_attn_wqc", "l1_wqc"),
                       ("L1_attn_wqr", "l1_wqr"), ("L1_attn_wkv", "l1_wkv"),
                       ("L1_attn_wkr", "l1_wkr"), ("L1_attn_wkc", "l1_wkc"),
                       ("L1_attn_wv", "l1_wv"), ("L1_attn_wo", "l1_wo"),
                       ("L1_ffn_W1", "l1_W1"), ("L1_ffn_W2", "l1_W2")):
        d[nm] = f32(inputs[nm_src])
    d["l1_wqr_rot"] = _rot_halves(d["l1_wqr"], 128)
    d["l1_wkr_rot"] = _rot_halves(d["l1_wkr"], 128)

    def cols(v):
        v = np.asarray(v, dtype=np.float32).reshape(-1)
        return np.ascontiguousarray(v.reshape(-1, 128).T)

    def rot_col(v):
        v = np.asarray(v, dtype=np.float32).reshape(128)
        return np.ascontiguousarray(np.concatenate([v[64:], v[:64]]).reshape(128, 1))

    d["b_bq"] = cols(inputs["L1_attn_bq"])
    d["b_bqc"] = cols(inputs["L1_attn_bqc"])
    d["b_bqr"] = cols(inputs["L1_attn_bqr"])
    d["b_bqr_rot"] = rot_col(inputs["L1_attn_bqr"])
    d["b_bkv"] = cols(inputs["L1_attn_bkv"])
    d["b_bkr"] = cols(inputs["L1_attn_bkr"])
    d["b_bkr_rot"] = rot_col(inputs["L1_attn_bkr"])
    d["b_bkc"] = cols(inputs["L1_attn_bkc"])
    d["b_b1"] = cols(inputs["L1_ffn_b1"])
    bv = np.asarray(inputs["L1_attn_bv"], dtype=np.float32).reshape(L1H, 64).T
    d["b_bv"] = np.ascontiguousarray(np.concatenate([bv, np.zeros((64, L1H), np.float32)]))
    d["r_bo"] = np.asarray(inputs["L1_attn_bo"], dtype=np.float32).reshape(1, DM)
    d["r_b2"] = np.asarray(inputs["L1_ffn_b2"], dtype=np.float32).reshape(1, DM)
    return d



class Sem:
    def __init__(self, ctx, nc, name):
        self.h = ctx.enter_context(nc.semaphore(name))
        self.n = 0


class B:
    pass


ENGS = ("t", "s", "v", "g", "y")


def em(eng, fn):
    B.streams[eng].append(fn)


def barrier(name):
    if not hasattr(B, "barsem"):
        B.barsem = {k: Sem(B.ctx, B.nc, "bar_" + k) for k in ENGS}
        B.barn = 0
    B.barn += 1
    n = B.barn
    for k in ENGS:
        qs = ()
        if hasattr(B, "qsem"):
            qs = tuple((B.qsem[q].h, B.qsem[q].n) for q in B.qsem
                       if q.startswith(k) and B.qsem[q].n > 0)
        def _f(e, _h=B.barsem[k].h, _k=k, _qs=qs):
            for (_qh, _qn) in _qs:
                e.wait_ge(_qh, _qn)
            if _k != "y":
                e.drain()
            e.nop().then_inc(_h, 1)
        em(k, _f)
    for k in ENGS:
        for k2 in ENGS:
            if k2 != k:
                em(k, lambda e, _h=B.barsem[k2].h, _n=n: e.wait_ge(_h, _n))


def dma(q, dst, src, waits=()):
    sem = B.qsem[q]
    eng = "y" if q.startswith("y") else "g"

    def f(e, _d=dst, _s=src, _h=sem.h, _w=tuple(waits)):
        for (sm, val) in _w:
            e.wait_ge(sm.h, val)
        e.dma_start(out=_d, in_=_s).then_inc(_h, 16)
    em(eng, f)
    sem.n += 16
    return sem.n


def t_mm(out, lhsT, rhs, start, stop, waits=(), inc=False):
    s_t = B.S["t"]

    def f(e, _o=out, _l=lhsT, _r=rhs, _a=start, _b=stop, _w=tuple(waits), _i=inc):
        for (sm, val) in _w:
            e.wait_ge(sm.h, val)
        ins = e.matmul(_o, _l, _r, start=_a, stop=_b)
        if _i:
            ins.then_inc(s_t.h, 1)
    em("t", f)
    if inc:
        s_t.n += 1
    return s_t.n


def vc(e, ins, eng="v"):
    sem = B.chain[eng]
    B.chain_n[eng] += 1
    ins.then_inc(sem.h, 1)
    e.wait_ge(sem.h, B.chain_n[eng])


def e_op(eng, fn, waits=(), inc=False):
    s = B.S[eng]

    def f(e, _f=fn, _w=tuple(waits), _i=inc):
        for (sm, val) in _w:
            e.wait_ge(sm.h, val)
        ins = _f(e)
        if _i:
            ins.then_inc(s.h, 1)
    em(eng, f)
    if inc:
        s.n += 1
    return s.n


def AP(t, off, pat):
    return bass.AP(t, off, pat)



_CACHE = {}

WB_COLS = 20480
W_SLOT = [0, 5120]
W_KIND = {"l0_wq": 0, "l0_wq_rot": 1024, "l0_wk": 2048, "l0_wk_rot": 3072,
          "l0_wv": 4096}
W_WO = 0
W_C = {"l1_wq": 0, "l1_wqr": 2048, "l1_wqr_rot": 3072, "l1_wkv": 4096,
       "l1_wkr": 6144, "l1_wkr_rot": 7168}
W_PS = [8192, 9728]
W_BO = 11264
W_B2 = 12288
W_CQ = 13312
W_QRG = 17408


def build_program():
    if "nc" in _CACHE:
        return _CACHE["nc"]
    nc = bass.Bass("TRN2", target_bir_lowering=False)
    ctx = ExitStack()
    B.nc, B.ctx = nc, ctx
    B.streams = {k: [] for k in ENGS}
    B.S = {k: Sem(ctx, nc, "p_" + k) for k in ("t", "s", "v", "g")}
    B.qsem = {k: Sem(ctx, nc, "q_" + k) for k in
              ("y", "g", "gw0", "gw1", "gu0", "gu1", "gd0", "gd1", "go",
               "ys", "yc0", "yc1", "gp0", "gp1", "yx")}
    B.ccs = Sem(ctx, nc, "q_cc")
    B.chain = {"v": Sem(ctx, nc, "ch_v"), "s": Sem(ctx, nc, "ch_s")}
    B.chain_n = {"v": 0, "s": 0}
    B._ptev = {0: 0, 1: 0}

    d = {}
    B.d = d

    def din(name, shape, dt):
        d[name] = nc.dram_tensor(name, shape, dt, kind="ExternalInput").ap()

    din("x_perm", [T, DM], F32)
    din("xT_perm", [DM, T], F32)
    din("tabk_a", [128, T], F32)
    din("tabk_b", [128, T], F32)
    din("tabq_a", [128, TOWN], F32)
    din("tabq_b", [128, TOWN], F32)
    din("masks", [128, 4 * QG], BF16)
    din("ident", [128, 128], BF16)
    din("ones_b", [128, 128], BF16)
    din("ones_f", [128, 128], F32)
    din("selA", [128, 144], BF16)
    din("selB", [128, 32], BF16)
    din("peer_off", [1, 1], U32)
    for nm in ("l0_wq", "l0_wk", "l0_wv", "l0_wo", "l0_wq_rot", "l0_wk_rot", "l1_wo"):
        din(nm, [DM, DM], F32)
    din("l0_wg", [FF0, DM], I32)
    din("l0_wu", [FF0, DM], I32)
    din("l0_wd", [DM, FF0], I32)
    din("l1_wq", [DM, DCV], F32)
    din("l1_wqc", [DCV, DM - DRP], F32)
    din("l1_wqr", [DM, DRP], F32)
    din("l1_wqr_rot", [DM, DRP], F32)
    din("l1_wkv", [DM, DCV], F32)
    din("l1_wkr", [DM, DRP], F32)
    din("l1_wkr_rot", [DM, DRP], F32)
    din("l1_wkc", [DCV, DM - DRP], F32)
    din("l1_wv", [DCV, DM], F32)
    din("l1_W1", [FF1, DM], F32)
    din("l1_W2", [DM, FF1], F32)
    for nm, n in (("b_bq", 2), ("b_bqc", 7), ("b_bqr", 1), ("b_bqr_rot", 1),
                  ("b_bkv", 2), ("b_bkr", 1), ("b_bkr_rot", 1), ("b_bkc", 7),
                  ("b_b1", 32), ("b_bv", 16)):
        din(nm, [128, n], F32)
    din("r_bo", [1, DM], F32)
    din("r_b2", [1, DM], F32)
    d["y"] = nc.dram_tensor("y", [TOWN, DM], F32, kind="ExternalOutput").ap()
    d["cc_in"] = nc.dram_tensor("cc_in", [TOWN, DM], BF16).ap()
    d["cc_out"] = nc.dram_tensor("cc_out", [2 * TOWN, DM], BF16).ap()
    d["spill_x"] = nc.dram_tensor("spill_x", [TOWN, DM], F32).ap()
    d["spill_o"] = nc.dram_tensor("spill_o", [DM, TOWN], BF16).ap()

    sb = {}
    B.sb = sb

    def sbuf(name, shape, dt):
        sb[name] = ctx.enter_context(nc.sbuf_tensor("s_" + name, shape, dt))
        return sb[name]

    B.AB = sbuf("ABar", [128, AB_COLS], BF16)
    B.AF = sbuf("AFar", [128, AF_COLS], F32)
    B.WB = sbuf("WBar", [128, WB_COLS], BF16)
    for nm, shape, dt in (
            ("ident", [128, 128], BF16), ("ones_b", [128, 128], BF16),
            ("ones_f", [128, 128], F32), ("selA", [128, 144], BF16),
            ("selB", [128, 32], BF16), ("masks", [128, 4 * QG], BF16),
            ("cbias", [128, 32], F32), ("nscr", [128, 8], F32),
            ("tiny", [128, 16], F32), ("tinyb", [128, 16], BF16),
            ("sumb", [128, 512], F32), ("lnscr", [128, 8], F32),
            ("u32s", [1, 1], U32), ("thr", [128, 4], F32),
            ("asum", [128, 66], F32)):
        sbuf(nm, shape, dt)
    for nm, n in (("b_bq", 2), ("b_bqc", 7), ("b_bqr", 1), ("b_bqr_rot", 1),
                  ("b_bkv", 2), ("b_bkr", 1), ("b_bkr_rot", 1), ("b_bkc", 7),
                  ("b_b1", 32), ("b_bv", 16)):
        sbuf(nm, [128, n], F32)

    ps = {}
    B.ps = ps
    ps["pS0"] = ctx.enter_context(nc.psum_tensor("pS0", [128, 1024], F32))
    ps["pS1"] = ctx.enter_context(nc.psum_tensor("pS1", [128, 1024], F32))
    ps["pAV"] = ctx.enter_context(nc.psum_tensor("pAV", [128, 512], F32))
    ps["pA"] = ctx.enter_context(nc.psum_tensor("pA", [128, 512], F32))
    ps["pB"] = ctx.enter_context(nc.psum_tensor("pB", [128, 512], F32))
    ps["pT"] = ctx.enter_context(nc.psum_tensor("pT", [128, 1024], BF16))

    _emit()

    with nc.Block() as block:
        @block.tensor
        def _(e):
            for fn in B.streams["t"]:
                fn(e)

        @block.scalar
        def _(e):
            for fn in B.streams["s"]:
                fn(e)

        @block.vector
        def _(e):
            for fn in B.streams["v"]:
                fn(e)

        @block.gpsimd
        def _(e):
            for fn in B.streams["g"]:
                fn(e)

        @block.sync
        def _(e):
            for fn in B.streams["y"]:
                fn(e)

    _CACHE["nc"] = nc
    return nc


def _load_l0_head_w(h, waits=()):
    slot = W_SLOT[h % 2]
    q = "gw0" if h % 2 == 0 else "gw1"
    n = 0
    for nm, ko in W_KIND.items():
        src = AP(B.d[nm].tensor, h * 128, [[1024, 128], [131072, 8], [1, 128]])
        dst = AP(B.WB, slot + ko, [[WB_COLS, 128], [128, 8], [1, 128]])
        n = dma(q, dst, src, waits=waits)
        waits = ()
    return n


def _load_w_mk(q, nm, ncols, nchunk, woff, waits=()):
    src = B.d[nm]
    dst = AP(B.WB, woff, [[WB_COLS, 128], [ncols, nchunk], [1, ncols]])
    sap = AP(src.tensor, 0, [[ncols, 128], [128 * ncols, nchunk], [1, ncols]])
    return dma(q, dst, sap, waits=waits)


def _emit():
    d, sb, AB, AFt = B.d, B.sb, B.AB, B.AF

    for nm in ("ident", "ones_b", "ones_f", "selA", "selB", "masks",
               "b_bq", "b_bqc", "b_bqr", "b_bqr_rot", "b_bkv", "b_bkr",
               "b_bkr_rot", "b_bkc", "b_b1", "b_bv"):
        dma("y", sb[nm][:], d[nm])
    dma("y", sb["u32s"][:], d["peer_off"])
    dma("y", AFt[:, TKA:TKA + T], d["tabk_a"])
    dma("y", AFt[:, TKB:TKB + T], d["tabk_b"])
    dma("y", AFt[:, TQA:TQA + TOWN], d["tabq_a"])
    dma("y", AFt[:, TQB:TQB + TOWN], d["tabq_b"])
    for mc in range(8):
        dma("g", AB[:, XT + mc * T:XT + (mc + 1) * T],
            d["xT_perm"][mc * 128:(mc + 1) * 128, :])
    n0 = _load_l0_head_w(0)
    n1 = _load_l0_head_w(1)

    nld, ngl = B.qsem["y"].n, B.qsem["g"].n
    for k in ENGS:
        em(k, lambda e, _h=B.qsem["y"].h, _n=nld: e.wait_ge(_h, _n))
        em(k, lambda e, _h=B.qsem["g"].h, _n=ngl: e.wait_ge(_h, _n))

    B._attn_prev_ev = None
    B._attn_ost_dma = None
    B._l0_wld = {0: n0, 1: n1}

    def cut(ph):
        if ATH_PH == ph:
            _dbg_out()
            return True
        return False

    if cut("none"):
        return
    nheads = L0H if ATH_PH not in ("h1", "h2") else int(ATH_PH[1])
    for h in range(nheads):
        _proj_head_l0(h)
        _bound_l0(h)
        _attn_head(0, h)
    if cut("l0a") or ATH_PH in ("h1", "h2") and (_dbg_out() or True):
        return
    _tail(0)
    if cut("tail0"):
        return
    _ffn0()
    if cut("ffn0"):
        return
    _exchange()
    if cut("xc"):
        return
    _l1_proj_common()
    if cut("l1p"):
        return
    for p in range(4):
        _l1_pass(p)
        if cut(f"pass{p}"):
            return
    _tail(1)
    if cut("tail1"):
        return
    _ffn1()
    if cut("ffn1"):
        return
    _final_ln_out()


def _dbg_out():
    barrier("dbg")
    for tt in range(16):
        dma("y", B.d["y"][tt * 128:(tt + 1) * 128, :],
            B.AF[:, STRM + tt * 1024: STRM + (tt + 1) * 1024])
    em("y", lambda e, _h=B.qsem["y"].h, _n=B.qsem["y"].n: e.wait_ge(_h, _n))
    em("g", lambda e, _h=B.qsem["g"].h, _n=B.qsem["g"].n: e.wait_ge(_h, _n))



def _rope_unit(kind, idx, h, unit, rope_done, extra_w=()):
    AB, WB, ps = B.AB, B.WB, B.ps
    S = B.S
    w = 512 if kind == "k" else 256
    slot = W_SLOT[h % 2]
    if kind == "k":
        rhs_off = idx * 512
        wofA, wofB = slot + W_KIND["l0_wk"], slot + W_KIND["l0_wk_rot"]
        dst = KTA + idx * 512
        tba, tbb = TKA + idx * 512, TKB + idx * 512
    else:
        rhs_off = (2 * idx) * BLK
        wofA, wofB = slot + W_KIND["l0_wq"], slot + W_KIND["l0_wq_rot"]
        dst = QTA + idx * 256
        tba, tbb = TQA + idx * 256, TQB + idx * 256
    if unit % 2 == 0:
        pA, pB = ps["pA"][:, 0:w], ps["pB"][:, 0:w]
    else:
        pA, pB = ps["pS0"][:, 0:w], ps["pS0"][:, 512:512 + w]
    waits = tuple(extra_w)
    if unit - 2 in rope_done and rope_done[unit - 2] is not None:
        waits += ((S["v"], rope_done[unit - 2]),)
    for mc in range(8):
        t_mm(pA, WB[:, wofA + mc * 128: wofA + mc * 128 + 128],
             AB[:, XT + mc * T + rhs_off: XT + mc * T + rhs_off + w],
             start=(mc == 0), stop=(mc == 7), waits=(waits if mc == 0 else ()))
    for mc in range(8):
        t_mm(pB, WB[:, wofB + mc * 128: wofB + mc * 128 + 128],
             AB[:, XT + mc * T + rhs_off: XT + mc * T + rhs_off + w],
             start=(mc == 0), stop=(mc == 7), inc=(mc == 7))
    tN = S["t"].n

    def f(e, _pA=pA, _pB=pB, _dst=dst, _ta=tba, _tb=tbb, _w=w, _u=unit):
        AFl = B.AF
        fsb = FS + (_u % 2) * 1024
        t1 = AFl[:, fsb:fsb + _w]
        t2 = AFl[:, fsb + 512:fsb + 512 + _w]
        e.tensor_tensor(t1, _pA, AFl[:, _ta:_ta + _w], ALU.mult)
        vc(e, e.tensor_tensor(t2, _pB, AFl[:, _tb:_tb + _w], ALU.mult))
        return e.tensor_tensor(B.AB[:, _dst:_dst + _w], t1, t2, ALU.add)
    rope_done[unit] = e_op("v", f, waits=((S["t"], tN),), inc=True)


def _proj_head_l0(h):
    AB, WB, ps = B.AB, B.WB, B.ps
    S = B.S
    wldw = ((B.qsem["gw0" if h % 2 == 0 else "gw1"], B._l0_wld[h]),) \
        if hasattr(B, "_l0_wld") and h in B._l0_wld else ()
    wldw += ((S["s"], S["s"].n), (S["v"], S["v"].n))
    vpad = e_op("v", lambda e: e.memset(
        AP(B.AB, VA + 64, [[AB_COLS, 128], [VW0, 32], [1, 1]]), 1.0), inc=True)
    rope_done = {}
    for tt in range(8):
        if tt == 0:
            rope_done[-2] = None
        _rope_unit("k", tt, h, tt, rope_done, extra_w=(wldw if tt == 0 else ()))
        wldw = ()
    for jb in range(8):
        _rope_unit("q", jb, h, 8 + jb, rope_done)
    B.l0_rope_last = rope_done[15]

    vev = {}
    for vb in range(8):
        half = vb % 2
        waits = ((S["s"], vev[vb - 2]),) if vb >= 2 else ()
        for i in range(4):
            tl = vb * 4 + i
            for mc in range(8):
                t_mm(ps["pS1"][:, half * 512 + i * 128: half * 512 + i * 128 + 128],
                     AB[:, XT + mc * T + tl * 128: XT + mc * T + tl * 128 + 128],
                     WB[:, W_SLOT[h % 2] + W_KIND["l0_wv"] + mc * 128: W_SLOT[h % 2] + W_KIND["l0_wv"] + mc * 128 + 128],
                     start=(mc == 0), stop=(mc == 7),
                     waits=(waits if (i == 0 and mc == 0) else ()),
                     inc=(i == 3 and mc == 7))
        tN = S["t"].n

        def f_ev(e, _vb=vb, _half=half):
            src1 = AP(B.ps["pS1"], _half * 512, [[1024, 128], [128, 4], [1, 64]])
            dst1 = AP(B.AB, VA + (_vb * 4) * VW0, [[AB_COLS, 128], [VW0, 4], [1, 64]])
            e.activation(dst1, src1, AF.Copy)
            src2 = AP(B.ps["pS1"], _half * 512 + 64, [[1024, 128], [128, 4], [1, 64]])
            dst2 = AP(B.AB, VA + (_vb * 4) * VW0 + 65, [[AB_COLS, 128], [VW0, 4], [1, 64]])
            return e.activation(dst2, src2, AF.Copy)
        vev[vb] = e_op("s", f_ev, waits=((S["t"], tN), (S["v"], vpad)), inc=True)
    B.l0_vev_last = vev[7]
    B._l0_projt = S["t"].n
    if h + 2 < L0H:
        if not hasattr(B, "_l0_wld"):
            B._l0_wld = {}
        B._l0_wld[h + 2] = _load_l0_head_w(h + 2, waits=((S["t"], S["t"].n),))



def _bound_l0(h):
    AB, ps, sb = B.AB, B.ps, B.sb
    S = B.S
    units = [("k", tc) for tc in range(8)] + [("q", tc) for tc in range(4)]
    mmN = {}
    for u, (kind, tc) in enumerate(units):
        base = (KTA if kind == "k" else QTA) + tc * 512
        w = ((S["v"], B.l0_rope_last),) if u == 0 else ()
        if u - 2 in mmN:
            w += ((S["t"], mmN[u - 2]),)
        eb = EA + (u % 2) * 1024

        def f_sq(e, _b=base, _e=eb):
            return e.tensor_tensor(B.AB[:, _e:_e + 512], B.AB[:, _b:_b + 512],
                                   B.AB[:, _b:_b + 512], ALU.mult)
        vN = e_op("v", f_sq, waits=w, inc=True)
        row = tc if kind == "k" else 8 + tc
        mmN[u] = t_mm(ps["pA"][0:12, 0:512], sb["selA"][:, row * 12: row * 12 + 12],
                      AB[:, eb:eb + 512], start=(u == 0), stop=(u == len(units) - 1),
                      waits=((S["v"], vN),), inc=True)
    tN = S["t"].n
    def f_mxc(e):
        vc(e, e.reduce_max(B.sb["nscr"][0:12, 0:1], B.ps["pA"][0:12, 0:512], AX.X))
        return e.tensor_copy(B.sb["tinyb"][0:12, 0:1], B.sb["nscr"][0:12, 0:1])
    vN = e_op("v", f_mxc, waits=((S["t"], tN),), inc=True)
    tN = e_op("t", lambda e: e.transpose(B.ps["pT"][0:1, 0:12],
                                         B.sb["tinyb"][0:12, 0:1],
                                         B.sb["ident"][0:12, 0:12]),
              waits=((S["v"], vN),), inc=True)

    def f_mx(e):
        e.reduce_max(B.sb["tiny"][0:1, 0:1], B.ps["pT"][0:1, 0:8], AX.X)
        vc(e, e.reduce_max(B.sb["tiny"][0:1, 1:2], B.ps["pT"][0:1, 8:12], AX.X))
        return e.tensor_tensor(B.sb["tiny"][0:1, 2:3], B.sb["tiny"][0:1, 0:1],
                               B.sb["tiny"][0:1, 1:2], ALU.mult)
    vN = e_op("v", f_mx, waits=((S["t"], tN),), inc=True)
    sN = e_op("s", lambda e: e.activation(B.sb["tinyb"][0:1, 3:4],
                                          B.sb["tiny"][0:1, 2:3], AF.Sqrt),
              waits=((S["v"], vN),), inc=True)
    tN = t_mm(ps["pB"][:, 0:1], sb["ones_b"][0:1, :], sb["tinyb"][0:1, 3:4],
              start=True, stop=True, waits=((S["s"], sN),), inc=True)
    e_op("v", lambda e, _h=h: e.tensor_scalar(B.sb["cbias"][:, _h:_h + 1],
                                              B.ps["pB"][:, 0:1], -1.004 / 128.0,
                                              None, ALU.mult),
         waits=((S["t"], tN),), inc=True)


def _bound_l1(p):
    AB, ps, sb = B.AB, B.ps, B.sb
    S = B.S
    units = ([("k", cc, tc) for cc in range(2) for tc in range(8)] +
             [("q", cc, tc) for cc in range(2) for tc in range(4)])
    first = True
    mmN = {}
    for nu, (kind, cc, tc) in enumerate(units):
        c = 2 * p + cc
        if kind == "k":
            base = (L1_KRG + tc * 512) if c == 7 else (L1_KT + cc * T + tc * 512)
            arr = B.AB
        else:
            base = (W_QRG + tc * 512) if c == 7 else (L1_QT + cc * TOWN + tc * 512)
            arr = B.WB if c == 7 else B.AB
        w = ((S["s"], S["s"].n),) if first else ()
        if nu - 2 in mmN:
            w += ((S["t"], mmN[nu - 2]),)
        eb = L1_E + (nu % 2) * 1024

        def f_sq(e, _a=arr, _b=base, _e=eb):
            return e.tensor_tensor(B.AB[:, _e:_e + 512], _a[:, _b:_b + 512],
                                   _a[:, _b:_b + 512], ALU.mult)
        vN = e_op("v", f_sq, waits=w, inc=True)
        v = (0 if kind == "k" else 2) + cc
        mmN[nu] = t_mm(ps["pA"][0:8, 0:512], sb["selB"][:, v * 8: v * 8 + 8],
                       AB[:, eb:eb + 512], start=first, stop=(nu == len(units) - 1),
                       waits=((S["v"], vN),), inc=True)
        first = False
    tN = S["t"].n
    def f_mxc(e):
        vc(e, e.reduce_max(B.sb["nscr"][0:8, 0:1], B.ps["pA"][0:8, 0:512], AX.X))
        return e.tensor_copy(B.sb["tinyb"][0:8, 0:1], B.sb["nscr"][0:8, 0:1])
    vN = e_op("v", f_mxc, waits=((S["t"], tN),), inc=True)
    tN = e_op("t", lambda e: e.transpose(B.ps["pT"][0:1, 0:8],
                                         B.sb["tinyb"][0:8, 0:1],
                                         B.sb["ident"][0:8, 0:8]),
              waits=((S["v"], vN),), inc=True)

    def f_pm(e):
        vc(e, e.tensor_copy(B.sb["tiny"][0:1, 8:16], B.ps["pT"][0:1, 0:8]))
        return e.tensor_tensor(B.sb["tiny"][0:1, 0:4], B.sb["tiny"][0:1, 8:12],
                               B.sb["tiny"][0:1, 12:16], ALU.mult)
    vN = e_op("v", f_pm, waits=((S["t"], tN),), inc=True)
    sN = e_op("s", lambda e: e.activation(B.sb["tinyb"][0:1, 4:8],
                                          B.sb["tiny"][0:1, 0:4], AF.Sqrt),
              waits=((S["v"], vN),), inc=True)
    tN = t_mm(ps["pB"][:, 0:4], sb["ones_b"][0:1, :], sb["tinyb"][0:1, 4:8],
              start=True, stop=True, waits=((S["s"], sN),), inc=True)
    e_op("v", lambda e: e.tensor_scalar(B.sb["cbias"][:, 16:20], B.ps["pB"][:, 0:4],
                                        -1.004 / 8.0, None, ALU.mult),
         waits=((S["t"], tN),), inc=True)



def _attn_head(layer, h, l1ctx=None):
    AB, ps, sb = B.AB, B.ps, B.sb
    S = B.S
    scale = (1.0 / 128.0) if layer == 0 else (1.0 / 8.0)
    col = h if layer == 0 else 16 + (h % 4)
    ebase = EA if layer == 0 else L1_E
    ebuf = [ebase, ebase + 1024]
    ost = OST0 if layer == 0 else L1_OST
    flat = [(j, b) for j in range(NQG) for b in range(j + 1)]
    exp_done = {}
    av_done = {}
    mask_done = {}
    prev_ev = B._attn_prev_ev
    ost_dma = B._attn_ost_dma
    head_w = ((S["s"], S["s"].n), (S["v"], S["v"].n))

    state = {"prev_ev": prev_ev, "ost_dma": ost_dma}
    defq = []

    def flush():
        for f in defq:
            f()
        del defq[:]

    def emit_av(fi, j, b, eb, gate):
        nb = j + 1
        bl = 4
        avcol = (j % 2) * 256
        if layer == 0:
            for half in range(2):
                for i in range(bl):
                    first = (b == 0 and i == 0)
                    last = (b == nb - 1 and i == bl - 1)
                    kt = 4 * b + i
                    co = kt * VW0 + (65 if half else 0)
                    vt = AB[:, VA + co: VA + co + (64 if half else 65)]
                    pp = (ps["pB"][0:64, avcol:avcol + 256] if half else
                          ps["pAV"][0:65, avcol:avcol + 256])
                    w = (gate,) if (i == 0 and half == 0) else ()
                    if first and half == 0 and state["prev_ev"] is not None:
                        w += (state["prev_ev"],)
                    t_mm(pp, vt, AB[:, eb + i * 256: eb + (i + 1) * 256],
                         start=first, stop=last, waits=w,
                         inc=(i == bl - 1 and half == 1))
        else:
            for i in range(bl):
                first = (b == 0 and i == 0)
                last = (b == nb - 1 and i == bl - 1)
                kt = 4 * b + i
                vlo = l1ctx["V_sl"](h, kt)
                w = (gate,) if i == 0 else ()
                if first and state["prev_ev"] is not None:
                    w += (state["prev_ev"],)
                t_mm(ps["pAV"][0:65, avcol:avcol + 256], vlo,
                     AB[:, eb + i * 256: eb + (i + 1) * 256],
                     start=first, stop=last, waits=w, inc=(i == bl - 1))
        av_done[fi] = S["t"].n
        if b != nb - 1:
            return
        tN_av = S["t"].n
        scol = 0

        def f_ln(e, _sc=scol, _ac=avcol, _o=ost):
            vc(e, e.activation(B.sb["sumb"][64:65, _sc:_sc + 256],
                               B.ps["pAV"][64:65, _ac:_ac + 256], AF.Ln), "s")
            return e.activation(B.AB[64:65, _o:_o + 256],
                                B.sb["sumb"][64:65, _sc:_sc + 256], AF.Exp,
                                scale=-1.0)
        sN = e_op("s", f_ln, waits=((S["t"], tN_av),), inc=True)
        tN = t_mm(ps["pA"][:, 0:256], sb["ones_b"][64:65, :],
                  AB[64:65, ost:ost + 256],
                  start=True, stop=True, waits=((S["s"], sN),), inc=True)
        w_ost = [(S["t"], tN)]
        if state["ost_dma"] is not None:
            w_ost.append((B.qsem["go"], state["ost_dma"]))

        def f_on(e, _ac=avcol, _h=h, _lay=layer, _o=ost):
            rb = B.sb["sumb"][0:64, 0:256]
            vc(e, e.tensor_copy(rb, B.ps["pA"][0:64, 0:256]))
            if _lay == 0:
                e.tensor_tensor(B.AB[0:64, _o:_o + 256],
                                B.ps["pAV"][0:64, _ac:_ac + 256], rb, ALU.mult)
                return e.tensor_tensor(B.AB[0:64, _o + 256:_o + 512],
                                       B.ps["pB"][0:64, _ac:_ac + 256], rb, ALU.mult)
            vc(e, e.tensor_tensor(B.AB[0:64, _o:_o + 256],
                                  B.ps["pAV"][0:64, _ac:_ac + 256], rb, ALU.mult))
            return e.tensor_scalar(B.AB[0:64, _o:_o + 256],
                                   B.AB[0:64, _o:_o + 256],
                                   B.sb["b_bv"][0:64, _h:_h + 1], None, ALU.add)
        vN = e_op("v", f_on, waits=tuple(w_ost), inc=True)
        state["prev_ev"] = (S["v"], vN)
        if layer == 0:
            r0 = h * 128
            dma("go", B.d["spill_o"][r0:r0 + 64, j * 256:(j + 1) * 256],
                AB[0:64, ost:ost + 256], waits=((S["v"], vN),))
            state["ost_dma"] = dma("go",
                                   B.d["spill_o"][r0 + 64:r0 + 128, j * 256:(j + 1) * 256],
                                   AB[0:64, ost + 256:ost + 512])
        else:
            r0 = h * 64
            state["ost_dma"] = dma("go",
                                   B.d["spill_o"][r0:r0 + 64, j * 256:(j + 1) * 256],
                                   AB[0:64, ost:ost + 256], waits=((S["v"], vN),))

    for fi, (j, b) in enumerate(flat):
        nb = j + 1
        pS = ps["pS0"] if fi % 2 == 0 else ps["pS1"]
        bl = 4
        for i in range(bl):
            kt = 4 * b + i
            if layer == 0:
                lhsT = AB[:, KTA + kt * 128: KTA + kt * 128 + 128]
                rhs = AB[:, QTA + j * 256: QTA + j * 256 + 256]
            else:
                lhsT = l1ctx["KT_sl"](h, kt)
                rhs = l1ctx["QT_sl"](h, j)
            w = ()
            if i == 0:
                if fi - 2 in exp_done:
                    w += ((S["s"], exp_done[fi - 2]),)
                    if fi - 2 in mask_done:
                        w += ((S["v"], mask_done[fi - 2]),)
                if fi == 0:
                    w += head_w
            t_mm(pS[:, i * 256:(i + 1) * 256], lhsT, rhs,
                 start=True, stop=True, waits=w, inc=(i == bl - 1))
        tN = S["t"].n
        flush()
        eb = ebuf[fi % 2]
        w_exp = [(S["t"], tN)]
        if fi - 2 in av_done:
            w_exp.append((S["t"], av_done[fi - 2]))

        def f_exp(e, _pS=pS, _eb=eb, _c=col, _sc=scale):
            return e.activation(B.AB[:, _eb:_eb + 1024], _pS[:, 0:1024],
                                AF.Exp, bias=B.sb["cbias"][:, _c:_c + 1], scale=_sc)
        sN = e_op("s", f_exp, waits=tuple(w_exp), inc=True)
        exp_done[fi] = sN
        if b == nb - 1:
            def f_mask(e, _eb=eb):
                return e.tensor_tensor(B.AB[:, _eb:_eb + 1024],
                                       B.AB[:, _eb:_eb + 1024],
                                       B.sb["masks"][:, 0:1024], ALU.mult)
            vN = e_op("v", f_mask, waits=((S["s"], sN),), inc=True)
            mask_done[fi] = vN
            gate = (S["v"], vN)
        else:
            gate = (S["s"], sN)
        defq.append(lambda _fi=fi, _j=j, _b=b, _eb=eb, _g=gate:
                    emit_av(_fi, _j, _b, _eb, _g))
    flush()
    if layer == 0:
        if not hasattr(B, "_l0_go_n"):
            B._l0_go_n = {}
        B._l0_go_n[h] = B.qsem["go"].n
    B._attn_prev_ev = state["prev_ev"]
    B._attn_ost_dma = state["ost_dma"]


def _attn_head_pair(h0, l1ctx):
    AB, ps, sb = B.AB, B.ps, B.sb
    S = B.S
    scale = 1.0 / 8.0
    flat = [(j, b) for j in range(NQG) for b in range(j + 1)]
    exp_done = {}
    av_done = {}
    mask_done = {}
    state = B._l1_pair_state
    head_w = ((S["s"], S["s"].n), (S["v"], S["v"].n))
    pSs = (ps["pS0"], ps["pS1"])
    pAVs = (ps["pAV"], ps["pB"])
    defq = []

    def flush():
        for f in defq:
            f()
        del defq[:]

    def emit_av(fi, j, b, hp, gate):
        h = h0 + hp
        nb = j + 1
        avcol = (j % 2) * 256
        eb = L1_E + ((fi % 2) * 2 + hp) * 1024
        pp = pAVs[hp]
        for i in range(4):
            first = (b == 0 and i == 0)
            last = (b == nb - 1 and i == 3)
            kt = 4 * b + i
            vlo = l1ctx["V_sl"](h, kt)
            w = (gate,) if i == 0 else ()
            if first and state["prev_ev"][hp] is not None:
                w += (state["prev_ev"][hp],)
            t_mm(pp[0:65, avcol:avcol + 256], vlo,
                 AB[:, eb + i * 256: eb + (i + 1) * 256],
                 start=first, stop=last, waits=w, inc=(i == 3))
        av_done[(fi, hp)] = S["t"].n
        if b != nb - 1:
            return
        tN_av = S["t"].n
        ost = L1_OST + hp * 512
        if hp == 0:
            ln_t = sb["sumb"][64:65, 0:256]
            rb_t = sb["sumb"][0:64, 0:256]
        else:
            ln_t = B.AF[64:65, FS:FS + 256]
            rb_t = B.AF[0:64, FS + 512:FS + 768]
        rc_t = AB[64:65, ost + 256:ost + 512]

        def f_ln(e, _ac=avcol, _pp=pp, _ln=ln_t, _rc=rc_t):
            vc(e, e.activation(_ln, _pp[64:65, _ac:_ac + 256], AF.Ln), "s")
            return e.activation(_rc, _ln, AF.Exp, scale=-1.0)
        sN = e_op("s", f_ln, waits=((S["t"], tN_av),), inc=True)
        wb = [(S["s"], sN)]
        if state["fon"] is not None:
            wb.append((S["v"], state["fon"]))
        tN = t_mm(ps["pA"][:, hp * 256: hp * 256 + 256],
                  sb["ones_b"][64:65, :], rc_t,
                  start=True, stop=True, waits=tuple(wb), inc=True)
        qn = "go" if hp == 0 else "gp0"
        w_ost = [(S["t"], tN)]
        if state["ost_dma"][hp] is not None:
            w_ost.append((B.qsem[qn], state["ost_dma"][hp]))

        def f_on(e, _ac=avcol, _h=h, _o=ost, _hp=hp, _pp=pp, _rb=rb_t):
            vc(e, e.tensor_copy(_rb, B.ps["pA"][0:64, _hp * 256:_hp * 256 + 256]))
            vc(e, e.tensor_tensor(B.AB[0:64, _o:_o + 256],
                                  _pp[0:64, _ac:_ac + 256], _rb, ALU.mult))
            return e.tensor_scalar(B.AB[0:64, _o:_o + 256],
                                   B.AB[0:64, _o:_o + 256],
                                   B.sb["b_bv"][0:64, _h:_h + 1], None, ALU.add)
        vN = e_op("v", f_on, waits=tuple(w_ost), inc=True)
        state["prev_ev"][hp] = (S["v"], vN)
        state["fon"] = vN
        r0 = h * 64
        state["ost_dma"][hp] = dma(
            qn, B.d["spill_o"][r0:r0 + 64, j * 256:(j + 1) * 256],
            AB[0:64, ost:ost + 256], waits=((S["v"], vN),))

    for fi, (j, b) in enumerate(flat):
        nb = j + 1
        newq = []
        for hp in range(2):
            h = h0 + hp
            pS = pSs[hp]
            for i in range(4):
                kt = 4 * b + i
                lhsT = l1ctx["KT_sl"](h, kt)
                rhs = l1ctx["QT_sl"](h, j)
                w = ()
                if i == 0:
                    if (fi - 1, hp) in exp_done:
                        w += ((S["s"], exp_done[(fi - 1, hp)]),)
                        if (fi - 1, hp) in mask_done:
                            w += ((S["v"], mask_done[(fi - 1, hp)]),)
                    if fi == 0:
                        w += head_w
                t_mm(pS[:, i * 256:(i + 1) * 256], lhsT, rhs,
                     start=True, stop=True, waits=w, inc=(i == 3))
            tNs = S["t"].n
            eb = L1_E + ((fi % 2) * 2 + hp) * 1024
            col = 16 + (h % 4)
            w_exp = [(S["t"], tNs)]
            if (fi - 2, hp) in av_done:
                w_exp.append((S["t"], av_done[(fi - 2, hp)]))

            def f_exp(e, _pS=pS, _eb=eb, _c=col):
                return e.activation(B.AB[:, _eb:_eb + 1024], _pS[:, 0:1024],
                                    AF.Exp, bias=B.sb["cbias"][:, _c:_c + 1],
                                    scale=scale)
            sN = e_op("s", f_exp, waits=tuple(w_exp), inc=True)
            exp_done[(fi, hp)] = sN
            if b == nb - 1:
                def f_mask(e, _eb=eb):
                    return e.tensor_tensor(B.AB[:, _eb:_eb + 1024],
                                           B.AB[:, _eb:_eb + 1024],
                                           B.sb["masks"][:, 0:1024], ALU.mult)
                vN = e_op("v", f_mask, waits=((S["s"], sN),), inc=True)
                mask_done[(fi, hp)] = vN
                gate = (S["v"], vN)
            else:
                gate = (S["s"], sN)
            newq.append(lambda _fi=fi, _j=j, _b=b, _hp=hp, _g=gate:
                        emit_av(_fi, _j, _b, _hp, _g))
        flush()
        defq.extend(newq)
    flush()



def _tail(layer):
    AB, WB, ps, sb, AFl = B.AB, B.WB, B.ps, B.sb, B.AF
    d = B.d
    S = B.S
    if layer == 0:
        for tt in range(16):
            jj = tt // 2
            src = d["x_perm"][(2 * jj) * BLK + (tt % 2) * 128:
                              (2 * jj) * BLK + (tt % 2) * 128 + 128, :]
            w = ((S["v"], B.l0_rope_last),) if tt == 0 else ()
            dma("y" if tt % 2 == 0 else "ys",
                AFl[:, STRM + tt * 1024: STRM + (tt + 1) * 1024], src, waits=w)
        wT = ((S["t"], B._l0_projt),)
        for oc in range(8):
            dma("y" if oc % 2 == 0 else "ys",
                AB[:, ORB + oc * 2048: ORB + (oc + 1) * 2048],
                d["spill_o"][oc * 128:(oc + 1) * 128, :],
                waits=wT + ((B.qsem["go"], B._l0_go_n[oc]),))
            wT = ()
    else:
        em("y", lambda e, _h=B.qsem["yx"].h, _n=B.qsem["yx"].n: e.wait_ge(_h, _n))
        for tt in range(16):
            src = d["spill_x"][tt * 128:(tt + 1) * 128, :]
            w = ((S["v"], B._l1p_v_end),) if tt == 0 else ()
            dma("y" if tt % 2 == 0 else "ys",
                AFl[:, STRM + tt * 1024: STRM + (tt + 1) * 1024], src, waits=w)
    barrier(f"tl{layer}")
    B._attn_prev_ev = None
    B._attn_ost_dma = None

    wof = W_WO
    if layer == 0:
        _load_w_mk("g", "l0_wo", 1024, 8, wof)
    if layer == 1:
        em("y", lambda e, _h=B.qsem["go"].h, _n=B.qsem["go"].n: e.wait_ge(_h, _n))
        em("y", lambda e, _h=B.qsem["gp0"].h, _n=B.qsem["gp0"].n: e.wait_ge(_h, _n))
        for oc in range(8):
            dma("y" if oc % 2 == 0 else "ys",
                AB[:, ORB + oc * 2048: ORB + (oc + 1) * 2048],
                d["spill_o"][oc * 128:(oc + 1) * 128, :])
    for k in ("t", "s", "v"):
        for qn in ("y", "ys", "g", "gw0", "gw1"):
            em(k, lambda e, _h=B.qsem[qn].h, _n=B.qsem[qn].n: e.wait_ge(_h, _n))

    B._tail_v = {}
    if ATH_TC == "dumpo":
        em("y", lambda e, _h=B.qsem["g"].h, _n=B.qsem["g"].n: e.wait_ge(_h, _n))
        for i in range(8):
            ld = dma("y", AB[:, ABM_TMP: ABM_TMP + 2048] if False else AB[:, 0:2048],
                     d["spill_o"][i * 128:(i + 1) * 128, :],
                     waits=())
            def f_cp(e, _i=i):
                return e.tensor_copy(B.AF[:, STRM + _i * 2048: STRM + (_i + 1) * 2048],
                                     B.AB[:, 0:2048])
            vN = e_op("v", f_cp, waits=((B.qsem["y"], ld),), inc=True)
            em("y", lambda e, _h=S["v"].h, _n=vN: e.wait_ge(_h, _n))
        for tt in range(16):
            dma("y", B.d["y"][tt * 128:(tt + 1) * 128, :],
                B.AF[:, STRM + tt * 1024: STRM + (tt + 1) * 1024])
        em("y", lambda e, _h=B.qsem["y"].h, _n=B.qsem["y"].n: e.wait_ge(_h, _n))
        barrier(f"tle{layer}")
        return
    if ATH_TC in ("w_t", "w_s", "w_v"):
        sem = {"w_t": S["t"], "w_s": S["s"], "w_v": S["v"]}[ATH_TC]
        em("y", lambda e, _h=sem.h, _n=sem.n: e.wait_ge(_h, _n))
        barrier(f"tle{layer}")
        return
    if ATH_TC == "ld":
        barrier(f"tle{layer}")
        return
    ntt = 16
    if ATH_TC.startswith("mm") and ATH_TC[2:].isdigit():
        ntt = int(ATH_TC[2:])
    if ATH_TC == "cponly":
        tN = e_op("t", lambda e: e.nop(), inc=True)
        e_op("v", lambda e: e.tensor_copy(B.AF[:, FS:FS + 512], ps["pS0"][:, 0:512]),
             waits=((S["t"], tN),), inc=True)
        barrier(f"tle{layer}")
        return
    B._tail_defer = []
    for tt in range(ntt):
        pL = ps["pS0"][:, 0:512] if tt % 2 == 0 else ps["pS1"][:, 0:512]
        pR = ps["pS0"][:, 512:1024] if tt % 2 == 0 else ps["pS1"][:, 512:1024]
        w = ((S["v"], B._tail_v[tt - 2]),) if tt >= 2 else ()
        for half, pp in ((0, pL), (1, pR)):
            for oc in range(8):
                t_mm(pp, AB[:, ORB + oc * 2048 + tt * 128: ORB + oc * 2048 + tt * 128 + 128],
                     WB[:, wof + oc * 1024 + half * 512: wof + oc * 1024 + half * 512 + 512],
                     start=(oc == 0), stop=(oc == 7 and layer == 0),
                     waits=(w if (half == 0 and oc == 0) else ()),
                     inc=(oc == 7 and layer == 0))
            if layer == 1:
                t_mm(pp, sb["ones_b"][0:1, :],
                     WB[0:1, W_BO + half * 512: W_BO + half * 512 + 512],
                     start=False, stop=True, inc=True)
        tN = B.S["t"].n
        for f in B._tail_defer:
            f()
        del B._tail_defer[:]
        if ATH_TC.startswith("mm"):
            if ATH_TC == "mmnc":
                B._tail_v[tt] = e_op("v", lambda e: e.nop(), waits=((S["t"], tN),), inc=True)
            elif ATH_TC == "mmd":
                def f_d(e, _pL=pL, _pR=pR, _tt=tt):
                    stc = STRM + _tt * 1024
                    e.tensor_copy(B.AF[:, stc:stc + 512], _pL)
                    return e.tensor_copy(B.AF[:, stc + 512:stc + 1024], _pR)
                B._tail_v[tt] = e_op("v", f_d, waits=((S["t"], tN),), inc=True)
            else:
                B._tail_v[tt] = e_op("v", lambda e, _pL=pL: e.tensor_copy(
                    B.AF[:, FS:FS + 512], _pL), waits=((S["t"], tN),), inc=True)
        else:
            _ln_tile(tt, tN, pL, pR, mode="attn")
    for f in B._tail_defer:
        f()
    del B._tail_defer[:]
    barrier(f"tle{layer}")


def _ln_tile(tt, tN, pL, pR, mode):
    AB, ps, sb, AFl = B.AB, B.ps, B.sb, B.AF
    S = B.S
    st = STRM + tt * 1024

    if ATH_TC == "lnr":
        def f_r(e, _st=st, _pL=pL, _pR=pR):
            e.tensor_tensor(AFl[:, _st:_st + 512], AFl[:, _st:_st + 512], _pL, ALU.add)
            return e.tensor_tensor(AFl[:, _st + 512:_st + 1024],
                                   AFl[:, _st + 512:_st + 1024], _pR, ALU.add)
        B._tail_v[tt] = e_op("v", f_r, waits=(((S["t"], tN),) if tN else ()), inc=True)
        return
    if ATH_TC == "lnr2":
        def f_r(e, _st=st, _pL=pL, _pR=pR):
            e.tensor_tensor(AFl[:, _st:_st + 512], _pL, AFl[:, _st:_st + 512], ALU.add)
            return e.tensor_tensor(AFl[:, _st + 512:_st + 1024], _pR,
                                   AFl[:, _st + 512:_st + 1024], ALU.add)
        B._tail_v[tt] = e_op("v", f_r, waits=(((S["t"], tN),) if tN else ()), inc=True)
        return
    if ATH_TC == "lnr4":
        if tt % 2 == 0:
            def f_r(e, _st=st):
                return e.tensor_scalar(AFl[:, _st:_st + 1024], AFl[:, _st:_st + 1024],
                                       1.0, None, ALU.mult)
        else:
            def f_r(e, _st=st, _pL=pL):
                e.tensor_copy(AFl[:, FS:FS + 512], _pL)
                return e.tensor_tensor(AFl[:, _st:_st + 512], AFl[:, _st:_st + 512],
                                       AFl[:, FS:FS + 512], ALU.add)
        B._tail_v[tt] = e_op("v", f_r, waits=(((S["t"], tN),) if tN else ()), inc=True)
        return
    if ATH_TC == "lnr3":
        def f_r(e, _st=st, _pL=pL, _pR=pR):
            e.scalar_tensor_tensor(AFl[:, _st:_st + 512], _pL, 1.0,
                                   AFl[:, _st:_st + 512], ALU.mult, ALU.add)
            return e.scalar_tensor_tensor(AFl[:, _st + 512:_st + 1024], _pR, 1.0,
                                          AFl[:, _st + 512:_st + 1024], ALU.mult, ALU.add)
        B._tail_v[tt] = e_op("v", f_r, waits=(((S["t"], tN),) if tN else ()), inc=True)
        return

    def f_res(e, _st=st, _pL=pL, _pR=pR, _m=mode):
        if _m == "attn":
            e.tensor_tensor(AFl[:, _st:_st + 512], AFl[:, _st:_st + 512], _pL, ALU.add)
            vc(e, e.tensor_tensor(AFl[:, _st + 512:_st + 1024],
                                  AFl[:, _st + 512:_st + 1024], _pR, ALU.add))
        vc(e, e.reduce_sum(sb["lnscr"][:, 0:1], AFl[:, _st:_st + 1024], AX.X))
        vc(e, e.tensor_scalar(sb["lnscr"][:, 1:2], sb["lnscr"][:, 0:1], 1.0 / DM,
                              None, ALU.mult))
        return e.tensor_scalar(AFl[:, _st:_st + 1024], AFl[:, _st:_st + 1024],
                               sb["lnscr"][:, 1:2], None, ALU.subtract)
    vN = e_op("v", f_res, waits=(((S["t"], tN),) if tN else ()), inc=True)
    sN = e_op("s", lambda e, _st=st: e.activation(AFl[:, FS:FS + 1024],
                                                  AFl[:, _st:_st + 1024], AF.Square,
                                                  accum_out=sb["lnscr"][:, 2:3]),
              waits=((S["v"], vN),), inc=True)

    def f_rstd(e):
        vc(e, e.tensor_scalar(sb["lnscr"][:, 3:4], sb["lnscr"][:, 2:3], 1.0 / DM,
                              EPS, ALU.mult, ALU.add))
        return e.reciprocal(sb["lnscr"][:, 4:5], sb["lnscr"][:, 3:4])
    vN = e_op("v", f_rstd, waits=((S["s"], sN),), inc=True)
    sN = e_op("s", lambda e: e.activation(sb["lnscr"][:, 5:6], sb["lnscr"][:, 4:5],
                                          AF.Sqrt),
              waits=((S["v"], vN),), inc=True)
    if ATH_TC == "lnp":
        def f_p(e, _st=st):
            return e.tensor_copy(AFl[:, _st:_st + 8], sb["lnscr"][:, 0:8])
        B._tail_v[tt] = e_op("v", f_p, waits=((S["s"], sN),), inc=True)
        return
    vN = e_op("v", lambda e, _st=st: e.tensor_scalar(AFl[:, _st:_st + 1024],
                                                     AFl[:, _st:_st + 1024],
                                                     sb["lnscr"][:, 5:6], None,
                                                     ALU.mult),
              waits=((S["s"], sN),), inc=True)
    B._tail_v[tt] = vN

    if mode == "ffn1":
        dma("y", B.d["y"][tt * 128:(tt + 1) * 128, :], AFl[:, st:st + 1024],
            waits=((S["v"], vN),))
        return
    if mode == "ffn0":
        dma("yx", B.d["spill_x"][tt * 128:(tt + 1) * 128, :], AFl[:, st:st + 1024],
            waits=((S["v"], vN),))
        dma("g", B.d["cc_in"][tt * 128:(tt + 1) * 128, :], AFl[:, st:st + 1024],
            waits=((S["v"], vN),))
        return
    if ATH_TC == "ln":
        return
    stg = EA + (tt % 2) * 1024
    sN = e_op("s", lambda e, _st=st, _sg=stg: e.activation(
        AB[:, _sg:_sg + 1024], AFl[:, _st:_st + 1024], AF.Copy),
        waits=((S["v"], vN),), inc=True)
    jj = tt // 2
    poscol = jj * 256 + (tt % 2) * 128

    def emit_tp(_sN=sN, _stg=stg, _pc=poscol):
        for half in range(2):
            tNt = 0
            for mb in range(4):
                mc = half * 4 + mb
                w0 = ()
                if mb == 0:
                    if hasattr(B, "_pt_ev"):
                        w0 = ((S["v"], B._pt_ev),)
                    if half == 0:
                        w0 += ((S["s"], _sN),)
                tNt = e_op("t", lambda e, _mc=mc, _mb=mb, _sg=_stg, _hf=half:
                           e.transpose(B.ps["pT"][:, _hf * 512 + _mb * 128: _hf * 512 + (_mb + 1) * 128],
                                       B.AB[:, _sg + _mc * 128: _sg + _mc * 128 + 128],
                                       B.sb["ident"][:]),
                           waits=w0, inc=(mb == 3))

            def f_tev(e, _half=half, _pc2=_pc):
                src = AP(B.ps["pT"], _half * 512, [[1024, 128], [128, 4], [1, 128]])
                dst = AP(B.AB, XTO + (_half * 4) * 2048 + _pc2,
                         [[AB_COLS, 128], [2048, 4], [1, 128]])
                return e.tensor_copy(dst, src)
            B._pt_ev = e_op("v", f_tev, waits=((S["t"], tNt),), inc=True)
    B._tail_defer.append(emit_tp)



def _ffn0():
    AB, WB, ps, sb, AFl = B.AB, B.WB, B.ps, B.sb, B.AF
    S = B.S
    d = B.d
    barrier("f0")
    stats = _tern_stats_units()
    _statn = [0]

    def stats_slot():
        if _statn[0] < len(stats):
            stats[_statn[0]]()
            _statn[0] += 1
    gu_tp = {}
    d_tp = {}
    comp_v = {}
    stt_v = {}
    acc_v = {}
    guu = 0
    duu = 0

    def mk_prep(c):
        ws = F0_WS[c % 2]
        units = []
        for wi, nm in ((0, "l0_wg"), (1, "l0_wu")):
            for st_i in range(4):
                g_ = c * 8 + wi * 4 + st_i

                def u_gu(_c=c, _ws=ws, _wi=wi, _st=st_i, _g=g_, _nm=nm):
                    gs = F0_GS[_g % 2]
                    wld = ((S["t"], gu_tp[_g - 2]),) if _g - 2 >= 0 else ()
                    ld = dma("gu0" if _g % 2 == 0 else "gu1", AB[:, gs:gs + 1024],
                             AP(d[_nm].tensor, (_c * 512 + _st * 128) * 1024,
                                [[1024, 128], [1, 1024]]), waits=wld)
                    tN = 0
                    for mg in range(2):
                        par = (_g * 2 + mg) % 2
                        w0 = ((S["v"], max(B._ptev.values())),)
                        if mg == 0:
                            w0 += ((B.qsem["gu0" if _g % 2 == 0 else "gu1"], ld),)
                        for mb in range(4):
                            mc = mg * 4 + mb
                            tN = e_op("t", lambda e, _gs=gs, _mc=mc, _mb=mb, _p=par:
                                      e.transpose(B.ps["pT"][:, _p * 512 + _mb * 128: _p * 512 + (_mb + 1) * 128],
                                                  B.AB[:, _gs + _mc * 128: _gs + _mc * 128 + 128],
                                                  B.sb["ident"][:]),
                                      waits=(w0 if mb == 0 else ()),
                                      inc=(mb == 3))

                        def f_ev(e, _ws2=_ws, _wi2=_wi, _st2=_st, _mg=mg, _p=par):
                            src = AP(B.ps["pT"], _p * 512, [[1024, 128], [128, 4], [1, 128]])
                            dst = AP(B.AB, _ws2 + _wi2 * 4096 + (_mg * 4) * 512 + _st2 * 128,
                                     [[AB_COLS, 128], [512, 4], [1, 128]])
                            return e.tensor_copy(dst, src)
                        B._ptev[par] = e_op("v", f_ev, waits=((S["t"], tN),), inc=True)
                    gu_tp[_g] = tN
                    stats_slot()
                units.append(u_gu)
        for dsub in range(8):
            d_ = c * 8 + dsub

            def u_d(_c=c, _ws=ws, _dsub=dsub, _d=d_):
                dsb = F0_DS[_d % 2]
                wld = ((S["t"], d_tp[_d - 2]),) if _d - 2 >= 0 else ()
                ld = dma("gd0" if _d % 2 == 0 else "gd1", AB[:, dsb:dsb + 512],
                         AP(d["l0_wd"].tensor, _dsub * 128 * FF0 + _c * 512,
                            [[FF0, 128], [1, 512]]), waits=wld)
                tN = 0
                par = _d % 2
                w0 = ((S["v"], max(B._ptev.values())),
                      (B.qsem["gd0" if _d % 2 == 0 else "gd1"], ld))
                for hb in range(4):
                    tN = e_op("t", lambda e, _dsb=dsb, _hb=hb, _p=par:
                              e.transpose(B.ps["pT"][:, _p * 512 + _hb * 128: _p * 512 + (_hb + 1) * 128],
                                          B.AB[:, _dsb + _hb * 128: _dsb + _hb * 128 + 128],
                                          B.sb["ident"][:]),
                              waits=(w0 if hb == 0 else ()),
                              inc=(hb == 3))

                def f_evd(e, _ws2=_ws, _dsub2=_dsub, _p=par):
                    src = AP(B.ps["pT"], _p * 512, [[1024, 128], [128, 4], [1, 128]])
                    dst = AP(B.AB, _ws2 + 8192 + _dsub2 * 128,
                             [[AB_COLS, 128], [1024, 4], [1, 128]])
                    return e.tensor_copy(dst, src)
                B._ptev[par] = e_op("v", f_evd, waits=((S["t"], tN),), inc=True)
                d_tp[_d] = tN
                stats_slot()
            units.append(u_d)
        return units

    prep = [mk_prep(c) for c in range(6)]
    prep_i = [0] * 6
    f0pd = {}

    def prep_step(c):
        if c < 6 and prep_i[c] < len(prep[c]):
            prep[c][prep_i[c]]()
            prep_i[c] += 1
            if prep_i[c] == len(prep[c]):
                f0pd[c] = S["v"].n

    def prep_finish(c):
        while c < 6 and prep_i[c] < len(prep[c]):
            prep_step(c)

    for c in range(6):
        ws = F0_WS[c % 2]
        prep_finish(c)
        tp_v = f0pd[c]

        for tq in range(8):
            for hs in range(4):
                bank = hs % 2
                pg = ps["pS0"][:, bank * 512: bank * 512 + 256]
                pu = ps["pS0"][:, bank * 512 + 256: bank * 512 + 512]
                w0 = ()
                if tq == 0 and hs == 0:
                    w0 = ((S["v"], tp_v),)
                key = (c, tq, hs - 2) if hs >= 2 else (c, tq - 1, hs + 2)
                if key in stt_v:
                    w0 += ((S["v"], stt_v[key]),)
                for mc in range(8):
                    t_mm(pg, AB[:, ws + (mc * 4 + hs) * 128: ws + (mc * 4 + hs) * 128 + 128],
                         AB[:, XTO + mc * 2048 + tq * 256: XTO + mc * 2048 + tq * 256 + 256],
                         start=(mc == 0), stop=(mc == 7),
                         waits=(w0 if mc == 0 else ()))
                for mc in range(8):
                    t_mm(pu, AB[:, ws + 4096 + (mc * 4 + hs) * 128: ws + 4096 + (mc * 4 + hs) * 128 + 128],
                         AB[:, XTO + mc * 2048 + tq * 256: XTO + mc * 2048 + tq * 256 + 256],
                         start=(mc == 0), stop=(mc == 7), inc=(mc == 7))
                tN = S["t"].n
                w_sil = [(S["t"], tN)]
                if (c, tq, hs, "stt") in stt_v:
                    pass
                prevstt = stt_v.get((c, tq, hs - 1), stt_v.get((c, tq - 1, 3) if hs == 0 else None))
                if prevstt:
                    w_sil.append((S["v"], prevstt))
                sN = e_op("s", lambda e, _pg=pg: e.activation(
                    B.AB[:, F0_SIL:F0_SIL + 256], _pg, AF.Silu, scale=1.0 / 127.0),
                    waits=tuple(w_sil), inc=True)
                vN = e_op("v", lambda e, _pu=pu, _hs=hs: e.scalar_tensor_tensor(
                    B.AB[:, F0_ST + _hs * 256: F0_ST + (_hs + 1) * 256], _pu,
                    1.0 / 127.0, B.AB[:, F0_SIL:F0_SIL + 256], ALU.mult, ALU.mult),
                    waits=((S["s"], sN),), inc=True)
                stt_v[(c, tq, hs)] = vN
                prep_step(c + 1)
            st_v = S["v"].n
            for ti in range(2):
                tt = tq * 2 + ti
                wdn = ((S["v"], st_v),)
                pk = (c, tq, ti - 1) if ti else ((c, tq - 1, 1) if tq else (c - 1, 7, 1))
                if pk in acc_v:
                    wdn += ((S["v"], acc_v[pk]),)
                for half, pp in ((0, ps["pAV"][:, 0:512]), (1, ps["pB"][:, 0:512])):
                    for hs in range(4):
                        t_mm(pp, AB[:, F0_ST + hs * 256 + ti * 128: F0_ST + hs * 256 + ti * 128 + 128],
                             AB[:, ws + 8192 + hs * 1024 + half * 512: ws + 8192 + hs * 1024 + half * 512 + 512],
                             start=(hs == 0), stop=(hs == 3),
                             waits=(wdn if (half == 0 and hs == 0) else ()),
                             inc=(hs == 3 and half == 1))
                tN = S["t"].n

                def f_acc(e, _tt=tt):
                    stc = STRM + _tt * 1024
                    e.scalar_tensor_tensor(B.AF[:, stc:stc + 512], B.ps["pAV"][:, 0:512],
                                           1.0 / 127.0, B.AF[:, stc:stc + 512],
                                           ALU.mult, ALU.add)
                    return e.scalar_tensor_tensor(B.AF[:, stc + 512:stc + 1024],
                                                  B.ps["pB"][:, 0:512], 1.0 / 127.0,
                                                  B.AF[:, stc + 512:stc + 1024],
                                                  ALU.mult, ALU.add)
                acc_v[(c, tq, ti)] = e_op("v", f_acc, waits=((S["t"], tN),), inc=True)
            stats_slot()
    while _statn[0] < len(stats):
        stats_slot()
    barrier("f0e")
    B._tail_v = {}
    for tt in range(16):
        _ln_tile(tt, 0, None, None, mode="ffn0")
    ng = B.qsem["g"].n
    em("g", lambda e, _h=B.qsem["g"].h, _n=ng: e.wait_ge(_h, _n))

    def f_cc(e):
        e.collective_compute(
            "AllGather", ALU.bypass,
            replica_groups=[[0, 1], [2, 3], [4, 5], [6, 7]],
            ins=[B.d["cc_in"].opt()],
            outs=[B.d["cc_out"].opt()]).then_inc(B.ccs.h, 1)
    em("g", f_cc)
    B.ccs.n += 1
    barrier("f0z")



def _exchange():
    AB, WB, ps, sb, AFl = B.AB, B.WB, B.ps, B.sb, B.AF
    S = B.S
    d = B.d
    nyx = B.qsem["yx"].n
    em("y", lambda e, _h=B.qsem["yx"].h, _n=nyx: e.wait_ge(_h, _n))
    barrier("xc")

    dma("y", AFl[:, TKA:TKA + T], d["tabk_a"])
    dma("y", AFl[:, TKB:TKB + T], d["tabk_b"])
    dma("y", AFl[:, TQA:TQA + TOWN], d["tabq_a"])
    dma("y", AFl[:, TQB:TQB + TOWN], d["tabq_b"])
    _load_w_mk("g", "l1_wq", 256, 8, W_C["l1_wq"])
    _load_w_mk("g", "l1_wqr", 128, 8, W_C["l1_wqr"])
    _load_w_mk("g", "l1_wqr_rot", 128, 8, W_C["l1_wqr_rot"])
    _load_w_mk("g", "l1_wkv", 256, 8, W_C["l1_wkv"])
    _load_w_mk("g", "l1_wkr", 128, 8, W_C["l1_wkr"])
    _load_w_mk("g", "l1_wkr_rot", 128, 8, W_C["l1_wkr_rot"])
    dma("g", AP(B.WB, W_BO, [[WB_COLS, 1], [1, DM]]), d["r_bo"])
    dma("g", AP(B.WB, W_B2, [[WB_COLS, 1], [1, DM]]), d["r_b2"])

    _tern_stats()

    B._xtev = {}
    for tt in range(16):
        stg = PSTG + (tt % 2) * 1024
        wld = ((S["v"], B._xtev[tt - 2]),) if (tt - 2) in B._xtev else ()
        qn = "yc0" if tt % 2 == 0 else "yc1"
        ld = dma(qn, AB[:, stg:stg + 1024], d["cc_in"][tt * 128:(tt + 1) * 128, :],
                 waits=wld)
        _xt_transpose(tt, stg, (qn, ld), own=True)
    for tt in range(16):
        stg = PSTG + (tt % 2) * 1024
        prev_key = (tt - 2 + 100) if tt >= 2 else (14 + tt)
        wv = B._xtev.get(prev_key)
        ld = _peer_tile_dma(tt, stg, wv)
        _xt_transpose(tt, stg, ("gp0" if tt % 2 == 0 else "gp1", ld), own=False)
    barrier("xce")
    e_op("v", lambda e: e.tensor_scalar(B.WB[0:1, W_B2:W_B2 + DM],
                                        B.WB[0:1, W_B2:W_B2 + DM], -1.0,
                                        None, ALU.mult), inc=True)


def _peer_tile_dma(tt, stg, wv):
    sem = B.qsem["gp0" if tt % 2 == 0 else "gp1"]
    S = B.S

    def f(e, _tt=tt, _stg=stg, _h=sem.h, _wv=wv):
        if _tt == 0:
            e.wait_ge(B.ccs.h, B.ccs.n)
        if _wv is not None:
            e.wait_ge(S["v"].h, _wv)
        with e.register(f"po{_tt}") as po:
            e.reg_load(po, B.sb["u32s"][0:1, 0:1])
            e.reg_add(po, po, _tt * 131072)
            e.dma_start(out=B.AB[:, _stg:_stg + 1024],
                        in_=AP(B.d["cc_out"].tensor, po, [[1024, 128], [1, 1024]])
                        ).then_inc(_h, 16)
    ccn = B.ccs.n
    em("g", f)
    sem.n += 16
    return sem.n


def _xt_transpose(tt, stg, ld, own):
    S = B.S
    q, ldn = ld
    jj = tt // 2
    pos = (2 * jj + (0 if own else 1)) * BLK + (tt % 2) * 128
    tN = 0
    vN = 0
    for half in range(2):
        for mb in range(4):
            mc = half * 4 + mb
            w0 = ()
            if mb == 0:
                if half == 0:
                    w0 = ((B.qsem[q], ldn),)
                if hasattr(B, "_pt_ev"):
                    w0 += ((S["v"], B._pt_ev),)
            tN = e_op("t", lambda e, _stg=stg, _mc=mc, _mb=mb, _hf=half:
                      e.transpose(B.ps["pT"][:, _hf * 512 + _mb * 128: _hf * 512 + (_mb + 1) * 128],
                                  B.AB[:, _stg + _mc * 128: _stg + _mc * 128 + 128],
                                  B.sb["ident"][:]),
                      waits=w0, inc=(mb == 3))

        def f_ev(e, _half=half, _pos=pos):
            src = AP(B.ps["pT"], _half * 512, [[1024, 128], [128, 4], [1, 128]])
            dst = AP(B.AB, L1_X2T + (_half * 4) * T + _pos,
                     [[AB_COLS, 128], [T, 4], [1, 128]])
            return e.tensor_copy(dst, src)
        vN = e_op("v", f_ev, waits=((S["t"], tN),), inc=True)
        B._pt_ev = vN
    B._xtev[tt + (0 if own else 100)] = vN


def _tern_stats_units():
    AFl, sb = B.AF, B.sb
    S = B.S
    d = B.d
    B._stv = {-1: (S["s"], S["s"].n)}
    units = []
    uu = 0
    for wi, nm in ((0, "l1_W1"), (1, "l1_W2")):
        for i in range(32):
            def emit(_uu=uu, _wi=wi, _nm=nm, _i=i):
                gs = F1_GS[0]
                wld = (B._stv[_uu - 1],)
                if _wi == 0:
                    src = d[_nm][_i * 128:(_i + 1) * 128, :]
                else:
                    rt, ct = _i // 4, _i % 4
                    src = AP(d[_nm].tensor, rt * 128 * FF1 + ct * 1024,
                             [[FF1, 128], [1, 1024]])
                ld = dma("ys", AFl[:, gs:gs + 1024], src, waits=wld)

                def f_r(e, _gs=gs, _u=_uu):
                    return e.activation(B.AF[:, _gs:_gs + 1024],
                                        B.AF[:, _gs:_gs + 1024], AF.Abs,
                                        accum_out=B.sb["asum"][:, _u:_u + 1])
                sN = e_op("s", f_r, waits=((B.qsem["ys"], ld),), inc=True)
                B._stv[_uu] = (S["s"], sN)
            units.append(emit)
            uu += 1
    return units


def _tern_stats():
    sb, ps = B.sb, B.ps
    S = B.S

    def f_colsum(e):
        vc(e, e.reduce_sum(B.sb["asum"][:, 64:65], B.sb["asum"][:, 0:32], AX.X))
        return e.reduce_sum(B.sb["asum"][:, 65:66], B.sb["asum"][:, 32:64], AX.X)
    vN = e_op("v", f_colsum, waits=(B._stv[63],), inc=True)
    for wi in range(2):
        t_mm(ps["pA"][0:1, wi:wi + 1], sb["asum"][:, 64 + wi:65 + wi],
             sb["ones_f"][:, 0:1], start=True, stop=True,
             waits=(((S["v"], vN),) if wi == 0 else ()), inc=(wi == 1))
    tN = S["t"].n

    def f_thr(e):
        vc(e, e.tensor_scalar(B.sb["tiny"][0:1, 6:8], B.ps["pA"][0:1, 0:2],
                              1.0 / (FF1 * DM), None, ALU.mult))
        return e.tensor_scalar(B.sb["tiny"][0:1, 6:8], B.sb["tiny"][0:1, 6:8],
                               1e-8, 0.5, ALU.max, ALU.mult)
    vN = e_op("v", f_thr, waits=((S["t"], tN),), inc=True)
    tN = t_mm(ps["pA"][:, 2:4], sb["ones_f"][0:1, :], sb["tiny"][0:1, 6:8],
              start=True, stop=True, waits=((S["v"], vN),), inc=True)

    def f_b(e):
        e.tensor_copy(B.sb["thr"][:, 0:1], B.ps["pA"][:, 2:3])
        e.tensor_scalar(B.sb["thr"][:, 1:2], B.ps["pA"][:, 2:3], -1.0, None, ALU.mult)
        e.tensor_copy(B.sb["thr"][:, 2:3], B.ps["pA"][:, 3:4])
        return e.tensor_scalar(B.sb["thr"][:, 3:4], B.ps["pA"][:, 3:4], -1.0,
                               None, ALU.mult)
    e_op("v", f_b, waits=((S["t"], tN),), inc=True)



def _l1_proj_common():
    AB, WB, ps, sb = B.AB, B.WB, B.ps, B.sb
    S = B.S
    barrier("l1p")
    prev = {}
    for u in range(16):
        dc, tt = u // 8, u % 8
        pp = ps["pA"][:, 0:512] if u % 2 == 0 else ps["pS0"][:, 0:512]
        w = ((S["s"], prev[u - 2]),) if u - 2 in prev else ()
        for mc in range(8):
            t_mm(pp, WB[:, W_C["l1_wkv"] + mc * 256 + dc * 128: W_C["l1_wkv"] + mc * 256 + dc * 128 + 128],
                 AB[:, L1_X2T + mc * T + tt * 512: L1_X2T + mc * T + tt * 512 + 512],
                 start=(mc == 0), stop=(mc == 7), waits=(w if mc == 0 else ()),
                 inc=(mc == 7))
        tN = S["t"].n
        prev[u] = e_op("s", lambda e, _pp=pp, _dc=dc, _tt=tt: e.activation(
            B.AB[:, L1_CKV + _dc * T + _tt * 512: L1_CKV + _dc * T + _tt * 512 + 512],
            _pp, AF.Gelu, bias=B.sb["b_bkv"][:, _dc:_dc + 1]),
            waits=((S["t"], tN),), inc=True)
    prev_v = {-2: prev[14], -1: prev[15]}
    for tt in range(8):
        if tt % 2 == 0:
            pA, pB = ps["pA"][:, 0:512], ps["pB"][:, 0:512]
        else:
            pA, pB = ps["pS0"][:, 0:512], ps["pS0"][:, 512:1024]
        w = ((S["s"], prev_v[tt - 2]),) if tt - 2 in prev_v else ()
        for mc in range(8):
            t_mm(pA, WB[:, W_C["l1_wkr"] + mc * 128: W_C["l1_wkr"] + mc * 128 + 128],
                 AB[:, L1_X2T + mc * T + tt * 512: L1_X2T + mc * T + tt * 512 + 512],
                 start=(mc == 0), stop=(mc == 7), waits=(w if mc == 0 else ()))
        for mc in range(8):
            t_mm(pB, WB[:, W_C["l1_wkr_rot"] + mc * 128: W_C["l1_wkr_rot"] + mc * 128 + 128],
                 AB[:, L1_X2T + mc * T + tt * 512: L1_X2T + mc * T + tt * 512 + 512],
                 start=(mc == 0), stop=(mc == 7), inc=(mc == 7))
        tN = S["t"].n

        def f_rope(e, _pA=pA, _pB=pB, _tt=tt):
            fsb = FS + (_tt % 2) * 1024
            t1 = B.AF[:, fsb:fsb + 512]
            t2 = B.AF[:, fsb + 512:fsb + 1024]
            e.scalar_tensor_tensor(t1, _pA, B.sb["b_bkr"][:, 0:1],
                                   B.AF[:, TKA + _tt * 512: TKA + _tt * 512 + 512],
                                   ALU.add, ALU.mult)
            vc(e, e.scalar_tensor_tensor(t2, _pB, B.sb["b_bkr_rot"][:, 0:1],
                                         B.AF[:, TKB + _tt * 512: TKB + _tt * 512 + 512],
                                         ALU.add, ALU.mult))
            return e.tensor_tensor(t1, t1, t2, ALU.add)
        vN = e_op("v", f_rope, waits=((S["t"], tN),), inc=True)
        prev_v[tt] = e_op("s", lambda e, _tt=tt: e.activation(
            B.AB[:, L1_KRG + _tt * 512: L1_KRG + _tt * 512 + 512],
            B.AF[:, FS + (_tt % 2) * 1024: FS + (_tt % 2) * 1024 + 512], AF.Gelu),
            waits=((S["v"], vN),), inc=True)
    kr_v_last = vN
    prevq = {}
    for u in range(16):
        dc, jb = u // 8, u % 8
        pp = ps["pA"][:, 0:256] if u % 2 == 0 else ps["pS0"][:, 0:256]
        if u - 2 in prevq:
            w = ((S["s"], prevq[u - 2]),)
        else:
            w = ((S["v"], kr_v_last),)
        pos = (2 * jb) * BLK
        for mc in range(8):
            t_mm(pp, WB[:, W_C["l1_wq"] + mc * 256 + dc * 128: W_C["l1_wq"] + mc * 256 + dc * 128 + 128],
                 AB[:, L1_X2T + mc * T + pos: L1_X2T + mc * T + pos + 256],
                 start=(mc == 0), stop=(mc == 7), waits=(w if mc == 0 else ()),
                 inc=(mc == 7))
        tN = S["t"].n
        prevq[u] = e_op("s", lambda e, _pp=pp, _dc=dc, _jb=jb: e.activation(
            B.WB[:, W_CQ + _dc * TOWN + _jb * 256: W_CQ + _dc * TOWN + _jb * 256 + 256],
            _pp, AF.Gelu, bias=B.sb["b_bq"][:, _dc:_dc + 1]),
            waits=((S["t"], tN),), inc=True)
    prevr = {-2: prevq[14], -1: prevq[15]}
    for jb in range(8):
        if jb % 2 == 0:
            pA, pB = ps["pA"][:, 0:256], ps["pB"][:, 0:256]
        else:
            pA, pB = ps["pS0"][:, 0:256], ps["pS0"][:, 512:768]
        w = ((S["s"], prevr[jb - 2]),) if jb - 2 in prevr else ()
        pos = (2 * jb) * BLK
        for mc in range(8):
            t_mm(pA, WB[:, W_C["l1_wqr"] + mc * 128: W_C["l1_wqr"] + mc * 128 + 128],
                 AB[:, L1_X2T + mc * T + pos: L1_X2T + mc * T + pos + 256],
                 start=(mc == 0), stop=(mc == 7), waits=(w if mc == 0 else ()))
        for mc in range(8):
            t_mm(pB, WB[:, W_C["l1_wqr_rot"] + mc * 128: W_C["l1_wqr_rot"] + mc * 128 + 128],
                 AB[:, L1_X2T + mc * T + pos: L1_X2T + mc * T + pos + 256],
                 start=(mc == 0), stop=(mc == 7), inc=(mc == 7))
        tN = S["t"].n

        def f_rope(e, _pA=pA, _pB=pB, _jb=jb):
            fsb = FS + (_jb % 2) * 1024
            t1 = B.AF[:, fsb:fsb + 256]
            t2 = B.AF[:, fsb + 512:fsb + 768]
            e.scalar_tensor_tensor(t1, _pA, B.sb["b_bqr"][:, 0:1],
                                   B.AF[:, TQA + _jb * 256: TQA + _jb * 256 + 256],
                                   ALU.add, ALU.mult)
            vc(e, e.scalar_tensor_tensor(t2, _pB, B.sb["b_bqr_rot"][:, 0:1],
                                         B.AF[:, TQB + _jb * 256: TQB + _jb * 256 + 256],
                                         ALU.add, ALU.mult))
            return e.tensor_tensor(t1, t1, t2, ALU.add)
        vN = e_op("v", f_rope, waits=((S["t"], tN),), inc=True)
        prevr[jb] = e_op("s", lambda e, _jb=jb: e.activation(
            B.WB[:, W_QRG + _jb * 256: W_QRG + _jb * 256 + 256],
            B.AF[:, FS + (_jb % 2) * 1024: FS + (_jb % 2) * 1024 + 256], AF.Gelu),
            waits=((S["v"], vN),), inc=True)
    B._l1p_v_end = S["v"].n
    barrier("l1pe")


def _l1_pass(p):
    AB, WB, ps, sb = B.AB, B.WB, B.ps, B.sb
    S = B.S
    d = B.d
    barrier(f"l1s{p}")
    pb = W_PS[p % 2]
    qp = "gw0" if p % 2 == 0 else "gw1"
    for cc in range(2):
        c = 2 * p + cc
        if c < 7:
            dma(qp, AP(WB, pb + cc * 256 + 0, [[WB_COLS, 128], [128, 2], [1, 128]]),
                AP(d["l1_wkc"].tensor, c * 128, [[896, 128], [896 * 128, 2], [1, 128]]))
            dma(qp, AP(WB, pb + 512 + cc * 256, [[WB_COLS, 128], [128, 2], [1, 128]]),
                AP(d["l1_wqc"].tensor, c * 128, [[896, 128], [896 * 128, 2], [1, 128]]))
    dma(qp, AP(WB, pb + 1024, [[WB_COLS, 128], [256, 2], [1, 256]]),
        AP(d["l1_wv"].tensor, 4 * p * 64, [[1024, 128], [131072, 2], [1, 256]]))
    if p == 0:
        _load_w_mk(qp, "l1_wo", 1024, 8, W_WO,
                   waits=((S["t"], S["t"].n),))
    em("t", lambda e, _h=B.qsem[qp].h, _n=B.qsem[qp].n: e.wait_ge(_h, _n))
    vpad = e_op("v", lambda e: e.memset(
        AP(B.AB, L1_V + 64, [[AB_COLS, 128], [65, 4 * 32], [1, 1]]), 1.0), inc=True)

    prev = {}
    u = 0
    for cc in range(2):
        c = 2 * p + cc
        if c == 7:
            continue
        for tt in range(8):
            pp = ps["pA"][:, 0:512] if u % 2 == 0 else ps["pS0"][:, 0:512]
            w = ((S["s"], prev[u - 2]),) if u - 2 in prev else ()
            for dc in range(2):
                t_mm(pp, WB[:, pb + cc * 256 + dc * 128: pb + cc * 256 + dc * 128 + 128],
                     AB[:, L1_CKV + dc * T + tt * 512: L1_CKV + dc * T + tt * 512 + 512],
                     start=(dc == 0), stop=(dc == 1), waits=(w if dc == 0 else ()),
                     inc=(dc == 1))
            tN = S["t"].n
            prev[u] = e_op("s", lambda e, _pp=pp, _cc=cc, _tt=tt, _c=c: e.activation(
                B.AB[:, L1_KT + _cc * T + _tt * 512: L1_KT + _cc * T + _tt * 512 + 512],
                _pp, AF.Gelu, bias=B.sb["b_bkc"][:, _c:_c + 1]),
                waits=((S["t"], tN),), inc=True)
            u += 1
    for cc in range(2):
        c = 2 * p + cc
        if c == 7:
            continue
        for jb in range(8):
            pp = ps["pA"][:, 0:256] if u % 2 == 0 else ps["pS0"][:, 0:256]
            w = ((S["s"], prev[u - 2]),) if u - 2 in prev else ()
            for dc in range(2):
                t_mm(pp, WB[:, pb + 512 + cc * 256 + dc * 128: pb + 512 + cc * 256 + dc * 128 + 128],
                     WB[:, W_CQ + dc * TOWN + jb * 256: W_CQ + dc * TOWN + jb * 256 + 256],
                     start=(dc == 0), stop=(dc == 1), waits=(w if dc == 0 else ()),
                     inc=(dc == 1))
            tN = S["t"].n
            prev[u] = e_op("s", lambda e, _pp=pp, _cc=cc, _jb=jb, _c=c: e.activation(
                B.AB[:, L1_QT + _cc * TOWN + _jb * 256: L1_QT + _cc * TOWN + _jb * 256 + 256],
                _pp, AF.Gelu, bias=B.sb["b_bqc"][:, _c:_c + 1]),
                waits=((S["t"], tN),), inc=True)
            u += 1
    prevv = {}
    for kt in range(32):
        if kt % 2 == 0:
            pt, off = ps["pB"], 0
        else:
            pt, off = ps["pS1"], 0
        pp = pt[:, off:off + 256]
        w = ((S["s"], prevv[kt - 2]),) if kt - 2 in prevv else ()
        if kt < 2:
            w += ((S["s"], prev[u - 1] if (u - 1) in prev else S["s"].n),)
        for dc in range(2):
            t_mm(pp, AB[:, L1_CKV + dc * T + kt * 128: L1_CKV + dc * T + kt * 128 + 128],
                 WB[:, pb + 1024 + dc * 256: pb + 1024 + dc * 256 + 256],
                 start=(dc == 0), stop=(dc == 1), waits=(w if dc == 0 else ()),
                 inc=(dc == 1))
        tN = S["t"].n
        rowext = 512 if kt % 2 == 0 else 1024

        def f_ev(e, _pt=pt, _kt=kt, _re=rowext):
            src = AP(_pt, 0, [[_re, 128], [64, 4], [1, 64]])
            dst = AP(B.AB, L1_V + _kt * VW1, [[AB_COLS, 128], [65, 4], [1, 64]])
            return e.activation(dst, src, AF.Copy)
        prevv[kt] = e_op("s", f_ev, waits=((S["t"], tN), (S["v"], vpad)), inc=True)

    _bound_l1(p)

    def KT_sl(h, kt):
        pr = slice((h % 2) * 64, (h % 2) * 64 + 64)
        if h // 2 == 7:
            return B.AB[pr, L1_KRG + kt * 128: L1_KRG + kt * 128 + 128]
        cc = h // 2 - 2 * p
        return B.AB[pr, L1_KT + cc * T + kt * 128: L1_KT + cc * T + kt * 128 + 128]

    def QT_sl(h, j):
        pr = slice((h % 2) * 64, (h % 2) * 64 + 64)
        if h // 2 == 7:
            return B.WB[pr, W_QRG + j * 256: W_QRG + j * 256 + 256]
        cc = h // 2 - 2 * p
        return B.AB[pr, L1_QT + cc * TOWN + j * 256: L1_QT + cc * TOWN + j * 256 + 256]

    def V_sl(h, kt):
        hh = h - 4 * p
        return B.AB[:, L1_V + kt * VW1 + hh * 65: L1_V + kt * VW1 + hh * 65 + 65]

    ctxd = {"KT_sl": KT_sl, "QT_sl": QT_sl, "V_sl": V_sl}
    if not hasattr(B, "_l1_pair_state"):
        B._l1_pair_state = {"prev_ev": {0: None, 1: None},
                            "ost_dma": {0: None, 1: None}, "fon": None}
    for hh in (0, 2):
        _attn_head_pair(4 * p + hh, ctxd)



def _ffn1():
    AB, WB, ps, sb, AFl = B.AB, B.WB, B.ps, B.sb, B.AF
    S = B.S
    d = B.d
    barrier("f1")
    B._f1par = 0
    tern_v = {}
    tp_t = {}
    gelu_s = {}
    acc_v = {}

    def mk_prep(c):
        ws = F1_WS[c % 2]
        units = []
        base = c * 12
        for wi, nm, nsub in ((0, "l1_W1", 4), (1, "l1_W2", 8)):
            for st_i in range(nsub):
                u_ = base + (st_i if wi == 0 else 4 + st_i)

                def unit(_c=c, _ws=ws, _wi=wi, _st=st_i, _u=u_, _nm=nm):
                    gs = F1_GS[0] if _u % 2 == 0 else FS
                    wld = ((S["v"], tern_v[_u - 2]),) if _u - 2 in tern_v else ()
                    if _wi == 0:
                        src = AP(d[_nm].tensor, (_c * 512 + _st * 128) * DM,
                                 [[DM, 128], [1, DM]])
                        wcols = 1024
                    else:
                        src = AP(d[_nm].tensor, _st * 128 * FF1 + _c * 512,
                                 [[FF1, 128], [1, 512]])
                        wcols = 512
                    qn = "ys" if _u % 2 == 0 else "yc0"
                    ld = dma(qn, AFl[:, gs:gs + wcols], src, waits=wld)
                    thr_c = 0 if _wi == 0 else 2
                    wtp = ((S["t"], tp_t[_u - 2]),) if _u - 2 in tp_t else ()

                    def f_tern(e, _gs=gs, _wc=wcols, _tc=thr_c):
                        vc(e, e.tensor_scalar(B.AB[:, F1_TS:F1_TS + _wc],
                                              B.AF[:, _gs:_gs + _wc],
                                              B.sb["thr"][:, _tc:_tc + 1], None,
                                              ALU.is_ge))
                        return e.scalar_tensor_tensor(
                            B.AB[:, F1_TS + 2048:F1_TS + 2048 + _wc],
                            B.AF[:, _gs:_gs + _wc], B.sb["thr"][:, _tc + 1:_tc + 2],
                            B.AB[:, F1_TS:F1_TS + _wc], ALU.is_le, ALU.subtract)
                    vN = e_op("v", f_tern, waits=((B.qsem[qn], ld),) + wtp, inc=True)
                    tern_v[_u] = vN
                    nb = 8 if _wi == 0 else 4
                    tN = 0
                    for bg in range(nb // 4):
                        par = B._f1par
                        B._f1par = 1 - B._f1par
                        w0 = ((S["v"], max(B._ptev.values())),)
                        if bg == 0:
                            w0 += ((S["v"], vN),)
                        for bb in range(4):
                            blk = bg * 4 + bb
                            tN = e_op("t", lambda e, _blk=blk, _bb=bb, _p=par:
                                      e.transpose(B.ps["pT"][:, _p * 512 + _bb * 128: _p * 512 + (_bb + 1) * 128],
                                                  B.AB[:, F1_TS + 2048 + _blk * 128: F1_TS + 2048 + _blk * 128 + 128],
                                                  B.sb["ident"][:]),
                                      waits=(w0 if bb == 0 else ()),
                                      inc=(bb == 3))
                        if _wi == 0:
                            def f_ev(e, _ws2=_ws, _st2=_st, _bg=bg, _p=par):
                                src = AP(B.ps["pT"], _p * 512, [[1024, 128], [128, 4], [1, 128]])
                                dst = AP(B.AB, _ws2 + (_bg * 4) * 512 + _st2 * 128,
                                         [[AB_COLS, 128], [512, 4], [1, 128]])
                                return e.tensor_copy(dst, src)
                        else:
                            def f_ev(e, _ws2=_ws, _st2=_st, _bg=bg, _p=par):
                                src = AP(B.ps["pT"], _p * 512, [[1024, 128], [128, 4], [1, 128]])
                                dst = AP(B.AB, _ws2 + 4096 + _st2 * 128,
                                         [[AB_COLS, 128], [1024, 4], [1, 128]])
                                return e.tensor_copy(dst, src)
                        B._ptev[par] = e_op("v", f_ev, waits=((S["t"], tN),), inc=True)
                    tp_t[_u] = tN
                units.append(unit)
        return units

    prep = [mk_prep(c) for c in range(8)]
    prep_i = [0] * 8
    f1pd = {}

    def prep_step(c):
        if c < 8 and prep_i[c] < len(prep[c]):
            prep[c][prep_i[c]]()
            prep_i[c] += 1
            if prep_i[c] == len(prep[c]):
                f1pd[c] = S["v"].n

    def prep_finish(c):
        while c < 8 and prep_i[c] < len(prep[c]):
            prep_step(c)

    for c in range(8):
        ws = F1_WS[c % 2]
        prep_finish(c)
        tp_v = f1pd[c]
        for tq in range(8):
            for fs in range(4):
                bank = fs % 2
                ph = ps["pS0"][:, bank * 512: bank * 512 + 256]
                w0 = ()
                if tq == 0 and fs == 0:
                    w0 = ((S["v"], tp_v),)
                key = (c, tq, fs - 2) if fs >= 2 else (c, tq - 1, fs + 2)
                if key in gelu_s:
                    w0 += ((S["s"], gelu_s[key]),)
                for mc in range(8):
                    t_mm(ph, AB[:, ws + (mc * 4 + fs) * 128: ws + (mc * 4 + fs) * 128 + 128],
                         AB[:, XTO + mc * 2048 + tq * 256: XTO + mc * 2048 + tq * 256 + 256],
                         start=(mc == 0), stop=(mc == 7),
                         waits=(w0 if mc == 0 else ()), inc=(mc == 7))
                tN = S["t"].n
                gelu_s[(c, tq, fs)] = e_op("s", lambda e, _ph=ph, _fs=fs, _c=c: e.activation(
                    B.AB[:, F1_HT + _fs * 256: F1_HT + (_fs + 1) * 256], _ph, AF.Gelu,
                    bias=B.sb["b_b1"][:, _c * 4 + _fs:_c * 4 + _fs + 1], scale=-1.0),
                    waits=((S["t"], tN),), inc=True)
                prep_step(c + 1)
            ht_s = S["s"].n
            for ti in range(2):
                tt = tq * 2 + ti
                wdn = ((S["s"], ht_s),)
                pk = (c, tq, ti - 1) if ti else ((c, tq - 1, 1) if tq else (c - 1, 7, 1))
                if pk in acc_v:
                    wdn += ((S["v"], acc_v[pk]),)
                for half, pp in ((0, ps["pAV"][:, 0:512]), (1, ps["pB"][:, 0:512])):
                    for fs in range(4):
                        laststop = (fs == 3 and c < 7)
                        t_mm(pp, AB[:, F1_HT + fs * 256 + ti * 128: F1_HT + fs * 256 + ti * 128 + 128],
                             AB[:, ws + 4096 + fs * 1024 + half * 512: ws + 4096 + fs * 1024 + half * 512 + 512],
                             start=(fs == 0), stop=laststop,
                             waits=(wdn if (half == 0 and fs == 0) else ()),
                             inc=(laststop and half == 1))
                    if c == 7:
                        t_mm(pp, sb["ones_b"][0:1, :],
                             B.WB[0:1, W_B2 + half * 512: W_B2 + half * 512 + 512],
                             start=False, stop=True, inc=(half == 1))
                tN = S["t"].n

                def f_acc(e, _tt=tt):
                    stc = STRM + _tt * 1024
                    e.tensor_tensor(B.AF[:, stc:stc + 512], B.AF[:, stc:stc + 512],
                                    B.ps["pAV"][:, 0:512], ALU.subtract)
                    return e.tensor_tensor(B.AF[:, stc + 512:stc + 1024],
                                           B.AF[:, stc + 512:stc + 1024],
                                           B.ps["pB"][:, 0:512], ALU.subtract)
                acc_v[(c, tq, ti)] = e_op("v", f_acc, waits=((S["t"], tN),), inc=True)
    barrier("f1e")


def _final_ln_out():
    B._tail_v = {}
    for tt in range(16):
        _ln_tile(tt, 0, None, None, mode="ffn1")
    em("y", lambda e, _h=B.qsem["y"].h, _n=B.qsem["y"].n: e.wait_ge(_h, _n))
    em("g", lambda e, _h=B.qsem["g"].h, _n=B.qsem["g"].n: e.wait_ge(_h, _n))


def kernel(**inputs):
    nc = build_program()
    in_maps = [make_core_inputs(inputs, c // 2, c % 2) for c in range(8)]
    res = run_bass_kernel_spmd(nc, in_maps, core_ids=list(range(8)))
    out = np.zeros((4, T, DM), dtype=np.float32)
    for c in range(8):
        s, r = c // 2, c % 2
        yc = res.results[c]["y"]
        for j in range(NBLK):
            g = 2 * j + r
            out[s, g * BLK:(g + 1) * BLK, :] = yc[j * BLK:(j + 1) * BLK, :]
    return out
